# version 10
# baseline (speedup 1.0000x reference)
import contextlib
import numpy as np

import concourse.bass as bass
import concourse.mybir as mybir
from concourse.bass_utils import run_bass_kernel_spmd
from concourse.aot_env import allow_ici_all_to_all

F32 = mybir.dt.float32
BF16 = mybir.dt.bfloat16
I32 = mybir.dt.int32
AF = mybir.ActivationFunctionType
ALU = mybir.AluOpType
AX = mybir.AxisListType

NCORES = 8
ENGS = ("sync", "scalar", "vector", "tensor", "gpsimd")


def ceil_div(a, b):
    return (a + b - 1) // b


class Prog:

    def __init__(self, name="prog", n_cc=24):
        self.nc = bass.Bass("TRN2", target_bir_lowering=False, debug=False)
        self.stack = contextlib.ExitStack()
        self.thunks = {e: [] for e in ENGS}
        self.count = {e: 0 for e in ENGS}
        self.dma_count = {e: 0 for e in ENGS}
        self.sems = {}
        for e in ENGS:
            self.sems["c_" + e] = self.stack.enter_context(self.nc.semaphore("c_" + e))
            self.sems["d_" + e] = self.stack.enter_context(self.nc.semaphore("d_" + e))
            self.sems["f_" + e] = self.stack.enter_context(self.nc.semaphore("f_" + e))
        self.fcount = {e: 0 for e in ENGS}
        for i in range(n_cc):
            self.sems[f"cc{i}"] = self.stack.enter_context(self.nc.semaphore(f"cc{i}"))
        self.cc_i = 0
        self.cc_tokens = []
        self.stage = ""
        self.stage_stack = None
        self._names = set()
        self._fences = {}

    def begin_stage(self, name):
        assert self.stage_stack is None
        self.stage = name + "_"
        self.stage_stack = contextlib.ExitStack()

    def end_stage(self):
        self.stage_stack.close()
        self.stage_stack = None
        self.stage = ""

    def _uname(self, n):
        assert n not in self._names, f"duplicate tensor name {n}"
        self._names.add(n)
        return n

    def dram_in(self, name, shape, dt=F32):
        return self.nc.dram_tensor(self._uname(name), list(shape), dt, kind="ExternalInput").ap()

    def dram_out(self, name, shape, dt=F32):
        return self.nc.dram_tensor(self._uname(name), list(shape), dt, kind="ExternalOutput").ap()

    def dram_tmp(self, name, shape, dt=F32, addr_space="Local"):
        return self.nc.dram_tensor(self._uname(name), list(shape), dt, kind="Internal",
                                   addr_space=addr_space).ap()

    def sbuf(self, name, shape, dt=F32):
        st = self.stage_stack if self.stage_stack is not None else self.stack
        return st.enter_context(
            self.nc.sbuf_tensor(self._uname(self.stage + name), list(shape), dt))

    def psum(self, name, shape, dt=F32):
        st = self.stage_stack if self.stage_stack is not None else self.stack
        return st.enter_context(
            self.nc.psum_tensor(self._uname(self.stage + name), list(shape), dt))

    def _fence(self, q):
        total = self.dma_count[q]
        key = (q, total)
        if key in self._fences:
            return self._fences[key]
        self.fcount[q] += 1
        tok = ("f_" + q, self.fcount[q])
        self.thunks[q].append(((("d_" + q, total),), lambda e: e.nop(), "f_" + q, 1))
        self._fences[key] = tok
        return tok

    def _resolve(self, wait, eng):
        out = []
        for w in wait:
            if w is None:
                continue
            sname, v = w
            if sname.startswith("d_"):
                q = sname[2:]
                if q == eng:
                    out.append((sname, self.dma_count[q]))
                else:
                    out.append(self._fence(q))
            else:
                out.append((sname, v))
        return tuple(out)

    def op(self, eng, fn, wait=(), inc=True):
        wait = self._resolve(wait, eng)
        if inc:
            self.count[eng] += 1
            tok = ("c_" + eng, self.count[eng])
            self.thunks[eng].append((wait, fn, "c_" + eng, 1))
            return tok
        self.thunks[eng].append((wait, fn, None, 0))
        return None

    def dma(self, eng, out, in_, wait=(), **kw):
        wait = self._resolve(wait, eng)
        self.dma_count[eng] += 16
        tok = ("d_" + eng, self.dma_count[eng])
        fn = lambda e: e.dma_start(out=out, in_=in_, **kw)
        self.thunks[eng].append((wait, fn, "d_" + eng, 16))
        return tok

    def collective(self, kind, ins, outs, wait=()):
        wait = self._resolve(wait, "gpsimd")
        sem = f"cc{self.cc_i}"
        self.cc_i += 1
        tok = (sem, 1)
        rg = [list(range(NCORES))]

        def fn(g, kind=kind, ins=ins, outs=outs):
            with allow_ici_all_to_all():
                return g.collective_compute(kind, ALU.bypass, replica_groups=rg,
                                            ins=ins, outs=outs)
        self.thunks["gpsimd"].append((wait, fn, sem, 1))
        self.cc_tokens.append(tok)
        return tok

    def barrier_tokens(self):
        toks = []
        for e in ENGS:
            if self.count[e]:
                toks.append(("c_" + e, self.count[e]))
            if self.dma_count[e]:
                toks.append(("d_" + e, self.dma_count[e]))
        toks += self.cc_tokens
        return toks

    def join(self, extra=()):
        toks = tuple(self.barrier_tokens()) + tuple(extra)
        return tuple(self.op(e, lambda en: en.nop(), wait=toks) for e in ENGS)

    def build(self):
        nc = self.nc
        prog_self = self
        with nc.Block() as block:
            for e in ENGS:
                thunks = self.thunks[e]
                if not thunks:
                    continue
                section = getattr(block, e)

                def body(engobj, items=thunks):
                    hw = {}
                    for waits, fn, semname, inc in items:
                        for (sname, val) in waits:
                            if hw.get(sname, 0) < val:
                                engobj.wait_ge(prog_self.sems[sname], val)
                                hw[sname] = val
                        inst = fn(engobj)
                        if semname is not None:
                            inst.then_inc(prog_self.sems[semname], inc)
                section(body)
        return nc

    def run(self, in_maps, **kw):
        nc = self.build()
        try:
            res = run_bass_kernel_spmd(nc, in_maps, core_ids=list(range(len(in_maps))), **kw)
        finally:
            self.stack.close()
        return res


class PsumPool:

    def __init__(self, p, name, n=8, shape=(128, 512), dt=F32):
        self.p = p
        self.tiles = [p.psum(f"{name}{i}", list(shape), dt) for i in range(n)]
        self.war = [None] * n
        self.i = 0
        self.n = n

    def get(self):
        i = self.i
        self.i = (self.i + 1) % self.n
        w = (self.war[i],) if self.war[i] is not None else ()
        return i, self.tiles[i], w

    def consumed(self, bank_id, tok):
        self.war[bank_id] = tok


def linear(p, *, n_k, n_m, T, x_tile, w_tile, pool, epilogue, TJ=512, wait=(),
           extra_mm=None, mw=128):
    toks = []
    n_j = ceil_div(T, TJ)
    n_extra = len(extra_mm) if extra_mm else 0
    for m in range(n_m):
        for j in range(n_j):
            t0, t1 = j * TJ, min((j + 1) * TJ, T)
            bank, ps, war = pool.get()
            last_tok = None
            for k in range(n_k):
                st = (k == 0)
                sp = (k == n_k - 1) and n_extra == 0
                is_last = sp

                def fn(t, m=m, k=k, t0=t0, t1=t1, ps=ps, st=st, sp=sp):
                    return t.matmul(ps[0:mw, 0:t1 - t0], w_tile(k, m), x_tile(k)[:, t0:t1],
                                    start=st, stop=sp)
                w = tuple(wait) + war if k == 0 else ()
                last_tok = p.op("tensor", fn, wait=w, inc=is_last)
            if extra_mm:
                for i, (lhsT_fn, rhs_fn) in enumerate(extra_mm):
                    sp = i == n_extra - 1

                    def fn(t, m=m, t0=t0, t1=t1, ps=ps, sp=sp, lf=lhsT_fn, rf=rhs_fn):
                        return t.matmul(ps[0:mw, 0:t1 - t0], lf(m), rf(m)[:, t0:t1],
                                        start=False, stop=sp)
                    last_tok = p.op("tensor", fn, inc=sp)
            etok = epilogue(m, j, t0, t1, ps, (last_tok,))
            pool.consumed(bank, etok)
            toks.append(etok)
    return toks


def act_epilogue(p, dst_fn, func=AF.Copy, bias_fn=None, scale=1.0, eng="scalar", mw=128):
    def ep(m, j, t0, t1, ps, wait):
        kw = {}
        if bias_fn is not None:
            kw["bias"] = bias_fn(m)
        if scale != 1.0:
            kw["scale"] = scale

        def fn(s, m=m, t0=t0, t1=t1, ps=ps, kw=kw):
            return s.activation(out=dst_fn(m, t0, t1), in_=ps[0:mw, 0:t1 - t0], func=func, **kw)
        return p.op(eng, fn, wait=wait)
    return ep


def chunks_of(T, TJ=512):
    return [(t, min(t + TJ, T)) for t in range(0, T, TJ)]



B = 2
LLOC = 1024
HALO = 3
TH = LLOC + HALO
TT = B * TH


def chunks_of(T, TJ=512):
    return [(t, min(t + TJ, T)) for t in range(0, T, TJ)]


def load_col_vec(p, dst, src_1d, nf=None):
    return p.dma("sync", dst[:, :], src_1d.rearrange("(m f) -> f m", f=128),
                 allow_slow_non_contiguous=True)


def emit_l0pre(p, io):
    D, NF = 1024, 8
    xh_d = io["xh"]
    win_d = io["W_in"]
    nw_d = io["norm_w"]
    cw_d = io["conv_w"]
    cb_d = io["conv_b"]
    wx_d = io["W_x"]
    res_d = io["res"]
    xc_d = io["xc"]
    dlt_d = io["dlt"]
    bm_d = io["Bm"]
    cm_d = io["Cm"]
    ab_d = io["ab"]

    x_bf = p.sbuf("x_bf", [128, NF * TT], BF16)
    sq_bf = p.sbuf("sq_bf", [128, NF * 512], BF16)
    w_sb = p.sbuf("w_sb", [128, NF * 2048], BF16)
    wx_sb = p.sbuf("wx_sb", [128, 16 * 96], BF16)
    gam = p.sbuf("gam", [128, NF], F32)
    cw_sb = p.sbuf("cw_sb", [128, 16 * 4], F32)
    cb_sb = p.sbuf("cb_sb", [128, 16], F32)
    ones_bf = p.sbuf("ones_bf", [128, 1], BF16)
    ones_row = p.sbuf("ones_row", [1, 128], F32)
    eps_sb = p.sbuf("eps_sb", [1, 1], F32)
    rows = p.sbuf("rows", [1, 4 * TT], F32)
    ab_sb = p.sbuf("ab_sb", [128, 2 * TT], F32)
    h_bf = p.sbuf("h_bf", [128, NF * TT], BF16)
    tmp_f = p.sbuf("tmp_f", [128, TT], F32)
    xpre_st = p.sbuf("xpre_st", [128, 2 * TT], F32)
    conv_st = p.sbuf("conv_st", [128, 2 * LLOC], F32)
    xc_st = p.sbuf("xc_st", [128, B * LLOC], BF16)
    res_st = p.sbuf("res_st", [128, TT], BF16)
    dlt_sb = p.sbuf("dlt_sb", [64, 512], BF16)
    bc_sb = p.sbuf("bc_sb", [32, 512], F32)

    pool = PsumPool(p, "ps", n=2)
    ps_xd = [p.psum(f"ps_xd{i}", [96, 512], F32) for i in range(B * 2)]
    ps_a = p.psum("ps_a", [128, 512], F32)
    ps_b = p.psum("ps_b", [128, 512], F32)

    xld = []
    for f in range(NF):
        for b in range(B):
            xld.append(p.dma("gpsimd", x_bf[:, f * TT + b * TH: f * TT + (b + 1) * TH],
                             xh_d[b, f * 128:(f + 1) * 128, :]))
    wld1 = []
    for f in range(NF):
        wld1.append(p.dma("gpsimd", w_sb[:, f * 2048:(f + 1) * 2048],
                          win_d[f * 128:(f + 1) * 128, 0:2048]))
    wxld = [p.dma("gpsimd", wx_sb[:, c * 96:(c + 1) * 96],
                  wx_d[c * 128:(c + 1) * 128, :]) for c in range(16)]
    cld = [load_col_vec(p, gam, nw_d),
           p.dma("sync", cw_sb[:, :].rearrange("p (m j) -> p m j", j=4),
                 cw_d.rearrange("(m f) j -> f m j", f=128),
                 allow_slow_non_contiguous=True),
           load_col_vec(p, cb_sb, cb_d)]
    t_ones = p.op("vector", lambda v: v.memset(ones_bf[:, :], 1.0))
    t_ones2 = p.op("vector", lambda v: v.memset(ones_row[:, :], 1.0))
    t_eps = p.op("vector", lambda v: v.memset(eps_sb[:, :], 1e-5))
    xld = tuple(xld)

    ev_prev = {}
    sq_prev = {}
    ev1p = ev2p = None
    for ci, (t0, t1) in enumerate(chunks_of(TT)):
        cw_ = t1 - t0
        tok1 = tok2 = None
        for f in range(NF):
            w_sq = xld + ((sq_prev[f],) if f in sq_prev else ())
            sqt = p.op("scalar", lambda s, f=f, t0=t0, t1=t1: s.activation(
                out=sq_bf[:, f * 512:f * 512 + (t1 - t0)],
                in_=x_bf[:, f * TT + t0:f * TT + t1], func=AF.Square), wait=w_sq)
            w1 = xld + (t_ones,) + ((ev1p,) if (f == 0 and ev1p) else ())
            tok1 = p.op("tensor", lambda t, f=f, t0=t0, t1=t1, cw_=cw_: t.matmul(
                ps_a[0:1, 0:cw_], ones_bf[:, :], x_bf[:, f * TT + t0:f * TT + t1],
                start=(f == 0), stop=(f == NF - 1)), wait=w1, inc=True)
            w2 = (sqt,) + ((ev2p,) if (f == 0 and ev2p) else ())
            tok2 = p.op("tensor", lambda t, f=f, cw_=cw_: t.matmul(
                ps_b[0:1, 0:cw_], ones_bf[:, :], sq_bf[:, f * 512:f * 512 + cw_],
                start=(f == 0), stop=(f == NF - 1)), wait=w2, inc=True)
            sq_prev[f] = tok2
        ev1p = p.op("vector", lambda v, t0=t0, t1=t1: v.tensor_copy(
            rows[0:1, 0 * TT + t0:0 * TT + t1], ps_a[0:1, 0:t1 - t0]), wait=(tok1,))
        ev2p = p.op("vector", lambda v, t0=t0, t1=t1: v.tensor_copy(
            rows[0:1, 1 * TT + t0:1 * TT + t1], ps_b[0:1, 0:t1 - t0]), wait=(tok2,))

    S = lambda i: rows[0:1, i * TT:(i + 1) * TT]
    rtok = (ev1p, ev2p, t_eps)
    r1 = p.op("vector", lambda v: v.tensor_single_scalar(S(2), S(0), 1.0 / D, ALU.mult), wait=rtok)
    r2 = p.op("vector", lambda v: v.tensor_single_scalar(S(3), S(1), 1.0 / D, ALU.mult), wait=rtok)
    r3 = p.op("vector", lambda v: v.tensor_mul(S(0), S(2), S(2)), wait=(r1,))
    r4 = p.op("vector", lambda v: v.tensor_sub(S(1), S(3), S(0)), wait=(r2, r3))
    r5 = p.op("scalar", lambda s: s.activation(out=S(0), in_=S(1), func=AF.Sqrt,
                                               bias=eps_sb[0:1, 0:1]), wait=(r4,))
    r6 = p.op("vector", lambda v: v.reciprocal(S(3), S(0)), wait=(r5,))
    ab_dma = []
    for b in range(B):
        ab_dma.append(p.dma("sync", ab_d[0, b, :],
                            rows[0:1, 3 * TT + b * TH + HALO:3 * TT + (b + 1) * TH], wait=(r6,)))
    r7 = p.op("vector", lambda v: v.tensor_mul(S(0), S(0), S(0)), wait=(r6,) + tuple(ab_dma))
    r8 = p.op("vector", lambda v: v.reciprocal(S(0), S(0)), wait=(r7,))
    r9 = p.op("vector", lambda v: v.tensor_mul(S(0), S(1), S(0)), wait=(r8,))
    r10 = p.op("scalar", lambda s: s.activation(out=S(0), in_=S(0), func=AF.Sqrt,
                                                bias=eps_sb[0:1, 0:1]), wait=(r9,))
    r11 = p.op("vector", lambda v: v.reciprocal(S(0), S(0)), wait=(r10,))
    r12 = p.op("vector", lambda v: v.tensor_mul(S(1), S(2), S(3)), wait=(r11,))
    r13 = p.op("vector", lambda v: v.tensor_single_scalar(S(1), S(1), -1.0, ALU.mult), wait=(r12,))
    for b in range(B):
        ab_dma.append(p.dma("sync", ab_d[1, b, :],
                            rows[0:1, 1 * TT + b * TH + HALO:1 * TT + (b + 1) * TH], wait=(r13,)))
    r14 = p.op("vector", lambda v: v.tensor_mul(S(2), S(3), S(0)), wait=(r13,))
    r15 = p.op("vector", lambda v: v.tensor_mul(S(3), S(1), S(0)),
               wait=(r14,) + tuple(ab_dma))

    bc_toks = []
    ea = eb = None
    for (t0, t1) in chunks_of(TT):
        tba = p.op("tensor", lambda t, t0=t0, t1=t1: t.matmul(
            ps_a[:, 0:t1 - t0], ones_row[:, :], rows[0:1, 2 * TT + t0:2 * TT + t1],
            start=True, stop=True), wait=(r14, ev1p, t_ones2) + ((ea,) if ea else ()))
        tbb = p.op("tensor", lambda t, t0=t0, t1=t1: t.matmul(
            ps_b[:, 0:t1 - t0], ones_row[:, :], rows[0:1, 3 * TT + t0:3 * TT + t1],
            start=True, stop=True), wait=(r15, ev2p) + ((eb,) if eb else ()))
        ea = p.op("vector", lambda v, t0=t0, t1=t1: v.tensor_copy(
            ab_sb[:, t0:t1], ps_a[:, 0:t1 - t0]), wait=(tba,))
        eb = p.op("vector", lambda v, t0=t0, t1=t1: v.tensor_copy(
            ab_sb[:, TT + t0:TT + t1], ps_b[:, 0:t1 - t0]), wait=(tbb,))
        bc_toks += [ea, eb]
    bc_toks = tuple(bc_toks)

    h_toks = []
    prev = None
    for f in range(NF):
        w = bc_toks + xld + (cld[0],) + ((prev,) if prev else ())
        t1_ = p.op("vector", lambda v, f=f: v.scalar_tensor_tensor(
            out=tmp_f[:, :], in0=x_bf[:, f * TT:(f + 1) * TT], scalar=gam[:, f:f + 1],
            in1=ab_sb[:, 0:TT], op0=ALU.mult, op1=ALU.mult), wait=w)
        t2_ = p.op("vector", lambda v, f=f: v.scalar_tensor_tensor(
            out=h_bf[:, f * TT:(f + 1) * TT], in0=ab_sb[:, TT:2 * TT],
            scalar=gam[:, f:f + 1], in1=tmp_f[:, :], op0=ALU.mult, op1=ALU.add), wait=(t1_,))
        prev = t2_
        h_toks.append(t2_)
    h_toks = tuple(h_toks)

    store_toks = []
    war_xpre = {}
    war_xc = {}
    xdbl_last = {}
    for m in range(16):
        ping = m % 2
        evs = []
        for (t0, t1) in chunks_of(TT):
            bank, ps, war = pool.get()
            tok = None
            for k in range(NF):
                wk = h_toks + tuple(wld1) + war + tuple(war_xpre.get(ping, ())) \
                    if k == 0 else ()

                def fn(t, m=m, k=k, t0=t0, t1=t1, ps=ps):
                    return t.matmul(ps[:, 0:t1 - t0],
                                    w_sb[:, k * 2048 + m * 128:k * 2048 + (m + 1) * 128],
                                    h_bf[:, k * TT + t0:k * TT + t1],
                                    start=(k == 0), stop=(k == NF - 1))
                tok = p.op("tensor", fn, wait=wk, inc=(k == NF - 1))
            ev = p.op("scalar", lambda s, t0=t0, t1=t1, ps=ps, ping=ping: s.activation(
                out=xpre_st[:, ping * TT + t0:ping * TT + t1], in_=ps[:, 0:t1 - t0],
                func=AF.Copy), wait=(tok,))
            evs.append(ev)
            pool.consumed(bank, ev)
        conv_toks = []
        xc_ready = []
        for b in range(B):
            base = ping * TT + b * TH
            cbank = b
            c_ops = []
            w0 = tuple(evs) + tuple(cld[1:]) + tuple(war_xc.get(0, ()))
            c1 = p.op("vector", lambda v, m=m, base=base, cbank=cbank: v.tensor_scalar(
                out=conv_st[:, cbank * LLOC:(cbank + 1) * LLOC],
                in0=xpre_st[:, base + 0:base + 0 + LLOC],
                scalar1=cw_sb[:, m * 4 + 0:m * 4 + 1], scalar2=cb_sb[:, m:m + 1],
                op0=ALU.mult, op1=ALU.add), wait=w0)
            prev_c = c1
            for j in range(1, 4):
                src = cbank if (j % 2 == 0) else (2 + 0)
                def fnc(v, m=m, j=j, base=base, cbank=cbank, prev_j=j - 1):
                    dst_is_tmp = (j % 2 == 1)
                    dst = tmp_f[:, cbank * LLOC:(cbank + 1) * LLOC] if dst_is_tmp \
                        else conv_st[:, cbank * LLOC:(cbank + 1) * LLOC]
                    srcb = conv_st[:, cbank * LLOC:(cbank + 1) * LLOC] if dst_is_tmp \
                        else tmp_f[:, cbank * LLOC:(cbank + 1) * LLOC]
                    return v.scalar_tensor_tensor(
                        out=dst, in0=xpre_st[:, base + j:base + j + LLOC],
                        scalar=cw_sb[:, m * 4 + j:m * 4 + j + 1], in1=srcb,
                        op0=ALU.mult, op1=ALU.add)
                prev_c = p.op("vector", fnc, wait=(prev_c,))
            conv_toks.append(prev_c)
            sil = p.op("scalar", lambda s, b=b, cbank=cbank: s.activation(
                out=xc_st[:, b * LLOC:(b + 1) * LLOC],
                in_=tmp_f[:, cbank * LLOC:(cbank + 1) * LLOC], func=AF.Silu),
                wait=(prev_c,) + tuple(war_xc.get(0, ())))
            xc_ready.append(sil)
        war_xpre[ping] = tuple(conv_toks)
        new_war = []
        for b in range(B):
            dmat = p.dma("sync", xc_d[b, m * 128:(m + 1) * 128, :],
                         xc_st[:, b * LLOC:(b + 1) * LLOC],
                         wait=(xc_ready[b],))
            new_war.append(dmat)
            for jj in range(2):
                def fx(t, m=m, b=b, jj=jj, ping=ping):
                    return t.matmul(
                        ps_xd[b * 2 + jj][:, :],
                        wx_sb[:, m * 96:(m + 1) * 96],
                        xc_st[:, b * LLOC + jj * 512:b * LLOC + (jj + 1) * 512],
                        start=(m == 0), stop=(m == 15), skip_group_check=True)
                xt = p.op("tensor", fx, wait=(xc_ready[b],), inc=True)
                new_war.append(xt)
                xdbl_last[(b, jj)] = xt
        war_xc[0] = tuple(new_war)
        store_toks += new_war

    zx_done = []
    wld2 = []
    w2wait = tuple(tok for tok in store_toks if tok is not None) + h_toks
    for f in range(NF):
        wld2.append(p.dma("gpsimd", w_sb[:, f * 2048:(f + 1) * 2048],
                          win_d[f * 128:(f + 1) * 128, 2048:4096], wait=w2wait))
    res_dma = []
    war_res = {}
    for m in range(16):
        ping = m % 2
        evs = []
        for (t0, t1) in chunks_of(TT):
            bank, ps, war = pool.get()
            tok = None
            for k in range(NF):
                wk = tuple(wld2) + war + tuple(war_res.get(ping, ())) if k == 0 else ()

                def fn(t, m=m, k=k, t0=t0, t1=t1, ps=ps):
                    return t.matmul(ps[:, 0:t1 - t0],
                                    w_sb[:, k * 2048 + m * 128:k * 2048 + (m + 1) * 128],
                                    h_bf[:, k * TT + t0:k * TT + t1],
                                    start=(k == 0), stop=(k == NF - 1))
                tok = p.op("tensor", fn, wait=wk, inc=(k == NF - 1))
            ev = p.op("scalar", lambda s, t0=t0, t1=t1, ps=ps: s.activation(
                out=res_st[:, t0:t1], in_=ps[:, 0:t1 - t0],
                func=AF.Copy), wait=(tok,) + tuple(war_res.get(0, ())))
            evs.append(ev)
            pool.consumed(bank, ev)
        drs = []
        for b in range(B):
            dr = p.dma("sync", res_d[b, m * 128:(m + 1) * 128, :],
                       res_st[:, b * TH + HALO:(b + 1) * TH],
                       wait=tuple(evs))
            drs.append(dr)
            res_dma.append(dr)
        war_res[0] = tuple(drs)

    fin_dma = list(res_dma) + ab_dma
    war_small = ()
    for b in range(B):
        for jj in range(2):
            wt = (xdbl_last[(b, jj)],) + war_small
            e1 = p.op("scalar", lambda s, b=b, jj=jj: s.activation(
                out=dlt_sb[0:64, :], in_=ps_xd[b * 2 + jj][0:64, :], func=AF.Copy), wait=wt)
            e2 = p.op("vector", lambda v, b=b, jj=jj: v.tensor_copy(
                bc_sb[0:32, :], ps_xd[b * 2 + jj][64:96, :]), wait=wt)
            d1 = p.dma("sync", dlt_d[b, :, jj * 512:(jj + 1) * 512], dlt_sb[0:64, :], wait=(e1,))
            d2 = p.dma("gpsimd", bm_d[b, :, jj * 512:(jj + 1) * 512], bc_sb[0:16, :], wait=(e2,))
            d3 = p.dma("gpsimd", cm_d[b, :, jj * 512:(jj + 1) * 512], bc_sb[16:32, :], wait=(e2,))
            war_small = (d1, d2, d3)
            fin_dma += [d1, d2, d3]
    fin_dma += [t for t in store_toks if t is not None and t[0].startswith("d_")]
    p.op("sync", lambda s: s.nop(), wait=tuple(fin_dma), inc=False)


def emit_l0scan(p, io):
    S = 8192
    HF = 4096
    xc_d = io["xc"]
    dlt_d = io["dlt"]
    wdt_d = io["W_dt"]
    bdt_d = io["b_dt"]
    a_d = io["A"]
    bm_d = io["Bm"]
    cm_d = io["Cm"]
    y_d = io["y"]

    dlt_sb = p.sbuf("dlt_sb", [64, B * S], BF16)
    wdt_sb = p.sbuf("wdt_sb", [64, 256], BF16)
    bdt_sb = p.sbuf("bdt_sb", [128, 2], F32)
    asb = p.sbuf("asb", [128, 2 * 16], F32)
    one_sb = p.sbuf("one_sb", [128, 1], F32)
    delta = p.sbuf("delta", [128, HF], F32)
    dx = p.sbuf("dx", [128, HF], F32)
    xc_sb = p.sbuf("xc_sb", [128, HF], BF16)
    brep = p.sbuf("brep", [128, 2 * HF], BF16)
    crep = p.sbuf("crep", [128, 2 * HF], BF16)
    a_t = p.sbuf("a_t", [128, HF], F32)
    u_t = p.sbuf("u_t", [128, 2 * HF], F32)
    h_t = p.sbuf("h_t", [128, HF], F32)
    tmp = p.sbuf("tmp", [128, HF], F32)
    yacc = p.sbuf("yacc", [128, HF], F32)
    carry = p.sbuf("carry", [128, 16], F32)
    pool = PsumPool(p, "ps", n=4)

    ld0 = [p.dma("gpsimd", dlt_sb[:, b * S:(b + 1) * S], dlt_d[b]) for b in range(B)]
    ld0.append(p.dma("gpsimd", wdt_sb[:, :], wdt_d))
    ld0.append(load_col_vec(p, bdt_sb, bdt_d))
    ld0.append(p.dma("sync", asb[:, :].rearrange("p (c n) -> p c n", c=2),
                     a_d.rearrange("(c p) n -> p c n", p=128),
                     allow_slow_non_contiguous=True))
    t_one = p.op("vector", lambda v: v.memset(one_sb[:, :], 1.0))
    ld0 = tuple(ld0)

    last_b = {0: (), 1: ()}
    last_c = {0: (), 1: ()}
    BSL = lambda sl: brep[:, sl * HF:(sl + 1) * HF]
    CSL = lambda sl: crep[:, sl * HF:(sl + 1) * HF]
    lb_t = {}
    lc_t = {}
    sc_slot = {0: None, 1: None}
    prev_at = prev_ut = prev_yt = prev_tt = prev_sc = None
    prev_ctok = None
    y_dma_prev = ()
    for ct in range(2):
        for b in range(B):
            carry_toks = {}
            for half in range(2):
                t0 = half * HF
                ev_toks = []
                for jj in range(8):
                    bank, ps, war = pool.get()
                    mm = p.op("tensor", lambda t, ct=ct, b=b, t0=t0, jj=jj, ps=ps:
                              t.matmul(ps[:, :],
                                       wdt_sb[0:64, ct * 128:(ct + 1) * 128],
                                       dlt_sb[0:64, b * S + t0 + jj * 512:
                                              b * S + t0 + (jj + 1) * 512],
                                       start=True, stop=True),
                              wait=ld0 + war)
                    wtmp = (mm,) + ((prev_yt,) if jj == 0 and prev_yt else ())
                    ev = p.op("vector", lambda v, jj=jj, ps=ps, ct=ct: v.tensor_scalar_add(
                        tmp[:, jj * 512:(jj + 1) * 512], ps[:, :],
                        bdt_sb[:, ct:ct + 1]), wait=wtmp)
                    pool.consumed(bank, ev)
                    ev_toks.append(ev)
                ev_toks = tuple(ev_toks)
                s0a = p.op("vector", lambda v: v.tensor_single_scalar(
                    dx[:, :], tmp[:, :], -1.0, ALU.mult),
                    wait=ev_toks + ((prev_ut,) if prev_ut else ()))
                s1 = p.op("vector", lambda v: v.tensor_max(
                    dx[:, :], dx[:, :], tmp[:, :]), wait=(s0a,))
                s2 = p.op("scalar", lambda s: s.activation(
                    out=dx[:, :], in_=dx[:, :], func=AF.Exp, scale=-1.0), wait=(s1,))
                s3 = p.op("scalar", lambda s: s.activation(
                    out=dx[:, :], in_=dx[:, :], func=AF.Ln,
                    bias=one_sb[:, 0:1]), wait=(s2, t_one))
                s4 = p.op("vector", lambda v: v.tensor_relu(delta[:, :], tmp[:, :]),
                          wait=ev_toks + (s3,) + ((prev_at,) if prev_at else ()))
                s5 = p.op("vector", lambda v: v.tensor_add(delta[:, :], delta[:, :], dx[:, :]),
                          wait=(s3, s4))
                lxc = p.dma("gpsimd", xc_sb[:, :], xc_d[b, ct * 128:(ct + 1) * 128, t0:t0 + HF],
                            wait=y_dma_prev)
                s6 = p.op("vector", lambda v: v.tensor_mul(dx[:, :], delta[:, :], xc_sb[:, :]),
                          wait=(s5, lxc))
                for n0 in range(2):
                    lb_t[n0] = p.dma("sync", BSL(n0),
                                     bm_d[b, n0:n0 + 1, t0:t0 + HF].broadcast_to([128, HF]),
                                     wait=last_b[n0])
                    lc_t[n0] = p.dma("scalar", CSL(n0),
                                     cm_d[b, n0:n0 + 1, t0:t0 + HF].broadcast_to([128, HF]),
                                     wait=last_c[n0])
                ytoks = []
                for n in range(16):
                    sl = n % 2
                    at = p.op("scalar", lambda s, ct=ct, n=n: s.activation(
                        out=a_t[:, :], in_=delta[:, :], func=AF.Exp,
                        scale=asb[:, ct * 16 + n:ct * 16 + n + 1]),
                        wait=(s5,) + ((prev_sc,) if prev_sc else ()))
                    prev_at = at
                    ut = p.op("gpsimd", lambda g, sl=sl: g.tensor_mul(
                        u_t[:, sl * HF:(sl + 1) * HF], dx[:, :], BSL(sl)),
                        wait=(s6, lb_t[n]) +
                        ((sc_slot[sl],) if sc_slot[sl] else ()))
                    last_b[sl] = (ut,)
                    prev_ut = ut
                    if half == 0:
                        init = 0.0
                        iw = ()
                    else:
                        init = carry[:, n:n + 1]
                        iw = (carry_toks[n],)
                    sc = p.op("vector", lambda v, init=init, sl=sl: v.tensor_tensor_scan(
                        h_t[:, :], a_t[:, :], u_t[:, sl * HF:(sl + 1) * HF],
                        init, ALU.mult, ALU.add),
                        wait=(at, ut) + iw +
                        tuple(t for t in (prev_tt, prev_ctok) if t))
                    sc_slot[sl] = sc
                    prev_sc = sc
                    if half == 0:
                        ctok = p.op("vector", lambda v, n=n: v.tensor_copy(
                            carry[:, n:n + 1], h_t[:, HF - 1:HF]), wait=(sc,))
                        carry_toks[n] = ctok
                        prev_ctok = ctok
                    else:
                        prev_ctok = None
                    if n == 0:
                        yt = p.op("vector", lambda v, sl=sl: v.tensor_mul(
                            yacc[:, :], h_t[:, :], CSL(sl)),
                            wait=(sc, lc_t[n]) + y_dma_prev)
                        last_c[sl] = (yt,)
                        prev_tt = yt
                    else:
                        tt_ = p.op("vector", lambda v, sl=sl: v.tensor_mul(
                            tmp[:, :], h_t[:, :], CSL(sl)),
                            wait=(sc, lc_t[n]) + ((prev_yt,) if prev_yt else ()))
                        last_c[sl] = (tt_,)
                        prev_tt = tt_
                        yt = p.op("vector", lambda v: v.tensor_add(
                            yacc[:, :], yacc[:, :], tmp[:, :]), wait=(tt_,))
                    prev_yt = yt
                    ytoks.append(yt)
                    if n + 2 < 16:
                        lb_t[n + 2] = p.dma("sync", BSL(sl),
                                            bm_d[b, n + 2:n + 3, t0:t0 + HF]
                                            .broadcast_to([128, HF]), wait=last_b[sl])
                        lc_t[n + 2] = p.dma("scalar", CSL(sl),
                                            cm_d[b, n + 2:n + 3, t0:t0 + HF]
                                            .broadcast_to([128, HF]), wait=last_c[sl])
                yd = p.dma("gpsimd", y_d[b, ct * 128:(ct + 1) * 128, t0:t0 + HF], yacc[:, :],
                           wait=(ytoks[-1],))
                y_dma_prev = (yd, ytoks[-1])
    p.op("sync", lambda s: s.nop(), wait=y_dma_prev, inc=False)


def emit_l0post(p, io):
    S = LLOC
    T = B * S
    x_d = io["x"]
    ab_d = io["ab"]
    y_dd = io["y"]
    xc_d = io["xc"]
    res_d = io["res"]
    dcol_d = io["D"]
    wout_d = io["W_out"]
    s0_d = io["s0"]

    wo_sb = p.sbuf("wo_sb", [128, 16 * 1024], BF16)
    dcol = p.sbuf("dcol", [128, 16], F32)
    ab_sb = p.sbuf("ab_sb", [128, 2 * T], F32)
    rows = p.sbuf("rows", [1, 2 * T], F32)
    ones_row = p.sbuf("ones_row", [1, 128], F32)
    y_t = p.sbuf("y_t", [128, 2 * 512], BF16)
    xc_t = p.sbuf("xc_t", [128, 2 * 512], BF16)
    res_t = p.sbuf("res_t", [128, 2 * 512], BF16)
    t1_f = p.sbuf("t1_f", [128, 512], F32)
    srs_f = p.sbuf("srs_f", [128, 512], F32)
    u_bf = p.sbuf("u_bf", [128, 2 * 512], BF16)
    x0_t = p.sbuf("x0_t", [128, 512], F32)
    s0_t = p.sbuf("s0_t", [128, 512], F32)
    out_ps = [p.psum(f"ops{i}", [128, 512], F32) for i in range(8)]

    wld = [p.dma("gpsimd", wo_sb[:, c * 1024:(c + 1) * 1024],
                 wout_d[c * 128:(c + 1) * 128, :]) for c in range(16)]
    wld.append(load_col_vec(p, dcol, dcol_d))
    rld = [p.dma("sync", rows[0:1, b * S:(b + 1) * S], ab_d[0, b, :]) for b in range(B)]
    rld += [p.dma("sync", rows[0:1, T + b * S:T + (b + 1) * S], ab_d[1, b, :]) for b in range(B)]
    t_ones = p.op("vector", lambda v: v.memset(ones_row[:, :], 1.0))

    bc_toks = []
    ea = eb = None
    for (t0, t1) in chunks_of(T):
        ta = p.op("tensor", lambda t, t0=t0, t1=t1: t.matmul(
            out_ps[0][:, 0:t1 - t0], ones_row[:, :], rows[0:1, t0:t1],
            start=True, stop=True), wait=tuple(rld) + (t_ones,) + ((ea,) if ea else ()))
        tb = p.op("tensor", lambda t, t0=t0, t1=t1: t.matmul(
            out_ps[1][:, 0:t1 - t0], ones_row[:, :], rows[0:1, T + t0:T + t1],
            start=True, stop=True), wait=(t_ones,) + ((eb,) if eb else ()))
        ea = p.op("vector", lambda v, t0=t0, t1=t1: v.tensor_copy(
            ab_sb[:, t0:t1], out_ps[0][:, 0:t1 - t0]), wait=(ta,))
        eb = p.op("vector", lambda v, t0=t0, t1=t1: v.tensor_copy(
            ab_sb[:, T + t0:T + t1], out_ps[1][:, 0:t1 - t0]), wait=(tb,))
        bc_toks += [ea, eb]
    bc_toks = tuple(bc_toks)

    war_u = {}
    war_ld = {}
    war_out = {}
    s0_dma = []
    prev_tu = ()
    prev_s0 = ()
    for j in range(4):
        b, sj = divmod(j * 512, S)
        mm_last = [None] * 8
        for ct in range(16):
            ping = ct % 2
            lw = tuple(war_ld.get(ping, ()))
            l1 = p.dma("gpsimd", y_t[:, ping * 512:(ping + 1) * 512],
                       y_dd[b, ct * 128:(ct + 1) * 128, sj:sj + 512], wait=lw)
            l2 = p.dma("gpsimd", xc_t[:, ping * 512:(ping + 1) * 512],
                       xc_d[b, ct * 128:(ct + 1) * 128, sj:sj + 512], wait=lw)
            l3 = p.dma("scalar", res_t[:, ping * 512:(ping + 1) * 512],
                       res_d[b, ct * 128:(ct + 1) * 128, sj:sj + 512], wait=lw)
            wu = tuple(war_u.get(ping, ())) + prev_tu
            tt1 = p.op("vector", lambda v, ct=ct, ping=ping: v.scalar_tensor_tensor(
                out=t1_f[:, :], in0=xc_t[:, ping * 512:(ping + 1) * 512],
                scalar=dcol[:, ct:ct + 1], in1=y_t[:, ping * 512:(ping + 1) * 512],
                op0=ALU.mult, op1=ALU.add), wait=(l1, l2) + wu + tuple(wld[-1:]))
            ts_ = p.op("scalar", lambda s, ping=ping: s.activation(
                out=srs_f[:, :], in_=res_t[:, ping * 512:(ping + 1) * 512],
                func=AF.Silu), wait=(l3,) + wu)
            tu = p.op("vector", lambda v, ping=ping: v.tensor_mul(
                u_bf[:, ping * 512:(ping + 1) * 512], t1_f[:, :], srs_f[:, :]),
                wait=(tt1, ts_))
            prev_tu = (tu,)
            war_ld[ping] = (tt1, ts_)
            mts = []
            for mt in range(8):
                def fm(t, ct=ct, mt=mt, ping=ping):
                    return t.matmul(out_ps[mt][:, :],
                                    wo_sb[:, ct * 1024 + mt * 128:ct * 1024 + (mt + 1) * 128],
                                    u_bf[:, ping * 512:(ping + 1) * 512],
                                    start=(ct == 0), stop=(ct == 15), skip_group_check=True)
                w_mm = (tu,)
                if ct == 0:
                    w_mm = w_mm + (bc_toks if j == 0 else ()) + tuple(war_out.get(mt, ()))
                mts.append(p.op("tensor", fm, wait=w_mm, inc=(ct == 15 or mt == 7)))
            war_u[ping] = (mts[7],)
            if ct == 15:
                mm_last = mts
        for f in range(8):
            lx = p.dma("sync", x0_t[:, :], x_d[b, f * 128:(f + 1) * 128, sj:sj + 512],
                       wait=prev_s0)
            e1 = p.op("vector", lambda v, f=f, j=j: v.scalar_tensor_tensor(
                out=s0_t[:, :], in0=x0_t[:, :], scalar=1.0,
                in1=ab_sb[:, j * 512:(j + 1) * 512], op0=ALU.bypass, op1=ALU.mult),
                wait=(lx, mm_last[f]) + prev_s0)
            e2 = p.op("vector", lambda v, j=j: v.tensor_add(
                s0_t[:, :], s0_t[:, :], ab_sb[:, T + j * 512:T + (j + 1) * 512]), wait=(e1,))
            e3 = p.op("vector", lambda v, f=f: v.tensor_add(
                s0_t[:, :], s0_t[:, :], out_ps[f][:, :]), wait=(e2,))
            sd = p.dma("gpsimd", s0_d[b, f * 128:(f + 1) * 128, sj:sj + 512], s0_t[:, :],
                       wait=(e3,))
            prev_s0 = (sd, e3)
            war_out[f] = (e3,)
            s0_dma.append(sd)
    p.op("sync", lambda s: s.nop(), wait=tuple(s0_dma), inc=False)


def emit_l0ffn(p, io):
    S = LLOC
    T = B * S
    x_d = io["x"]
    ab_d = io["ab"]
    s0_d = io["s0"]
    gw_d = io["gate_w"]
    w1_d = io["w1"]
    b1_d = io["b1"]
    w2_d = io["w2"]
    b2_d = io["b2"]
    sw1_d = io["sw1"]
    sb1_d = io["sb1"]
    sw2_d = io["sw2"]
    sb2_d = io["sb2"]
    x1_d = io["x1"]
    halo_d = io["halo_in"]

    xn_bf = p.sbuf("xn_bf", [128, 8 * T], BF16)
    xm_bf = p.sbuf("xm_bf", [128, 8 * 512], BF16)
    w1_sb = p.sbuf("w1_sb", [128, 8 * 2048], BF16)
    w2_sb = p.sbuf("w2_sb", [128, 16 * 1024], BF16)
    h_bf = p.sbuf("h_bf", [128, 16 * 512], BF16)
    y_acc = p.sbuf("y_acc", [128, 8 * T], BF16)
    ab_sb = p.sbuf("ab_sb", [128, 2 * T], F32)
    rows = p.sbuf("rows", [1, 3 * T], F32)
    ones_row = p.sbuf("ones_row", [1, 128], F32)
    gw_sb = p.sbuf("gw_sb", [128, 8 * 2], BF16)
    b1_sb = p.sbuf("b1_sb", [128, 3 * 32], F32)
    bc_sb = p.sbuf("bc_sb", [128, 8], F32)
    m_bf = p.sbuf("m_bf", [128, 2 * T], BF16)
    x0_t = p.sbuf("x0_t", [128, 512], F32)
    x1_t = p.sbuf("x1_t", [128, 512], F32)
    s0_t = p.sbuf("s0_t", [128, 512], BF16)
    pool = PsumPool(p, "ps", n=6)
    ps_r = p.psum("ps_r", [128, 512], F32)

    wldA = [load_col_vec(p, bc_sb, sb2_d)]
    b2ld = [p.dma("sync", rows[0:1, 0:1024], b2_d[0, :]),
            p.dma("sync", rows[0:1, 1024:2048], b2_d[1, :])]
    gld = [p.dma("gpsimd", gw_sb[:, :].rearrange("f (k e) -> f k e", e=2)[:, :, e],
                 gw_d[e].rearrange("(k f) -> f k", f=128), allow_slow_non_contiguous=True)
           for e in range(2)]
    for e in range(2):
        gld.append(p.dma("sync", b1_sb[:, e * 32:(e + 1) * 32],
                         b1_d[e].rearrange("(m f) -> f m", f=128),
                         allow_slow_non_contiguous=True))
    gld.append(p.dma("sync", b1_sb[:, 64:96], sb1_d.rearrange("(m f) -> f m", f=128),
                     allow_slow_non_contiguous=True))
    rlda = [p.dma("sync", rows[0:1, 2 * T + b * S:2 * T + (b + 1) * S], ab_d[0, b, :])
            for b in range(B)]
    t_ones = p.op("vector", lambda v: v.memset(ones_row[:, :], 1.0))

    b2c = p.sbuf("b2c", [128, 16], F32)
    c1 = p.dma("sync", b2c[:, 0:8], b2_d[0].rearrange("(m f) -> f m", f=128),
               allow_slow_non_contiguous=True)
    c2 = p.dma("sync", b2c[:, 8:16], b2_d[1].rearrange("(m f) -> f m", f=128),
               allow_slow_non_contiguous=True)
    a1 = p.op("vector", lambda v: v.tensor_add(bc_sb[:, :], bc_sb[:, :], b2c[:, 0:8]),
              wait=(c1,) + tuple(wldA))
    a2 = p.op("vector", lambda v: v.tensor_add(bc_sb[:, :], bc_sb[:, :], b2c[:, 8:16]),
              wait=(c2, a1))
    bias_tok = (a2,)

    bc_toks = []
    ea = None
    for (t0, t1) in chunks_of(T):
        ta = p.op("tensor", lambda t, t0=t0, t1=t1: t.matmul(
            ps_r[:, 0:t1 - t0], ones_row[:, :], rows[0:1, 2 * T + t0:2 * T + t1],
            start=True, stop=True), wait=tuple(rlda) + (t_ones,) + ((ea,) if ea else ()))
        ea = p.op("vector", lambda v, t0=t0, t1=t1: v.tensor_copy(
            ab_sb[:, t0:t1], ps_r[:, 0:t1 - t0]), wait=(ta,))
        bc_toks.append(ea)
    rldb = [p.dma("sync", rows[0:1, 2 * T + b * S:2 * T + (b + 1) * S], ab_d[1, b, :],
                  wait=tuple(bc_toks)) for b in range(B)]
    eb = None
    for (t0, t1) in chunks_of(T):
        tb = p.op("tensor", lambda t, t0=t0, t1=t1: t.matmul(
            ps_r[:, 0:t1 - t0], ones_row[:, :], rows[0:1, 2 * T + t0:2 * T + t1],
            start=True, stop=True), wait=tuple(rldb) + ((eb,) if eb else (bc_toks[-1],)))
        eb = p.op("vector", lambda v, t0=t0, t1=t1: v.tensor_copy(
            ab_sb[:, T + t0:T + t1], ps_r[:, 0:t1 - t0]), wait=(tb,))
        bc_toks.append(eb)
    bc_toks = tuple(bc_toks)

    xn_toks = []
    prev = ()
    for f in range(8):
        for b in range(B):
            for (c0, c1_) in chunks_of(S):
                lx = p.dma("sync", x0_t[:, 0:c1_ - c0],
                           x_d[b, f * 128:(f + 1) * 128, c0:c1_], wait=prev)
                tok = p.op("vector", lambda v, f=f, b=b, c0=c0, c1_=c1_: v.scalar_tensor_tensor(
                    out=xn_bf[:, f * T + b * S + c0:f * T + b * S + c1_],
                    in0=x0_t[:, 0:c1_ - c0], scalar=1.0,
                    in1=ab_sb[:, b * S + c0:b * S + c1_], op0=ALU.bypass, op1=ALU.mult),
                    wait=(lx,) + bc_toks)
                tok2 = p.op("vector", lambda v, f=f, b=b, c0=c0, c1_=c1_: v.tensor_add(
                    xn_bf[:, f * T + b * S + c0:f * T + b * S + c1_],
                    xn_bf[:, f * T + b * S + c0:f * T + b * S + c1_],
                    ab_sb[:, T + b * S + c0:T + b * S + c1_]), wait=(tok,))
                prev = (tok2,)
                xn_toks.append(tok2)
    xn_toks = tuple(xn_toks)

    grow_ev = []
    ge_prev = None
    for e in range(2):
        for (t0, t1) in chunks_of(T):
            tok = None
            for k in range(8):
                def fg(t, e=e, k=k, t0=t0, t1=t1):
                    return t.matmul(ps_r[0:1, 0:t1 - t0], gw_sb[:, k * 2 + e:k * 2 + e + 1],
                                    xn_bf[:, k * T + t0:k * T + t1],
                                    start=(k == 0), stop=(k == 7))
                tok = p.op("tensor", fg, wait=(xn_toks + tuple(gld) +
                                               ((ge_prev,) if ge_prev else ())) if k == 0 else (),
                           inc=(k == 7))
            ge_prev = p.op("vector", lambda v, e=e, t0=t0, t1=t1: v.tensor_copy(
                rows[0:1, e * T + t0:e * T + t1], ps_r[0:1, 0:t1 - t0]), wait=(tok,))
            grow_ev.append(ge_prev)
    d1 = p.op("vector", lambda v: v.tensor_sub(rows[0:1, 2 * T:3 * T], rows[0:1, 0:T],
                                               rows[0:1, T:2 * T]), wait=tuple(grow_ev))
    m0r = p.op("vector", lambda v: v.tensor_single_scalar(rows[0:1, 0:T],
                                                          rows[0:1, 2 * T:3 * T], 0.0,
                                                          ALU.is_ge), wait=(d1,))
    m1r = p.op("vector", lambda v: v.tensor_single_scalar(rows[0:1, T:2 * T],
                                                          rows[0:1, 2 * T:3 * T], 0.0,
                                                          ALU.is_lt), wait=(d1,))
    mtoks = []
    me_prev = None
    for e in range(2):
        for (t0, t1) in chunks_of(T):
            tm = p.op("tensor", lambda t, e=e, t0=t0, t1=t1: t.matmul(
                ps_r[:, 0:t1 - t0], ones_row[:, :], rows[0:1, e * T + t0:e * T + t1],
                start=True, stop=True), wait=(m0r, m1r) + ((me_prev,) if me_prev else ()))
            me_prev = p.op("scalar", lambda s, e=e, t0=t0, t1=t1: s.activation(
                out=m_bf[:, e * T + t0:e * T + t1], in_=ps_r[:, 0:t1 - t0], func=AF.Copy),
                wait=(tm,))
            mtoks.append(me_prev)
    mtoks = tuple(mtoks)

    branches = [("e0", 0), ("e1", 1), ("sh", 2)]
    wload_prev = ()
    yacc_tok = {}
    xm_readers = {}
    for (bname, bi) in branches:
        for hh in range(2):
            wl = []
            for k in range(8):
                srcw = (w1_d[bi, k * 128:(k + 1) * 128, hh * 2048:(hh + 1) * 2048]
                        if bi < 2 else sw1_d[k * 128:(k + 1) * 128, hh * 2048:(hh + 1) * 2048])
                wl.append(p.dma("gpsimd", w1_sb[:, k * 2048:(k + 1) * 2048], srcw,
                                wait=wload_prev))
            for k in range(16):
                srcw = (w2_d[bi, hh * 2048 + k * 128:hh * 2048 + (k + 1) * 128, :]
                        if bi < 2 else sw2_d[hh * 2048 + k * 128:hh * 2048 + (k + 1) * 128, :])
                wl.append(p.dma("gpsimd", w2_sb[:, k * 1024:(k + 1) * 1024], srcw,
                                wait=wload_prev))
            wl = tuple(wl)
            wuse = []
            hb_war = ()
            for j in range(4):
                if bi < 2:
                    mmk = []
                    for f in range(8):
                        mmk.append(p.op("vector", lambda v, f=f, bi=bi, j=j: v.tensor_mul(
                            xm_bf[:, f * 512:(f + 1) * 512],
                            xn_bf[:, f * T + j * 512:f * T + (j + 1) * 512],
                            m_bf[:, bi * T + j * 512:bi * T + (j + 1) * 512]),
                            wait=mtoks + xn_toks + tuple(xm_readers.get(0, ()))))
                    xin_toks = tuple(mmk)
                    xtile = lambda k: xm_bf[:, k * 512:(k + 1) * 512]
                else:
                    xin_toks = xn_toks
                    xtile = lambda k, j=j: xn_bf[:, k * T + j * 512:k * T + (j + 1) * 512]
                htoks = []
                f1_toks = []
                for mt in range(16):
                    bank, ps, war = pool.get()
                    tok = None
                    for k in range(8):
                        def f1(t, mt=mt, k=k, ps=ps, xtile=xtile):
                            return t.matmul(ps[:, :],
                                            w1_sb[:, k * 2048 + mt * 128:
                                                  k * 2048 + (mt + 1) * 128],
                                            xtile(k),
                                            start=(k == 0), stop=(k == 7))
                        tok = p.op("tensor", f1,
                                   wait=(wl + war + xin_toks + hb_war) if k == 0 else (),
                                   inc=(k == 7))
                    f1_toks.append(tok)
                    bcol = (bi if bi < 2 else 2) * 32 + hh * 16 + mt
                    ev = p.op("scalar", lambda s, mt=mt, bcol=bcol, ps=ps: s.activation(
                        out=h_bf[:, mt * 512:(mt + 1) * 512], in_=ps[:, :], func=AF.Relu,
                        bias=b1_sb[:, bcol:bcol + 1]), wait=(tok,) + tuple(gld))
                    pool.consumed(bank, ev)
                    htoks.append(ev)
                htoks = tuple(htoks)
                if bi < 2:
                    xm_readers[0] = tuple(f1_toks)
                wuse += f1_toks
                f2_toks = []
                for mt in range(8):
                    bank, ps, war = pool.get()
                    tok = None
                    for k in range(16):
                        def f2(t, mt=mt, k=k, ps=ps):
                            return t.matmul(ps[:, :],
                                            w2_sb[:, k * 1024 + mt * 128:
                                                  k * 1024 + (mt + 1) * 128],
                                            h_bf[:, k * 512:(k + 1) * 512],
                                            start=(k == 0), stop=(k == 15))
                        tok = p.op("tensor", f2, wait=(htoks + war) if k == 0 else (),
                                   inc=(k == 15))
                    f2_toks.append(tok)
                    first = (bname == "e0" and hh == 0)
                    prev_y = yacc_tok.get((j, mt), ())
                    if first:
                        ev = p.op("vector", lambda v, mt=mt, j=j, ps=ps: v.tensor_scalar_add(
                            y_acc[:, mt * T + j * 512:mt * T + (j + 1) * 512], ps[:, :],
                            bc_sb[:, mt:mt + 1]), wait=(tok,) + bias_tok + prev_y)
                    else:
                        ev = p.op("vector", lambda v, mt=mt, j=j, ps=ps: v.tensor_add(
                            y_acc[:, mt * T + j * 512:mt * T + (j + 1) * 512],
                            y_acc[:, mt * T + j * 512:mt * T + (j + 1) * 512],
                            ps[:, :]), wait=(tok,) + prev_y)
                    pool.consumed(bank, ev)
                    yacc_tok[(j, mt)] = (ev,)
                    wuse.append(ev)
                hb_war = tuple(f2_toks)
            wload_prev = tuple(wuse)
    hpack = p.sbuf("hpack", [128, 48], F32)
    hpT = p.sbuf("hpT", [48, 128], F32)
    hident = p.sbuf("hident", [128, 128], F32)
    hid_sc = p.sbuf("hid_sc", [128, 128], F32)
    ps_hp = p.psum("ps_hp", [128, 128], F32)
    hi1 = p.op("gpsimd", lambda g: g.iota(hid_sc[:, :], pattern=[[1, 128]], base=0,
                                          channel_multiplier=-1,
                                          allow_small_or_imprecise_dtypes=True))
    hi2 = p.op("vector", lambda v: v.tensor_single_scalar(hident[:, :], hid_sc[:, :], 0.0,
                                                          ALU.is_equal), wait=(hi1,))
    fin = []
    hp_toks = []
    prev = (xn_toks[-1],)
    for j in range(4):
        b, sj = divmod(j * 512, S)
        for f in range(8):
            gt = p.op("scalar", lambda s, f=f, j=j: s.activation(
                out=xm_bf[:, 0:512], in_=y_acc[:, f * T + j * 512:f * T + (j + 1) * 512],
                func=AF.Silu), wait=tuple(yacc_tok[(j, f)]) + prev)
            lx = p.dma("sync", x0_t[:, :], x_d[b, f * 128:(f + 1) * 128, sj:sj + 512],
                       wait=prev)
            ls = p.dma("scalar", s0_t[:, :], s0_d[b, f * 128:(f + 1) * 128, sj:sj + 512],
                       wait=prev)
            t1_ = p.op("vector", lambda v: v.tensor_mul(
                x1_t[:, :], s0_t[:, :], xm_bf[:, 0:512]), wait=(gt, ls) + prev)
            t2_ = p.op("vector", lambda v: v.tensor_add(x1_t[:, :], x1_t[:, :], x0_t[:, :]),
                       wait=(t1_, lx))
            xd = p.dma("gpsimd", x1_d[b, f * 128:(f + 1) * 128, sj:sj + 512], x1_t[:, :],
                       wait=(t2_,))
            prev = (xd, t2_)
            if sj == 512:
                hc = p.op("vector", lambda v, f=f, b=b: v.tensor_copy(
                    hpack[:, f * 6 + b * 3:f * 6 + b * 3 + 3], x1_t[:, 509:512]),
                    wait=(t2_,))
                hp_toks.append(hc)
                prev = prev + (hc,)
            fin.append(xd)
    htr = p.op("tensor", lambda t: t.transpose(ps_hp[0:48, 0:128], hpack[:, 0:48],
                                               hident[:, :]),
               wait=tuple(hp_toks) + (hi2,))
    hev = p.op("vector", lambda v: v.tensor_copy(hpT[:, :], ps_hp[0:48, 0:128]),
               wait=(htr,))
    hdma = p.dma("sync", halo_d[:, :], hpT[:, :], wait=(hev,))
    fin.append(hdma)
    p.op("sync", lambda s: s.nop(), wait=tuple(fin), inc=False)


def emit_l1pre(p, io):
    D, NF = 1024, 8
    x_d = io["x"]
    hag_d = io["halo_ag"]
    hmask_d = io["hmask"]
    win_d = io["W_in"]
    lnw_d = io["ln_w"]
    lnb_d = io["ln_b"]
    cw_d = io["conv_w"]
    cb_d = io["conv_b"]
    dtb_d = io["dt_bias"]
    alog_d = io["A_log"]
    z_d = io["z"]
    xhc_d = io["xh_c"]
    bm_d = io["Bm"]
    cm_d = io["Cm"]
    dt_d = io["dt1"]
    dta_d = io["dtA"]
    ab_d = io["ab"]

    x_bf = p.sbuf("x_bf", [128, NF * TT], BF16)
    w_sb = p.sbuf("w_sb", [128, NF * 2320], BF16)
    gam = p.sbuf("gam", [128, 2 * NF], F32)
    cw_sb = p.sbuf("cw_sb", [128, 18 * 4], F32)
    cb_sb = p.sbuf("cb_sb", [128, 18], F32)
    dtb_sb = p.sbuf("dtb_sb", [16, 1], F32)
    acol = p.sbuf("acol", [16, 1], F32)
    ones_bf = p.sbuf("ones_bf", [128, 1], BF16)
    ones_row = p.sbuf("ones_row", [1, 128], F32)
    eps_sb = p.sbuf("eps_sb", [1, 1], F32)
    one_col = p.sbuf("one_col", [128, 1], F32)
    rows = p.sbuf("rows", [1, 4 * TT], F32)
    ab_sb = p.sbuf("ab_sb", [128, 2 * TT], F32)
    h_bf = p.sbuf("h_bf", [128, NF * TT], BF16)
    tmp_f = p.sbuf("tmp_f", [128, TT], F32)
    xpre_st = p.sbuf("xpre_st", [128, 2 * TT], F32)
    conv_st = p.sbuf("conv_st", [128, 2 * LLOC], F32)
    xc_st = p.sbuf("xc_st", [128, B * LLOC], BF16)
    dt_st = p.sbuf("dt_st", [16, 2 * TT], F32)
    pool = PsumPool(p, "ps", n=4)
    ps_a = p.psum("ps_a", [128, 512], F32)
    ps_b = p.psum("ps_b", [128, 512], F32)

    hblk = p.sbuf("hblk", [48, 8 * 128], F32)
    hacc = p.sbuf("hacc", [48, 128], F32)
    h48 = p.sbuf("h48", [128, 48], F32)
    hmask_sb = p.sbuf("hmask_sb", [128, 8], F32)
    hident = p.sbuf("hident", [48, 48], F32)
    hid_sc = p.sbuf("hid_sc", [48, 48], F32)
    ps_h = p.psum("ps_h", [128, 48], F32)
    hld = [p.dma("sync", hblk[:, s * 128:(s + 1) * 128], hag_d[s * 48:(s + 1) * 48, :])
           for s in range(8)]
    hml = p.dma("sync", hmask_sb[:, :], hmask_d)
    ji1 = p.op("gpsimd", lambda g: g.iota(hid_sc[:, :], pattern=[[1, 48]], base=0,
                                          channel_multiplier=-1,
                                          allow_small_or_imprecise_dtypes=True))
    ji2 = p.op("vector", lambda v: v.tensor_single_scalar(hident[:, :], hid_sc[:, :], 0.0,
                                                          ALU.is_equal), wait=(ji1,))
    hto = p.op("vector", lambda v: v.tensor_scalar(
        out=hacc[:, :], in0=hblk[:, 0:128], scalar1=hmask_sb[0:48, 0:1], scalar2=None,
        op0=ALU.mult), wait=(hld[0], hml))
    for s in range(1, 8):
        hto = p.op("vector", lambda v, s=s: v.scalar_tensor_tensor(
            out=hacc[:, :], in0=hblk[:, s * 128:(s + 1) * 128],
            scalar=hmask_sb[0:48, s:s + 1], in1=hacc[:, :],
            op0=ALU.mult, op1=ALU.add), wait=(hld[s], hto))
    htr = p.op("tensor", lambda t: t.transpose(ps_h[:, 0:48], hacc[:, :], hident[:, :]),
               wait=(hto, ji2))
    hcp = p.op("scalar", lambda s: s.activation(out=h48[:, :], in_=ps_h[:, 0:48],
                                                func=AF.Copy), wait=(htr,))

    xld = []
    for f in range(NF):
        for b in range(B):
            xld.append(p.dma("gpsimd",
                             x_bf[:, f * TT + b * TH + HALO: f * TT + (b + 1) * TH],
                             x_d[b, f * 128:(f + 1) * 128, :]))
            xld.append(p.op("scalar", lambda s, f=f, b=b: s.activation(
                out=x_bf[:, f * TT + b * TH:f * TT + b * TH + HALO],
                in_=h48[:, f * 6 + b * 3:f * 6 + b * 3 + 3], func=AF.Copy),
                wait=(hcp,)))
    xld = tuple(xld)
    wld1 = tuple(p.dma("gpsimd", w_sb[:, f * 2048:(f + 1) * 2048],
                       win_d[f * 128:(f + 1) * 128, 0:2048]) for f in range(NF))
    cld = [load_col_vec(p, gam[:, 0:NF], lnw_d),
           p.dma("sync", gam[:, NF:2 * NF], lnb_d.rearrange("(m f) -> f m", f=128),
                 allow_slow_non_contiguous=True),
           p.dma("sync", cw_sb[:, :].rearrange("p (m j) -> p m j", j=4),
                 cw_d.rearrange("(m f) j -> f m j", f=128), allow_slow_non_contiguous=True),
           p.dma("sync", cb_sb[:, :], cb_d.rearrange("(m f) -> f m", f=128),
                 allow_slow_non_contiguous=True),
           p.dma("sync", dtb_sb[:, :], dtb_d[:, None]),
           p.dma("sync", acol[:, :], alog_d[:, None])]
    t_ones = p.op("vector", lambda v: v.memset(ones_bf[:, :], 1.0))
    t_ones2 = p.op("vector", lambda v: v.memset(ones_row[:, :], 1.0))
    t_eps = p.op("vector", lambda v: v.memset(eps_sb[:, :], 1e-5))
    t_one1 = p.op("vector", lambda v: v.memset(one_col[:, :], 1.0))
    ta1 = p.op("scalar", lambda s: s.activation(out=acol[:, :], in_=acol[:, :], func=AF.Exp),
               wait=(cld[5],))
    ta2 = p.op("vector", lambda v: v.tensor_single_scalar(acol[:, :], acol[:, :], -1.0,
                                                          ALU.mult), wait=(ta1,))

    sq_prev = {}
    ev1p = ev2p = None
    for ci, (t0, t1) in enumerate(chunks_of(TT)):
        cw_ = t1 - t0
        tok1 = tok2 = None
        for f in range(NF):
            w_sq = xld + ((sq_prev[f],) if f in sq_prev else ())
            sqt = p.op("scalar", lambda s, f=f, t0=t0, t1=t1: s.activation(
                out=h_bf[:, f * TT:f * TT + (t1 - t0)],
                in_=x_bf[:, f * TT + t0:f * TT + t1], func=AF.Square), wait=w_sq)
            w1 = xld + (t_ones,) + ((ev1p,) if (f == 0 and ev1p) else ())
            tok1 = p.op("tensor", lambda t, f=f, t0=t0, t1=t1, cw_=cw_: t.matmul(
                ps_a[0:1, 0:cw_], ones_bf[:, :], x_bf[:, f * TT + t0:f * TT + t1],
                start=(f == 0), stop=(f == NF - 1)), wait=w1, inc=True)
            w2 = (sqt,) + ((ev2p,) if (f == 0 and ev2p) else ())
            tok2 = p.op("tensor", lambda t, f=f, cw_=cw_: t.matmul(
                ps_b[0:1, 0:cw_], ones_bf[:, :], h_bf[:, f * TT:f * TT + cw_],
                start=(f == 0), stop=(f == NF - 1)), wait=w2, inc=True)
            sq_prev[f] = tok2
        ev1p = p.op("vector", lambda v, t0=t0, t1=t1: v.tensor_copy(
            rows[0:1, 0 * TT + t0:0 * TT + t1], ps_a[0:1, 0:t1 - t0]), wait=(tok1,))
        ev2p = p.op("vector", lambda v, t0=t0, t1=t1: v.tensor_copy(
            rows[0:1, 1 * TT + t0:1 * TT + t1], ps_b[0:1, 0:t1 - t0]), wait=(tok2,))

    S_ = lambda i: rows[0:1, i * TT:(i + 1) * TT]
    rtok = (ev1p, ev2p, t_eps)
    r1 = p.op("vector", lambda v: v.tensor_single_scalar(S_(2), S_(0), 1.0 / D, ALU.mult), wait=rtok)
    r2 = p.op("vector", lambda v: v.tensor_single_scalar(S_(3), S_(1), 1.0 / D, ALU.mult), wait=rtok)
    r3 = p.op("vector", lambda v: v.tensor_mul(S_(0), S_(2), S_(2)), wait=(r1,))
    r4 = p.op("vector", lambda v: v.tensor_sub(S_(1), S_(3), S_(0)), wait=(r2, r3))
    r5 = p.op("scalar", lambda s: s.activation(out=S_(0), in_=S_(1), func=AF.Sqrt,
                                               bias=eps_sb[0:1, 0:1]), wait=(r4,))
    r6 = p.op("vector", lambda v: v.reciprocal(S_(3), S_(0)), wait=(r5,))
    ab_dma = []
    for b in range(B):
        ab_dma.append(p.dma("sync", ab_d[0, b, :],
                            rows[0:1, 3 * TT + b * TH + HALO:3 * TT + (b + 1) * TH], wait=(r6,)))
    r7 = p.op("vector", lambda v: v.tensor_mul(S_(0), S_(0), S_(0)), wait=(r6,) + tuple(ab_dma))
    r8 = p.op("vector", lambda v: v.reciprocal(S_(0), S_(0)), wait=(r7,))
    r9 = p.op("vector", lambda v: v.tensor_mul(S_(0), S_(1), S_(0)), wait=(r8,))
    r10 = p.op("scalar", lambda s: s.activation(out=S_(0), in_=S_(0), func=AF.Sqrt,
                                                bias=eps_sb[0:1, 0:1]), wait=(r9,))
    r11 = p.op("vector", lambda v: v.reciprocal(S_(0), S_(0)), wait=(r10,))
    r12 = p.op("vector", lambda v: v.tensor_mul(S_(1), S_(2), S_(3)), wait=(r11,))
    r13 = p.op("vector", lambda v: v.tensor_single_scalar(S_(1), S_(1), -1.0, ALU.mult), wait=(r12,))
    for b in range(B):
        ab_dma.append(p.dma("sync", ab_d[1, b, :],
                            rows[0:1, 1 * TT + b * TH + HALO:1 * TT + (b + 1) * TH], wait=(r13,)))
    r14 = p.op("vector", lambda v: v.tensor_mul(S_(2), S_(3), S_(0)), wait=(r13,))
    r15 = p.op("vector", lambda v: v.tensor_mul(S_(3), S_(1), S_(0)),
               wait=(r14,) + tuple(ab_dma))

    bc_toks = []
    ea = eb = None
    for (t0, t1) in chunks_of(TT):
        tba = p.op("tensor", lambda t, t0=t0, t1=t1: t.matmul(
            ps_a[:, 0:t1 - t0], ones_row[:, :], rows[0:1, 2 * TT + t0:2 * TT + t1],
            start=True, stop=True), wait=(r14, ev1p, t_ones2) + ((ea,) if ea else ()))
        tbb = p.op("tensor", lambda t, t0=t0, t1=t1: t.matmul(
            ps_b[:, 0:t1 - t0], ones_row[:, :], rows[0:1, 3 * TT + t0:3 * TT + t1],
            start=True, stop=True), wait=(r15, ev2p) + ((eb,) if eb else ()))
        ea = p.op("vector", lambda v, t0=t0, t1=t1: v.tensor_copy(
            ab_sb[:, t0:t1], ps_a[:, 0:t1 - t0]), wait=(tba,))
        eb = p.op("vector", lambda v, t0=t0, t1=t1: v.tensor_copy(
            ab_sb[:, TT + t0:TT + t1], ps_b[:, 0:t1 - t0]), wait=(tbb,))
        bc_toks += [ea, eb]
    bc_toks = tuple(bc_toks)

    h_toks = []
    prev = None
    for f in range(NF):
        w = bc_toks + xld + (cld[0], cld[1]) + ((prev,) if prev else ())
        t1_ = p.op("vector", lambda v, f=f: v.scalar_tensor_tensor(
            out=tmp_f[:, :], in0=x_bf[:, f * TT:(f + 1) * TT], scalar=1.0,
            in1=ab_sb[:, 0:TT], op0=ALU.bypass, op1=ALU.mult), wait=w)
        t2_ = p.op("vector", lambda v: v.tensor_add(tmp_f[:, :], tmp_f[:, :],
                                                    ab_sb[:, TT:2 * TT]), wait=(t1_,))
        t3_ = p.op("vector", lambda v, f=f: v.tensor_scalar(
            out=h_bf[:, f * TT:(f + 1) * TT], in0=tmp_f[:, :],
            scalar1=gam[:, f:f + 1], scalar2=gam[:, NF + f:NF + f + 1],
            op0=ALU.mult, op1=ALU.add), wait=(t2_,))
        prev = t3_
        h_toks.append(t3_)
    h_toks = tuple(h_toks)

    z_dma = []
    war_res = {}
    for m in range(16):
        evs = []
        for (t0, t1) in chunks_of(TT):
            bank, ps, war = pool.get()
            tok = None
            for k in range(NF):
                wk = h_toks + wld1 + war + tuple(war_res.get(0, ())) if k == 0 else ()

                def fn(t, m=m, k=k, t0=t0, t1=t1, ps=ps):
                    return t.matmul(ps[:, 0:t1 - t0],
                                    w_sb[:, k * 2048 + m * 128:k * 2048 + (m + 1) * 128],
                                    h_bf[:, k * TT + t0:k * TT + t1],
                                    start=(k == 0), stop=(k == NF - 1))
                tok = p.op("tensor", fn, wait=wk, inc=(k == NF - 1))
            ev = p.op("scalar", lambda s, t0=t0, t1=t1, ps=ps: s.activation(
                out=xpre_st[:, t0:t1], in_=ps[:, 0:t1 - t0], func=AF.Copy), wait=(tok,))
            evs.append(ev)
            pool.consumed(bank, ev)
        drs = []
        for b in range(B):
            dr = p.dma("gpsimd", z_d[b, m * 128:(m + 1) * 128, :],
                       xpre_st[:, b * TH + HALO:(b + 1) * TH], wait=tuple(evs))
            drs.append(dr)
            z_dma.append(dr)
        war_res[0] = tuple(drs) + tuple(evs)

    w2wait = tuple(t for t in z_dma) + h_toks
    wld2 = tuple(p.dma("gpsimd", w_sb[:, f * 2320:(f + 1) * 2320],
                       win_d[f * 128:(f + 1) * 128, 2048:4368], wait=w2wait) for f in range(NF))
    fin_dma = list(z_dma) + ab_dma
    war_xp = {}
    war_xc = {}
    for m in range(18):
        ping = m % 2
        evs = []
        for (t0, t1) in chunks_of(TT):
            bank, ps, war = pool.get()
            tok = None
            for k in range(NF):
                wk = wld2 + war + tuple(war_xp.get(ping, ())) if k == 0 else ()

                def fn(t, m=m, k=k, t0=t0, t1=t1, ps=ps):
                    return t.matmul(ps[:, 0:t1 - t0],
                                    w_sb[:, k * 2320 + m * 128:k * 2320 + (m + 1) * 128],
                                    h_bf[:, k * TT + t0:k * TT + t1],
                                    start=(k == 0), stop=(k == NF - 1))
                tok = p.op("tensor", fn, wait=wk, inc=(k == NF - 1))
            ev = p.op("scalar", lambda s, t0=t0, t1=t1, ps=ps, ping=ping: s.activation(
                out=xpre_st[:, ping * TT + t0:ping * TT + t1], in_=ps[:, 0:t1 - t0],
                func=AF.Copy), wait=(tok,))
            evs.append(ev)
            pool.consumed(bank, ev)
        conv_toks = []
        for b in range(B):
            base = ping * TT + b * TH
            cbank = b
            w0 = tuple(evs) + (cld[2], cld[3]) + tuple(war_xc.get(0, ()))
            c1 = p.op("vector", lambda v, m=m, base=base, cbank=cbank: v.tensor_scalar(
                out=conv_st[:, cbank * LLOC:(cbank + 1) * LLOC],
                in0=xpre_st[:, base + 0:base + 0 + LLOC],
                scalar1=cw_sb[:, m * 4 + 0:m * 4 + 1], scalar2=cb_sb[:, m:m + 1],
                op0=ALU.mult, op1=ALU.add), wait=w0)
            prev_c = c1
            for j in range(1, 4):
                def fnc(v, m=m, j=j, base=base, cbank=cbank):
                    dst_is_tmp = (j % 2 == 1)
                    dst = tmp_f[:, cbank * LLOC:(cbank + 1) * LLOC] if dst_is_tmp \
                        else conv_st[:, cbank * LLOC:(cbank + 1) * LLOC]
                    srcb = conv_st[:, cbank * LLOC:(cbank + 1) * LLOC] if dst_is_tmp \
                        else tmp_f[:, cbank * LLOC:(cbank + 1) * LLOC]
                    return v.scalar_tensor_tensor(
                        out=dst, in0=xpre_st[:, base + j:base + j + LLOC],
                        scalar=cw_sb[:, m * 4 + j:m * 4 + j + 1], in1=srcb,
                        op0=ALU.mult, op1=ALU.add)
                prev_c = p.op("vector", fnc, wait=(prev_c,))
            conv_toks.append(prev_c)
            sil = p.op("scalar", lambda s, b=b, cbank=cbank: s.activation(
                out=xc_st[:, b * LLOC:(b + 1) * LLOC],
                in_=tmp_f[:, cbank * LLOC:(cbank + 1) * LLOC], func=AF.Silu),
                wait=(prev_c,) + tuple(war_xc.get(0, ())))
            if m < 16:
                dst = xhc_d[b, m * 128:(m + 1) * 128, :]
            elif m == 16:
                dst = bm_d[b, :, :]
            else:
                dst = cm_d[b, :, :]
            dd = p.dma("sync", dst, xc_st[:, b * LLOC:(b + 1) * LLOC], wait=(sil,))
            fin_dma.append(dd)
            war_xc[0] = (dd, sil)
        war_xp[ping] = tuple(conv_toks)

    dt_evs = []
    for (t0, t1) in chunks_of(TT):
        bank, ps, war = pool.get()
        tok = None
        for k in range(NF):
            wk = wld2 + war if k == 0 else ()

            def fn(t, k=k, t0=t0, t1=t1, ps=ps):
                return t.matmul(ps[0:16, 0:t1 - t0],
                                w_sb[:, k * 2320 + 2304:k * 2320 + 2320],
                                h_bf[:, k * TT + t0:k * TT + t1],
                                start=(k == 0), stop=(k == NF - 1))
            tok = p.op("tensor", fn, wait=wk, inc=(k == NF - 1))
        ev = p.op("vector", lambda v, t0=t0, t1=t1, ps=ps: v.tensor_scalar_add(
            dt_st[0:16, t0:t1], ps[0:16, 0:t1 - t0], dtb_sb[:, 0:1]),
            wait=(tok, cld[4]))
        dt_evs.append(ev)
        pool.consumed(bank, ev)
    dt_evs = tuple(dt_evs)
    s0a = p.op("vector", lambda v: v.tensor_single_scalar(
        dt_st[0:16, TT:2 * TT], dt_st[0:16, 0:TT], -1.0, ALU.mult), wait=dt_evs)
    s1 = p.op("vector", lambda v: v.tensor_max(
        dt_st[0:16, TT:2 * TT], dt_st[0:16, TT:2 * TT], dt_st[0:16, 0:TT]), wait=(s0a,))
    s2 = p.op("scalar", lambda s: s.activation(
        out=dt_st[0:16, TT:2 * TT], in_=dt_st[0:16, TT:2 * TT], func=AF.Exp,
        scale=-1.0), wait=(s1,))
    s3 = p.op("scalar", lambda s: s.activation(
        out=dt_st[0:16, TT:2 * TT], in_=dt_st[0:16, TT:2 * TT], func=AF.Ln,
        bias=one_col[0:16, 0:1]), wait=(s2, t_one1))
    s4 = p.op("vector", lambda v: v.tensor_relu(dt_st[0:16, 0:TT],
                                                dt_st[0:16, 0:TT]), wait=(s3,) + dt_evs)
    s5 = p.op("vector", lambda v: v.tensor_add(dt_st[0:16, 0:TT],
                                               dt_st[0:16, 0:TT],
                                               dt_st[0:16, TT:2 * TT]), wait=(s4,))
    dtd = []
    for b in range(B):
        dtd.append(p.dma("sync", dt_d[b, :, :],
                         dt_st[0:16, b * TH + HALO:(b + 1) * TH], wait=(s5,)))
    s6 = p.op("vector", lambda v: v.scalar_tensor_tensor(
        out=dt_st[0:16, TT:2 * TT], in0=dt_st[0:16, 0:TT], scalar=acol[:, 0:1],
        in1=dt_st[0:16, 0:TT], op0=ALU.mult, op1=ALU.bypass), wait=(s5, ta2))
    fin_dma += dtd
    for b in range(B):
        fin_dma.append(p.dma("sync", dta_d[b, :, :],
                             dt_st[0:16, TT + b * TH + HALO:TT + (b + 1) * TH],
                             wait=(s6,)))
    p.op("sync", lambda s: s.nop(), wait=tuple(fin_dma), inc=False)
    return p


def emit_l1ssd(p, io):
    S = 8192
    NCH, SLAB = 64, 16
    x_d = io["X"]
    bm_d = io["BuB"]
    cm_d = io["CuB"]
    bsel_d = io["bsel"]
    dt_d = io["dt"]
    dta_d = io["dtA"]
    du_d = io["Du"]
    y_d = io["y"]
    scr = p.dram_tmp("scr", [2, 2048])

    X_sb = p.sbuf("X_sb", [128, S], BF16)
    B_sb = p.sbuf("B_sb", [128, S], BF16)
    C_sb = p.sbuf("C_sb", [128, S], BF16)
    rows = p.sbuf("rows", [1, 4 * 2048], F32)
    r16 = p.sbuf("r16", [1, 64], F32)
    zrow = p.sbuf("zrow", [1, 2048], F32)
    scol = p.sbuf("scol", [128, 16], F32)
    dtcol = p.sbuf("dtcol", [128, 16], F32)
    ecol = p.sbuf("ecol", [128, 16], F32)
    eqc = p.sbuf("eqc", [128, 16], F32)
    sqb = p.sbuf("sqb", [128, 16], F32)
    qcol = p.sbuf("qcol", [128, 16], F32)
    dcol = p.sbuf("dcol", [128, 4], F32)
    ident = p.sbuf("ident", [128, 128], BF16)
    mask01 = p.sbuf("mask01", [128, 128], F32)
    ones_row = p.sbuf("ones_row", [1, 128], F32)
    io_sc = p.sbuf("io_sc", [128, 128], F32)
    xt_sb = p.sbuf("xt_sb", [128, 128], BF16)
    xq_sb = p.sbuf("xq_sb", [128, 128], BF16)
    bt_sb = p.sbuf("bt_sb", [128, 128], BF16)
    dex_sb = p.sbuf("dex_sb", [128, 128], F32)
    dm_sb = p.sbuf("dm_sb", [128, 128], F32)
    m_sb = p.sbuf("m_sb", [128, 128], BF16)
    ysb = p.sbuf("ysb", [128, 128], BF16)
    y1s = p.sbuf("y1s", [128, 128], F32)
    H_sb = p.sbuf("H_sb", [128, 128], F32)
    H_bf = p.sbuf("H_bf", [128, 128], BF16)
    yst = p.sbuf("yst", [128, 2048], F32)
    ps_a1 = p.psum("ps_a1", [128, 512], F32)
    ps_gt = p.psum("ps_gt", [128, 512], F32)
    ps_y1 = p.psum("ps_y1", [128, 512], F32)
    ps_y2 = p.psum("ps_y2", [128, 512], F32)
    ps_cb = p.psum("ps_cb", [128, 512], F32)
    ps_t1 = p.psum("ps_t1", [128, 1024], BF16)
    ps_t2 = p.psum("ps_t2", [128, 1024], BF16)

    t_ones = p.op("vector", lambda v: v.memset(ones_row[:, :], 1.0))
    t_z = p.op("vector", lambda v: v.memset(zrow[:, :], 0.0))
    i1 = p.op("gpsimd", lambda g: g.iota(io_sc[:, :], pattern=[[1, 128]], base=0,
                                         channel_multiplier=0,
                                         allow_small_or_imprecise_dtypes=True))
    i2 = p.op("gpsimd", lambda g: g.iota(mask01[:, :], pattern=[[0, 128]], base=0,
                                         channel_multiplier=1,
                                         allow_small_or_imprecise_dtypes=True))
    i3 = p.op("vector", lambda v: v.tensor_tensor(ident[:, :], io_sc[:, :], mask01[:, :],
                                                  ALU.is_equal), wait=(i1, i2))
    i4 = p.op("gpsimd", lambda g: g.memset(mask01[:, :], 1.0), wait=(i3,))
    i5 = p.op("gpsimd", lambda g: g.affine_select(mask01[:, :], mask01[:, :],
                                                  pattern=[[1, 128]], compare_op=ALU.is_ge,
                                                  fill=0.0, base=0, channel_multiplier=-1),
              wait=(i4,))
    ldd = p.dma("sync", dcol[0:1, :], du_d)
    dbc = p.op("tensor", lambda t: t.matmul(ps_cb[0:128, 0:4], ones_row[:, :],
                                            dcol[0:1, :], start=True, stop=True),
               wait=(ldd, t_ones))
    dbe = p.op("vector", lambda v: v.tensor_copy(dcol[:, :], ps_cb[:, 0:4]), wait=(dbc,))

    btmp = p.sbuf("btmp", [128, S], BF16)
    bsel_sb = p.sbuf("bsel_sb", [128, 2], F32)
    bsl = p.dma("sync", bsel_sb[:, :], bsel_d)
    lb0 = p.dma("gpsimd", B_sb[:, :], bm_d[0])
    lb1 = p.dma("gpsimd", btmp[:, :], bm_d[1])
    sB1 = p.op("vector", lambda v: v.tensor_scalar(
        out=B_sb[:, :], in0=B_sb[:, :], scalar1=bsel_sb[:, 0:1], scalar2=None,
        op0=ALU.mult), wait=(lb0, bsl))
    sB2 = p.op("vector", lambda v: v.scalar_tensor_tensor(
        out=B_sb[:, :], in0=btmp[:, :], scalar=bsel_sb[:, 1:2], in1=B_sb[:, :],
        op0=ALU.mult, op1=ALU.add), wait=(lb1, sB1))
    lcb0 = p.dma("gpsimd", C_sb[:, :], cm_d[0])
    lcb1 = p.dma("gpsimd", btmp[:, :], cm_d[1], wait=(sB2,))
    sC1 = p.op("vector", lambda v: v.tensor_scalar(
        out=C_sb[:, :], in0=C_sb[:, :], scalar1=bsel_sb[:, 0:1], scalar2=None,
        op0=ALU.mult), wait=(lcb0, bsl))
    sC2 = p.op("vector", lambda v: v.scalar_tensor_tensor(
        out=C_sb[:, :], in0=btmp[:, :], scalar=bsel_sb[:, 1:2], in1=C_sb[:, :],
        op0=ALU.mult, op1=ALU.add), wait=(lcb1, sC1))

    prev_unit = ()
    for u in range(4):
        lX = p.dma("gpsimd", X_sb[:, :], x_d[u], wait=prev_unit)
        hz = p.op("vector", lambda v: v.memset(H_sb[:, :], 0.0), wait=prev_unit)
        hzb = p.op("vector", lambda v: v.memset(H_bf[:, :], 0.0), wait=prev_unit)
        unit_toks = [hz, hzb]
        ldw = (lX, sB2, sC2)
        prev_slab = (hz, hzb)
        h_toks = (hz, hzb)
        for sb_i in range(4):
            s0 = sb_i * 2048
            w0 = prev_slab
            lr1 = p.dma("sync", rows[0:1, 0:2048], dta_d[u, s0:s0 + 2048], wait=w0)
            lr2 = p.dma("sync", rows[0:1, 2 * 2048:3 * 2048], dt_d[u, s0:s0 + 2048], wait=w0)
            sg = p.op("vector", lambda v: v.tensor_tensor_scan(
                rows[0:1, 2048:2 * 2048], rows[0:1, 0:2048], zrow[0:1, :], 0.0,
                ALU.add, ALU.add), wait=(lr1, t_z) + w0)
            of1 = p.op("vector", lambda v: v.memset(r16[0:1, 0:1], 0.0), wait=w0)
            of2 = p.op("vector", lambda v: v.tensor_copy(
                r16[0:1, 1:16],
                rows[0:1, 2048:2 * 2048].rearrange("a (c t) -> a c t", t=128)[:, 0:15, 127]),
                wait=(sg, of1))
            sl = p.op("vector", lambda v: v.tensor_sub(
                rows[0:1, 2048:2 * 2048].rearrange("a (c t) -> a c t", t=128),
                rows[0:1, 2048:2 * 2048].rearrange("a (c t) -> a c t", t=128),
                r16[0:1, 0:16][:, :, None].broadcast_to([1, 16, 128])), wait=(of2,))
            sq_ = p.op("vector", lambda v: v.tensor_copy(
                r16[0:1, 16:32],
                rows[0:1, 2048:2 * 2048].rearrange("a (c t) -> a c t", t=128)[:, :, 127]),
                wait=(sl,))
            eq_ = p.op("scalar", lambda s: s.activation(out=r16[0:1, 32:48],
                                                        in_=r16[0:1, 16:32], func=AF.Exp),
                       wait=(sq_,))
            bo1 = p.dma("sync", scr[0, :], rows[0:1, 2048:2 * 2048], wait=(sl,))
            bo2 = p.dma("sync", scr[1, :], rows[0:1, 2 * 2048:3 * 2048], wait=(lr2,))
            lc1 = p.dma("sync", scol[:, :], scr[0].rearrange("(c t) -> t c", t=128),
                        wait=(bo1,), allow_slow_non_contiguous=True)
            lc2 = p.dma("sync", dtcol[:, :], scr[1].rearrange("(c t) -> t c", t=128),
                        wait=(bo2,), allow_slow_non_contiguous=True)
            ec = p.op("scalar", lambda s: s.activation(out=ecol[:, :], in_=scol[:, :],
                                                       func=AF.Exp), wait=(lc1,) + prev_slab)
            tb1 = p.op("tensor", lambda t: t.matmul(ps_cb[:, 0:16], ones_row[:, :],
                                                    r16[0:1, 32:48], start=True, stop=True),
                       wait=(eq_, dbe) + prev_slab)
            eb1 = p.op("vector", lambda v: v.tensor_copy(eqc[:, :], ps_cb[:, 0:16]),
                       wait=(tb1,))
            tb2 = p.op("tensor", lambda t: t.matmul(ps_cb[:, 16:32], ones_row[:, :],
                                                    r16[0:1, 16:32], start=True, stop=True),
                       wait=(sq_, eb1))
            eb2 = p.op("vector", lambda v: v.tensor_copy(sqb[:, :], ps_cb[:, 16:32]),
                       wait=(tb2,))
            q1 = p.op("vector", lambda v: v.tensor_sub(qcol[:, :], sqb[:, :], scol[:, :]),
                      wait=(eb2, lc1) + prev_slab)
            q2 = p.op("scalar", lambda s: s.activation(out=qcol[:, :], in_=qcol[:, :],
                                                       func=AF.Exp), wait=(q1,))
            q3 = p.op("vector", lambda v: v.tensor_mul(qcol[:, :], qcol[:, :], dtcol[:, :]),
                      wait=(q2, lc2))
            slab_toks = (ec, q3, eb1)
            yst_w = []
            for ci in range(SLAB):
                c = s0 + ci * 128
                csl = slice(c, c + 128)
                tx = p.op("tensor", lambda t, csl=csl: t.transpose(
                    ps_t1[:, 0:128], X_sb[:, csl], ident[:, :]),
                    wait=(ldw[0], i3) + h_toks + tuple(yst_w[-1:]))
                tb_ = p.op("tensor", lambda t, csl=csl: t.transpose(
                    ps_t2[:, 0:128], B_sb[:, csl], ident[:, :]), wait=(ldw[1], i3))
                ex1 = p.op("vector", lambda v, ci=ci: v.tensor_scalar(
                    out=xt_sb[:, :], in0=ps_t1[:, 0:128], scalar1=dtcol[:, ci:ci + 1],
                    scalar2=None, op0=ALU.mult), wait=(tx, lc2) + slab_toks)
                ex2 = p.op("vector", lambda v, ci=ci: v.tensor_scalar(
                    out=xq_sb[:, :], in0=ps_t1[:, 0:128], scalar1=qcol[:, ci:ci + 1],
                    scalar2=None, op0=ALU.mult), wait=(tx, q3))
                ex3 = p.op("vector", lambda v: v.tensor_copy(bt_sb[:, :], ps_t2[:, 0:128]),
                           wait=(tb_,))
                g1 = p.op("tensor", lambda t, csl=csl: t.matmul(
                    ps_gt[:, 0:128], B_sb[:, csl], C_sb[:, csl], start=True, stop=True),
                    wait=(ldw[1], ldw[2], ex3))
                a1 = p.op("tensor", lambda t, ci=ci: t.matmul(
                    ps_a1[:, 0:128], ones_row[:, :],
                    rows[0:1, 2048 + ci * 128:2048 + (ci + 1) * 128],
                    start=True, stop=True), wait=(sl, t_ones, ex1, ex2))
                d1 = p.op("vector", lambda v, ci=ci: v.tensor_scalar(
                    out=dex_sb[:, :], in0=ps_a1[:, 0:128], scalar1=scol[:, ci:ci + 1],
                    scalar2=0.0, op0=ALU.subtract, op1=ALU.min), wait=(a1, lc1))
                d2 = p.op("scalar", lambda s: s.activation(out=dex_sb[:, :], in_=dex_sb[:, :],
                                                           func=AF.Exp), wait=(d1,))
                d3 = p.op("vector", lambda v: v.tensor_mul(dm_sb[:, :], dex_sb[:, :],
                                                           mask01[:, :]), wait=(d2, i5))
                mm_ = p.op("vector", lambda v: v.tensor_mul(m_sb[:, :], ps_gt[:, 0:128],
                                                            dm_sb[:, :]), wait=(g1, d3))
                y1 = p.op("tensor", lambda t: t.matmul(ps_y1[:, 0:128], m_sb[:, :],
                                                       xt_sb[:, :], start=True, stop=True),
                          wait=(mm_, ex1))
                y2 = p.op("tensor", lambda t, csl=csl: t.matmul(
                    ps_y2[:, 0:128], C_sb[:, csl], H_bf[:, :], start=True, stop=True),
                    wait=(ldw[2],) + h_toks)
                cb_ = p.op("tensor", lambda t: t.matmul(ps_cb[:, 128:256], bt_sb[:, :],
                                                        xq_sb[:, :], start=True, stop=True),
                           wait=(ex3, ex2, eb2))
                hupd = p.op("vector", lambda v, ci=ci: v.scalar_tensor_tensor(
                    out=H_sb[:, :], in0=H_sb[:, :], scalar=eqc[:, ci:ci + 1],
                    in1=ps_cb[:, 128:256], op0=ALU.mult, op1=ALU.add),
                    wait=(cb_, y2, eb1))
                hcp = p.op("scalar", lambda s: s.activation(out=H_bf[:, :], in_=H_sb[:, :],
                                                            func=AF.Copy), wait=(hupd,))
                h_toks = (hupd, hcp)
                ye0 = p.op("vector", lambda v: v.tensor_copy(y1s[:, :], ps_y1[:, 0:128]),
                           wait=(y1,))
                ye = p.op("vector", lambda v, ci=ci: v.scalar_tensor_tensor(
                    out=ysb[:, :], in0=ps_y2[:, 0:128], scalar=ecol[:, ci:ci + 1],
                    in1=y1s[:, :], op0=ALU.mult, op1=ALU.add), wait=(ye0, y2, ec))
                ty = p.op("tensor", lambda t: t.transpose(ps_t1[:, 128:256], ysb[:, :],
                                                          ident[:, :]), wait=(ye,))
                last_ty = ty
                yo = p.op("vector", lambda v, ci=ci, csl=csl, u=u: v.scalar_tensor_tensor(
                    out=yst[:, ci * 128:(ci + 1) * 128], in0=X_sb[:, csl],
                    scalar=dcol[:, u:u + 1], in1=ps_t1[:, 128:256],
                    op0=ALU.mult, op1=ALU.add), wait=(ty, dbe) + tuple(yst_w[-1:]))
                yst_w.append(yo)
            yd = p.dma("gpsimd", y_d[u, :, s0:s0 + 2048], yst[:, :], wait=tuple(yst_w))
            prev_slab = (yd, last_ty, lc1, lc2) + tuple(yst_w[-1:]) + (sl, sq_, of2)
            unit_toks.append(yd)
        prev_unit = tuple(unit_toks) + h_toks + prev_slab
    p.op("sync", lambda s: s.nop(), wait=tuple(unit_toks) + prev_unit, inc=False)


def emit_l1post(p, io):
    S = LLOC
    T = B * S
    x_d = io["x"]
    ab_d = io["ab"]
    y_dd = io["y"]
    z_d = io["z"]
    rw_d = io["rms_w"]
    wout_d = io["W_out"]
    s1_d = io["s1"]

    wo_sb = p.sbuf("wo_sb", [128, 16 * 1024], BF16)
    rw_sb = p.sbuf("rw_sb", [128, 16], F32)
    ab_sb = p.sbuf("ab_sb", [128, 2 * T], F32)
    rows = p.sbuf("rows", [1, 2 * T], F32)
    ones_row = p.sbuf("ones_row", [1, 128], F32)
    ones_bf = p.sbuf("ones_bf", [128, 1], BF16)
    eps_sb = p.sbuf("eps_sb", [1, 1], F32)
    y_t = p.sbuf("y_t", [128, 2 * 512], BF16)
    z_t = p.sbuf("z_t", [128, 2 * 512], BF16)
    sz_f = p.sbuf("sz_f", [128, 512], F32)
    yg_sb = p.sbuf("yg_sb", [128, 16 * 512], BF16)
    sq_t = p.sbuf("sq_t", [128, 512], BF16)
    rb_sb = p.sbuf("rb_sb", [128, 512], F32)
    u_bf = p.sbuf("u_bf", [128, 2 * 512], BF16)
    x0_t = p.sbuf("x0_t", [128, 512], F32)
    s0_t = p.sbuf("s0_t", [128, 512], F32)
    out_ps = [p.psum(f"ops{i}", [128, 512], F32) for i in range(8)]

    wld = [p.dma("gpsimd", wo_sb[:, c * 1024:(c + 1) * 1024],
                 wout_d[c * 128:(c + 1) * 128, :]) for c in range(16)]
    wld.append(load_col_vec(p, rw_sb, rw_d))
    rld = [p.dma("sync", rows[0:1, b * S:(b + 1) * S], ab_d[0, b, :]) for b in range(B)]
    rld += [p.dma("sync", rows[0:1, T + b * S:T + (b + 1) * S], ab_d[1, b, :]) for b in range(B)]
    t_ones = p.op("vector", lambda v: v.memset(ones_row[:, :], 1.0))
    t_ob = p.op("vector", lambda v: v.memset(ones_bf[:, :], 1.0))
    t_eps = p.op("vector", lambda v: v.memset(eps_sb[:, :], 1e-5))

    bc_toks = []
    ea = eb = None
    for (t0, t1) in chunks_of(T):
        ta = p.op("tensor", lambda t, t0=t0, t1=t1: t.matmul(
            out_ps[0][:, 0:t1 - t0], ones_row[:, :], rows[0:1, t0:t1],
            start=True, stop=True), wait=tuple(rld) + (t_ones,) + ((ea,) if ea else ()))
        tb = p.op("tensor", lambda t, t0=t0, t1=t1: t.matmul(
            out_ps[1][:, 0:t1 - t0], ones_row[:, :], rows[0:1, T + t0:T + t1],
            start=True, stop=True), wait=(t_ones,) + ((eb,) if eb else ()))
        ea = p.op("vector", lambda v, t0=t0, t1=t1: v.tensor_copy(
            ab_sb[:, t0:t1], out_ps[0][:, 0:t1 - t0]), wait=(ta,))
        eb = p.op("vector", lambda v, t0=t0, t1=t1: v.tensor_copy(
            ab_sb[:, T + t0:T + t1], out_ps[1][:, 0:t1 - t0]), wait=(tb,))
        bc_toks += [ea, eb]
    bc_toks = tuple(bc_toks)

    s1_dma = []
    war_out = {}
    prev_j = ()
    for j in range(4):
        b, sj = divmod(j * 512, S)
        row_ev = None
        sq_prev = ()
        ygt = {}
        for ct in range(16):
            ping = ct % 2
            l1 = p.dma("gpsimd", y_t[:, ping * 512:(ping + 1) * 512],
                       y_dd[b, ct * 128:(ct + 1) * 128, sj:sj + 512],
                       wait=tuple(ygt.get(ping, ())))
            l2 = p.dma("scalar", z_t[:, ping * 512:(ping + 1) * 512],
                       z_d[b, ct * 128:(ct + 1) * 128, sj:sj + 512],
                       wait=tuple(ygt.get(ping, ())))
            sz = p.op("scalar", lambda s, ping=ping: s.activation(
                out=sz_f[:, :], in_=z_t[:, ping * 512:(ping + 1) * 512], func=AF.Silu),
                wait=(l2,) + tuple(ygt.get(ping, ())) + prev_j)
            yg = p.op("vector", lambda v, ct=ct, ping=ping: v.tensor_mul(
                yg_sb[:, ct * 512:(ct + 1) * 512], y_t[:, ping * 512:(ping + 1) * 512],
                sz_f[:, :]), wait=(sz, l1) + prev_j)
            sq = p.op("scalar", lambda s, ct=ct: s.activation(
                out=sq_t[:, :], in_=yg_sb[:, ct * 512:(ct + 1) * 512], func=AF.Square),
                wait=(yg,) + sq_prev)
            mm = p.op("tensor", lambda t, ct=ct: t.matmul(
                out_ps[2][0:1, :], ones_bf[:, :], sq_t[:, :],
                start=(ct == 0), stop=(ct == 15)),
                wait=(sq, t_ob) + (bc_toks if j == 0 and ct == 0 else ()) +
                     (tuple(war_out.get(2, ())) if ct == 0 else ()), inc=True)
            sq_prev = (mm,)
            ygt[ping] = (yg, sz)
        row_ev = p.op("vector", lambda v: v.tensor_copy(rows[0:1, 0:512], out_ps[2][0:1, :]),
                      wait=(mm,))
        r1 = p.op("vector", lambda v: v.tensor_single_scalar(
            rows[0:1, 0:512], rows[0:1, 0:512], 1.0 / 2048, ALU.mult), wait=(row_ev,))
        r2 = p.op("scalar", lambda s: s.activation(out=rows[0:1, 0:512], in_=rows[0:1, 0:512],
                                                   func=AF.Sqrt, bias=eps_sb[0:1, 0:1]),
                  wait=(r1, t_eps))
        r3 = p.op("vector", lambda v: v.reciprocal(rows[0:1, 0:512], rows[0:1, 0:512]),
                  wait=(r2,))
        tb_ = p.op("tensor", lambda t: t.matmul(out_ps[2][:, :], ones_row[:, :],
                                                rows[0:1, 0:512], start=True, stop=True),
                   wait=(r3,))
        rbe = p.op("vector", lambda v: v.tensor_copy(rb_sb[:, :], out_ps[2][:, :]),
                   wait=(tb_,))
        mm_last = [None] * 8
        war_u = {}
        for ct in range(16):
            ping = ct % 2
            tu = p.op("vector", lambda v, ct=ct, ping=ping: v.scalar_tensor_tensor(
                out=u_bf[:, ping * 512:(ping + 1) * 512],
                in0=yg_sb[:, ct * 512:(ct + 1) * 512], scalar=rw_sb[:, ct:ct + 1],
                in1=rb_sb[:, :], op0=ALU.mult, op1=ALU.mult),
                wait=(rbe, wld[-1]) + tuple(war_u.get(ping, ())))
            mts = []
            for mt in range(8):
                def fm_(t, ct=ct, mt=mt, ping=ping):
                    return t.matmul(out_ps[mt][:, :],
                                    wo_sb[:, ct * 1024 + mt * 128:ct * 1024 + (mt + 1) * 128],
                                    u_bf[:, ping * 512:(ping + 1) * 512],
                                    start=(ct == 0), stop=(ct == 15), skip_group_check=True)
                w_mm = (tu,)
                if ct == 0:
                    w_mm = w_mm + tuple(wld) + tuple(war_out.get(mt, ()))
                mts.append(p.op("tensor", fm_, wait=w_mm, inc=(ct == 15 or mt == 7)))
            war_u[ping] = (mts[7],)
            if ct == 15:
                mm_last = mts
        prev_s0 = prev_j
        for f in range(8):
            lx = p.dma("sync", x0_t[:, :], x_d[b, f * 128:(f + 1) * 128, sj:sj + 512],
                       wait=prev_s0)
            e1 = p.op("vector", lambda v, f=f, j=j: v.scalar_tensor_tensor(
                out=s0_t[:, :], in0=x0_t[:, :], scalar=1.0,
                in1=ab_sb[:, j * 512:(j + 1) * 512], op0=ALU.bypass, op1=ALU.mult),
                wait=(lx, mm_last[f]) + prev_s0)
            e2 = p.op("vector", lambda v, j=j: v.tensor_add(
                s0_t[:, :], s0_t[:, :], ab_sb[:, T + j * 512:T + (j + 1) * 512]), wait=(e1,))
            e3 = p.op("vector", lambda v, f=f: v.tensor_add(
                s0_t[:, :], s0_t[:, :], out_ps[f][:, :]), wait=(e2,))
            sd = p.dma("gpsimd", s1_d[b, f * 128:(f + 1) * 128, sj:sj + 512], s0_t[:, :],
                       wait=(e3,))
            prev_s0 = (sd, e3)
            war_out[f] = (e3,)
            s1_dma.append(sd)
        war_out[2] = war_out.get(2, ()) + (rbe,)
        prev_j = prev_s0
    p.op("sync", lambda s: s.nop(), wait=tuple(s1_dma), inc=False)


def emit_l1ffn(p, io):
    S = LLOC
    T = B * S
    x_d = io["x"]
    ab_d = io["ab"]
    s1_d = io["s1"]
    rw_d = io["router_w"]
    rb_d = io["router_b"]
    ew_d = io["expert_w"]
    eb_d = io["expert_b"]
    x2_d = io["x2"]

    xn_bf = p.sbuf("xn_bf", [128, 8 * T], BF16)
    xe_bf = p.sbuf("xe_bf", [128, 8 * T], BF16)
    ew_sb = p.sbuf("ew_sb", [128, 8 * 1024], BF16)
    ebias = p.sbuf("ebias", [8, 1024], BF16)
    y_acc = p.sbuf("y_acc", [128, 8 * T], BF16)
    ab_sb = p.sbuf("ab_sb", [128, 2 * T], F32)
    rows = p.sbuf("rows", [1, 2 * T], F32)
    ones_row = p.sbuf("ones_row", [1, 128], F32)
    rw_sb = p.sbuf("rw_sb", [128, 8 * 8], BF16)
    rbias = p.sbuf("rbias", [1, 8], F32)
    lg_sb = p.sbuf("lg_sb", [8, T], F32)
    lt_sb = p.sbuf("lt_sb", [128, 8], F32)
    lcol = p.sbuf("lcol", [128, 4], F32)
    crow = p.sbuf("crow", [8, T], F32)
    crow_bf = p.sbuf("crow_bf", [8, T], BF16)
    sel_sb = p.sbuf("sel_sb", [8, 8 * 128], BF16)
    io_sc = p.sbuf("io_sc", [8, 8 * 128], F32)
    cb_ps = p.sbuf("cb_ps", [128, 512], BF16)
    identf = p.sbuf("identf", [128, 128], F32)
    mk_sb = p.sbuf("mk_sb", [128, 8], F32)
    id_sc = p.sbuf("id_sc", [128, 128], F32)
    x0_t = p.sbuf("x0_t", [128, 512], F32)
    x1_t = p.sbuf("x1_t", [128, 512], F32)
    s0_t = p.sbuf("s0_t", [128, 512], BF16)
    g_t = p.sbuf("g_t", [128, 512], BF16)
    pool = PsumPool(p, "ps", n=4)
    ps_r = p.psum("ps_r", [128, 512], F32)
    ps_cbp = p.psum("ps_cbp", [128, 512], F32)
    ps_tf = p.psum("ps_tf", [128, 512], F32)

    t_ones = p.op("vector", lambda v: v.memset(ones_row[:, :], 1.0))
    j1 = p.op("gpsimd", lambda g: g.iota(id_sc[:, :], pattern=[[1, 128]], base=0,
                                         channel_multiplier=-1,
                                         allow_small_or_imprecise_dtypes=True))
    j3 = p.op("vector", lambda v: v.tensor_single_scalar(identf[:, :], id_sc[:, :], 0.0,
                                                         ALU.is_equal), wait=(j1,))

    k1 = p.op("gpsimd", lambda g: g.iota(io_sc[:, :].rearrange("p (e q) -> p e q", e=8),
                                         pattern=[[1, 8], [0, 128]], base=0,
                                         channel_multiplier=-1,
                                         allow_small_or_imprecise_dtypes=True))
    k2 = p.op("vector", lambda v: v.tensor_single_scalar(sel_sb[:, :], io_sc[:, :], 0.0,
                                                         ALU.is_equal), wait=(k1,))

    wldc = [p.dma("gpsimd", rw_sb[:, :].rearrange("f (k e) -> f k e", e=8)[:, :, e],
                  rw_d.rearrange("(k f) e -> f k e", f=128)[:, :, e],
                  allow_slow_non_contiguous=True) for e in range(8)]
    wldc.append(p.dma("sync", rbias[:, :], rb_d[None, :]))
    wldc.append(p.dma("gpsimd", ebias[:, :], eb_d))
    rlda = [p.dma("sync", rows[0:1, b * S:(b + 1) * S], ab_d[0, b, :]) for b in range(B)]
    rlda += [p.dma("sync", rows[0:1, T + b * S:T + (b + 1) * S], ab_d[1, b, :])
             for b in range(B)]

    bc_toks = []
    ea = eb = None
    for (t0, t1) in chunks_of(T):
        ta = p.op("tensor", lambda t, t0=t0, t1=t1: t.matmul(
            ps_r[:, 0:t1 - t0], ones_row[:, :], rows[0:1, t0:t1],
            start=True, stop=True), wait=tuple(rlda) + (t_ones,) + ((ea,) if ea else ()))
        ea = p.op("vector", lambda v, t0=t0, t1=t1: v.tensor_copy(
            ab_sb[:, t0:t1], ps_r[:, 0:t1 - t0]), wait=(ta,))
        bc_toks.append(ea)
    for (t0, t1) in chunks_of(T):
        tb = p.op("tensor", lambda t, t0=t0, t1=t1: t.matmul(
            ps_r[:, 0:t1 - t0], ones_row[:, :], rows[0:1, T + t0:T + t1],
            start=True, stop=True), wait=(bc_toks[-1],) + ((eb,) if eb else ()))
        eb = p.op("vector", lambda v, t0=t0, t1=t1: v.tensor_copy(
            ab_sb[:, T + t0:T + t1], ps_r[:, 0:t1 - t0]), wait=(tb,))
        bc_toks.append(eb)
    bc_toks = tuple(bc_toks)

    xn_toks = []
    prev = ()
    for f in range(8):
        for b in range(B):
            for (c0, c1_) in chunks_of(S):
                lx = p.dma("sync", x0_t[:, 0:c1_ - c0],
                           x_d[b, f * 128:(f + 1) * 128, c0:c1_], wait=prev)
                tok = p.op("vector", lambda v, f=f, b=b, c0=c0, c1_=c1_: v.scalar_tensor_tensor(
                    out=xn_bf[:, f * T + b * S + c0:f * T + b * S + c1_],
                    in0=x0_t[:, 0:c1_ - c0], scalar=1.0,
                    in1=ab_sb[:, b * S + c0:b * S + c1_], op0=ALU.bypass, op1=ALU.mult),
                    wait=(lx,) + bc_toks)
                tok2 = p.op("vector", lambda v, f=f, b=b, c0=c0, c1_=c1_: v.tensor_add(
                    xn_bf[:, f * T + b * S + c0:f * T + b * S + c1_],
                    xn_bf[:, f * T + b * S + c0:f * T + b * S + c1_],
                    ab_sb[:, T + b * S + c0:T + b * S + c1_]), wait=(tok,))
                prev = (tok2,)
                xn_toks.append(tok2)
    xn_toks = tuple(xn_toks)

    lg_toks = []
    le_prev = None
    for (t0, t1) in chunks_of(T):
        tok = None
        for k in range(8):
            def fg(t, k=k, t0=t0, t1=t1):
                return t.matmul(ps_r[0:8, 0:t1 - t0], rw_sb[:, k * 8:(k + 1) * 8],
                                xn_bf[:, k * T + t0:k * T + t1],
                                start=(k == 0), stop=(k == 7))
            tok = p.op("tensor", fg, wait=(xn_toks + tuple(wldc) +
                                           ((le_prev,) if le_prev else ())) if k == 0 else (),
                       inc=(k == 7))
        le_prev = p.op("vector", lambda v, t0=t0, t1=t1: v.tensor_copy(
            lg_sb[0:8, t0:t1], ps_r[0:8, 0:t1 - t0]), wait=(tok,))
        lg_toks.append(le_prev)
    lg_toks = tuple(lg_toks)

    tr_rb = p.op("tensor", lambda t: t.matmul(ps_r[:, 0:8], ones_row[:, :], rbias[0:1, 0:8],
                                              start=True, stop=True),
                 wait=(wldc[8], t_ones, bc_toks[-1]) + lg_toks)
    rb_t = p.sbuf("rb_t", [128, 8], F32)
    ev_rb = p.op("vector", lambda v: v.tensor_copy(rb_t[:, :], ps_r[:, 0:8]), wait=(tr_rb,))

    crow_toks = []
    prev_tt = ()
    for tt in range(16):
        tsl = slice(tt * 128, (tt + 1) * 128)
        tr1 = p.op("tensor", lambda t, tsl=tsl: t.transpose(
            ps_tf[:, 0:8], lg_sb[0:8, tsl], identf[0:8, 0:8]), wait=lg_toks + (j3,) + prev_tt)
        ev1 = p.op("vector", lambda v: v.tensor_copy(lt_sb[:, 0:8], ps_tf[:, 0:8]),
                   wait=(tr1,) + prev_tt)
        a1 = p.op("vector", lambda v: v.tensor_add(lt_sb[:, 0:8], lt_sb[:, 0:8], rb_t[:, :]),
                  wait=(ev1, ev_rb))
        ngm = p.op("vector", lambda v: v.reduce_max(lcol[:, 0:1], lt_sb[:, 0:8], axis=AX.X,
                                                    negate=True), wait=(a1,))
        mk0 = p.op("vector", lambda v: v.tensor_scalar(
            out=mk_sb[:, :], in0=lt_sb[:, 0:8], scalar1=lcol[:, 0:1], scalar2=0.0,
            op0=ALU.add, op1=ALU.is_ge), wait=(ngm,))
        ex = p.op("scalar", lambda s: s.activation(out=lt_sb[:, 0:8], in_=lt_sb[:, 0:8],
                                                   func=AF.Exp, bias=lcol[:, 0:1]),
                  wait=(mk0,))
        sm = p.op("vector", lambda v: v.reduce_sum(lcol[:, 1:2], lt_sb[:, 0:8], axis=AX.X),
                  wait=(ex,))
        gt = p.op("vector", lambda v: v.reciprocal(lcol[:, 2:3], lcol[:, 1:2]), wait=(sm,))
        mk = p.op("vector", lambda v: v.tensor_scalar(
            out=lt_sb[:, 0:8], in0=mk_sb[:, :], scalar1=lcol[:, 2:3], scalar2=None,
            op0=ALU.mult), wait=(gt,))
        trb = p.op("tensor", lambda t: t.transpose(ps_tf[0:8, 128:256], lt_sb[:, 0:8],
                                                   identf[:, :]), wait=(mk,))
        evc = p.op("vector", lambda v, tsl=tsl: v.tensor_copy(crow[0:8, tsl],
                                                              ps_tf[0:8, 128:256]), wait=(trb,))
        evcb = p.op("scalar", lambda s, tsl=tsl: s.activation(out=crow_bf[0:8, tsl],
                                                              in_=crow[0:8, tsl],
                                                              func=AF.Copy), wait=(evc,))
        crow_toks.append(evcb)
        prev_tt = (evc, mk, evcb)
    crow_toks = tuple(crow_toks)

    yacc_tok = {}
    wload_prev = ()
    for e in range(8):
        wl = tuple(p.dma("gpsimd", ew_sb[:, k * 1024:(k + 1) * 1024],
                         ew_d[e, k * 128:(k + 1) * 128, :], wait=wload_prev)
                   for k in range(8))
        cb_toks = []
        for (t0, t1) in chunks_of(T):
            tcb = p.op("tensor", lambda t, e=e, t0=t0, t1=t1: t.matmul(
                ps_cbp[:, 0:t1 - t0],
                sel_sb[0:8, e * 128:(e + 1) * 128], crow_bf[0:8, t0:t1],
                start=True, stop=True), wait=crow_toks + (k2,) +
                tuple(yacc_tok.get(("cb",), ())))
            ecb = p.op("vector", lambda v, t0=t0, t1=t1: v.tensor_copy(
                cb_ps[:, 0:t1 - t0], ps_cbp[:, 0:t1 - t0]), wait=(tcb,))
            mks = []
            for f in range(8):
                mks.append(p.op("vector", lambda v, f=f, t0=t0, t1=t1: v.tensor_mul(
                    xe_bf[:, f * T + t0:f * T + t1], xn_bf[:, f * T + t0:f * T + t1],
                    cb_ps[:, 0:t1 - t0]), wait=(ecb,) + xn_toks +
                    tuple(yacc_tok.get("xe_war", ()))))
            cb_toks.append((t0, t1, tuple(mks), ecb))
            yacc_tok[("cb",)] = (ecb,) + tuple(mks)
        for j in range(4):
            t0 = j * 512
            mk_toks = cb_toks[j][2]
            for mt in range(8):
                bank, ps, war = pool.get()
                tok = None
                for k in range(8):
                    def fe(t, mt=mt, k=k, t0=t0, ps=ps):
                        return t.matmul(ps[:, :],
                                        ew_sb[:, k * 1024 + mt * 128:k * 1024 + (mt + 1) * 128],
                                        xe_bf[:, k * T + t0:k * T + t0 + 512],
                                        start=(k == 0), stop=(k == 7))
                    tok = p.op("tensor", fe, wait=(wl + war + mk_toks) if k == 0 else (),
                               inc=(k == 7))
                yacc_tok["xe_war"] = (tok,)
                prev_y = yacc_tok.get((j, mt), ())
                if e == 0:
                    ev = p.op("vector", lambda v, mt=mt, t0=t0, ps=ps: v.tensor_copy(
                        y_acc[:, mt * T + t0:mt * T + t0 + 512], ps[:, :]),
                        wait=(tok,) + prev_y)
                else:
                    ev = p.op("vector", lambda v, mt=mt, t0=t0, ps=ps: v.tensor_add(
                        y_acc[:, mt * T + t0:mt * T + t0 + 512],
                        y_acc[:, mt * T + t0:mt * T + t0 + 512], ps[:, :]),
                        wait=(tok,) + prev_y)
                pool.consumed(bank, ev)
                yacc_tok[(j, mt)] = (ev,)
        wload_prev = tuple(yacc_tok[(j, mt)][0] for j in range(4) for mt in range(8))
    for j in range(4):
        t0 = j * 512
        for mt in range(8):
            bank, ps, war = pool.get()

            def fb(t, mt=mt, t0=t0, ps=ps):
                return t.matmul(ps[:, :], ebias[0:8, mt * 128:(mt + 1) * 128],
                                crow_bf[0:8, t0:t0 + 512], start=True, stop=True)
            tok = p.op("tensor", fb, wait=crow_toks + war + (wldc[9],), inc=True)
            ev = p.op("vector", lambda v, mt=mt, t0=t0, ps=ps: v.tensor_add(
                y_acc[:, mt * T + t0:mt * T + t0 + 512],
                y_acc[:, mt * T + t0:mt * T + t0 + 512], ps[:, :]),
                wait=(tok,) + yacc_tok[(j, mt)])
            pool.consumed(bank, ev)
            yacc_tok[(j, mt)] = (ev,)

    fin = []
    prev = (xn_toks[-1],)
    for j in range(4):
        b, sj = divmod(j * 512, S)
        for f in range(8):
            gt = p.op("scalar", lambda s, f=f, j=j: s.activation(
                out=g_t[:, :], in_=y_acc[:, f * T + j * 512:f * T + (j + 1) * 512],
                func=AF.Silu), wait=tuple(yacc_tok[(j, f)]) + prev)
            lx = p.dma("sync", x0_t[:, :], x_d[b, f * 128:(f + 1) * 128, sj:sj + 512],
                       wait=prev)
            ls = p.dma("scalar", s0_t[:, :], s1_d[b, f * 128:(f + 1) * 128, sj:sj + 512],
                       wait=prev)
            t1_ = p.op("vector", lambda v: v.tensor_mul(
                x1_t[:, :], s0_t[:, :], g_t[:, :]), wait=(gt, ls) + prev)
            t2_ = p.op("vector", lambda v: v.tensor_add(x1_t[:, :], x1_t[:, :], x0_t[:, :]),
                       wait=(t1_, lx))
            xd = p.dma("gpsimd", x2_d[b, f * 128:(f + 1) * 128, sj:sj + 512], x1_t[:, :],
                       wait=(t2_,))
            prev = (xd, t2_)
            fin.append(xd)
    p.op("sync", lambda s: s.nop(), wait=tuple(fin), inc=False)


def emit_l2pre(p, io):
    D, NF = 1024, 8
    T = B * LLOC
    x_d = io["x"]
    lnw_d = io["ln_w"]
    lnb_d = io["ln_b"]
    wq_d = io["Wq"]
    wk_d = io["Wk"]
    wv_d = io["Wv"]
    bq_d = io["bq"]
    bk_d = io["bk"]
    bv_d = io["bv"]
    q_o = io["q"]
    k_o = io["k"]
    v_o = io["v"]
    ab_d = io["ab"]

    x_bf = p.sbuf("x_bf", [128, NF * T], BF16)
    w_sb = p.sbuf("w_sb", [128, NF * D], BF16)
    bias_sb = p.sbuf("bias_sb", [128, 3 * NF], F32)
    gam = p.sbuf("gam", [128, 2 * NF], F32)
    ones_bf = p.sbuf("ones_bf", [128, 1], BF16)
    ones_row = p.sbuf("ones_row", [1, 128], F32)
    bd2 = p.sbuf("bd2", [2, 128], F32)
    bdr = p.sbuf("bdr", [128, 2], BF16)
    bdrf = p.sbuf("bdrf", [128, 2], F32)
    identf = p.sbuf("identf", [128, 128], F32)
    id_sc = p.sbuf("id_sc", [128, 128], F32)
    eps_sb = p.sbuf("eps_sb", [1, 1], F32)
    e12 = p.sbuf("e12", [1, 1], F32)
    rows = p.sbuf("rows", [1, 4 * T], F32)
    hr = p.sbuf("hr", [2, 2 * T], F32)
    ab_sb = p.sbuf("ab_sb", [128, 2 * T], F32)
    h_bf = p.sbuf("h_bf", [128, NF * T], BF16)
    tmp_f = p.sbuf("tmp_f", [128, T], F32)
    st_f = p.sbuf("st_f", [128, 512], F32)
    st_bf = p.sbuf("st_bf", [128, 512], BF16)
    nrm_sb = p.sbuf("nrm_sb", [128, 512], F32)
    pool = PsumPool(p, "ps", n=4)
    ps_a = p.psum("ps_a", [128, 512], F32)
    ps_b = p.psum("ps_b", [128, 512], F32)

    xld = []
    for f in range(NF):
        for b in range(B):
            xld.append(p.dma("gpsimd", x_bf[:, f * T + b * LLOC: f * T + (b + 1) * LLOC],
                             x_d[b, f * 128:(f + 1) * 128, :]))
    xld = tuple(xld)
    cld = [load_col_vec(p, gam[:, 0:NF], lnw_d),
           p.dma("sync", gam[:, NF:2 * NF], lnb_d.rearrange("(m f) -> f m", f=128),
                 allow_slow_non_contiguous=True),
           p.dma("sync", bias_sb[:, 0:NF], bq_d.rearrange("(m f) -> f m", f=128),
                 allow_slow_non_contiguous=True),
           p.dma("sync", bias_sb[:, NF:2 * NF], bk_d.rearrange("(m f) -> f m", f=128),
                 allow_slow_non_contiguous=True),
           p.dma("sync", bias_sb[:, 2 * NF:3 * NF], bv_d.rearrange("(m f) -> f m", f=128),
                 allow_slow_non_contiguous=True)]
    t_ones = p.op("vector", lambda v: v.memset(ones_bf[:, :], 1.0))
    t_ones2 = p.op("vector", lambda v: v.memset(ones_row[:, :], 1.0))
    t_eps = p.op("vector", lambda v: v.memset(eps_sb[:, :], 1e-5))
    t_e12 = p.op("vector", lambda v: v.memset(e12[:, :], 1e-12))
    b4 = p.op("vector", lambda v: v.memset(bdr[:, :], 0.0))
    b5 = p.op("vector", lambda v: v.memset(bdr[0:64, 0:1], 1.0), wait=(b4,))
    b6 = p.op("vector", lambda v: v.memset(bdr[64:128, 1:2], 1.0), wait=(b5,))
    b4f = p.op("vector", lambda v: v.memset(bdrf[:, :], 0.0))
    b5f = p.op("vector", lambda v: v.memset(bdrf[0:64, 0:1], 1.0), wait=(b4f,))
    b6f = p.op("vector", lambda v: v.memset(bdrf[64:128, 1:2], 1.0), wait=(b5f,))
    j1 = p.op("gpsimd", lambda g: g.iota(id_sc[:, :], pattern=[[1, 128]], base=0,
                                         channel_multiplier=-1,
                                         allow_small_or_imprecise_dtypes=True))
    j3 = p.op("vector", lambda v: v.tensor_single_scalar(identf[:, :], id_sc[:, :], 0.0,
                                                         ALU.is_equal), wait=(j1,))
    tb0 = p.op("tensor", lambda t: t.matmul(ps_a[0:2, 0:128], bdrf[:, :], identf[:, :],
                                            start=True, stop=True), wait=(b6f, j3))
    b3 = p.op("vector", lambda v: v.tensor_copy(bd2[:, :], ps_a[0:2, 0:128]), wait=(tb0,))
    p._wprev = {}

    sq_prev = {}
    ev1p = ev2p = None
    for (t0, t1) in chunks_of(T):
        cw_ = t1 - t0
        tok1 = tok2 = None
        for f in range(NF):
            w_sq = xld + ((sq_prev[f],) if f in sq_prev else ())
            sqt = p.op("scalar", lambda s, f=f, t0=t0, t1=t1: s.activation(
                out=h_bf[:, f * T:f * T + (t1 - t0)],
                in_=x_bf[:, f * T + t0:f * T + t1], func=AF.Square), wait=w_sq)
            w1 = xld + (t_ones, b3) + ((ev1p,) if (f == 0 and ev1p) else ())
            tok1 = p.op("tensor", lambda t, f=f, t0=t0, t1=t1, cw_=cw_: t.matmul(
                ps_a[0:1, 0:cw_], ones_bf[:, :], x_bf[:, f * T + t0:f * T + t1],
                start=(f == 0), stop=(f == NF - 1)), wait=w1, inc=True)
            w2 = (sqt,) + ((ev2p,) if (f == 0 and ev2p) else ())
            tok2 = p.op("tensor", lambda t, f=f, cw_=cw_: t.matmul(
                ps_b[0:1, 0:cw_], ones_bf[:, :], h_bf[:, f * T:f * T + cw_],
                start=(f == 0), stop=(f == NF - 1)), wait=w2, inc=True)
            sq_prev[f] = tok2
        ev1p = p.op("vector", lambda v, t0=t0, t1=t1: v.tensor_copy(
            rows[0:1, 0 * T + t0:0 * T + t1], ps_a[0:1, 0:t1 - t0]), wait=(tok1,))
        ev2p = p.op("vector", lambda v, t0=t0, t1=t1: v.tensor_copy(
            rows[0:1, 1 * T + t0:1 * T + t1], ps_b[0:1, 0:t1 - t0]), wait=(tok2,))
    S_ = lambda i: rows[0:1, i * T:(i + 1) * T]
    rtok = (ev1p, ev2p, t_eps)
    r1 = p.op("vector", lambda v: v.tensor_single_scalar(S_(2), S_(0), 1.0 / D, ALU.mult), wait=rtok)
    r2 = p.op("vector", lambda v: v.tensor_single_scalar(S_(3), S_(1), 1.0 / D, ALU.mult), wait=rtok)
    r3 = p.op("vector", lambda v: v.tensor_mul(S_(0), S_(2), S_(2)), wait=(r1,))
    r4 = p.op("vector", lambda v: v.tensor_sub(S_(1), S_(3), S_(0)), wait=(r2, r3))
    r5 = p.op("scalar", lambda s: s.activation(out=S_(0), in_=S_(1), func=AF.Sqrt,
                                               bias=eps_sb[0:1, 0:1]), wait=(r4,))
    r6 = p.op("vector", lambda v: v.reciprocal(S_(3), S_(0)), wait=(r5,))
    ab_dma = []
    for b in range(B):
        ab_dma.append(p.dma("sync", ab_d[0, b, :],
                            rows[0:1, 3 * T + b * LLOC:3 * T + (b + 1) * LLOC], wait=(r6,)))
    r7 = p.op("vector", lambda v: v.tensor_mul(S_(0), S_(0), S_(0)), wait=(r6,) + tuple(ab_dma))
    r8 = p.op("vector", lambda v: v.reciprocal(S_(0), S_(0)), wait=(r7,))
    r9 = p.op("vector", lambda v: v.tensor_mul(S_(0), S_(1), S_(0)), wait=(r8,))
    r10 = p.op("scalar", lambda s: s.activation(out=S_(0), in_=S_(0), func=AF.Sqrt,
                                                bias=eps_sb[0:1, 0:1]), wait=(r9,))
    r11 = p.op("vector", lambda v: v.reciprocal(S_(0), S_(0)), wait=(r10,))
    r12 = p.op("vector", lambda v: v.tensor_mul(S_(1), S_(2), S_(3)), wait=(r11,))
    r13 = p.op("vector", lambda v: v.tensor_single_scalar(S_(1), S_(1), -1.0, ALU.mult), wait=(r12,))
    for b in range(B):
        ab_dma.append(p.dma("sync", ab_d[1, b, :],
                            rows[0:1, 1 * T + b * LLOC:1 * T + (b + 1) * LLOC], wait=(r13,)))
    r14 = p.op("vector", lambda v: v.tensor_mul(S_(2), S_(3), S_(0)), wait=(r13,))
    r15 = p.op("vector", lambda v: v.tensor_mul(S_(3), S_(1), S_(0)),
               wait=(r14,) + tuple(ab_dma))
    bc_toks = []
    ea = eb = None
    for (t0, t1) in chunks_of(T):
        tba = p.op("tensor", lambda t, t0=t0, t1=t1: t.matmul(
            ps_a[:, 0:t1 - t0], ones_row[:, :], rows[0:1, 2 * T + t0:2 * T + t1],
            start=True, stop=True), wait=(r14, ev1p, t_ones2) + ((ea,) if ea else ()))
        tbb = p.op("tensor", lambda t, t0=t0, t1=t1: t.matmul(
            ps_b[:, 0:t1 - t0], ones_row[:, :], rows[0:1, 3 * T + t0:3 * T + t1],
            start=True, stop=True), wait=(r15, ev2p) + ((eb,) if eb else ()))
        ea = p.op("vector", lambda v, t0=t0, t1=t1: v.tensor_copy(
            ab_sb[:, t0:t1], ps_a[:, 0:t1 - t0]), wait=(tba,))
        eb = p.op("vector", lambda v, t0=t0, t1=t1: v.tensor_copy(
            ab_sb[:, T + t0:T + t1], ps_b[:, 0:t1 - t0]), wait=(tbb,))
        bc_toks += [ea, eb]
    bc_toks = tuple(bc_toks)

    h_toks = []
    prev = None
    for f in range(NF):
        w = bc_toks + xld + (cld[0], cld[1]) + ((prev,) if prev else ())
        t1_ = p.op("vector", lambda v, f=f: v.scalar_tensor_tensor(
            out=tmp_f[:, :], in0=x_bf[:, f * T:(f + 1) * T], scalar=1.0,
            in1=ab_sb[:, 0:T], op0=ALU.bypass, op1=ALU.mult), wait=w)
        t2_ = p.op("vector", lambda v: v.tensor_add(tmp_f[:, :], tmp_f[:, :],
                                                    ab_sb[:, T:2 * T]), wait=(t1_,))
        t3_ = p.op("vector", lambda v, f=f: v.tensor_scalar(
            out=h_bf[:, f * T:(f + 1) * T], in0=tmp_f[:, :],
            scalar1=gam[:, f:f + 1], scalar2=gam[:, NF + f:NF + f + 1],
            op0=ALU.mult, op1=ALU.add), wait=(t2_,))
        prev = t3_
        h_toks.append(t3_)
    h_toks = tuple(h_toks)

    for wi, (wd, out_d, bcol, do_norm) in enumerate([
            (wq_d, q_o, 0, True), (wk_d, k_o, 1, True), (wv_d, v_o, 2, False)]):
        wl = tuple(p.dma("gpsimd", w_sb[:, f * D:(f + 1) * D],
                         wd[f * 128:(f + 1) * 128, :],
                         wait=tuple(p._wprev.get(wi, ()))) for f in range(NF))
        st_prev = ()
        for m in range(NF):
            for j4, (t0, t1) in enumerate(chunks_of(T)):
                b, sj = divmod(t0, LLOC)
                bank, ps, war = pool.get()
                tok = None
                for k in range(NF):
                    wk_ = (wl + war + h_toks) if k == 0 else ()

                    def fn(t, m=m, k=k, t0=t0, t1=t1, ps=ps):
                        return t.matmul(ps[:, 0:t1 - t0],
                                        w_sb[:, k * D + m * 128:k * D + (m + 1) * 128],
                                        h_bf[:, k * T + t0:k * T + t1],
                                        start=(k == 0), stop=(k == NF - 1))
                    tok = p.op("tensor", fn, wait=wk_, inc=(k == NF - 1))
                e1 = p.op("vector", lambda v, ps=ps, m=m, bcol=bcol: v.tensor_scalar_add(
                    st_f[:, :], ps[:, :], bias_sb[:, bcol * NF + m:bcol * NF + m + 1]),
                    wait=(tok, cld[2 + bcol]) + st_prev)
                if not do_norm:
                    e2 = p.op("scalar", lambda s: s.activation(out=st_bf[:, :],
                                                               in_=st_f[:, :], func=AF.Copy),
                              wait=(e1,))
                    dd = p.dma("sync", out_d[b, m * 128:(m + 1) * 128, sj:sj + 512],
                               st_bf[:, :], wait=(e2,))
                    st_prev = (e2, dd)
                    pool.consumed(bank, e1)
                    continue
                sq2 = p.op("scalar", lambda s: s.activation(out=st_bf[:, :], in_=st_f[:, :],
                                                            func=AF.Square), wait=(e1,))
                mm2 = p.op("tensor", lambda t: t.matmul(ps_a[0:2, 0:512],
                                                        bdr[:, :], st_bf[:, :],
                                                        start=True, stop=True),
                           wait=(sq2, b6) + st_prev)
                h1 = p.op("vector", lambda v: v.tensor_copy(hr[0:2, 0:512], ps_a[0:2, 0:512]),
                          wait=(mm2,))
                h2 = p.op("scalar", lambda s: s.activation(out=hr[0:2, 512:1024],
                                                           in_=hr[0:2, 0:512], func=AF.Sqrt),
                          wait=(h1,))
                h3 = p.op("vector", lambda v: v.tensor_single_scalar(
                    hr[0:2, 512:1024], hr[0:2, 512:1024], 1e-12, ALU.max), wait=(h2,))
                h4 = p.op("vector", lambda v: v.reciprocal(hr[0:2, 512:1024],
                                                           hr[0:2, 512:1024]), wait=(h3,))
                mm3 = p.op("tensor", lambda t: t.matmul(ps_b[:, 0:512], bd2[:, :],
                                                        hr[0:2, 512:1024],
                                                        start=True, stop=True),
                           wait=(h4, b3))
                nrm = p.op("vector", lambda v: v.tensor_copy(nrm_sb[:, :], ps_b[:, 0:512]),
                           wait=(mm3,))
                e3 = p.op("vector", lambda v: v.tensor_mul(st_bf[:, :], st_f[:, :],
                                                           nrm_sb[:, :]), wait=(nrm, e1))
                dd = p.dma("sync", out_d[b, m * 128:(m + 1) * 128, sj:sj + 512],
                           st_bf[:, :], wait=(e3,))
                st_prev = (e3, dd, sq2, h1, nrm)
                pool.consumed(bank, e3)
        p._wprev = getattr(p, "_wprev", {})
        p._wprev[wi + 1] = st_prev
    p.op("sync", lambda s: s.nop(), wait=tuple(ab_dma) + st_prev, inc=False)


def emit_l2attn(p, io):
    S = 8192
    Q = LLOC
    q_d = io["q"]
    k_d = io["k"]
    v_d = io["v"]
    o_d = io["state"]

    ksb = [p.sbuf(f"ksb{i}", [64, S], BF16) for i in range(2)]
    vsb = [p.sbuf(f"vsb{i}", [64, S], BF16) for i in range(2)]
    vT = [p.sbuf(f"vT{i}", [128, 64 * 65], BF16) for i in range(2)]
    st1 = [p.sbuf(f"st1{i}", [64, Q], BF16) for i in range(2)]
    st2 = [p.sbuf(f"st2{i}", [64, Q], BF16) for i in range(2)]
    pr_sb = [p.sbuf(f"pr_sb{i}", [128, 2 * 512], BF16) for i in range(2)]
    rb_sb = p.sbuf("rb_sb", [1, 512], F32)
    rb64 = p.sbuf("rb64", [64, 512], F32)
    ones1 = p.sbuf("ones1", [1, 64], F32)
    identb = p.sbuf("identb", [128, 128], BF16)
    id_sc = p.sbuf("id_sc", [128, 128], F32)
    ps_sc = [[p.psum(f"ps_sc{i}_{j}", [128, 512], F32) for j in range(2)]
             for i in range(2)]
    ps_cx = [p.psum(f"ps_cx{i}", [128, 512], F32) for i in range(2)]
    ps_tr = p.psum("ps_tr", [128, 512], BF16)
    ps_rb = p.psum("ps_rb", [128, 512], F32)

    j1 = p.op("gpsimd", lambda g: g.iota(id_sc[:, :], pattern=[[1, 128]], base=0,
                                         channel_multiplier=-1,
                                         allow_small_or_imprecise_dtypes=True))
    j3 = p.op("vector", lambda v: v.tensor_single_scalar(identb[:, :], id_sc[:, :], 0.0,
                                                         ALU.is_equal), wait=(j1,))
    t_o1 = p.op("vector", lambda v: v.memset(ones1[:, :], 1.0))

    prev_pair = {0: (), 1: ()}
    last_tr_ev = None
    rb_war = ()
    out_dma = []
    for b in range(B):
        for hp in range(8):
            heads = (2 * hp, 2 * hp + 1)
            lk = {}
            lq = {}
            ev_toks = {}
            for i, h in enumerate(heads):
                lk[i] = tuple(p.dma("gpsimd", ksb[i][:, r * LLOC:(r + 1) * LLOC],
                                    k_d[r, b, h * 64:(h + 1) * 64, :],
                                    wait=prev_pair[i]) for r in range(8))
                lv = tuple(p.dma("gpsimd", vsb[i][:, r * LLOC:(r + 1) * LLOC],
                                 v_d[r, b, h * 64:(h + 1) * 64, :],
                                 wait=prev_pair[i]) for r in range(8))
                lq[i] = p.dma("sync", st1[i][:, :], q_d[b, h * 64:(h + 1) * 64, :],
                              wait=prev_pair[i])
                vm = p.op("vector", lambda v, i=i: v.memset(vT[i][:, :], 1.0),
                          wait=prev_pair[i])
                evs = []
                prev_tr = None
                for kt in range(64):
                    w0 = ()
                    if kt == 0:
                        w0 = lv + (j3,)
                        if last_tr_ev is not None:
                            w0 = w0 + (last_tr_ev,)
                    tr = p.op("tensor", lambda t, i=i, kt=kt: t.transpose(
                        ps_tr[:, 0:64], vsb[i][:, kt * 128:(kt + 1) * 128],
                        identb[0:64, 0:64]),
                        wait=w0 + ((prev_tr,) if prev_tr else ()))
                    ev = p.op("vector", lambda v, i=i, kt=kt: v.tensor_copy(
                        vT[i][:, kt * 65:kt * 65 + 64], ps_tr[:, 0:64]), wait=(tr, vm))
                    prev_tr = ev
                    evs.append(ev)
                ev_toks[i] = tuple(evs)
                last_tr_ev = evs[-1]
            st_cur, st_nxt = st1, st2
            st_tok = {0: (lq[0],), 1: (lq[1],)}
            prev_sn = {0: (), 1: ()}
            for step in range(2):
                new_tok = {0: [], 1: []}
                for qc in range(2):
                    qsl = slice(qc * 512, (qc + 1) * 512)
                    exv_t = {}
                    ctx_t = {}
                    for kk in range(65):
                        if kk < 64:
                            kt, pb = kk, kk % 2
                            for i in range(2):
                                wsc = lk[i] + st_tok[i] + (
                                    (exv_t[(i, kt - 2)],) if kt >= 2 else ())

                                def fsc(t, i=i, kt=kt, pb=pb, qsl=qsl, sc_=st_cur):
                                    return t.matmul(ps_sc[i][pb][:, :],
                                                    ksb[i][:, kt * 128:(kt + 1) * 128],
                                                    sc_[i][:, qsl],
                                                    start=True, stop=True)
                                msc = p.op("tensor", fsc, wait=wsc, inc=True)
                                exv = p.op("scalar", lambda s, i=i, pb=pb: s.activation(
                                    out=pr_sb[i][:, pb * 512:(pb + 1) * 512],
                                    in_=ps_sc[i][pb][:, :], func=AF.Exp),
                                    wait=(msc,) + ((ctx_t[(i, kt - 2)],)
                                                   if kt >= 2 else ()))
                                exv_t[(i, kt)] = exv
                        if kk >= 1:
                            kt, pb = kk - 1, (kk - 1) % 2
                            for i in range(2):
                                def fcx(t, i=i, kt=kt, pb=pb):
                                    return t.matmul(ps_cx[i][0:65, :],
                                                    vT[i][:, kt * 65:(kt + 1) * 65],
                                                    pr_sb[i][:, pb * 512:(pb + 1) * 512],
                                                    start=(kt == 0), stop=(kt == 63))
                                ctx = p.op("tensor", fcx,
                                           wait=(exv_t[(i, kt)],) +
                                           ((ev_toks[i] + prev_sn[i])
                                            if kt == 0 else ()),
                                           inc=(kt == 63))
                                ctx_t[(i, kt)] = ctx
                    for i in range(2):
                        rc = p.op("vector", lambda v, i=i: v.reciprocal(
                            rb_sb[0:1, :], ps_cx[i][64:65, :]),
                            wait=(ctx_t[(i, 63)],) + rb_war)
                        tb = p.op("tensor", lambda t: t.matmul(
                            ps_rb[0:64, :], ones1[:, :], rb_sb[0:1, :],
                            start=True, stop=True), wait=(rc, t_o1))
                        eb = p.op("vector", lambda v: v.tensor_copy(
                            rb64[:, :], ps_rb[0:64, :]), wait=(tb,))
                        sn = p.op("vector", lambda v, i=i, qsl=qsl, sn_=st_nxt: v.tensor_mul(
                            sn_[i][:, qsl], ps_cx[i][0:64, :], rb64[:, :]), wait=(eb,))
                        rb_war = (sn,)
                        prev_sn[i] = (sn,)
                        new_tok[i].append(sn)
                st_tok = {i: tuple(new_tok[i]) for i in range(2)}
                st_cur, st_nxt = st_nxt, st_cur
            for i, h in enumerate(heads):
                od = p.dma("sync", o_d[b, h * 64:(h + 1) * 64, :], st_cur[i][:, :],
                           wait=st_tok[i])
                out_dma.append(od)
                prev_pair[i] = st_tok[i] + (od,)
    p.op("sync", lambda s: s.nop(), wait=tuple(out_dma), inc=False)



def emit_l2post(p, io):
    SQ = LLOC
    T = B * SQ
    x_d = io["x"]
    ab_d = io["ab"]
    st_d = io["state"]
    wo_d = io["Wo"]
    bo_d = io["bo"]
    wfT_d = io["wfT"]
    wiT_d = io["wiT"]
    wcT_d = io["wcT"]
    ufT_d = io["ufT"]
    uiT_d = io["uiT"]
    ucT_d = io["ucT"]
    woT_d = io["woT"]
    uoT_d = io["uoT"]
    bf_d = io["bf_"]
    bi_d = io["bi"]
    bc_d = io["bc"]
    bo2_d = io["bo2"]
    inc_d = io["in_c"]
    x3_d = io["x3"]

    wbuf = p.sbuf("wbuf", [128, 6 * 8 * 1024], BF16)
    cols = p.sbuf("cols", [128, 10 * 8], F32)
    inc_bf = p.sbuf("inc_bf", [128, 8], BF16)
    ab_sb = p.sbuf("ab_sb", [128, 2 * T], F32)
    rows = p.sbuf("rows", [1, T], F32)
    ones_row = p.sbuf("ones_row", [1, 128], F32)
    xn_bf = p.sbuf("xn_bf", [128, 8 * T], BF16)
    gf = p.sbuf("gf", [128, 8 * 512], BF16)
    gi = p.sbuf("gi", [128, 8 * 512], BF16)
    gc = p.sbuf("gc", [128, 8 * 512], BF16)
    c_all = p.sbuf("c_all", [128, 8 * SQ], BF16)
    tmp_t = p.sbuf("tmp_t", [128, 512], F32)
    tmp2_t = p.sbuf("tmp2_t", [128, 512], F32)
    x0_t = p.sbuf("x0_t", [128, 512], F32)
    x1_t = p.sbuf("x1_t", [128, 512], F32)
    s2_t = p.sbuf("s2_t", [128, 512], F32)
    h_t = p.sbuf("h_t", [128, 512], BF16)
    pool = PsumPool(p, "ps", n=6)
    ps_r = p.psum("ps_r", [128, 512], F32)

    t_ones = p.op("vector", lambda v: v.memset(ones_row[:, :], 1.0))
    CB = lambda i: cols[:, i * 8:(i + 1) * 8]
    cld = [load_col_vec(p, CB(0), bo_d), load_col_vec(p, CB(1), bf_d),
           load_col_vec(p, CB(2), bi_d), load_col_vec(p, CB(3), bc_d),
           load_col_vec(p, CB(4), bo2_d), load_col_vec(p, CB(5), inc_d)]
    cv1 = p.op("scalar", lambda s: s.activation(out=inc_bf[:, :], in_=CB(5), func=AF.Copy),
               wait=(cld[5],))
    WSLOT = lambda g, k: wbuf[:, (g * 8 + k) * 1024:(g * 8 + k) * 1024 + 1024]
    wl1 = tuple(p.dma("gpsimd", WSLOT(g, f), wd[f * 128:(f + 1) * 128, :])
                for g, wd in enumerate([wfT_d, wiT_d, wcT_d, ufT_d, uiT_d, ucT_d])
                for f in range(8))

    bc_toks = []
    ea = None
    rlda = [p.dma("sync", rows[0:1, b * SQ:(b + 1) * SQ], ab_d[0, b, :]) for b in range(B)]
    for (t0, t1) in chunks_of(T):
        ta = p.op("tensor", lambda t, t0=t0, t1=t1: t.matmul(
            ps_r[:, 0:t1 - t0], ones_row[:, :], rows[0:1, t0:t1],
            start=True, stop=True), wait=tuple(rlda) + (t_ones,) + ((ea,) if ea else ()))
        ea = p.op("vector", lambda v, t0=t0, t1=t1: v.tensor_copy(
            ab_sb[:, t0:t1], ps_r[:, 0:t1 - t0]), wait=(ta,))
        bc_toks.append(ea)
    rldb = [p.dma("sync", rows[0:1, b * SQ:(b + 1) * SQ], ab_d[1, b, :],
                  wait=tuple(bc_toks)) for b in range(B)]
    eb = None
    for (t0, t1) in chunks_of(T):
        tb = p.op("tensor", lambda t, t0=t0, t1=t1: t.matmul(
            ps_r[:, 0:t1 - t0], ones_row[:, :], rows[0:1, t0:t1],
            start=True, stop=True), wait=tuple(rldb) + ((eb,) if eb else (bc_toks[-1],)))
        eb = p.op("vector", lambda v, t0=t0, t1=t1: v.tensor_copy(
            ab_sb[:, T + t0:T + t1], ps_r[:, 0:t1 - t0]), wait=(tb,))
        bc_toks.append(eb)
    bc_toks = tuple(bc_toks)

    xn_toks = []
    prev = ()
    for f in range(8):
        for b in range(B):
            for (c0, c1_) in chunks_of(SQ):
                lx = p.dma("sync", x0_t[:, 0:c1_ - c0],
                           x_d[b, f * 128:(f + 1) * 128, c0:c1_], wait=prev)
                tok = p.op("vector", lambda v, f=f, b=b, c0=c0, c1_=c1_: v.scalar_tensor_tensor(
                    out=xn_bf[:, f * T + b * SQ + c0:f * T + b * SQ + c1_],
                    in0=x0_t[:, 0:c1_ - c0], scalar=1.0,
                    in1=ab_sb[:, b * SQ + c0:b * SQ + c1_], op0=ALU.bypass, op1=ALU.mult),
                    wait=(lx,) + bc_toks)
                tok2 = p.op("vector", lambda v, f=f, b=b, c0=c0, c1_=c1_: v.tensor_add(
                    xn_bf[:, f * T + b * SQ + c0:f * T + b * SQ + c1_],
                    xn_bf[:, f * T + b * SQ + c0:f * T + b * SQ + c1_],
                    ab_sb[:, T + b * SQ + c0:T + b * SQ + c1_]), wait=(tok,))
                prev = (tok2,)
                xn_toks.append(tok2)
    xn_toks = tuple(xn_toks)

    iu_toks = []
    for g in range(3):
        for m in range(8):
            bank, ps, war = pool.get()
            tok = None
            for k in range(8):
                def fiu(t, g=g, m=m, k=k, ps=ps):
                    return t.matmul(ps[0:128, 0:1],
                                    wbuf[:, ((3 + g) * 8 + k) * 1024 + m * 128:
                                         ((3 + g) * 8 + k) * 1024 + (m + 1) * 128],
                                    inc_bf[:, k:k + 1], start=(k == 0), stop=(k == 7))
                tok = p.op("tensor", fiu, wait=(wl1 + war + (cv1,) + bc_toks)
                           if k == 0 else (), inc=(k == 7))
            ev = p.op("vector", lambda v, g=g, m=m, ps=ps: v.tensor_copy(
                cols[:, (6 + g) * 8 + m:(6 + g) * 8 + m + 1], ps[0:128, 0:1]), wait=(tok,))
            pool.consumed(bank, ev)
            iu_toks.append(ev)
    iu_toks = tuple(iu_toks)
    b0c = []
    for g in range(3):
        b0c.append(p.op("vector", lambda v, g=g: v.tensor_add(
            CB(6 + g), CB(6 + g), CB(1 + g)), wait=iu_toks + (cld[1 + g],)))
    b0c = tuple(b0c)

    gate_fn = {0: AF.Sigmoid, 1: AF.Sigmoid, 2: AF.Tanh}
    gbuf = {0: gf, 1: gi, 2: gc}
    ph1_toks = []
    c_toks = {}
    gate_war = {}
    for j in range(2):
        g_toks = {}
        for g in range(3):
            for m in range(8):
                bank, ps, war = pool.get()
                tok = None
                for k in range(8):
                    def f0(t, g=g, m=m, k=k, j=j, ps=ps):
                        return t.matmul(ps[:, :], WSLOT(g, k)[:, m * 128:(m + 1) * 128],
                                        xn_bf[:, k * T + 0 * SQ + j * 512:
                                              k * T + 0 * SQ + (j + 1) * 512],
                                        start=(k == 0), stop=(k == 7))
                    tok = p.op("tensor", f0,
                               wait=(wl1 + war + xn_toks + b0c +
                                     tuple(gate_war.get((g, m), ()))) if k == 0 else (),
                               inc=(k == 7))
                ev = p.op("scalar", lambda s, g=g, m=m, ps=ps: s.activation(
                    out=gbuf[g][:, m * 512:(m + 1) * 512], in_=ps[:, :], func=gate_fn[g],
                    bias=cols[:, (6 + g) * 8 + m:(6 + g) * 8 + m + 1]),
                    wait=(tok,) + tuple(gate_war.get((g, m), ())))
                pool.consumed(bank, ev)
                g_toks[(g, m)] = ev
        c1_toks = []
        prev_c1 = ()
        for m in range(8):
            t1_ = p.op("vector", lambda v, m=m: v.tensor_mul(
                tmp_t[:, :], gi[:, m * 512:(m + 1) * 512], gc[:, m * 512:(m + 1) * 512]),
                wait=(g_toks[(1, m)], g_toks[(2, m)]) + tuple(c_toks.get((j, m), ())) +
                prev_c1)
            t2_ = p.op("vector", lambda v, m=m, j=j: v.scalar_tensor_tensor(
                out=c_all[:, m * SQ + j * 512:m * SQ + (j + 1) * 512],
                in0=gf[:, m * 512:(m + 1) * 512], scalar=cols[:, 5 * 8 + m:5 * 8 + m + 1],
                in1=tmp_t[:, :], op0=ALU.mult, op1=ALU.add),
                wait=(t1_, g_toks[(0, m)], cld[5]))
            prev_c1 = (t2_,)
            c1_toks.append(t2_)
            c_toks[(j, m)] = (t2_,)
        c1_toks = tuple(c1_toks)
        for g in range(3):
            for m in range(8):
                bank, ps, war = pool.get()
                tok = None
                for k in range(8):
                    def f1a(t, g=g, m=m, k=k, j=j, ps=ps):
                        return t.matmul(ps[:, :], WSLOT(g, k)[:, m * 128:(m + 1) * 128],
                                        xn_bf[:, k * T + 1 * SQ + j * 512:
                                              k * T + 1 * SQ + (j + 1) * 512],
                                        start=(k == 0), stop=False)
                    tok = p.op("tensor", f1a,
                               wait=(war + c1_toks +
                                     tuple(gate_war.get((g, m), ()))) if k == 0 else (),
                               inc=False)
                for k in range(8):
                    def f1b(t, g=g, m=m, k=k, j=j, ps=ps):
                        return t.matmul(ps[:, :], WSLOT(3 + g, k)[:, m * 128:(m + 1) * 128],
                                        c_all[:, k * SQ + j * 512:k * SQ + (j + 1) * 512],
                                        start=False, stop=(k == 7))
                    tok = p.op("tensor", f1b, wait=(), inc=(k == 7))
                ev = p.op("scalar", lambda s, g=g, m=m, ps=ps: s.activation(
                    out=gbuf[g][:, m * 512:(m + 1) * 512], in_=ps[:, :], func=gate_fn[g],
                    bias=cols[:, (1 + g) * 8 + m:(1 + g) * 8 + m + 1]),
                    wait=(tok,) + tuple(gate_war.get((g, m), ())) + c1_toks)
                pool.consumed(bank, ev)
                g_toks[(g, m)] = ev
                gate_war[(g, m)] = (ev,)
        prev_c2 = ()
        all_evacs = tuple(g_toks.values())
        for m in range(8):
            t1_ = p.op("vector", lambda v, m=m: v.tensor_mul(
                tmp_t[:, :], gi[:, m * 512:(m + 1) * 512], gc[:, m * 512:(m + 1) * 512]),
                wait=all_evacs + prev_c2)
            t2_ = p.op("vector", lambda v, m=m, j=j: v.tensor_mul(
                tmp2_t[:, :], gf[:, m * 512:(m + 1) * 512],
                c_all[:, m * SQ + j * 512:m * SQ + (j + 1) * 512]),
                wait=(t1_, g_toks[(0, m)]))
            t3_ = p.op("vector", lambda v, m=m, j=j: v.tensor_add(
                c_all[:, m * SQ + j * 512:m * SQ + (j + 1) * 512], tmp2_t[:, :], tmp_t[:, :]),
                wait=(t2_,))
            prev_c2 = (t3_,)
            c_toks[(j, m)] = (t3_,)
            ph1_toks.append(t3_)
        for key in list(gate_war):
            gate_war[key] = gate_war[key] + prev_c2
    ph1_toks = tuple(ph1_toks) + tuple(gate_war[k][0] for k in gate_war)

    wl2 = tuple(p.dma("gpsimd", WSLOT(g, f), wd[f * 128:(f + 1) * 128, :], wait=ph1_toks)
                for g, wd in enumerate([woT_d, uoT_d, wo_d]) for f in range(8))
    fin = []
    prev = (xn_toks[-1],)
    st_ld_prev = ph1_toks
    for t in range(2):
        for j in range(2):
            stl = tuple(p.dma("scalar", gf[:, k * 512:(k + 1) * 512],
                              st_d[t, k * 128:(k + 1) * 128, j * 512:(j + 1) * 512],
                              wait=st_ld_prev) for k in range(8))
            new_prev = []
            for m in range(8):
                bank, ps, war = pool.get()
                tok = None
                for k in range(8):
                    def fh(t_, m=m, k=k, t=t, j=j, ps=ps):
                        return t_.matmul(ps[:, :], WSLOT(0, k)[:, m * 128:(m + 1) * 128],
                                         xn_bf[:, k * T + t * SQ + j * 512:
                                               k * T + t * SQ + (j + 1) * 512],
                                         start=(k == 0), stop=False)
                    tok = p.op("tensor", fh, wait=(wl2 + war + ph1_toks) if k == 0 else (),
                               inc=False)
                for k in range(8):
                    def fh2(t_, m=m, k=k, j=j, ps=ps):
                        return t_.matmul(ps[:, :], WSLOT(1, k)[:, m * 128:(m + 1) * 128],
                                        c_all[:, k * SQ + j * 512:k * SQ + (j + 1) * 512],
                                        start=False, stop=(k == 7))
                    tok = p.op("tensor", fh2, wait=(), inc=(k == 7))
                evh = p.op("scalar", lambda s, m=m, ps=ps: s.activation(
                    out=h_t[:, :], in_=ps[:, :], func=AF.Tanh,
                    bias=cols[:, 4 * 8 + m:4 * 8 + m + 1]), wait=(tok,) + prev)
                pool.consumed(bank, evh)
                bank2, ps2, war2 = pool.get()
                tok2 = None
                for k in range(8):
                    def fo(t_, m=m, k=k, t=t, j=j, ps2=ps2):
                        return t_.matmul(ps2[:, :], WSLOT(2, k)[:, m * 128:(m + 1) * 128],
                                         gf[:, k * 512:(k + 1) * 512],
                                         start=(k == 0), stop=(k == 7))
                    tok2 = p.op("tensor", fo, wait=(stl + war2) if k == 0 else (),
                                inc=(k == 7))
                e1 = p.op("vector", lambda v, m=m, ps2=ps2: v.tensor_scalar_add(
                    s2_t[:, :], ps2[:, :], cols[:, 0 * 8 + m:0 * 8 + m + 1]),
                    wait=(tok2, cld[0]) + prev)
                e2 = p.op("vector", lambda v, m=m, t=t, j=j: v.tensor_add(
                    s2_t[:, :], s2_t[:, :],
                    xn_bf[:, m * T + t * SQ + j * 512:m * T + t * SQ + (j + 1) * 512]),
                    wait=(e1,))
                pool.consumed(bank2, e2)
                gs = p.op("scalar", lambda s: s.activation(out=x1_t[:, :], in_=h_t[:, :],
                                                           func=AF.Silu), wait=(evh,) + prev)
                lx = p.dma("sync", x0_t[:, :], x_d[t, m * 128:(m + 1) * 128,
                                                   j * 512:(j + 1) * 512], wait=prev)
                t1_ = p.op("vector", lambda v: v.tensor_mul(x1_t[:, :], x1_t[:, :],
                                                            s2_t[:, :]), wait=(gs, e2))
                t2_ = p.op("vector", lambda v: v.tensor_add(x1_t[:, :], x1_t[:, :],
                                                            x0_t[:, :]), wait=(t1_, lx))
                xd = p.dma("gpsimd", x3_d[t, m * 128:(m + 1) * 128, j * 512:(j + 1) * 512],
                           x1_t[:, :], wait=(t2_,))
                prev = (xd, t2_, gs, e2)
                new_prev.append(t2_)
                fin.append(xd)
            st_ld_prev = tuple(new_prev) + (prev[0],)
    p.op("sync", lambda s: s.nop(), wait=tuple(fin), inc=False)


def emit_l3pre(p, io):
    D, NF = 1024, 8
    T = B * LLOC
    x_d = io["x"]
    wq_d = io["Wq"]
    wk_d = io["Wk"]
    wv_d = io["Wv"]
    q_o = io["q"]
    k_o = io["k"]
    v_o = io["v"]
    ab_d = io["ab"]

    x_bf = p.sbuf("x_bf", [128, NF * T], BF16)
    w_sb = p.sbuf("w_sb", [128, NF * D], BF16)
    ones_bf = p.sbuf("ones_bf", [128, 1], BF16)
    ones_row = p.sbuf("ones_row", [1, 128], F32)
    eps_sb = p.sbuf("eps_sb", [1, 1], F32)
    rows = p.sbuf("rows", [1, 4 * T], F32)
    ab_sb = p.sbuf("ab_sb", [128, 2 * T], F32)
    h_bf = p.sbuf("h_bf", [128, NF * T], BF16)
    st_bf = p.sbuf("st_bf", [128, 512], BF16)
    pool = PsumPool(p, "ps", n=4)
    ps_a = p.psum("ps_a", [128, 512], F32)
    ps_b = p.psum("ps_b", [128, 512], F32)

    xld = []
    for f in range(NF):
        for b in range(B):
            xld.append(p.dma("gpsimd", x_bf[:, f * T + b * LLOC: f * T + (b + 1) * LLOC],
                             x_d[b, f * 128:(f + 1) * 128, :]))
    xld = tuple(xld)
    t_ones = p.op("vector", lambda v: v.memset(ones_bf[:, :], 1.0))
    t_ones2 = p.op("vector", lambda v: v.memset(ones_row[:, :], 1.0))
    t_eps = p.op("vector", lambda v: v.memset(eps_sb[:, :], 1e-5))

    sq_prev = {}
    ev1p = ev2p = None
    for (t0, t1) in chunks_of(T):
        cw_ = t1 - t0
        tok1 = tok2 = None
        for f in range(NF):
            w_sq = xld + ((sq_prev[f],) if f in sq_prev else ())
            sqt = p.op("scalar", lambda s, f=f, t0=t0, t1=t1: s.activation(
                out=h_bf[:, f * T:f * T + (t1 - t0)],
                in_=x_bf[:, f * T + t0:f * T + t1], func=AF.Square), wait=w_sq)
            w1 = xld + (t_ones,) + ((ev1p,) if (f == 0 and ev1p) else ())
            tok1 = p.op("tensor", lambda t, f=f, t0=t0, t1=t1, cw_=cw_: t.matmul(
                ps_a[0:1, 0:cw_], ones_bf[:, :], x_bf[:, f * T + t0:f * T + t1],
                start=(f == 0), stop=(f == NF - 1)), wait=w1, inc=True)
            w2 = (sqt,) + ((ev2p,) if (f == 0 and ev2p) else ())
            tok2 = p.op("tensor", lambda t, f=f, cw_=cw_: t.matmul(
                ps_b[0:1, 0:cw_], ones_bf[:, :], h_bf[:, f * T:f * T + cw_],
                start=(f == 0), stop=(f == NF - 1)), wait=w2, inc=True)
            sq_prev[f] = tok2
        ev1p = p.op("vector", lambda v, t0=t0, t1=t1: v.tensor_copy(
            rows[0:1, 0 * T + t0:0 * T + t1], ps_a[0:1, 0:t1 - t0]), wait=(tok1,))
        ev2p = p.op("vector", lambda v, t0=t0, t1=t1: v.tensor_copy(
            rows[0:1, 1 * T + t0:1 * T + t1], ps_b[0:1, 0:t1 - t0]), wait=(tok2,))
    S_ = lambda i: rows[0:1, i * T:(i + 1) * T]
    rtok = (ev1p, ev2p, t_eps)
    r1 = p.op("vector", lambda v: v.tensor_single_scalar(S_(2), S_(0), 1.0 / D, ALU.mult), wait=rtok)
    r2 = p.op("vector", lambda v: v.tensor_single_scalar(S_(3), S_(1), 1.0 / D, ALU.mult), wait=rtok)
    r3 = p.op("vector", lambda v: v.tensor_mul(S_(0), S_(2), S_(2)), wait=(r1,))
    r4 = p.op("vector", lambda v: v.tensor_sub(S_(1), S_(3), S_(0)), wait=(r2, r3))
    r5 = p.op("scalar", lambda s: s.activation(out=S_(0), in_=S_(1), func=AF.Sqrt,
                                               bias=eps_sb[0:1, 0:1]), wait=(r4,))
    r6 = p.op("vector", lambda v: v.reciprocal(S_(3), S_(0)), wait=(r5,))
    ab_dma = []
    for b in range(B):
        ab_dma.append(p.dma("sync", ab_d[0, b, :],
                            rows[0:1, 3 * T + b * LLOC:3 * T + (b + 1) * LLOC], wait=(r6,)))
    r12 = p.op("vector", lambda v: v.tensor_mul(S_(1), S_(2), S_(3)), wait=(r6,))
    r13 = p.op("vector", lambda v: v.tensor_single_scalar(S_(1), S_(1), -1.0, ALU.mult), wait=(r12,))
    for b in range(B):
        ab_dma.append(p.dma("sync", ab_d[1, b, :],
                            rows[0:1, 1 * T + b * LLOC:1 * T + (b + 1) * LLOC], wait=(r13,)))
    bc_toks = []
    ea = eb = None
    for (t0, t1) in chunks_of(T):
        tba = p.op("tensor", lambda t, t0=t0, t1=t1: t.matmul(
            ps_a[:, 0:t1 - t0], ones_row[:, :], rows[0:1, 3 * T + t0:3 * T + t1],
            start=True, stop=True), wait=(r6, ev1p, t_ones2) + ((ea,) if ea else ()))
        tbb = p.op("tensor", lambda t, t0=t0, t1=t1: t.matmul(
            ps_b[:, 0:t1 - t0], ones_row[:, :], rows[0:1, 1 * T + t0:1 * T + t1],
            start=True, stop=True), wait=(r13, ev2p) + ((eb,) if eb else ()))
        ea = p.op("vector", lambda v, t0=t0, t1=t1: v.tensor_copy(
            ab_sb[:, t0:t1], ps_a[:, 0:t1 - t0]), wait=(tba,))
        eb = p.op("vector", lambda v, t0=t0, t1=t1: v.tensor_copy(
            ab_sb[:, T + t0:T + t1], ps_b[:, 0:t1 - t0]), wait=(tbb,))
        bc_toks += [ea, eb]
    bc_toks = tuple(bc_toks)
    h_toks = []
    prev = None
    for f in range(NF):
        w = bc_toks + xld + ((prev,) if prev else ())
        t1_ = p.op("vector", lambda v, f=f: v.scalar_tensor_tensor(
            out=h_bf[:, f * T:(f + 1) * T], in0=x_bf[:, f * T:(f + 1) * T], scalar=1.0,
            in1=ab_sb[:, 0:T], op0=ALU.bypass, op1=ALU.mult), wait=w)
        t2_ = p.op("vector", lambda v, f=f: v.tensor_add(
            h_bf[:, f * T:(f + 1) * T], h_bf[:, f * T:(f + 1) * T],
            ab_sb[:, T:2 * T]), wait=(t1_,))
        prev = t2_
        h_toks.append(t2_)
    h_toks = tuple(h_toks)

    st_prev = ()
    for wi, (wd, out_d) in enumerate([(wq_d, q_o), (wk_d, k_o), (wv_d, v_o)]):
        wl = tuple(p.dma("gpsimd", w_sb[:, f * D:(f + 1) * D],
                         wd[f * 128:(f + 1) * 128, :], wait=st_prev) for f in range(NF))
        for m in range(NF):
            for (t0, t1) in chunks_of(T):
                b, sj = divmod(t0, LLOC)
                bank, ps, war = pool.get()
                tok = None
                for k in range(NF):
                    wk_ = (wl + war + h_toks + st_prev) if k == 0 else ()

                    def fn(t, m=m, k=k, t0=t0, t1=t1, ps=ps):
                        return t.matmul(ps[:, 0:t1 - t0],
                                        w_sb[:, k * D + m * 128:k * D + (m + 1) * 128],
                                        h_bf[:, k * T + t0:k * T + t1],
                                        start=(k == 0), stop=(k == NF - 1))
                    tok = p.op("tensor", fn, wait=wk_, inc=(k == NF - 1))
                e1 = p.op("scalar", lambda s, ps=ps: s.activation(
                    out=st_bf[:, :], in_=ps[:, :], func=AF.Copy), wait=(tok,) + st_prev)
                dd = p.dma("sync", out_d[b, m * 128:(m + 1) * 128, sj:sj + 512],
                           st_bf[:, :], wait=(e1,))
                st_prev = (e1, dd)
                pool.consumed(bank, e1)
    p.op("sync", lambda s: s.nop(), wait=tuple(ab_dma) + st_prev, inc=False)


def emit_l3attn(p, io):
    S = 8192
    q_d = io["q"]
    k_d = io["k"]
    v_d = io["v"]
    bm_d = io["bandmul"]
    ba_d = io["bandadd"]
    r16_d = io["rel16"]
    ctx_d = io["ctx"]

    qsb = [p.sbuf(f"qsb{i}", [64, S], BF16) for i in range(2)]
    ksb = [p.sbuf(f"ksb{i}", [64, S], BF16) for i in range(2)]
    vsb = [p.sbuf(f"vsb{i}", [64, S], BF16) for i in range(2)]
    vT = [p.sbuf(f"vT{i}", [128, 64 * 65], BF16) for i in range(2)]
    cx_sb = [p.sbuf(f"cx_sb{i}", [64, S], BF16) for i in range(2)]
    tmpf = [p.sbuf(f"tmpf{i}", [128, 2 * 512], F32) for i in range(2)]
    pr_sb = [p.sbuf(f"pr_sb{i}", [128, 2 * 512], BF16) for i in range(2)]
    bmul = p.sbuf("bmul", [128, 5 * 512], F32)
    badd = p.sbuf("badd", [128, 5 * 512], F32)
    r16c = p.sbuf("r16c", [128, 1], F32)
    m30 = p.sbuf("m30", [128, 1], F32)
    rb_sb = p.sbuf("rb_sb", [1, 512], F32)
    rb64 = p.sbuf("rb64", [64, 512], F32)
    ones1 = p.sbuf("ones1", [1, 64], F32)
    identb = p.sbuf("identb", [128, 128], BF16)
    id_sc = p.sbuf("id_sc", [128, 128], F32)
    ps_sc = [[p.psum(f"ps_sc{i}_{j}", [128, 512], F32) for j in range(2)]
             for i in range(2)]
    ps_cx = [p.psum(f"ps_cx{i}", [128, 512], F32) for i in range(2)]
    ps_tr = p.psum("ps_tr", [128, 512], BF16)
    ps_rb = p.psum("ps_rb", [128, 512], F32)

    j1 = p.op("gpsimd", lambda g: g.iota(id_sc[:, :], pattern=[[1, 128]], base=0,
                                         channel_multiplier=-1,
                                         allow_small_or_imprecise_dtypes=True))
    j3 = p.op("vector", lambda v: v.tensor_single_scalar(identb[:, :], id_sc[:, :], 0.0,
                                                         ALU.is_equal), wait=(j1,))
    t_o1 = p.op("vector", lambda v: v.memset(ones1[:, :], 1.0))
    t_m3 = p.op("vector", lambda v: v.memset(m30[:, :], -30.0))
    bl = [p.dma("sync", bmul[:, :].rearrange("p (d q) -> p d q", d=5), bm_d.transpose([1, 0, 2])),
          p.dma("sync", badd[:, :].rearrange("p (d q) -> p d q", d=5), ba_d.transpose([1, 0, 2])),
          p.dma("sync", r16c[:, :], r16_d)]
    s16 = p.op("vector", lambda v: v.tensor_single_scalar(r16c[:, :], r16c[:, :], 0.125,
                                                          ALU.mult), wait=(bl[2],))

    out_dma = []
    prev_pair = {0: (), 1: ()}
    last_tr_ev = None
    rb_war = ()
    tmpf_war = {0: (), 1: ()}
    for up in range(2):
        units = (2 * up, 2 * up + 1)
        lq = {}
        lk = {}
        ev_toks = {}
        for i, u in enumerate(units):
            lq[i] = p.dma("gpsimd", qsb[i][:, :], q_d[u], wait=prev_pair[i])
            lk[i] = p.dma("gpsimd", ksb[i][:, :], k_d[u], wait=prev_pair[i])
            lv = p.dma("gpsimd", vsb[i][:, :], v_d[u], wait=prev_pair[i])
            vm = p.op("vector", lambda v, i=i: v.memset(vT[i][:, :], 1.0),
                      wait=prev_pair[i])
            evs = []
            prev_tr = None
            for kt in range(64):
                w0 = ()
                if kt == 0:
                    w0 = (lv, j3)
                    if last_tr_ev is not None:
                        w0 = w0 + (last_tr_ev,)
                tr = p.op("tensor", lambda t, i=i, kt=kt: t.transpose(
                    ps_tr[:, 0:64], vsb[i][:, kt * 128:(kt + 1) * 128],
                    identb[0:64, 0:64]),
                    wait=w0 + ((prev_tr,) if prev_tr else ()))
                ev = p.op("vector", lambda v, i=i, kt=kt: v.tensor_copy(
                    vT[i][:, kt * 65:kt * 65 + 64], ps_tr[:, 0:64]), wait=(tr, vm))
                prev_tr = ev
                evs.append(ev)
            ev_toks[i] = tuple(evs)
            last_tr_ev = evs[-1]
        prev_sn = {0: (), 1: ()}
        unit_toks = {0: [], 1: []}
        for qch in range(16):
            qsl = slice(qch * 512, (qch + 1) * 512)
            n_kt = 4 * qch + 4
            exv_t = {}
            ctx_t = {}
            ps_rd = {}
            for kk in range(n_kt + 1):
                if kk < n_kt:
                    kt, pb = kk, kk % 2
                    d = kt - 4 * qch
                    for i in range(2):
                        wsc = (lq[i], lk[i]) + ((ps_rd[(i, kt - 2)],) if kt >= 2 else ())

                        def fsc(t, i=i, kt=kt, pb=pb, qsl=qsl):
                            return t.matmul(ps_sc[i][pb][:, :],
                                            ksb[i][:, kt * 128:(kt + 1) * 128],
                                            qsb[i][:, qsl], start=True, stop=True)
                        msc = p.op("tensor", fsc, wait=wsc, inc=True)
                        if d >= -1:
                            di = d + 1
                            b1 = p.op("vector", lambda v, i=i, pb=pb, di=di: v.tensor_mul(
                                tmpf[i][:, pb * 512:(pb + 1) * 512], ps_sc[i][pb][:, :],
                                bmul[:, di * 512:(di + 1) * 512]),
                                wait=(msc, bl[0]) +
                                (((exv_t[(i, kt - 2)],) if kt >= 2 else tmpf_war[i])))
                            b2 = p.op("vector", lambda v, i=i, pb=pb, di=di: v.tensor_add(
                                tmpf[i][:, pb * 512:(pb + 1) * 512],
                                tmpf[i][:, pb * 512:(pb + 1) * 512],
                                badd[:, di * 512:(di + 1) * 512]),
                                wait=(b1, bl[1]))
                            exv = p.op("scalar", lambda s, i=i, pb=pb: s.activation(
                                out=pr_sb[i][:, pb * 512:(pb + 1) * 512],
                                in_=tmpf[i][:, pb * 512:(pb + 1) * 512],
                                func=AF.Exp, scale=0.125, bias=m30[:, 0:1]),
                                wait=(b2, t_m3) + ((ctx_t[(i, kt - 2)],)
                                                   if kt >= 2 else ()))
                            ps_rd[(i, kt)] = b1
                        else:
                            exv = p.op("scalar", lambda s, i=i, pb=pb: s.activation(
                                out=pr_sb[i][:, pb * 512:(pb + 1) * 512],
                                in_=ps_sc[i][pb][:, :],
                                func=AF.Exp, scale=r16c[:, 0:1], bias=m30[:, 0:1]),
                                wait=(msc, s16, t_m3) + ((ctx_t[(i, kt - 2)],)
                                                         if kt >= 2 else ()))
                            ps_rd[(i, kt)] = exv
                        exv_t[(i, kt)] = exv
                if kk >= 1:
                    kt, pb = kk - 1, (kk - 1) % 2
                    for i in range(2):
                        def fcx(t, i=i, kt=kt, pb=pb, n_kt=n_kt):
                            return t.matmul(ps_cx[i][0:65, :],
                                            vT[i][:, kt * 65:(kt + 1) * 65],
                                            pr_sb[i][:, pb * 512:(pb + 1) * 512],
                                            start=(kt == 0), stop=(kt == n_kt - 1))
                        ctx = p.op("tensor", fcx,
                                   wait=(exv_t[(i, kt)],) +
                                   ((ev_toks[i] + prev_sn[i]) if kt == 0 else ()),
                                   inc=(kt == n_kt - 1))
                        ctx_t[(i, kt)] = ctx
            for i in range(2):
                tmpf_war[i] = tuple(exv_t[k] for k in
                                    [(i, n_kt - 2), (i, n_kt - 1)] if k in exv_t)
                rc = p.op("vector", lambda v, i=i: v.reciprocal(
                    rb_sb[0:1, :], ps_cx[i][64:65, :]),
                    wait=(ctx_t[(i, n_kt - 1)],) + rb_war)
                tb = p.op("tensor", lambda t: t.matmul(ps_rb[0:64, :], ones1[:, :],
                                                       rb_sb[0:1, :], start=True,
                                                       stop=True), wait=(rc, t_o1))
                eb = p.op("vector", lambda v: v.tensor_copy(rb64[:, :], ps_rb[0:64, :]),
                          wait=(tb,))
                sn = p.op("vector", lambda v, i=i, qsl=qsl: v.tensor_mul(
                    cx_sb[i][:, qsl], ps_cx[i][0:64, :], rb64[:, :]), wait=(eb,))
                rb_war = (sn,)
                prev_sn[i] = (sn,)
                unit_toks[i].append(sn)
        for i, u in enumerate(units):
            od = p.dma("sync", ctx_d[u], cx_sb[i][:, :], wait=tuple(unit_toks[i]))
            out_dma.append(od)
            prev_pair[i] = (od,) + tuple(unit_toks[i][-1:]) + tmpf_war[i]
    p.op("sync", lambda s: s.nop(), wait=tuple(out_dma), inc=False)



def emit_l3post(p, io):
    S = LLOC
    T = B * S
    x_d = io["x"]
    ab_d = io["ab"]
    ctx_d = io["ctx"]
    wo_d = io["Wo"]
    bo_d = io["bo"]
    gw_d = io["gate_w"]
    w1_d = io["w1"]
    b1_d = io["b1"]
    w2_d = io["w2"]
    b2_d = io["b2"]
    out_d = io["out"]

    xn_bf = p.sbuf("xn_bf", [128, 8 * T], BF16)
    xe_bf = p.sbuf("xe_bf", [128, 8 * 512], BF16)
    wo_sb = p.sbuf("wo_sb", [128, 8 * 1024], BF16)
    w1_sb = p.sbuf("w1_sb", [128, 8 * 512], BF16)
    w2_sb = p.sbuf("w2_sb", [128, 4 * 1024], BF16)
    h_sb = p.sbuf("h_sb", [128, 4 * 512], BF16)
    eb1 = p.sbuf("eb1", [128, 8 * 4], F32)
    eb2 = p.sbuf("eb2", [8, 1024], BF16)
    s3_bf = p.sbuf("s3_bf", [128, 8 * T], BF16)
    y_acc = p.sbuf("y_acc", [128, 8 * T], BF16)
    ab_sb = p.sbuf("ab_sb", [128, 2 * T], F32)
    rows = p.sbuf("rows", [1, T], F32)
    ones_row = p.sbuf("ones_row", [1, 128], F32)
    gw_sb = p.sbuf("gw_sb", [128, 8 * 8], BF16)
    lg_sb = p.sbuf("lg_sb", [8, T], F32)
    lt_sb = p.sbuf("lt_sb", [128, 8], F32)
    et_sb = p.sbuf("et_sb", [128, 8], F32)
    mk_sb = p.sbuf("mk_sb", [128, 8], F32)
    lcol = p.sbuf("lcol", [128, 6], F32)
    crow = p.sbuf("crow", [8, T], F32)
    crow_bf = p.sbuf("crow_bf", [8, T], BF16)
    sel_sb = p.sbuf("sel_sb", [8, 8 * 128], BF16)
    io_sc = p.sbuf("io_sc", [8, 8 * 128], F32)
    cb_ps = p.sbuf("cb_ps", [128, 512], BF16)
    identf = p.sbuf("identf", [128, 128], F32)
    id_sc = p.sbuf("id_sc", [128, 128], F32)
    bocol = p.sbuf("bocol", [128, 8], F32)
    x0_t = p.sbuf("x0_t", [128, 512], F32)
    x1_t = p.sbuf("x1_t", [128, 512], F32)
    g_t = p.sbuf("g_t", [128, 512], BF16)
    ct_t = p.sbuf("ct_t", [128, 512], BF16)
    pool = PsumPool(p, "ps", n=4)
    ps_r = p.psum("ps_r", [128, 512], F32)
    ps_cbp = p.psum("ps_cbp", [128, 512], F32)
    ps_tf = p.psum("ps_tf", [128, 512], F32)

    t_ones = p.op("vector", lambda v: v.memset(ones_row[:, :], 1.0))
    j1 = p.op("gpsimd", lambda g: g.iota(id_sc[:, :], pattern=[[1, 128]], base=0,
                                         channel_multiplier=-1,
                                         allow_small_or_imprecise_dtypes=True))
    j3 = p.op("vector", lambda v: v.tensor_single_scalar(identf[:, :], id_sc[:, :], 0.0,
                                                         ALU.is_equal), wait=(j1,))
    k1 = p.op("gpsimd", lambda g: g.iota(io_sc[:, :].rearrange("p (e q) -> p e q", e=8),
                                         pattern=[[1, 8], [0, 128]], base=0,
                                         channel_multiplier=-1,
                                         allow_small_or_imprecise_dtypes=True))
    k2 = p.op("vector", lambda v: v.tensor_single_scalar(sel_sb[:, :], io_sc[:, :], 0.0,
                                                         ALU.is_equal), wait=(k1,))
    wldc = [p.dma("gpsimd", gw_sb[:, :].rearrange("f (k e) -> f k e", e=8)[:, :, e],
                  gw_d.rearrange("(k f) e -> f k e", f=128)[:, :, e],
                  allow_slow_non_contiguous=True) for e in range(8)]
    wldc.append(p.dma("gpsimd", eb2[:, :], b2_d))
    wldc.append(p.dma("sync", eb1[:, :].rearrange("f (e m) -> f e m", e=8),
                      b1_d.rearrange("e (m f) -> f e m", f=128),
                      allow_slow_non_contiguous=True))
    wldc.append(load_col_vec(p, bocol, bo_d))
    wlo = tuple(p.dma("gpsimd", wo_sb[:, f * 1024:(f + 1) * 1024],
                      wo_d[f * 128:(f + 1) * 128, :]) for f in range(8))
    rlda = [p.dma("sync", rows[0:1, b * S:(b + 1) * S], ab_d[0, b, :]) for b in range(B)]

    bc_toks = []
    ea = None
    for (t0, t1) in chunks_of(T):
        ta = p.op("tensor", lambda t, t0=t0, t1=t1: t.matmul(
            ps_r[:, 0:t1 - t0], ones_row[:, :], rows[0:1, t0:t1],
            start=True, stop=True), wait=tuple(rlda) + (t_ones,) + ((ea,) if ea else ()))
        ea = p.op("vector", lambda v, t0=t0, t1=t1: v.tensor_copy(
            ab_sb[:, t0:t1], ps_r[:, 0:t1 - t0]), wait=(ta,))
        bc_toks.append(ea)
    rldb = [p.dma("sync", rows[0:1, b * S:(b + 1) * S], ab_d[1, b, :],
                  wait=tuple(bc_toks)) for b in range(B)]
    eb_ = None
    for (t0, t1) in chunks_of(T):
        tb = p.op("tensor", lambda t, t0=t0, t1=t1: t.matmul(
            ps_r[:, 0:t1 - t0], ones_row[:, :], rows[0:1, t0:t1],
            start=True, stop=True), wait=tuple(rldb) + ((eb_,) if eb_ else (bc_toks[-1],)))
        eb_ = p.op("vector", lambda v, t0=t0, t1=t1: v.tensor_copy(
            ab_sb[:, T + t0:T + t1], ps_r[:, 0:t1 - t0]), wait=(tb,))
        bc_toks.append(eb_)
    bc_toks = tuple(bc_toks)

    xn_toks = []
    prev = ()
    for f in range(8):
        for b in range(B):
            for (c0, c1_) in chunks_of(S):
                lx = p.dma("sync", x0_t[:, 0:c1_ - c0],
                           x_d[b, f * 128:(f + 1) * 128, c0:c1_], wait=prev)
                tok = p.op("vector", lambda v, f=f, b=b, c0=c0, c1_=c1_: v.scalar_tensor_tensor(
                    out=xn_bf[:, f * T + b * S + c0:f * T + b * S + c1_],
                    in0=x0_t[:, 0:c1_ - c0], scalar=1.0,
                    in1=ab_sb[:, b * S + c0:b * S + c1_], op0=ALU.bypass, op1=ALU.mult),
                    wait=(lx,) + bc_toks)
                tok2 = p.op("vector", lambda v, f=f, b=b, c0=c0, c1_=c1_: v.tensor_add(
                    xn_bf[:, f * T + b * S + c0:f * T + b * S + c1_],
                    xn_bf[:, f * T + b * S + c0:f * T + b * S + c1_],
                    ab_sb[:, T + b * S + c0:T + b * S + c1_]), wait=(tok,))
                prev = (tok2,)
                xn_toks.append(tok2)
    xn_toks = tuple(xn_toks)

    s3_toks = {}
    prev_s3 = ()
    for j in range(4):
        b, sj = divmod(j * 512, S)
        ctl = tuple(p.dma("scalar", xe_bf[:, k * 512:(k + 1) * 512],
                          ctx_d[b, k * 128:(k + 1) * 128, sj:sj + 512], wait=prev_s3)
                    for k in range(8))
        newp = []
        for m in range(8):
            bank, ps, war = pool.get()
            tok = None
            for k in range(8):
                def fo(t, m=m, k=k, ps=ps):
                    return t.matmul(ps[:, :], wo_sb[:, k * 1024 + m * 128:
                                                    k * 1024 + (m + 1) * 128],
                                    xe_bf[:, k * 512:(k + 1) * 512],
                                    start=(k == 0), stop=(k == 7))
                tok = p.op("tensor", fo, wait=(ctl + war + wlo) if k == 0 else (),
                           inc=(k == 7))
            ev = p.op("vector", lambda v, m=m, j=j, ps=ps: v.tensor_scalar_add(
                s3_bf[:, m * T + j * 512:m * T + (j + 1) * 512], ps[:, :],
                bocol[:, m:m + 1]), wait=(tok, wldc[-1]))
            pool.consumed(bank, ev)
            newp.append(ev)
        prev_s3 = tuple(newp) + (tok,)
        s3_toks[j] = tuple(newp)
    s3_all = tuple(t for j in s3_toks for t in s3_toks[j])

    lg_toks = []
    le_prev = None
    for (t0, t1) in chunks_of(T):
        tok = None
        for k in range(8):
            def fg(t, k=k, t0=t0, t1=t1):
                return t.matmul(ps_r[0:8, 0:t1 - t0], gw_sb[:, k * 8:(k + 1) * 8],
                                xn_bf[:, k * T + t0:k * T + t1],
                                start=(k == 0), stop=(k == 7))
            tok = p.op("tensor", fg, wait=(xn_toks + tuple(wldc) +
                                           ((le_prev,) if le_prev else ())) if k == 0 else (),
                       inc=(k == 7))
        le_prev = p.op("vector", lambda v, t0=t0, t1=t1: v.tensor_copy(
            lg_sb[0:8, t0:t1], ps_r[0:8, 0:t1 - t0]), wait=(tok,))
        lg_toks.append(le_prev)
    lg_toks = tuple(lg_toks)

    crow_toks = []
    prev_tt = ()
    for tt in range(16):
        tsl = slice(tt * 128, (tt + 1) * 128)
        tr1 = p.op("tensor", lambda t, tsl=tsl: t.transpose(
            ps_tf[:, 0:8], lg_sb[0:8, tsl], identf[0:8, 0:8]), wait=lg_toks + (j3,) + prev_tt)
        ev1 = p.op("vector", lambda v: v.tensor_copy(lt_sb[:, 0:8], ps_tf[:, 0:8]),
                   wait=(tr1,) + prev_tt)
        ngm = p.op("vector", lambda v: v.reduce_max(lcol[:, 0:1], lt_sb[:, 0:8], axis=AX.X,
                                                    negate=True), wait=(ev1,))
        ex = p.op("scalar", lambda s: s.activation(out=et_sb[:, :], in_=lt_sb[:, 0:8],
                                                   func=AF.Exp, bias=lcol[:, 0:1]),
                  wait=(ngm,) + prev_tt)
        sm = p.op("vector", lambda v: v.reduce_sum(lcol[:, 1:2], et_sb[:, :], axis=AX.X),
                  wait=(ex,))
        rden = p.op("vector", lambda v: v.reciprocal(lcol[:, 2:3], lcol[:, 1:2]), wait=(sm,))
        pr = p.op("vector", lambda v: v.tensor_scalar(
            out=et_sb[:, :], in0=et_sb[:, :], scalar1=lcol[:, 2:3], scalar2=None,
            op0=ALU.mult), wait=(rden,))
        p1 = p.op("vector", lambda v: v.reduce_max(lcol[:, 3:4], et_sb[:, :], axis=AX.X),
                  wait=(pr,))
        m1 = p.op("vector", lambda v: v.tensor_scalar(
            out=mk_sb[:, :], in0=et_sb[:, :], scalar1=lcol[:, 3:4], scalar2=None,
            op0=ALU.is_lt), wait=(p1,))
        m2 = p.op("vector", lambda v: v.tensor_mul(mk_sb[:, :], mk_sb[:, :], et_sb[:, :]),
                  wait=(m1,))
        p2 = p.op("vector", lambda v: v.reduce_max(lcol[:, 4:5], mk_sb[:, :], axis=AX.X),
                  wait=(m2,))
        tkw = p.op("vector", lambda v: v.tensor_add(lcol[:, 5:6], lcol[:, 3:4],
                                                    lcol[:, 4:5]), wait=(p2,))
        s1_ = p.op("vector", lambda v: v.tensor_scalar(
            out=mk_sb[:, :], in0=et_sb[:, :], scalar1=lcol[:, 4:5], scalar2=None,
            op0=ALU.is_ge), wait=(p2, tkw))
        c1 = p.op("vector", lambda v: v.tensor_scalar(
            out=lt_sb[:, 0:8], in0=mk_sb[:, :], scalar1=lcol[:, 5:6], scalar2=None,
            op0=ALU.mult), wait=(s1_,))
        trb = p.op("tensor", lambda t: t.transpose(ps_tf[0:8, 128:256], lt_sb[:, 0:8],
                                                   identf[:, :]), wait=(c1,))
        evc = p.op("vector", lambda v, tsl=tsl: v.tensor_copy(crow[0:8, tsl],
                                                              ps_tf[0:8, 128:256]),
                   wait=(trb,))
        evcb = p.op("scalar", lambda s, tsl=tsl: s.activation(out=crow_bf[0:8, tsl],
                                                              in_=crow[0:8, tsl],
                                                              func=AF.Copy), wait=(evc,))
        crow_toks.append(evcb)
        prev_tt = (evc, evcb, c1, ex)
    crow_toks = tuple(crow_toks)

    yacc_tok = {}
    wload_prev = ()
    for e in range(8):
        wl = tuple(p.dma("gpsimd", w1_sb[:, k * 512:(k + 1) * 512],
                         w1_d[e, k * 128:(k + 1) * 128, :], wait=wload_prev)
                   for k in range(8))
        wl += tuple(p.dma("gpsimd", w2_sb[:, k * 1024:(k + 1) * 1024],
                          w2_d[e, k * 128:(k + 1) * 128, :], wait=wload_prev)
                    for k in range(4))
        for j in range(4):
            t0 = j * 512
            tcb = p.op("tensor", lambda t, e=e, t0=t0: t.matmul(
                ps_cbp[:, :], sel_sb[0:8, e * 128:(e + 1) * 128],
                crow_bf[0:8, t0:t0 + 512], start=True, stop=True),
                wait=crow_toks + (k2,) + tuple(yacc_tok.get("cb", ())))
            ecb = p.op("vector", lambda v: v.tensor_copy(cb_ps[:, :], ps_cbp[:, :]),
                       wait=(tcb,))
            yacc_tok["cb"] = (ecb,)
            htoks = []
            for mt in range(4):
                bank, ps, war = pool.get()
                tok = None
                for k in range(8):
                    def f1(t, mt=mt, k=k, t0=t0, ps=ps):
                        return t.matmul(ps[:, :],
                                        w1_sb[:, k * 512 + mt * 128:k * 512 + (mt + 1) * 128],
                                        xn_bf[:, k * T + t0:k * T + t0 + 512],
                                        start=(k == 0), stop=(k == 7))
                    tok = p.op("tensor", f1, wait=(wl + war + xn_toks) if k == 0 else (),
                               inc=(k == 7))
                ev = p.op("scalar", lambda s, mt=mt, e=e, ps=ps: s.activation(
                    out=h_sb[:, mt * 512:(mt + 1) * 512], in_=ps[:, :], func=AF.Gelu,
                    bias=eb1[:, e * 4 + mt:e * 4 + mt + 1]),
                    wait=(tok, wldc[-2]) + tuple(yacc_tok.get(("h", mt), ())))
                ev2 = p.op("vector", lambda v, mt=mt: v.tensor_mul(
                    h_sb[:, mt * 512:(mt + 1) * 512], h_sb[:, mt * 512:(mt + 1) * 512],
                    cb_ps[:, :]), wait=(ev, ecb))
                pool.consumed(bank, ev2)
                htoks.append(ev2)
            htoks = tuple(htoks)
            for mt in range(8):
                bank, ps, war = pool.get()
                tok = None
                for k in range(4):
                    def f2(t, mt=mt, k=k, ps=ps):
                        return t.matmul(ps[:, :],
                                        w2_sb[:, k * 1024 + mt * 128:k * 1024 + (mt + 1) * 128],
                                        h_sb[:, k * 512:(k + 1) * 512],
                                        start=(k == 0), stop=(k == 3))
                    tok = p.op("tensor", f2, wait=(htoks + war) if k == 0 else (),
                               inc=(k == 3))
                for hmt in range(4):
                    yacc_tok[("h", hmt)] = (tok,)
                prev_y = yacc_tok.get((j, mt), ())
                if e == 0:
                    ev = p.op("vector", lambda v, mt=mt, t0=t0, ps=ps: v.tensor_copy(
                        y_acc[:, mt * T + t0:mt * T + t0 + 512], ps[:, :]),
                        wait=(tok,) + prev_y)
                else:
                    ev = p.op("vector", lambda v, mt=mt, t0=t0, ps=ps: v.tensor_add(
                        y_acc[:, mt * T + t0:mt * T + t0 + 512],
                        y_acc[:, mt * T + t0:mt * T + t0 + 512], ps[:, :]),
                        wait=(tok,) + prev_y)
                pool.consumed(bank, ev)
                yacc_tok[(j, mt)] = (ev,)
        wload_prev = tuple(yacc_tok[(j, mt)][0] for j in range(4) for mt in range(8))
    for j in range(4):
        t0 = j * 512
        for mt in range(8):
            bank, ps, war = pool.get()

            def fb(t, mt=mt, t0=t0, ps=ps):
                return t.matmul(ps[:, :], eb2[0:8, mt * 128:(mt + 1) * 128],
                                crow_bf[0:8, t0:t0 + 512], start=True, stop=True)
            tok = p.op("tensor", fb, wait=crow_toks + war + (wldc[8],), inc=True)
            ev = p.op("vector", lambda v, mt=mt, t0=t0, ps=ps: v.tensor_add(
                y_acc[:, mt * T + t0:mt * T + t0 + 512],
                y_acc[:, mt * T + t0:mt * T + t0 + 512], ps[:, :]),
                wait=(tok,) + yacc_tok[(j, mt)])
            pool.consumed(bank, ev)
            yacc_tok[(j, mt)] = (ev,)

    fin = []
    prev = (xn_toks[-1],) + s3_all
    for j in range(4):
        b, sj = divmod(j * 512, S)
        for f in range(8):
            gt = p.op("scalar", lambda s, f=f, j=j: s.activation(
                out=g_t[:, :], in_=y_acc[:, f * T + j * 512:f * T + (j + 1) * 512],
                func=AF.Silu), wait=tuple(yacc_tok[(j, f)]) + prev)
            lx = p.dma("sync", x0_t[:, :], x_d[b, f * 128:(f + 1) * 128, sj:sj + 512],
                       wait=prev)
            t1_ = p.op("vector", lambda v, f=f, j=j: v.tensor_mul(
                x1_t[:, :], s3_bf[:, f * T + j * 512:f * T + (j + 1) * 512], g_t[:, :]),
                wait=(gt,) + prev)
            t2_ = p.op("vector", lambda v: v.tensor_add(x1_t[:, :], x1_t[:, :], x0_t[:, :]),
                       wait=(t1_, lx))
            xd = p.dma("gpsimd", out_d[b, f * 128:(f + 1) * 128, sj:sj + 512], x1_t[:, :],
                       wait=(t2_,))
            prev = (xd, t2_, gt)
            fin.append(xd)
    p.op("sync", lambda s: s.nop(), wait=tuple(fin), inc=False)



def build_fused():
    p = Prog("mega")
    ein = p.dram_in
    tmp = p.dram_tmp

    xh = ein("xh", [B, 1024, TH])
    w_a0 = dict(W_in=ein("a0_W_in", [1024, 4096]), norm_w=ein("a0_norm_w", [1024]),
                conv_w=ein("a0_conv_w", [2048, 4]), conv_b=ein("a0_conv_b", [2048]),
                W_x=ein("a0_W_x", [2048, 96]))
    w_s0 = dict(W_dt=ein("s0_W_dt", [64, 256]), b_dt=ein("s0_b_dt", [256]),
                A=ein("s0_A", [256, 16]))
    w_p0 = dict(D=ein("p0_D", [2048]), W_out=ein("p0_W_out", [2048, 1024]))
    w_f0 = dict(gate_w=ein("f0_gate_w", [2, 1024]), w1=ein("f0_w1", [2, 1024, 4096]),
                b1=ein("f0_b1", [2, 4096]), w2=ein("f0_w2", [2, 4096, 1024]),
                b2=ein("f0_b2", [2, 1024]), sw1=ein("f0_sw1", [1024, 4096]),
                sb1=ein("f0_sb1", [4096]), sw2=ein("f0_sw2", [4096, 1024]),
                sb2=ein("f0_sb2", [1024]))
    w_a1 = dict(W_in=ein("a1_W_in", [1024, 4368]), ln_w=ein("a1_ln_w", [1024]),
                ln_b=ein("a1_ln_b", [1024]), conv_w=ein("a1_conv_w", [2304, 4]),
                conv_b=ein("a1_conv_b", [2304]), dt_bias=ein("a1_dt_bias", [16]),
                A_log=ein("a1_A_log", [16]), hmask=ein("a1_hmask", [128, 8]))
    w_u1 = dict(bsel=ein("u1_bsel", [128, 2]), Du=ein("u1_Du", [1, 4]))
    w_p1 = dict(rms_w=ein("p1_rms_w", [2048]), W_out=ein("p1_W_out", [2048, 1024]))
    w_f1 = dict(router_w=ein("f1_router_w", [1024, 8]), router_b=ein("f1_router_b", [8]),
                expert_w=ein("f1_expert_w", [8, 1024, 1024]),
                expert_b=ein("f1_expert_b", [8, 1024]))
    w_a2 = dict(ln_w=ein("a2_ln_w", [1024]), ln_b=ein("a2_ln_b", [1024]),
                Wq=ein("a2_Wq", [1024, 1024]), Wk=ein("a2_Wk", [1024, 1024]),
                Wv=ein("a2_Wv", [1024, 1024]), bq=ein("a2_bq", [1024]),
                bk=ein("a2_bk", [1024]), bv=ein("a2_bv", [1024]))
    w_p2 = dict(Wo=ein("p2_Wo", [1024, 1024]), bo=ein("p2_bo", [1024]),
                wfT=ein("p2_wfT", [1024, 1024]), wiT=ein("p2_wiT", [1024, 1024]),
                wcT=ein("p2_wcT", [1024, 1024]), ufT=ein("p2_ufT", [1024, 1024]),
                uiT=ein("p2_uiT", [1024, 1024]), ucT=ein("p2_ucT", [1024, 1024]),
                woT=ein("p2_woT", [1024, 1024]), uoT=ein("p2_uoT", [1024, 1024]),
                bf_=ein("p2_bf", [1024]), bi=ein("p2_bi", [1024]),
                bc=ein("p2_bc", [1024]), bo2=ein("p2_bo2", [1024]),
                in_c=ein("p2_in_c", [1024]))
    w_a3 = dict(Wq=ein("a3_Wq", [1024, 1024]), Wk=ein("a3_Wk", [1024, 1024]),
                Wv=ein("a3_Wv", [1024, 1024]))
    w_t3 = dict(bandmul=ein("t3_bandmul", [5, 128, 512]),
                bandadd=ein("t3_bandadd", [5, 128, 512]),
                rel16=ein("t3_rel16", [128, 1]))
    w_f3 = dict(Wo=ein("f3_Wo", [1024, 1024]), bo=ein("f3_bo", [1024]),
                gate_w=ein("f3_gate_w", [1024, 8]), w1=ein("f3_w1", [8, 1024, 512]),
                b1=ein("f3_b1", [8, 512]), w2=ein("f3_w2", [8, 512, 1024]),
                b2=ein("f3_b2", [8, 1024]))
    out_d = p.dram_out("out", [B, 1024, LLOC], F32)

    res_loc = tmp("t_res", [B, 2048, LLOC], BF16)
    ab0 = tmp("t_ab0", [2, B, LLOC])
    xc_prod = tmp("t_xc_p", [B, 2048, LLOC], BF16)
    xc_ai = tmp("t_xc_ai", [8, B, 256, LLOC], BF16)
    xc_ao = tmp("t_xc_ao", [8, B, 256, LLOC], BF16)
    xc_cons = tmp("t_xc_c", [B, 256, NCORES * LLOC], BF16)
    dlt_loc = tmp("t_dlt", [B, 64, LLOC], BF16)
    dlt_ag = tmp("t_dlt_ag", [8, B, 64, LLOC], BF16)
    dlt_cons = tmp("t_dlt_c", [B, 64, NCORES * LLOC], BF16)
    bm0_loc = tmp("t_bm0", [B, 16, LLOC], BF16)
    bm0_ag = tmp("t_bm0_ag", [8, B, 16, LLOC], BF16)
    bm0_cons = tmp("t_bm0_c", [B, 16, NCORES * LLOC], BF16)
    cm0_loc = tmp("t_cm0", [B, 16, LLOC], BF16)
    cm0_ag = tmp("t_cm0_ag", [8, B, 16, LLOC], BF16)
    cm0_cons = tmp("t_cm0_c", [B, 16, NCORES * LLOC], BF16)
    y0_prod = tmp("t_y0_p", [B, 256, NCORES * LLOC], BF16)
    y0_ai = tmp("t_y0_ai", [8, B, 256, LLOC], BF16)
    y0_ao = tmp("t_y0_ao", [8, B, 256, LLOC], BF16)
    y0_cons = tmp("t_y0_c", [B, 2048, LLOC], BF16)
    s0_loc = tmp("t_s0", [B, 1024, LLOC], BF16)
    x1_loc = tmp("t_x1", [B, 1024, LLOC])
    halo_ai = tmp("t_halo_ai", [48, 128])
    halo_ag = tmp("t_halo_ag", [8 * 48, 128])
    ab1 = tmp("t_ab1", [2, B, LLOC])
    z_loc = tmp("t_z", [B, 2048, LLOC], BF16)
    X_ai = tmp("t_X_ai", [8, 4, 128, LLOC], BF16)
    X_ao = tmp("t_X_ao", [8, 4, 128, LLOC], BF16)
    X_cons = tmp("t_X_c", [4, 128, NCORES * LLOC], BF16)
    bm1_loc = tmp("t_bm1", [B, 128, LLOC], BF16)
    bm1_ag = tmp("t_bm1_ag", [8, B, 128, LLOC], BF16)
    bm1_cons = tmp("t_bm1_c", [B, 128, NCORES * LLOC], BF16)
    cm1_loc = tmp("t_cm1", [B, 128, LLOC], BF16)
    cm1_ag = tmp("t_cm1_ag", [8, B, 128, LLOC], BF16)
    cm1_cons = tmp("t_cm1_c", [B, 128, NCORES * LLOC], BF16)
    dt_ai = tmp("t_dt_ai", [8, 4, LLOC])
    dt_ao = tmp("t_dt_ao", [8, 4, LLOC])
    dt_cons = tmp("t_dt_c", [4, NCORES * LLOC])
    dta_ai = tmp("t_dta_ai", [8, 4, LLOC])
    dta_ao = tmp("t_dta_ao", [8, 4, LLOC])
    dta_cons = tmp("t_dta_c", [4, NCORES * LLOC])
    y1_prod = tmp("t_y1_p", [4, 128, NCORES * LLOC], BF16)
    y1_ai = tmp("t_y1_ai", [8, 4, 128, LLOC], BF16)
    y1_ao = tmp("t_y1_ao", [8, 4, 128, LLOC], BF16)
    s1_loc = tmp("t_s1", [B, 1024, LLOC], BF16)
    x2_loc = tmp("t_x2", [B, 1024, LLOC])
    ab2 = tmp("t_ab2", [2, B, LLOC])
    q2_loc = tmp("t_q2", [B, 1024, LLOC], BF16)
    k2_loc = tmp("t_k2", [B, 1024, LLOC], BF16)
    v2_loc = tmp("t_v2", [B, 1024, LLOC], BF16)
    k2_ag = tmp("t_k2_ag", [8, B, 1024, LLOC], BF16)
    v2_ag = tmp("t_v2_ag", [8, B, 1024, LLOC], BF16)
    st_loc = tmp("t_st", [B, 1024, LLOC], BF16)
    x3_loc = tmp("t_x3", [B, 1024, LLOC])
    ab3 = tmp("t_ab3", [2, B, LLOC])
    q3_ai = tmp("t_q3_ai", [8, 4, 64, LLOC], BF16)
    q3_ao = tmp("t_q3_ao", [8, 4, 64, LLOC], BF16)
    q3_cons = tmp("t_q3_c", [4, 64, NCORES * LLOC], BF16)
    k3_ai = tmp("t_k3_ai", [8, 4, 64, LLOC], BF16)
    k3_ao = tmp("t_k3_ao", [8, 4, 64, LLOC], BF16)
    k3_cons = tmp("t_k3_c", [4, 64, NCORES * LLOC], BF16)
    v3_ai = tmp("t_v3_ai", [8, 4, 64, LLOC], BF16)
    v3_ao = tmp("t_v3_ao", [8, 4, 64, LLOC], BF16)
    v3_cons = tmp("t_v3_c", [4, 64, NCORES * LLOC], BF16)
    ctx_prod = tmp("t_ctx_p", [4, 64, NCORES * LLOC], BF16)
    ctx_ai = tmp("t_ctx_ai", [8, 4, 64, LLOC], BF16)
    ctx_ao = tmp("t_ctx_ao", [8, 4, 64, LLOC], BF16)

    x0_plain = xh[:, :, HALO:]
    R2 = lambda a: a.rearrange("r b c t -> (r b c) t")
    R2u = lambda a: a.rearrange("r i c t -> (r i c) t")
    R2d = lambda a: a.rearrange("r i t -> (r i) t")
    R2h = lambda a: a.rearrange("r i d t -> (r i d) t")
    QS = ("sync", "scalar", "gpsimd")

    def boundary(prod_pairs, colls, cons_pairs):
        p.join()
        pt = tuple(p.dma(QS[i % 3], dst, src)
                   for i, (dst, src) in enumerate(prod_pairs))
        for kind, i_, o_ in colls:
            p.collective(kind, [i_], [o_], wait=pt)
        p.join()
        for i, (dst, src) in enumerate(cons_pairs):
            p.dma(QS[i % 3], dst, src)
        p.join()

    def sc_pairs(ai, prod, rows):
        return [(ai[r, b], prod[b, r * rows:(r + 1) * rows, :])
                for r in range(8) for b in range(B)]

    def tslice_pairs(ai, prod, n0):
        return [(ai[r, u], prod[u, :, r * LLOC:(r + 1) * LLOC])
                for r in range(8) for u in range(n0)]

    def cons_seq_pairs(cons, ao, n0):
        return [(cons[u, :, s * LLOC:(s + 1) * LLOC], ao[s, u])
                for s in range(8) for u in range(n0)]

    def cons_row_pairs(cons, ao, rows):
        return [(cons[b, s * rows:(s + 1) * rows, :], ao[s, b])
                for s in range(8) for b in range(B)]

    p.begin_stage("a0")
    emit_l0pre(p, dict(xh=xh, **w_a0, res=res_loc, xc=xc_prod,
                       dlt=dlt_loc, Bm=bm0_loc, Cm=cm0_loc, ab=ab0))
    p.end_stage()
    boundary(
        sc_pairs(xc_ai, xc_prod, 256),
        [("AllToAll", R2(xc_ai), R2(xc_ao)),
         ("AllGather", dlt_loc.rearrange("b c t -> (b c) t"), R2(dlt_ag)),
         ("AllGather", bm0_loc.rearrange("b c t -> (b c) t"), R2(bm0_ag)),
         ("AllGather", cm0_loc.rearrange("b c t -> (b c) t"), R2(cm0_ag))],
        cons_seq_pairs(xc_cons, xc_ao, B)
        + cons_seq_pairs(dlt_cons, dlt_ag, B)
        + cons_seq_pairs(bm0_cons, bm0_ag, B)
        + cons_seq_pairs(cm0_cons, cm0_ag, B))

    p.begin_stage("s0")
    emit_l0scan(p, dict(xc=xc_cons, dlt=dlt_cons, **w_s0,
                        Bm=bm0_cons, Cm=cm0_cons, y=y0_prod))
    p.end_stage()
    boundary(
        tslice_pairs(y0_ai, y0_prod, B),
        [("AllToAll", R2(y0_ai), R2(y0_ao))],
        cons_row_pairs(y0_cons, y0_ao, 256))

    p.begin_stage("p0")
    emit_l0post(p, dict(x=x0_plain, ab=ab0, y=y0_cons, xc=xc_prod,
                        res=res_loc, **w_p0, s0=s0_loc))
    p.end_stage()
    p.join()

    p.begin_stage("f0")
    emit_l0ffn(p, dict(x=x0_plain, ab=ab0, s0=s0_loc, **w_f0, x1=x1_loc,
                       halo_in=halo_ai))
    p.end_stage()
    boundary([], [("AllGather", halo_ai[:, :], halo_ag[:, :])], [])

    p.begin_stage("a1")
    emit_l1pre(p, dict(x=x1_loc, halo_ag=halo_ag, **w_a1, z=z_loc,
                       xh_c=X_ai.rearrange("(b r) i c t -> b (r i c) t", b=2),
                       Bm=bm1_loc, Cm=cm1_loc,
                       dt1=dt_ai.rearrange("(b r) i t -> b (r i) t", b=2),
                       dtA=dta_ai.rearrange("(b r) i t -> b (r i) t", b=2),
                       ab=ab1))
    p.end_stage()
    boundary(
        [],
        [("AllToAll", R2u(X_ai), R2u(X_ao)),
         ("AllGather", bm1_loc.rearrange("b c t -> (b c) t"), R2(bm1_ag)),
         ("AllGather", cm1_loc.rearrange("b c t -> (b c) t"), R2(cm1_ag)),
         ("AllToAll", R2d(dt_ai), R2d(dt_ao)),
         ("AllToAll", R2d(dta_ai), R2d(dta_ao))],
        cons_seq_pairs(X_cons, X_ao, 4)
        + cons_seq_pairs(bm1_cons, bm1_ag, B)
        + cons_seq_pairs(cm1_cons, cm1_ag, B)
        + [(dt_cons[:, s * LLOC:(s + 1) * LLOC], dt_ao[s]) for s in range(8)]
        + [(dta_cons[:, s * LLOC:(s + 1) * LLOC], dta_ao[s]) for s in range(8)])

    p.begin_stage("u1")
    emit_l1ssd(p, dict(X=X_cons, BuB=bm1_cons, CuB=cm1_cons, **w_u1,
                       dt=dt_cons, dtA=dta_cons, y=y1_prod))
    p.end_stage()
    boundary(
        tslice_pairs(y1_ai, y1_prod, 4),
        [("AllToAll", R2u(y1_ai), R2u(y1_ao))],
        [])

    p.begin_stage("p1")
    emit_l1post(p, dict(x=x1_loc, ab=ab1,
                        y=y1_ao.rearrange("(b s) i c t -> b (s i c) t", b=2),
                        z=z_loc, **w_p1, s1=s1_loc))
    p.end_stage()
    p.join()

    p.begin_stage("f1")
    emit_l1ffn(p, dict(x=x1_loc, ab=ab1, s1=s1_loc, **w_f1, x2=x2_loc))
    p.end_stage()
    p.join()

    p.begin_stage("a2")
    emit_l2pre(p, dict(x=x2_loc, **w_a2, q=q2_loc, k=k2_loc, v=v2_loc, ab=ab2))
    p.end_stage()
    boundary(
        [],
        [("AllGather", k2_loc.rearrange("b f t -> (b f) t"),
          k2_ag.rearrange("r b f t -> (r b f) t")),
         ("AllGather", v2_loc.rearrange("b f t -> (b f) t"),
          v2_ag.rearrange("r b f t -> (r b f) t"))],
        [])

    p.begin_stage("t2")
    emit_l2attn(p, dict(q=q2_loc, k=k2_ag, v=v2_ag, state=st_loc))
    p.end_stage()
    p.join()

    p.begin_stage("p2")
    emit_l2post(p, dict(x=x2_loc, ab=ab2, state=st_loc, **w_p2, x3=x3_loc))
    p.end_stage()
    p.join()

    p.begin_stage("a3")
    emit_l3pre(p, dict(x=x3_loc, **w_a3,
                       q=q3_ai.rearrange("(b r) i d t -> b (r i d) t", b=2),
                       k=k3_ai.rearrange("(b r) i d t -> b (r i d) t", b=2),
                       v=v3_ai.rearrange("(b r) i d t -> b (r i d) t", b=2),
                       ab=ab3))
    p.end_stage()
    boundary(
        [],
        [("AllToAll", R2h(q3_ai), R2h(q3_ao)),
         ("AllToAll", R2h(k3_ai), R2h(k3_ao)),
         ("AllToAll", R2h(v3_ai), R2h(v3_ao))],
        cons_seq_pairs(q3_cons, q3_ao, 4)
        + cons_seq_pairs(k3_cons, k3_ao, 4)
        + cons_seq_pairs(v3_cons, v3_ao, 4))

    p.begin_stage("t3")
    emit_l3attn(p, dict(q=q3_cons, k=k3_cons, v=v3_cons, **w_t3, ctx=ctx_prod))
    p.end_stage()
    boundary(
        tslice_pairs(ctx_ai, ctx_prod, 4),
        [("AllToAll", R2h(ctx_ai), R2h(ctx_ao))],
        [])

    p.begin_stage("f3")
    emit_l3post(p, dict(x=x3_loc, ab=ab3,
                        ctx=ctx_ao.rearrange("(b s) i d t -> b (s i d) t", b=2),
                        **w_f3, out=out_d))
    p.end_stage()
    p.join()
    return p


def kernel(**inp):
    import ml_dtypes
    A = lambda a: np.ascontiguousarray(np.asarray(a, np.float32))
    tr = lambda a: np.ascontiguousarray(np.asarray(a, np.float32).T)
    x0 = np.ascontiguousarray(np.asarray(inp["L0_attn_hidden_states"]).transpose(0, 2, 1))
    xpad = np.concatenate([np.zeros((B, 1024, HALO), np.float32), x0], axis=2)

    rel = np.asarray(inp["L3_attn_rel_emb"])
    bmb = np.zeros((5, 128, 512), np.float32)
    bab = np.zeros((5, 128, 512), np.float32)
    for di, d in enumerate([-1, 0, 1, 2, 3]):
        kk = np.arange(128)[:, None]
        qq = np.arange(512)[None, :]
        diff = qq - kk - 128 * d
        dist = np.clip(np.abs(diff), 0, 16)
        bmb[di] = rel[dist]
        bab[di] = np.where(diff >= 0, 0.0, -1e30)
    r16 = np.ascontiguousarray(np.repeat(rel[16:17], 128)[:, None].astype(np.float32))

    shared = {
        "a0_W_in": A(inp["L0_attn_W_in"]), "a0_norm_w": A(inp["L0_attn_norm_w"]),
        "a0_conv_w": A(inp["L0_attn_conv_w"]), "a0_conv_b": A(inp["L0_attn_conv_b"]),
        "a0_W_x": A(inp["L0_attn_W_x"]),
        "p0_D": A(inp["L0_attn_D"]), "p0_W_out": A(inp["L0_attn_W_out"]),
        "f0_gate_w": A(inp["L0_ffn_gate_w"]), "f0_w1": A(inp["L0_ffn_w1"]),
        "f0_b1": A(inp["L0_ffn_b1"]), "f0_w2": A(inp["L0_ffn_w2"]),
        "f0_b2": A(inp["L0_ffn_b2"]), "f0_sw1": A(inp["L0_ffn_sw1"]),
        "f0_sb1": A(inp["L0_ffn_sb1"]), "f0_sw2": A(inp["L0_ffn_sw2"]),
        "f0_sb2": A(inp["L0_ffn_sb2"]),
        "a1_W_in": A(inp["L1_attn_W_in"]), "a1_ln_w": A(inp["L1_attn_ln_w"]),
        "a1_ln_b": A(inp["L1_attn_ln_b"]), "a1_conv_w": A(inp["L1_attn_conv_w"]),
        "a1_conv_b": A(inp["L1_attn_conv_b"]), "a1_dt_bias": A(inp["L1_attn_dt_bias"]),
        "a1_A_log": A(inp["L1_attn_A_log"]),
        "p1_rms_w": A(inp["L1_attn_rms_w"]), "p1_W_out": A(inp["L1_attn_W_out"]),
        "f1_router_w": A(inp["L1_ffn_router_w"]), "f1_router_b": A(inp["L1_ffn_router_b"]),
        "f1_expert_w": A(inp["L1_ffn_expert_w"]), "f1_expert_b": A(inp["L1_ffn_expert_b"]),
        "a2_ln_w": A(inp["L2_attn_ln_w"]), "a2_ln_b": A(inp["L2_attn_ln_b"]),
        "a2_Wq": A(inp["L2_attn_Wq"]), "a2_Wk": A(inp["L2_attn_Wk"]),
        "a2_Wv": A(inp["L2_attn_Wv"]), "a2_bq": A(inp["L2_attn_bq"]),
        "a2_bk": A(inp["L2_attn_bk"]), "a2_bv": A(inp["L2_attn_bv"]),
        "p2_Wo": A(inp["L2_attn_Wo"]), "p2_bo": A(inp["L2_attn_bo"]),
        "p2_wfT": tr(inp["L2_ffn_wf"]), "p2_wiT": tr(inp["L2_ffn_wi"]),
        "p2_wcT": tr(inp["L2_ffn_wc"]), "p2_ufT": tr(inp["L2_ffn_uf"]),
        "p2_uiT": tr(inp["L2_ffn_ui"]), "p2_ucT": tr(inp["L2_ffn_uc"]),
        "p2_woT": tr(inp["L2_ffn_wo"]), "p2_uoT": tr(inp["L2_ffn_uo"]),
        "p2_bf": A(inp["L2_ffn_bf"]), "p2_bi": A(inp["L2_ffn_bi"]),
        "p2_bc": A(inp["L2_ffn_bc"]), "p2_bo2": A(inp["L2_ffn_bo"]),
        "p2_in_c": A(inp["L2_ffn_in_c"]),
        "a3_Wq": A(inp["L3_attn_Wq"]), "a3_Wk": A(inp["L3_attn_Wk"]),
        "a3_Wv": A(inp["L3_attn_Wv"]),
        "t3_bandmul": bmb, "t3_bandadd": bab, "t3_rel16": r16,
        "f3_Wo": A(inp["L3_attn_Wo"]), "f3_bo": A(inp["L3_attn_bo"]),
        "f3_gate_w": A(inp["L3_ffn_gate_w"]), "f3_w1": A(inp["L3_ffn_w1"]),
        "f3_b1": A(inp["L3_ffn_b1"]), "f3_w2": A(inp["L3_ffn_w2"]),
        "f3_b2": A(inp["L3_ffn_b2"]),
    }
    wdt = np.asarray(inp["L0_attn_W_dt"])
    bdt = np.asarray(inp["L0_attn_b_dt"])
    a0m = np.asarray(inp["L0_attn_A"])
    d1 = np.asarray(inp["L1_attn_D"])
    maps = []
    for c in range(NCORES):
        cs = slice(c * 256, (c + 1) * 256)
        units = [c * 4 + i for i in range(4)]
        hmask = np.zeros((128, 8), np.float32)
        if c > 0:
            hmask[:, c - 1] = 1.0
        bsel = np.zeros((128, 2), np.float32)
        bsel[:, c // 4] = 1.0
        m = dict(shared)
        m["xh"] = np.ascontiguousarray(xpad[:, :, c * LLOC:c * LLOC + TH])
        m["s0_W_dt"] = np.ascontiguousarray(wdt[:, cs].astype(np.float32))
        m["s0_b_dt"] = np.ascontiguousarray(bdt[cs].astype(np.float32))
        m["s0_A"] = np.ascontiguousarray(a0m[cs, :].astype(np.float32))
        m["a1_hmask"] = hmask
        m["u1_bsel"] = bsel
        m["u1_Du"] = np.asarray([d1[u % 16] for u in units],
                                np.float32)[None, :]
        maps.append(m)
    pgm = build_fused()
    res = pgm.run(maps)
    out = np.concatenate([np.asarray(r["out"]) for r in res.results], axis=-1)
    return np.ascontiguousarray(out.transpose(0, 2, 1))


# version 12
# speedup vs baseline: 1.1611x; 1.1611x over previous
import contextlib
import numpy as np

import concourse.bass as bass
import concourse.mybir as mybir
from concourse.bass_utils import run_bass_kernel_spmd
from concourse.aot_env import allow_ici_all_to_all

F32 = mybir.dt.float32
BF16 = mybir.dt.bfloat16
I32 = mybir.dt.int32
AF = mybir.ActivationFunctionType
ALU = mybir.AluOpType
AX = mybir.AxisListType

NCORES = 8
ENGS = ("sync", "scalar", "vector", "tensor", "gpsimd")


def ceil_div(a, b):
    return (a + b - 1) // b


class Prog:

    def __init__(self, name="prog", n_cc=24):
        self.nc = bass.Bass("TRN2", target_bir_lowering=False, debug=False)
        self.stack = contextlib.ExitStack()
        self.thunks = {e: [] for e in ENGS}
        self.count = {e: 0 for e in ENGS}
        self.dma_count = {e: 0 for e in ENGS}
        self.sems = {}
        for e in ENGS:
            self.sems["c_" + e] = self.stack.enter_context(self.nc.semaphore("c_" + e))
            self.sems["d_" + e] = self.stack.enter_context(self.nc.semaphore("d_" + e))
            self.sems["f_" + e] = self.stack.enter_context(self.nc.semaphore("f_" + e))
        self.fcount = {e: 0 for e in ENGS}
        for i in range(n_cc):
            self.sems[f"cc{i}"] = self.stack.enter_context(self.nc.semaphore(f"cc{i}"))
        self.cc_i = 0
        self.cc_tokens = []
        self.stage = ""
        self.stage_stack = None
        self._names = set()
        self._fences = {}

    def begin_stage(self, name):
        assert self.stage_stack is None
        self.stage = name + "_"
        self.stage_stack = contextlib.ExitStack()

    def end_stage(self):
        self.stage_stack.close()
        self.stage_stack = None
        self.stage = ""

    def _uname(self, n):
        assert n not in self._names, f"duplicate tensor name {n}"
        self._names.add(n)
        return n

    def dram_in(self, name, shape, dt=F32):
        return self.nc.dram_tensor(self._uname(name), list(shape), dt, kind="ExternalInput").ap()

    def dram_out(self, name, shape, dt=F32):
        return self.nc.dram_tensor(self._uname(name), list(shape), dt, kind="ExternalOutput").ap()

    def dram_tmp(self, name, shape, dt=F32, addr_space="Local"):
        return self.nc.dram_tensor(self._uname(name), list(shape), dt, kind="Internal",
                                   addr_space=addr_space).ap()

    def sbuf(self, name, shape, dt=F32):
        st = self.stage_stack if self.stage_stack is not None else self.stack
        return st.enter_context(
            self.nc.sbuf_tensor(self._uname(self.stage + name), list(shape), dt))

    def psum(self, name, shape, dt=F32):
        st = self.stage_stack if self.stage_stack is not None else self.stack
        return st.enter_context(
            self.nc.psum_tensor(self._uname(self.stage + name), list(shape), dt))

    def _fence(self, q):
        total = self.dma_count[q]
        key = (q, total)
        if key in self._fences:
            return self._fences[key]
        self.fcount[q] += 1
        tok = ("f_" + q, self.fcount[q])
        self.thunks[q].append(((("d_" + q, total),), lambda e: e.nop(), "f_" + q, 1))
        self._fences[key] = tok
        return tok

    def _resolve(self, wait, eng):
        out = []
        for w in wait:
            if w is None:
                continue
            sname, v = w
            if sname.startswith("d_"):
                q = sname[2:]
                if q == eng:
                    out.append((sname, self.dma_count[q]))
                else:
                    out.append(self._fence(q))
            else:
                out.append((sname, v))
        return tuple(out)

    def op(self, eng, fn, wait=(), inc=True):
        wait = self._resolve(wait, eng)
        if inc:
            self.count[eng] += 1
            tok = ("c_" + eng, self.count[eng])
            self.thunks[eng].append((wait, fn, "c_" + eng, 1))
            return tok
        self.thunks[eng].append((wait, fn, None, 0))
        return None

    def dma(self, eng, out, in_, wait=(), **kw):
        wait = self._resolve(wait, eng)
        self.dma_count[eng] += 16
        tok = ("d_" + eng, self.dma_count[eng])
        fn = lambda e: e.dma_start(out=out, in_=in_, **kw)
        self.thunks[eng].append((wait, fn, "d_" + eng, 16))
        return tok

    def collective(self, kind, ins, outs, wait=()):
        wait = self._resolve(wait, "gpsimd")
        sem = f"cc{self.cc_i}"
        self.cc_i += 1
        tok = (sem, 1)
        rg = [list(range(NCORES))]

        def fn(g, kind=kind, ins=ins, outs=outs):
            with allow_ici_all_to_all():
                return g.collective_compute(kind, ALU.bypass, replica_groups=rg,
                                            ins=ins, outs=outs)
        self.thunks["gpsimd"].append((wait, fn, sem, 1))
        self.cc_tokens.append(tok)
        return tok

    def barrier_tokens(self):
        toks = []
        for e in ENGS:
            if self.count[e]:
                toks.append(("c_" + e, self.count[e]))
            if self.dma_count[e]:
                toks.append(("d_" + e, self.dma_count[e]))
        toks += self.cc_tokens
        return toks

    def join(self, extra=()):
        toks = tuple(self.barrier_tokens()) + tuple(extra)
        return tuple(self.op(e, lambda en: en.nop(), wait=toks) for e in ENGS)

    def build(self):
        nc = self.nc
        prog_self = self
        with nc.Block() as block:
            for e in ENGS:
                thunks = self.thunks[e]
                if not thunks:
                    continue
                section = getattr(block, e)

                def body(engobj, items=thunks):
                    hw = {}
                    for waits, fn, semname, inc in items:
                        for (sname, val) in waits:
                            if hw.get(sname, 0) < val:
                                engobj.wait_ge(prog_self.sems[sname], val)
                                hw[sname] = val
                        inst = fn(engobj)
                        if semname is not None:
                            inst.then_inc(prog_self.sems[semname], inc)
                section(body)
        return nc

    def run(self, in_maps, **kw):
        nc = self.build()
        try:
            res = run_bass_kernel_spmd(nc, in_maps, core_ids=list(range(len(in_maps))), **kw)
        finally:
            self.stack.close()
        return res


class PsumPool:

    def __init__(self, p, name, n=8, shape=(128, 512), dt=F32):
        self.p = p
        self.tiles = [p.psum(f"{name}{i}", list(shape), dt) for i in range(n)]
        self.war = [None] * n
        self.i = 0
        self.n = n

    def get(self):
        i = self.i
        self.i = (self.i + 1) % self.n
        w = (self.war[i],) if self.war[i] is not None else ()
        return i, self.tiles[i], w

    def consumed(self, bank_id, tok):
        self.war[bank_id] = tok


def linear(p, *, n_k, n_m, T, x_tile, w_tile, pool, epilogue, TJ=512, wait=(),
           extra_mm=None, mw=128):
    toks = []
    n_j = ceil_div(T, TJ)
    n_extra = len(extra_mm) if extra_mm else 0
    for m in range(n_m):
        for j in range(n_j):
            t0, t1 = j * TJ, min((j + 1) * TJ, T)
            bank, ps, war = pool.get()
            last_tok = None
            for k in range(n_k):
                st = (k == 0)
                sp = (k == n_k - 1) and n_extra == 0
                is_last = sp

                def fn(t, m=m, k=k, t0=t0, t1=t1, ps=ps, st=st, sp=sp):
                    return t.matmul(ps[0:mw, 0:t1 - t0], w_tile(k, m), x_tile(k)[:, t0:t1],
                                    start=st, stop=sp)
                w = tuple(wait) + war if k == 0 else ()
                last_tok = p.op("tensor", fn, wait=w, inc=is_last)
            if extra_mm:
                for i, (lhsT_fn, rhs_fn) in enumerate(extra_mm):
                    sp = i == n_extra - 1

                    def fn(t, m=m, t0=t0, t1=t1, ps=ps, sp=sp, lf=lhsT_fn, rf=rhs_fn):
                        return t.matmul(ps[0:mw, 0:t1 - t0], lf(m), rf(m)[:, t0:t1],
                                        start=False, stop=sp)
                    last_tok = p.op("tensor", fn, inc=sp)
            etok = epilogue(m, j, t0, t1, ps, (last_tok,))
            pool.consumed(bank, etok)
            toks.append(etok)
    return toks


def act_epilogue(p, dst_fn, func=AF.Copy, bias_fn=None, scale=1.0, eng="scalar", mw=128):
    def ep(m, j, t0, t1, ps, wait):
        kw = {}
        if bias_fn is not None:
            kw["bias"] = bias_fn(m)
        if scale != 1.0:
            kw["scale"] = scale

        def fn(s, m=m, t0=t0, t1=t1, ps=ps, kw=kw):
            return s.activation(out=dst_fn(m, t0, t1), in_=ps[0:mw, 0:t1 - t0], func=func, **kw)
        return p.op(eng, fn, wait=wait)
    return ep


def chunks_of(T, TJ=512):
    return [(t, min(t + TJ, T)) for t in range(0, T, TJ)]



B = 2
LLOC = 1024
HALO = 3
TH = LLOC + HALO
TT = B * TH


def chunks_of(T, TJ=512):
    return [(t, min(t + TJ, T)) for t in range(0, T, TJ)]


def load_col_vec(p, dst, src_1d, nf=None):
    return p.dma("sync", dst[:, :], src_1d.rearrange("(m f) -> f m", f=128),
                 allow_slow_non_contiguous=True)


def emit_l0pre(p, io):
    D, NF = 1024, 8
    xh_d = io["xh"]
    win_d = io["W_in"]
    nw_d = io["norm_w"]
    cw_d = io["conv_w"]
    cb_d = io["conv_b"]
    wx_d = io["W_x"]
    res_d = io["res"]
    xc_d = io["xc"]
    dlt_d = io["dlt"]
    bm_d = io["Bm"]
    cm_d = io["Cm"]
    ab_d = io["ab"]

    x_bf = p.sbuf("x_bf", [128, NF * TT], BF16)
    sq_bf = p.sbuf("sq_bf", [128, NF * 512], BF16)
    w_sb = p.sbuf("w_sb", [128, NF * 2048], BF16)
    wx_sb = p.sbuf("wx_sb", [128, 16 * 96], BF16)
    gam = p.sbuf("gam", [128, NF], F32)
    cw_sb = p.sbuf("cw_sb", [128, 16 * 4], F32)
    cb_sb = p.sbuf("cb_sb", [128, 16], F32)
    ones_bf = p.sbuf("ones_bf", [128, 1], BF16)
    ones_row = p.sbuf("ones_row", [1, 128], F32)
    eps_sb = p.sbuf("eps_sb", [1, 1], F32)
    rows = p.sbuf("rows", [1, 4 * TT], F32)
    ab_sb = p.sbuf("ab_sb", [128, 2 * TT], F32)
    h_bf = p.sbuf("h_bf", [128, NF * TT], BF16)
    tmp_f = p.sbuf("tmp_f", [128, TT], F32)
    xpre_st = p.sbuf("xpre_st", [128, 2 * TT], F32)
    conv_st = p.sbuf("conv_st", [128, 2 * LLOC], F32)
    xc_st = p.sbuf("xc_st", [128, B * LLOC], BF16)
    res_st = p.sbuf("res_st", [128, TT], BF16)
    dlt_sb = p.sbuf("dlt_sb", [64, 512], BF16)
    bc_sb = p.sbuf("bc_sb", [32, 512], F32)

    pool = PsumPool(p, "ps", n=2)
    ps_xd = [p.psum(f"ps_xd{i}", [96, 512], F32) for i in range(B * 2)]
    ps_a = p.psum("ps_a", [128, 512], F32)
    ps_b = p.psum("ps_b", [128, 512], F32)

    xld = []
    for f in range(NF):
        for b in range(B):
            xld.append(p.dma("gpsimd", x_bf[:, f * TT + b * TH: f * TT + (b + 1) * TH],
                             xh_d[b, f * 128:(f + 1) * 128, :]))
    wld1 = []
    for f in range(NF):
        wld1.append(p.dma("gpsimd", w_sb[:, f * 2048:(f + 1) * 2048],
                          win_d[f * 128:(f + 1) * 128, 0:2048]))
    wxld = [p.dma("gpsimd", wx_sb[:, c * 96:(c + 1) * 96],
                  wx_d[c * 128:(c + 1) * 128, :]) for c in range(16)]
    cld = [load_col_vec(p, gam, nw_d),
           p.dma("sync", cw_sb[:, :].rearrange("p (m j) -> p m j", j=4),
                 cw_d.rearrange("(m f) j -> f m j", f=128),
                 allow_slow_non_contiguous=True),
           load_col_vec(p, cb_sb, cb_d)]
    t_ones = p.op("vector", lambda v: v.memset(ones_bf[:, :], 1.0))
    t_ones2 = p.op("vector", lambda v: v.memset(ones_row[:, :], 1.0))
    t_eps = p.op("vector", lambda v: v.memset(eps_sb[:, :], 1e-5))
    xld = tuple(xld)

    ev_prev = {}
    sq_prev = {}
    ev1p = ev2p = None
    for ci, (t0, t1) in enumerate(chunks_of(TT)):
        cw_ = t1 - t0
        tok1 = tok2 = None
        for f in range(NF):
            w_sq = xld + ((sq_prev[f],) if f in sq_prev else ())
            sqt = p.op("scalar", lambda s, f=f, t0=t0, t1=t1: s.activation(
                out=sq_bf[:, f * 512:f * 512 + (t1 - t0)],
                in_=x_bf[:, f * TT + t0:f * TT + t1], func=AF.Square), wait=w_sq)
            w1 = xld + (t_ones,) + ((ev1p,) if (f == 0 and ev1p) else ())
            tok1 = p.op("tensor", lambda t, f=f, t0=t0, t1=t1, cw_=cw_: t.matmul(
                ps_a[0:1, 0:cw_], ones_bf[:, :], x_bf[:, f * TT + t0:f * TT + t1],
                start=(f == 0), stop=(f == NF - 1)), wait=w1, inc=True)
            w2 = (sqt,) + ((ev2p,) if (f == 0 and ev2p) else ())
            tok2 = p.op("tensor", lambda t, f=f, cw_=cw_: t.matmul(
                ps_b[0:1, 0:cw_], ones_bf[:, :], sq_bf[:, f * 512:f * 512 + cw_],
                start=(f == 0), stop=(f == NF - 1)), wait=w2, inc=True)
            sq_prev[f] = tok2
        ev1p = p.op("vector", lambda v, t0=t0, t1=t1: v.tensor_copy(
            rows[0:1, 0 * TT + t0:0 * TT + t1], ps_a[0:1, 0:t1 - t0]), wait=(tok1,))
        ev2p = p.op("vector", lambda v, t0=t0, t1=t1: v.tensor_copy(
            rows[0:1, 1 * TT + t0:1 * TT + t1], ps_b[0:1, 0:t1 - t0]), wait=(tok2,))

    S = lambda i: rows[0:1, i * TT:(i + 1) * TT]
    rtok = (ev1p, ev2p, t_eps)
    r1 = p.op("vector", lambda v: v.tensor_single_scalar(S(2), S(0), 1.0 / D, ALU.mult), wait=rtok)
    r2 = p.op("vector", lambda v: v.tensor_single_scalar(S(3), S(1), 1.0 / D, ALU.mult), wait=rtok)
    r3 = p.op("vector", lambda v: v.tensor_mul(S(0), S(2), S(2)), wait=(r1,))
    r4 = p.op("vector", lambda v: v.tensor_sub(S(1), S(3), S(0)), wait=(r2, r3))
    r5 = p.op("scalar", lambda s: s.activation(out=S(0), in_=S(1), func=AF.Sqrt,
                                               bias=eps_sb[0:1, 0:1]), wait=(r4,))
    r6 = p.op("vector", lambda v: v.reciprocal(S(3), S(0)), wait=(r5,))
    ab_dma = []
    for b in range(B):
        ab_dma.append(p.dma("sync", ab_d[0, b, :],
                            rows[0:1, 3 * TT + b * TH + HALO:3 * TT + (b + 1) * TH], wait=(r6,)))
    r7 = p.op("vector", lambda v: v.tensor_mul(S(0), S(0), S(0)), wait=(r6,) + tuple(ab_dma))
    r8 = p.op("vector", lambda v: v.reciprocal(S(0), S(0)), wait=(r7,))
    r9 = p.op("vector", lambda v: v.tensor_mul(S(0), S(1), S(0)), wait=(r8,))
    r10 = p.op("scalar", lambda s: s.activation(out=S(0), in_=S(0), func=AF.Sqrt,
                                                bias=eps_sb[0:1, 0:1]), wait=(r9,))
    r11 = p.op("vector", lambda v: v.reciprocal(S(0), S(0)), wait=(r10,))
    r12 = p.op("vector", lambda v: v.tensor_mul(S(1), S(2), S(3)), wait=(r11,))
    r13 = p.op("vector", lambda v: v.tensor_single_scalar(S(1), S(1), -1.0, ALU.mult), wait=(r12,))
    for b in range(B):
        ab_dma.append(p.dma("sync", ab_d[1, b, :],
                            rows[0:1, 1 * TT + b * TH + HALO:1 * TT + (b + 1) * TH], wait=(r13,)))
    r14 = p.op("vector", lambda v: v.tensor_mul(S(2), S(3), S(0)), wait=(r13,))
    r15 = p.op("vector", lambda v: v.tensor_mul(S(3), S(1), S(0)),
               wait=(r14,) + tuple(ab_dma))

    bc_toks = []
    ea = eb = None
    for (t0, t1) in chunks_of(TT):
        tba = p.op("tensor", lambda t, t0=t0, t1=t1: t.matmul(
            ps_a[:, 0:t1 - t0], ones_row[:, :], rows[0:1, 2 * TT + t0:2 * TT + t1],
            start=True, stop=True), wait=(r14, ev1p, t_ones2) + ((ea,) if ea else ()))
        tbb = p.op("tensor", lambda t, t0=t0, t1=t1: t.matmul(
            ps_b[:, 0:t1 - t0], ones_row[:, :], rows[0:1, 3 * TT + t0:3 * TT + t1],
            start=True, stop=True), wait=(r15, ev2p) + ((eb,) if eb else ()))
        ea = p.op("vector", lambda v, t0=t0, t1=t1: v.tensor_copy(
            ab_sb[:, t0:t1], ps_a[:, 0:t1 - t0]), wait=(tba,))
        eb = p.op("vector", lambda v, t0=t0, t1=t1: v.tensor_copy(
            ab_sb[:, TT + t0:TT + t1], ps_b[:, 0:t1 - t0]), wait=(tbb,))
        bc_toks += [ea, eb]
    bc_toks = tuple(bc_toks)

    h_toks = []
    prev = None
    for f in range(NF):
        w = bc_toks + xld + (cld[0],) + ((prev,) if prev else ())
        t1_ = p.op("vector", lambda v, f=f: v.scalar_tensor_tensor(
            out=tmp_f[:, :], in0=x_bf[:, f * TT:(f + 1) * TT], scalar=gam[:, f:f + 1],
            in1=ab_sb[:, 0:TT], op0=ALU.mult, op1=ALU.mult), wait=w)
        t2_ = p.op("vector", lambda v, f=f: v.scalar_tensor_tensor(
            out=h_bf[:, f * TT:(f + 1) * TT], in0=ab_sb[:, TT:2 * TT],
            scalar=gam[:, f:f + 1], in1=tmp_f[:, :], op0=ALU.mult, op1=ALU.add), wait=(t1_,))
        prev = t2_
        h_toks.append(t2_)
    h_toks = tuple(h_toks)

    store_toks = []
    war_xpre = {}
    war_xc = {}
    xdbl_last = {}
    for m in range(16):
        ping = m % 2
        evs = []
        for (t0, t1) in chunks_of(TT):
            bank, ps, war = pool.get()
            tok = None
            for k in range(NF):
                wk = h_toks + tuple(wld1) + war + tuple(war_xpre.get(ping, ())) \
                    if k == 0 else ()

                def fn(t, m=m, k=k, t0=t0, t1=t1, ps=ps):
                    return t.matmul(ps[:, 0:t1 - t0],
                                    w_sb[:, k * 2048 + m * 128:k * 2048 + (m + 1) * 128],
                                    h_bf[:, k * TT + t0:k * TT + t1],
                                    start=(k == 0), stop=(k == NF - 1))
                tok = p.op("tensor", fn, wait=wk, inc=(k == NF - 1))
            ev = p.op("scalar", lambda s, t0=t0, t1=t1, ps=ps, ping=ping: s.activation(
                out=xpre_st[:, ping * TT + t0:ping * TT + t1], in_=ps[:, 0:t1 - t0],
                func=AF.Copy), wait=(tok,))
            evs.append(ev)
            pool.consumed(bank, ev)
        conv_toks = []
        xc_ready = []
        for b in range(B):
            base = ping * TT + b * TH
            cbank = b
            c_ops = []
            w0 = tuple(evs) + tuple(cld[1:]) + tuple(war_xc.get(0, ()))
            c1 = p.op("vector", lambda v, m=m, base=base, cbank=cbank: v.tensor_scalar(
                out=conv_st[:, cbank * LLOC:(cbank + 1) * LLOC],
                in0=xpre_st[:, base + 0:base + 0 + LLOC],
                scalar1=cw_sb[:, m * 4 + 0:m * 4 + 1], scalar2=cb_sb[:, m:m + 1],
                op0=ALU.mult, op1=ALU.add), wait=w0)
            prev_c = c1
            for j in range(1, 4):
                src = cbank if (j % 2 == 0) else (2 + 0)
                def fnc(v, m=m, j=j, base=base, cbank=cbank, prev_j=j - 1):
                    dst_is_tmp = (j % 2 == 1)
                    dst = tmp_f[:, cbank * LLOC:(cbank + 1) * LLOC] if dst_is_tmp \
                        else conv_st[:, cbank * LLOC:(cbank + 1) * LLOC]
                    srcb = conv_st[:, cbank * LLOC:(cbank + 1) * LLOC] if dst_is_tmp \
                        else tmp_f[:, cbank * LLOC:(cbank + 1) * LLOC]
                    return v.scalar_tensor_tensor(
                        out=dst, in0=xpre_st[:, base + j:base + j + LLOC],
                        scalar=cw_sb[:, m * 4 + j:m * 4 + j + 1], in1=srcb,
                        op0=ALU.mult, op1=ALU.add)
                prev_c = p.op("vector", fnc, wait=(prev_c,))
            conv_toks.append(prev_c)
            sil = p.op("scalar", lambda s, b=b, cbank=cbank: s.activation(
                out=xc_st[:, b * LLOC:(b + 1) * LLOC],
                in_=tmp_f[:, cbank * LLOC:(cbank + 1) * LLOC], func=AF.Silu),
                wait=(prev_c,) + tuple(war_xc.get(0, ())))
            xc_ready.append(sil)
        war_xpre[ping] = tuple(conv_toks)
        new_war = []
        for b in range(B):
            dmat = p.dma("sync", xc_d[b, m * 128:(m + 1) * 128, :],
                         xc_st[:, b * LLOC:(b + 1) * LLOC],
                         wait=(xc_ready[b],))
            new_war.append(dmat)
            for jj in range(2):
                def fx(t, m=m, b=b, jj=jj, ping=ping):
                    return t.matmul(
                        ps_xd[b * 2 + jj][:, :],
                        wx_sb[:, m * 96:(m + 1) * 96],
                        xc_st[:, b * LLOC + jj * 512:b * LLOC + (jj + 1) * 512],
                        start=(m == 0), stop=(m == 15), skip_group_check=True)
                xt = p.op("tensor", fx, wait=(xc_ready[b],), inc=True)
                new_war.append(xt)
                xdbl_last[(b, jj)] = xt
        war_xc[0] = tuple(new_war)
        store_toks += new_war

    zx_done = []
    wld2 = []
    w2wait = tuple(tok for tok in store_toks if tok is not None) + h_toks
    for f in range(NF):
        wld2.append(p.dma("gpsimd", w_sb[:, f * 2048:(f + 1) * 2048],
                          win_d[f * 128:(f + 1) * 128, 2048:4096], wait=w2wait))
    res_dma = []
    war_res = {}
    for m in range(16):
        ping = m % 2
        evs = []
        for (t0, t1) in chunks_of(TT):
            bank, ps, war = pool.get()
            tok = None
            for k in range(NF):
                wk = tuple(wld2) + war + tuple(war_res.get(ping, ())) if k == 0 else ()

                def fn(t, m=m, k=k, t0=t0, t1=t1, ps=ps):
                    return t.matmul(ps[:, 0:t1 - t0],
                                    w_sb[:, k * 2048 + m * 128:k * 2048 + (m + 1) * 128],
                                    h_bf[:, k * TT + t0:k * TT + t1],
                                    start=(k == 0), stop=(k == NF - 1))
                tok = p.op("tensor", fn, wait=wk, inc=(k == NF - 1))
            ev = p.op("scalar", lambda s, t0=t0, t1=t1, ps=ps: s.activation(
                out=res_st[:, t0:t1], in_=ps[:, 0:t1 - t0],
                func=AF.Copy), wait=(tok,) + tuple(war_res.get(0, ())))
            evs.append(ev)
            pool.consumed(bank, ev)
        drs = []
        for b in range(B):
            dr = p.dma("sync", res_d[b, m * 128:(m + 1) * 128, :],
                       res_st[:, b * TH + HALO:(b + 1) * TH],
                       wait=tuple(evs))
            drs.append(dr)
            res_dma.append(dr)
        war_res[0] = tuple(drs)

    fin_dma = list(res_dma) + ab_dma
    war_small = ()
    for b in range(B):
        for jj in range(2):
            wt = (xdbl_last[(b, jj)],) + war_small
            e1 = p.op("scalar", lambda s, b=b, jj=jj: s.activation(
                out=dlt_sb[0:64, :], in_=ps_xd[b * 2 + jj][0:64, :], func=AF.Copy), wait=wt)
            e2 = p.op("vector", lambda v, b=b, jj=jj: v.tensor_copy(
                bc_sb[0:32, :], ps_xd[b * 2 + jj][64:96, :]), wait=wt)
            d1 = p.dma("sync", dlt_d[b, :, jj * 512:(jj + 1) * 512], dlt_sb[0:64, :], wait=(e1,))
            d2 = p.dma("gpsimd", bm_d[b, :, jj * 512:(jj + 1) * 512], bc_sb[0:16, :], wait=(e2,))
            d3 = p.dma("gpsimd", cm_d[b, :, jj * 512:(jj + 1) * 512], bc_sb[16:32, :], wait=(e2,))
            war_small = (d1, d2, d3)
            fin_dma += [d1, d2, d3]
    fin_dma += [t for t in store_toks if t is not None and t[0].startswith("d_")]
    p.op("sync", lambda s: s.nop(), wait=tuple(fin_dma), inc=False)


def emit_l0scan(p, io):
    S = 8192
    HF = 4096
    xc_d = io["xc"]
    dlt_d = io["dlt"]
    wdt_d = io["W_dt"]
    bdt_d = io["b_dt"]
    a_d = io["A"]
    bm_d = io["Bm"]
    cm_d = io["Cm"]
    y_d = io["y"]

    dlt_sb = p.sbuf("dlt_sb", [64, B * S], BF16)
    wdt_sb = p.sbuf("wdt_sb", [64, 256], BF16)
    bdt_sb = p.sbuf("bdt_sb", [128, 2], F32)
    asb = p.sbuf("asb", [128, 2 * 16], F32)
    one_sb = p.sbuf("one_sb", [128, 1], F32)
    delta = p.sbuf("delta", [128, HF], F32)
    dx = p.sbuf("dx", [128, HF], F32)
    xc_sb = p.sbuf("xc_sb", [128, HF], BF16)
    brep = p.sbuf("brep", [128, 2 * HF], BF16)
    crep = p.sbuf("crep", [128, 2 * HF], BF16)
    a_t = p.sbuf("a_t", [128, HF], F32)
    u_t = p.sbuf("u_t", [128, 2 * HF], F32)
    h_t = p.sbuf("h_t", [128, HF], F32)
    tmp = p.sbuf("tmp", [128, HF], F32)
    yacc = p.sbuf("yacc", [128, HF], F32)
    carry = p.sbuf("carry", [128, 16], F32)
    pool = PsumPool(p, "ps", n=4)

    ld0 = [p.dma("gpsimd", dlt_sb[:, b * S:(b + 1) * S], dlt_d[b]) for b in range(B)]
    ld0.append(p.dma("gpsimd", wdt_sb[:, :], wdt_d))
    ld0.append(load_col_vec(p, bdt_sb, bdt_d))
    ld0.append(p.dma("sync", asb[:, :].rearrange("p (c n) -> p c n", c=2),
                     a_d.rearrange("(c p) n -> p c n", p=128),
                     allow_slow_non_contiguous=True))
    t_one = p.op("vector", lambda v: v.memset(one_sb[:, :], 1.0))
    ld0 = tuple(ld0)

    last_b = {0: (), 1: ()}
    last_c = {0: (), 1: ()}
    BSL = lambda sl: brep[:, sl * HF:(sl + 1) * HF]
    CSL = lambda sl: crep[:, sl * HF:(sl + 1) * HF]
    lb_t = {}
    lc_t = {}
    sc_slot = {0: None, 1: None}
    prev_at = prev_ut = prev_yt = prev_tt = prev_sc = None
    prev_ctok = None
    y_dma_prev = ()
    for ct in range(2):
        for b in range(B):
            carry_toks = {}
            for half in range(2):
                t0 = half * HF
                ev_toks = []
                for jj in range(8):
                    bank, ps, war = pool.get()
                    mm = p.op("tensor", lambda t, ct=ct, b=b, t0=t0, jj=jj, ps=ps:
                              t.matmul(ps[:, :],
                                       wdt_sb[0:64, ct * 128:(ct + 1) * 128],
                                       dlt_sb[0:64, b * S + t0 + jj * 512:
                                              b * S + t0 + (jj + 1) * 512],
                                       start=True, stop=True),
                              wait=ld0 + war)
                    wtmp = (mm,) + ((prev_yt,) if jj == 0 and prev_yt else ())
                    ev = p.op("vector", lambda v, jj=jj, ps=ps, ct=ct: v.tensor_scalar_add(
                        tmp[:, jj * 512:(jj + 1) * 512], ps[:, :],
                        bdt_sb[:, ct:ct + 1]), wait=wtmp)
                    pool.consumed(bank, ev)
                    ev_toks.append(ev)
                ev_toks = tuple(ev_toks)
                s0a = p.op("vector", lambda v: v.tensor_single_scalar(
                    dx[:, :], tmp[:, :], -1.0, ALU.mult),
                    wait=ev_toks + ((prev_ut,) if prev_ut else ()))
                s1 = p.op("vector", lambda v: v.tensor_max(
                    dx[:, :], dx[:, :], tmp[:, :]), wait=(s0a,))
                s2 = p.op("scalar", lambda s: s.activation(
                    out=dx[:, :], in_=dx[:, :], func=AF.Exp, scale=-1.0), wait=(s1,))
                s3 = p.op("scalar", lambda s: s.activation(
                    out=dx[:, :], in_=dx[:, :], func=AF.Ln,
                    bias=one_sb[:, 0:1]), wait=(s2, t_one))
                s4 = p.op("vector", lambda v: v.tensor_relu(delta[:, :], tmp[:, :]),
                          wait=ev_toks + (s3,) + ((prev_at,) if prev_at else ()))
                s5 = p.op("vector", lambda v: v.tensor_add(delta[:, :], delta[:, :], dx[:, :]),
                          wait=(s3, s4))
                lxc = p.dma("gpsimd", xc_sb[:, :], xc_d[b, ct * 128:(ct + 1) * 128, t0:t0 + HF],
                            wait=y_dma_prev)
                s6 = p.op("vector", lambda v: v.tensor_mul(dx[:, :], delta[:, :], xc_sb[:, :]),
                          wait=(s5, lxc))
                for n0 in range(2):
                    lb_t[n0] = p.dma("sync", BSL(n0),
                                     bm_d[b, n0:n0 + 1, t0:t0 + HF].broadcast_to([128, HF]),
                                     wait=last_b[n0])
                    lc_t[n0] = p.dma("scalar", CSL(n0),
                                     cm_d[b, n0:n0 + 1, t0:t0 + HF].broadcast_to([128, HF]),
                                     wait=last_c[n0])
                ytoks = []
                for n in range(16):
                    sl = n % 2
                    at = p.op("scalar", lambda s, ct=ct, n=n: s.activation(
                        out=a_t[:, :], in_=delta[:, :], func=AF.Exp,
                        scale=asb[:, ct * 16 + n:ct * 16 + n + 1]),
                        wait=(s5,) + ((prev_sc,) if prev_sc else ()))
                    prev_at = at
                    ut = p.op("gpsimd", lambda g, sl=sl: g.tensor_mul(
                        u_t[:, sl * HF:(sl + 1) * HF], dx[:, :], BSL(sl)),
                        wait=(s6, lb_t[n]) +
                        ((sc_slot[sl],) if sc_slot[sl] else ()))
                    last_b[sl] = (ut,)
                    prev_ut = ut
                    if half == 0:
                        init = 0.0
                        iw = ()
                    else:
                        init = carry[:, n:n + 1]
                        iw = (carry_toks[n],)
                    sc = p.op("vector", lambda v, init=init, sl=sl: v.tensor_tensor_scan(
                        h_t[:, :], a_t[:, :], u_t[:, sl * HF:(sl + 1) * HF],
                        init, ALU.mult, ALU.add),
                        wait=(at, ut) + iw +
                        tuple(t for t in (prev_tt, prev_ctok) if t))
                    sc_slot[sl] = sc
                    prev_sc = sc
                    if half == 0:
                        ctok = p.op("vector", lambda v, n=n: v.tensor_copy(
                            carry[:, n:n + 1], h_t[:, HF - 1:HF]), wait=(sc,))
                        carry_toks[n] = ctok
                        prev_ctok = ctok
                    else:
                        prev_ctok = None
                    if n == 0:
                        yt = p.op("vector", lambda v, sl=sl: v.tensor_mul(
                            yacc[:, :], h_t[:, :], CSL(sl)),
                            wait=(sc, lc_t[n]) + y_dma_prev)
                        last_c[sl] = (yt,)
                        prev_tt = yt
                    else:
                        tt_ = p.op("vector", lambda v, sl=sl: v.tensor_mul(
                            tmp[:, :], h_t[:, :], CSL(sl)),
                            wait=(sc, lc_t[n]) + ((prev_yt,) if prev_yt else ()))
                        last_c[sl] = (tt_,)
                        prev_tt = tt_
                        yt = p.op("vector", lambda v: v.tensor_add(
                            yacc[:, :], yacc[:, :], tmp[:, :]), wait=(tt_,))
                    prev_yt = yt
                    ytoks.append(yt)
                    if n + 2 < 16:
                        lb_t[n + 2] = p.dma("sync", BSL(sl),
                                            bm_d[b, n + 2:n + 3, t0:t0 + HF]
                                            .broadcast_to([128, HF]), wait=last_b[sl])
                        lc_t[n + 2] = p.dma("scalar", CSL(sl),
                                            cm_d[b, n + 2:n + 3, t0:t0 + HF]
                                            .broadcast_to([128, HF]), wait=last_c[sl])
                yd = p.dma("gpsimd", y_d[b, ct * 128:(ct + 1) * 128, t0:t0 + HF], yacc[:, :],
                           wait=(ytoks[-1],))
                y_dma_prev = (yd, ytoks[-1])
    p.op("sync", lambda s: s.nop(), wait=y_dma_prev, inc=False)


def emit_l0post(p, io):
    S = LLOC
    T = B * S
    x_d = io["x"]
    ab_d = io["ab"]
    y_dd = io["y"]
    xc_d = io["xc"]
    res_d = io["res"]
    dcol_d = io["D"]
    wout_d = io["W_out"]
    s0_d = io["s0"]

    wo_sb = p.sbuf("wo_sb", [128, 16 * 1024], BF16)
    dcol = p.sbuf("dcol", [128, 16], F32)
    ab_sb = p.sbuf("ab_sb", [128, 2 * T], F32)
    rows = p.sbuf("rows", [1, 2 * T], F32)
    ones_row = p.sbuf("ones_row", [1, 128], F32)
    y_t = p.sbuf("y_t", [128, 2 * 512], BF16)
    xc_t = p.sbuf("xc_t", [128, 2 * 512], BF16)
    res_t = p.sbuf("res_t", [128, 2 * 512], BF16)
    t1_f = p.sbuf("t1_f", [128, 512], F32)
    srs_f = p.sbuf("srs_f", [128, 512], F32)
    u_bf = p.sbuf("u_bf", [128, 2 * 512], BF16)
    x0_t = p.sbuf("x0_t", [128, 1024], F32)
    s0_t = p.sbuf("s0_t", [128, 512], F32)
    out_ps = [p.psum(f"ops{i}", [128, 512], F32) for i in range(8)]

    wld = [p.dma("gpsimd", wo_sb[:, c * 1024:(c + 1) * 1024],
                 wout_d[c * 128:(c + 1) * 128, :]) for c in range(16)]
    wld.append(load_col_vec(p, dcol, dcol_d))
    rld = [p.dma("sync", rows[0:1, b * S:(b + 1) * S], ab_d[0, b, :]) for b in range(B)]
    rld += [p.dma("sync", rows[0:1, T + b * S:T + (b + 1) * S], ab_d[1, b, :]) for b in range(B)]
    t_ones = p.op("vector", lambda v: v.memset(ones_row[:, :], 1.0))

    bc_toks = []
    ea = eb = None
    for (t0, t1) in chunks_of(T):
        ta = p.op("tensor", lambda t, t0=t0, t1=t1: t.matmul(
            out_ps[0][:, 0:t1 - t0], ones_row[:, :], rows[0:1, t0:t1],
            start=True, stop=True), wait=tuple(rld) + (t_ones,) + ((ea,) if ea else ()))
        tb = p.op("tensor", lambda t, t0=t0, t1=t1: t.matmul(
            out_ps[1][:, 0:t1 - t0], ones_row[:, :], rows[0:1, T + t0:T + t1],
            start=True, stop=True), wait=(t_ones,) + ((eb,) if eb else ()))
        ea = p.op("vector", lambda v, t0=t0, t1=t1: v.tensor_copy(
            ab_sb[:, t0:t1], out_ps[0][:, 0:t1 - t0]), wait=(ta,))
        eb = p.op("vector", lambda v, t0=t0, t1=t1: v.tensor_copy(
            ab_sb[:, T + t0:T + t1], out_ps[1][:, 0:t1 - t0]), wait=(tb,))
        bc_toks += [ea, eb]
    bc_toks = tuple(bc_toks)

    war_u = {}
    war_ld = {}
    war_out = {}
    s0_dma = []
    prev_tu = ()
    prev_s0 = ()
    for j in range(4):
        b, sj = divmod(j * 512, S)
        mm_last = [None] * 8
        for ct in range(16):
            ping = ct % 2
            lw = tuple(war_ld.get(ping, ()))
            l1 = p.dma("gpsimd", y_t[:, ping * 512:(ping + 1) * 512],
                       y_dd[b, ct * 128:(ct + 1) * 128, sj:sj + 512], wait=lw)
            l2 = p.dma("gpsimd", xc_t[:, ping * 512:(ping + 1) * 512],
                       xc_d[b, ct * 128:(ct + 1) * 128, sj:sj + 512], wait=lw)
            l3 = p.dma("scalar", res_t[:, ping * 512:(ping + 1) * 512],
                       res_d[b, ct * 128:(ct + 1) * 128, sj:sj + 512], wait=lw)
            wu = tuple(war_u.get(ping, ())) + prev_tu
            tt1 = p.op("vector", lambda v, ct=ct, ping=ping: v.scalar_tensor_tensor(
                out=t1_f[:, :], in0=xc_t[:, ping * 512:(ping + 1) * 512],
                scalar=dcol[:, ct:ct + 1], in1=y_t[:, ping * 512:(ping + 1) * 512],
                op0=ALU.mult, op1=ALU.add), wait=(l1, l2) + wu + tuple(wld[-1:]))
            ts_ = p.op("scalar", lambda s, ping=ping: s.activation(
                out=srs_f[:, :], in_=res_t[:, ping * 512:(ping + 1) * 512],
                func=AF.Silu), wait=(l3,) + wu)
            tu = p.op("vector", lambda v, ping=ping: v.tensor_mul(
                u_bf[:, ping * 512:(ping + 1) * 512], t1_f[:, :], srs_f[:, :]),
                wait=(tt1, ts_))
            prev_tu = (tu,)
            war_ld[ping] = (tt1, ts_)
            mts = []
            for mt in range(8):
                def fm(t, ct=ct, mt=mt, ping=ping):
                    return t.matmul(out_ps[mt][:, :],
                                    wo_sb[:, ct * 1024 + mt * 128:ct * 1024 + (mt + 1) * 128],
                                    u_bf[:, ping * 512:(ping + 1) * 512],
                                    start=(ct == 0), stop=(ct == 15), skip_group_check=True)
                w_mm = (tu,)
                if ct == 0:
                    w_mm = w_mm + (bc_toks if j == 0 else ()) + tuple(war_out.get(mt, ()))
                mts.append(p.op("tensor", fm, wait=w_mm, inc=(ct == 15 or mt == 7)))
            war_u[ping] = (mts[7],)
            if ct == 15:
                mm_last = mts
        for f in range(8):
            lx = p.dma("sync", x0_t[:, 0:512], x_d[b, f * 128:(f + 1) * 128, sj:sj + 512],
                       wait=prev_s0)
            e1 = p.op("vector", lambda v, f=f, j=j: v.scalar_tensor_tensor(
                out=s0_t[:, :], in0=x0_t[:, 0:512], scalar=1.0,
                in1=ab_sb[:, j * 512:(j + 1) * 512], op0=ALU.bypass, op1=ALU.mult),
                wait=(lx, mm_last[f]) + prev_s0)
            e2 = p.op("vector", lambda v, j=j: v.tensor_add(
                s0_t[:, :], s0_t[:, :], ab_sb[:, T + j * 512:T + (j + 1) * 512]), wait=(e1,))
            e3 = p.op("vector", lambda v, f=f: v.tensor_add(
                s0_t[:, :], s0_t[:, :], out_ps[f][:, :]), wait=(e2,))
            sd = p.dma("gpsimd", s0_d[b, f * 128:(f + 1) * 128, sj:sj + 512], s0_t[:, :],
                       wait=(e3,))
            prev_s0 = (sd, e3)
            war_out[f] = (e3,)
            s0_dma.append(sd)
    p.op("sync", lambda s: s.nop(), wait=tuple(s0_dma), inc=False)


def emit_l0ffn(p, io):
    S = LLOC
    T = B * S
    x_d = io["x"]
    ab_d = io["ab"]
    s0_d = io["s0"]
    gw_d = io["gate_w"]
    w1_d = io["w1"]
    b1_d = io["b1"]
    w2_d = io["w2"]
    b2_d = io["b2"]
    sw1_d = io["sw1"]
    sb1_d = io["sb1"]
    sw2_d = io["sw2"]
    sb2_d = io["sb2"]
    x1_d = io["x1"]
    halo_d = io["halo_in"]

    xn_bf = p.sbuf("xn_bf", [128, 8 * T], BF16)
    xm_bf = p.sbuf("xm_bf", [128, 8 * 512], BF16)
    w1_sb = p.sbuf("w1_sb", [128, 8 * 2048], BF16)
    w2_sb = p.sbuf("w2_sb", [128, 16 * 1024], BF16)
    h_bf = p.sbuf("h_bf", [128, 16 * 512], BF16)
    y_acc = p.sbuf("y_acc", [128, 8 * T], BF16)
    ab_sb = p.sbuf("ab_sb", [128, 2 * T], F32)
    rows = p.sbuf("rows", [1, 3 * T], F32)
    ones_row = p.sbuf("ones_row", [1, 128], F32)
    gw_sb = p.sbuf("gw_sb", [128, 8 * 2], BF16)
    b1_sb = p.sbuf("b1_sb", [128, 3 * 32], F32)
    bc_sb = p.sbuf("bc_sb", [128, 8], F32)
    m_bf = p.sbuf("m_bf", [128, 2 * T], BF16)
    x0_t = p.sbuf("x0_t", [128, 512], F32)
    x1_t = p.sbuf("x1_t", [128, 512], F32)
    s0_t = p.sbuf("s0_t", [128, 512], BF16)
    pool = PsumPool(p, "ps", n=6)
    ps_r = p.psum("ps_r", [128, 512], F32)

    wldA = [load_col_vec(p, bc_sb, sb2_d)]
    b2ld = [p.dma("sync", rows[0:1, 0:1024], b2_d[0, :]),
            p.dma("sync", rows[0:1, 1024:2048], b2_d[1, :])]
    gld = [p.dma("gpsimd", gw_sb[:, :].rearrange("f (k e) -> f k e", e=2)[:, :, e],
                 gw_d[e].rearrange("(k f) -> f k", f=128), allow_slow_non_contiguous=True)
           for e in range(2)]
    for e in range(2):
        gld.append(p.dma("sync", b1_sb[:, e * 32:(e + 1) * 32],
                         b1_d[e].rearrange("(m f) -> f m", f=128),
                         allow_slow_non_contiguous=True))
    gld.append(p.dma("sync", b1_sb[:, 64:96], sb1_d.rearrange("(m f) -> f m", f=128),
                     allow_slow_non_contiguous=True))
    rlda = [p.dma("sync", rows[0:1, 2 * T + b * S:2 * T + (b + 1) * S], ab_d[0, b, :])
            for b in range(B)]
    t_ones = p.op("vector", lambda v: v.memset(ones_row[:, :], 1.0))

    b2c = p.sbuf("b2c", [128, 16], F32)
    c1 = p.dma("sync", b2c[:, 0:8], b2_d[0].rearrange("(m f) -> f m", f=128),
               allow_slow_non_contiguous=True)
    c2 = p.dma("sync", b2c[:, 8:16], b2_d[1].rearrange("(m f) -> f m", f=128),
               allow_slow_non_contiguous=True)
    a1 = p.op("vector", lambda v: v.tensor_add(bc_sb[:, :], bc_sb[:, :], b2c[:, 0:8]),
              wait=(c1,) + tuple(wldA))
    a2 = p.op("vector", lambda v: v.tensor_add(bc_sb[:, :], bc_sb[:, :], b2c[:, 8:16]),
              wait=(c2, a1))
    bias_tok = (a2,)

    bc_toks = []
    ea = None
    for (t0, t1) in chunks_of(T):
        ta = p.op("tensor", lambda t, t0=t0, t1=t1: t.matmul(
            ps_r[:, 0:t1 - t0], ones_row[:, :], rows[0:1, 2 * T + t0:2 * T + t1],
            start=True, stop=True), wait=tuple(rlda) + (t_ones,) + ((ea,) if ea else ()))
        ea = p.op("vector", lambda v, t0=t0, t1=t1: v.tensor_copy(
            ab_sb[:, t0:t1], ps_r[:, 0:t1 - t0]), wait=(ta,))
        bc_toks.append(ea)
    rldb = [p.dma("sync", rows[0:1, 2 * T + b * S:2 * T + (b + 1) * S], ab_d[1, b, :],
                  wait=tuple(bc_toks)) for b in range(B)]
    eb = None
    for (t0, t1) in chunks_of(T):
        tb = p.op("tensor", lambda t, t0=t0, t1=t1: t.matmul(
            ps_r[:, 0:t1 - t0], ones_row[:, :], rows[0:1, 2 * T + t0:2 * T + t1],
            start=True, stop=True), wait=tuple(rldb) + ((eb,) if eb else (bc_toks[-1],)))
        eb = p.op("vector", lambda v, t0=t0, t1=t1: v.tensor_copy(
            ab_sb[:, T + t0:T + t1], ps_r[:, 0:t1 - t0]), wait=(tb,))
        bc_toks.append(eb)
    bc_toks = tuple(bc_toks)

    xn_toks = []
    xn_rd = {0: (), 1: ()}
    xi = 0
    for f in range(8):
        for b in range(B):
            for (c0, c1_) in chunks_of(S):
                pp = 0
                xi += 1
                lx = p.dma("sync", x0_t[:, pp * 512:pp * 512 + (c1_ - c0)],
                           x_d[b, f * 128:(f + 1) * 128, c0:c1_], wait=xn_rd[pp])
                tok = p.op("vector", lambda v, f=f, b=b, c0=c0, c1_=c1_, pp=pp: v.scalar_tensor_tensor(
                    out=xn_bf[:, f * T + b * S + c0:f * T + b * S + c1_],
                    in0=x0_t[:, pp * 512:pp * 512 + (c1_ - c0)], scalar=1.0,
                    in1=ab_sb[:, b * S + c0:b * S + c1_], op0=ALU.bypass, op1=ALU.mult),
                    wait=(lx,) + bc_toks)
                xn_rd[pp] = (tok,)
                tok2 = p.op("vector", lambda v, f=f, b=b, c0=c0, c1_=c1_: v.tensor_add(
                    xn_bf[:, f * T + b * S + c0:f * T + b * S + c1_],
                    xn_bf[:, f * T + b * S + c0:f * T + b * S + c1_],
                    ab_sb[:, T + b * S + c0:T + b * S + c1_]), wait=(tok,))
                xn_toks.append(tok2)
    xn_toks = tuple(xn_toks)

    grow_ev = []
    ge_prev = None
    for e in range(2):
        for (t0, t1) in chunks_of(T):
            tok = None
            for k in range(8):
                def fg(t, e=e, k=k, t0=t0, t1=t1):
                    return t.matmul(ps_r[0:1, 0:t1 - t0], gw_sb[:, k * 2 + e:k * 2 + e + 1],
                                    xn_bf[:, k * T + t0:k * T + t1],
                                    start=(k == 0), stop=(k == 7))
                tok = p.op("tensor", fg, wait=(xn_toks + tuple(gld) +
                                               ((ge_prev,) if ge_prev else ())) if k == 0 else (),
                           inc=(k == 7))
            ge_prev = p.op("vector", lambda v, e=e, t0=t0, t1=t1: v.tensor_copy(
                rows[0:1, e * T + t0:e * T + t1], ps_r[0:1, 0:t1 - t0]), wait=(tok,))
            grow_ev.append(ge_prev)
    d1 = p.op("vector", lambda v: v.tensor_sub(rows[0:1, 2 * T:3 * T], rows[0:1, 0:T],
                                               rows[0:1, T:2 * T]), wait=tuple(grow_ev))
    m0r = p.op("vector", lambda v: v.tensor_single_scalar(rows[0:1, 0:T],
                                                          rows[0:1, 2 * T:3 * T], 0.0,
                                                          ALU.is_ge), wait=(d1,))
    m1r = p.op("vector", lambda v: v.tensor_single_scalar(rows[0:1, T:2 * T],
                                                          rows[0:1, 2 * T:3 * T], 0.0,
                                                          ALU.is_lt), wait=(d1,))
    mtoks = []
    me_prev = None
    for e in range(2):
        for (t0, t1) in chunks_of(T):
            tm = p.op("tensor", lambda t, e=e, t0=t0, t1=t1: t.matmul(
                ps_r[:, 0:t1 - t0], ones_row[:, :], rows[0:1, e * T + t0:e * T + t1],
                start=True, stop=True), wait=(m0r, m1r) + ((me_prev,) if me_prev else ()))
            me_prev = p.op("scalar", lambda s, e=e, t0=t0, t1=t1: s.activation(
                out=m_bf[:, e * T + t0:e * T + t1], in_=ps_r[:, 0:t1 - t0], func=AF.Copy),
                wait=(tm,))
            mtoks.append(me_prev)
    mtoks = tuple(mtoks)

    branches = [("e0", 0), ("e1", 1), ("sh", 2)]
    wload_prev = ()
    yacc_tok = {}
    xm_readers = {}
    for (bname, bi) in branches:
        for hh in range(2):
            wl = []
            for k in range(8):
                srcw = (w1_d[bi, k * 128:(k + 1) * 128, hh * 2048:(hh + 1) * 2048]
                        if bi < 2 else sw1_d[k * 128:(k + 1) * 128, hh * 2048:(hh + 1) * 2048])
                wl.append(p.dma("gpsimd", w1_sb[:, k * 2048:(k + 1) * 2048], srcw,
                                wait=wload_prev))
            for k in range(16):
                srcw = (w2_d[bi, hh * 2048 + k * 128:hh * 2048 + (k + 1) * 128, :]
                        if bi < 2 else sw2_d[hh * 2048 + k * 128:hh * 2048 + (k + 1) * 128, :])
                wl.append(p.dma("gpsimd", w2_sb[:, k * 1024:(k + 1) * 1024], srcw,
                                wait=wload_prev))
            wl = tuple(wl)
            wuse = []
            hb_war = ()
            for j in range(4):
                if bi < 2:
                    mmk = []
                    for f in range(8):
                        mmk.append(p.op("vector", lambda v, f=f, bi=bi, j=j: v.tensor_mul(
                            xm_bf[:, f * 512:(f + 1) * 512],
                            xn_bf[:, f * T + j * 512:f * T + (j + 1) * 512],
                            m_bf[:, bi * T + j * 512:bi * T + (j + 1) * 512]),
                            wait=mtoks + xn_toks + tuple(xm_readers.get(0, ()))))
                    xin_toks = tuple(mmk)
                    xtile = lambda k: xm_bf[:, k * 512:(k + 1) * 512]
                else:
                    xin_toks = xn_toks
                    xtile = lambda k, j=j: xn_bf[:, k * T + j * 512:k * T + (j + 1) * 512]
                htoks = []
                f1_toks = []
                for mt in range(16):
                    bank, ps, war = pool.get()
                    tok = None
                    for k in range(8):
                        def f1(t, mt=mt, k=k, ps=ps, xtile=xtile):
                            return t.matmul(ps[:, :],
                                            w1_sb[:, k * 2048 + mt * 128:
                                                  k * 2048 + (mt + 1) * 128],
                                            xtile(k),
                                            start=(k == 0), stop=(k == 7))
                        tok = p.op("tensor", f1,
                                   wait=(wl + war + xin_toks + hb_war) if k == 0 else (),
                                   inc=(k == 7))
                    f1_toks.append(tok)
                    bcol = (bi if bi < 2 else 2) * 32 + hh * 16 + mt
                    ev = p.op("scalar", lambda s, mt=mt, bcol=bcol, ps=ps: s.activation(
                        out=h_bf[:, mt * 512:(mt + 1) * 512], in_=ps[:, :], func=AF.Relu,
                        bias=b1_sb[:, bcol:bcol + 1]), wait=(tok,) + tuple(gld))
                    pool.consumed(bank, ev)
                    htoks.append(ev)
                htoks = tuple(htoks)
                if bi < 2:
                    xm_readers[0] = tuple(f1_toks)
                wuse += f1_toks
                f2_toks = []
                for mt in range(8):
                    bank, ps, war = pool.get()
                    tok = None
                    for k in range(16):
                        def f2(t, mt=mt, k=k, ps=ps):
                            return t.matmul(ps[:, :],
                                            w2_sb[:, k * 1024 + mt * 128:
                                                  k * 1024 + (mt + 1) * 128],
                                            h_bf[:, k * 512:(k + 1) * 512],
                                            start=(k == 0), stop=(k == 15))
                        tok = p.op("tensor", f2, wait=(htoks + war) if k == 0 else (),
                                   inc=(k == 15))
                    f2_toks.append(tok)
                    first = (bname == "e0" and hh == 0)
                    prev_y = yacc_tok.get((j, mt), ())
                    if first:
                        ev = p.op("vector", lambda v, mt=mt, j=j, ps=ps: v.tensor_scalar_add(
                            y_acc[:, mt * T + j * 512:mt * T + (j + 1) * 512], ps[:, :],
                            bc_sb[:, mt:mt + 1]), wait=(tok,) + bias_tok + prev_y)
                    else:
                        ev = p.op("vector", lambda v, mt=mt, j=j, ps=ps: v.tensor_add(
                            y_acc[:, mt * T + j * 512:mt * T + (j + 1) * 512],
                            y_acc[:, mt * T + j * 512:mt * T + (j + 1) * 512],
                            ps[:, :]), wait=(tok,) + prev_y)
                    pool.consumed(bank, ev)
                    yacc_tok[(j, mt)] = (ev,)
                    wuse.append(ev)
                hb_war = tuple(f2_toks)
            wload_prev = tuple(wuse)
    hpack = p.sbuf("hpack", [128, 48], F32)
    hpT = p.sbuf("hpT", [48, 128], F32)
    hident = p.sbuf("hident", [128, 128], F32)
    hid_sc = p.sbuf("hid_sc", [128, 128], F32)
    ps_hp = p.psum("ps_hp", [128, 128], F32)
    hi1 = p.op("gpsimd", lambda g: g.iota(hid_sc[:, :], pattern=[[1, 128]], base=0,
                                          channel_multiplier=-1,
                                          allow_small_or_imprecise_dtypes=True))
    hi2 = p.op("vector", lambda v: v.tensor_single_scalar(hident[:, :], hid_sc[:, :], 0.0,
                                                          ALU.is_equal), wait=(hi1,))
    fin = []
    hp_toks = []
    prev = (xn_toks[-1],)
    for j in range(4):
        b, sj = divmod(j * 512, S)
        for f in range(8):
            gt = p.op("scalar", lambda s, f=f, j=j: s.activation(
                out=xm_bf[:, 0:512], in_=y_acc[:, f * T + j * 512:f * T + (j + 1) * 512],
                func=AF.Silu), wait=tuple(yacc_tok[(j, f)]) + prev)
            lx = p.dma("sync", x0_t[:, 0:512], x_d[b, f * 128:(f + 1) * 128, sj:sj + 512],
                       wait=prev)
            ls = p.dma("scalar", s0_t[:, :], s0_d[b, f * 128:(f + 1) * 128, sj:sj + 512],
                       wait=prev)
            t1_ = p.op("vector", lambda v: v.tensor_mul(
                x1_t[:, :], s0_t[:, :], xm_bf[:, 0:512]), wait=(gt, ls) + prev)
            t2_ = p.op("vector", lambda v: v.tensor_add(x1_t[:, :], x1_t[:, :], x0_t[:, 0:512]),
                       wait=(t1_, lx))
            xd = p.dma("gpsimd", x1_d[b, f * 128:(f + 1) * 128, sj:sj + 512], x1_t[:, :],
                       wait=(t2_,))
            prev = (xd, t2_)
            if sj == 512:
                hc = p.op("vector", lambda v, f=f, b=b: v.tensor_copy(
                    hpack[:, f * 6 + b * 3:f * 6 + b * 3 + 3], x1_t[:, 509:512]),
                    wait=(t2_,))
                hp_toks.append(hc)
                prev = prev + (hc,)
            fin.append(xd)
    htr = p.op("tensor", lambda t: t.transpose(ps_hp[0:48, 0:128], hpack[:, 0:48],
                                               hident[:, :]),
               wait=tuple(hp_toks) + (hi2,))
    hev = p.op("vector", lambda v: v.tensor_copy(hpT[:, :], ps_hp[0:48, 0:128]),
               wait=(htr,))
    hdma = p.dma("sync", halo_d[:, :], hpT[:, :], wait=(hev,))
    fin.append(hdma)
    p.op("sync", lambda s: s.nop(), wait=tuple(fin), inc=False)


def emit_l1pre(p, io):
    D, NF = 1024, 8
    x_d = io["x"]
    hag_d = io["halo_ag"]
    hmask_d = io["hmask"]
    win_d = io["W_in"]
    lnw_d = io["ln_w"]
    lnb_d = io["ln_b"]
    cw_d = io["conv_w"]
    cb_d = io["conv_b"]
    dtb_d = io["dt_bias"]
    alog_d = io["A_log"]
    z_d = io["z"]
    xhc_d = io["xh_c"]
    bm_d = io["Bm"]
    cm_d = io["Cm"]
    dt_d = io["dt1"]
    dta_d = io["dtA"]
    ab_d = io["ab"]

    x_bf = p.sbuf("x_bf", [128, NF * TT], BF16)
    w_sb = p.sbuf("w_sb", [128, NF * 2320], BF16)
    gam = p.sbuf("gam", [128, 2 * NF], F32)
    cw_sb = p.sbuf("cw_sb", [128, 18 * 4], F32)
    cb_sb = p.sbuf("cb_sb", [128, 18], F32)
    dtb_sb = p.sbuf("dtb_sb", [16, 1], F32)
    acol = p.sbuf("acol", [16, 1], F32)
    ones_bf = p.sbuf("ones_bf", [128, 1], BF16)
    ones_row = p.sbuf("ones_row", [1, 128], F32)
    eps_sb = p.sbuf("eps_sb", [1, 1], F32)
    one_col = p.sbuf("one_col", [128, 1], F32)
    rows = p.sbuf("rows", [1, 4 * TT], F32)
    ab_sb = p.sbuf("ab_sb", [128, 2 * TT], F32)
    h_bf = p.sbuf("h_bf", [128, NF * TT], BF16)
    tmp_f = p.sbuf("tmp_f", [128, TT], F32)
    xpre_st = p.sbuf("xpre_st", [128, 2 * TT], F32)
    conv_st = p.sbuf("conv_st", [128, 2 * LLOC], F32)
    xc_st = p.sbuf("xc_st", [128, B * LLOC], BF16)
    dt_st = p.sbuf("dt_st", [16, 2 * TT], F32)
    pool = PsumPool(p, "ps", n=4)
    ps_a = p.psum("ps_a", [128, 512], F32)
    ps_b = p.psum("ps_b", [128, 512], F32)

    hblk = p.sbuf("hblk", [48, 8 * 128], F32)
    hacc = p.sbuf("hacc", [48, 128], F32)
    h48 = p.sbuf("h48", [128, 48], F32)
    hmask_sb = p.sbuf("hmask_sb", [128, 8], F32)
    hident = p.sbuf("hident", [48, 48], F32)
    hid_sc = p.sbuf("hid_sc", [48, 48], F32)
    ps_h = p.psum("ps_h", [128, 48], F32)
    hld = [p.dma("sync", hblk[:, s * 128:(s + 1) * 128], hag_d[s * 48:(s + 1) * 48, :])
           for s in range(8)]
    hml = p.dma("sync", hmask_sb[:, :], hmask_d)
    ji1 = p.op("gpsimd", lambda g: g.iota(hid_sc[:, :], pattern=[[1, 48]], base=0,
                                          channel_multiplier=-1,
                                          allow_small_or_imprecise_dtypes=True))
    ji2 = p.op("vector", lambda v: v.tensor_single_scalar(hident[:, :], hid_sc[:, :], 0.0,
                                                          ALU.is_equal), wait=(ji1,))
    hto = p.op("vector", lambda v: v.tensor_scalar(
        out=hacc[:, :], in0=hblk[:, 0:128], scalar1=hmask_sb[0:48, 0:1], scalar2=None,
        op0=ALU.mult), wait=(hld[0], hml))
    for s in range(1, 8):
        hto = p.op("vector", lambda v, s=s: v.scalar_tensor_tensor(
            out=hacc[:, :], in0=hblk[:, s * 128:(s + 1) * 128],
            scalar=hmask_sb[0:48, s:s + 1], in1=hacc[:, :],
            op0=ALU.mult, op1=ALU.add), wait=(hld[s], hto))
    htr = p.op("tensor", lambda t: t.transpose(ps_h[:, 0:48], hacc[:, :], hident[:, :]),
               wait=(hto, ji2))
    hcp = p.op("scalar", lambda s: s.activation(out=h48[:, :], in_=ps_h[:, 0:48],
                                                func=AF.Copy), wait=(htr,))

    xld = []
    for f in range(NF):
        for b in range(B):
            xld.append(p.dma("gpsimd",
                             x_bf[:, f * TT + b * TH + HALO: f * TT + (b + 1) * TH],
                             x_d[b, f * 128:(f + 1) * 128, :]))
            xld.append(p.op("scalar", lambda s, f=f, b=b: s.activation(
                out=x_bf[:, f * TT + b * TH:f * TT + b * TH + HALO],
                in_=h48[:, f * 6 + b * 3:f * 6 + b * 3 + 3], func=AF.Copy),
                wait=(hcp,)))
    xld = tuple(xld)
    wld1 = tuple(p.dma("gpsimd", w_sb[:, f * 2048:(f + 1) * 2048],
                       win_d[f * 128:(f + 1) * 128, 0:2048]) for f in range(NF))
    cld = [load_col_vec(p, gam[:, 0:NF], lnw_d),
           p.dma("sync", gam[:, NF:2 * NF], lnb_d.rearrange("(m f) -> f m", f=128),
                 allow_slow_non_contiguous=True),
           p.dma("sync", cw_sb[:, :].rearrange("p (m j) -> p m j", j=4),
                 cw_d.rearrange("(m f) j -> f m j", f=128), allow_slow_non_contiguous=True),
           p.dma("sync", cb_sb[:, :], cb_d.rearrange("(m f) -> f m", f=128),
                 allow_slow_non_contiguous=True),
           p.dma("sync", dtb_sb[:, :], dtb_d[:, None]),
           p.dma("sync", acol[:, :], alog_d[:, None])]
    t_ones = p.op("vector", lambda v: v.memset(ones_bf[:, :], 1.0))
    t_ones2 = p.op("vector", lambda v: v.memset(ones_row[:, :], 1.0))
    t_eps = p.op("vector", lambda v: v.memset(eps_sb[:, :], 1e-5))
    t_one1 = p.op("vector", lambda v: v.memset(one_col[:, :], 1.0))
    ta1 = p.op("scalar", lambda s: s.activation(out=acol[:, :], in_=acol[:, :], func=AF.Exp),
               wait=(cld[5],))
    ta2 = p.op("vector", lambda v: v.tensor_single_scalar(acol[:, :], acol[:, :], -1.0,
                                                          ALU.mult), wait=(ta1,))

    sq_prev = {}
    ev1p = ev2p = None
    for ci, (t0, t1) in enumerate(chunks_of(TT)):
        cw_ = t1 - t0
        tok1 = tok2 = None
        for f in range(NF):
            w_sq = xld + ((sq_prev[f],) if f in sq_prev else ())
            sqt = p.op("scalar", lambda s, f=f, t0=t0, t1=t1: s.activation(
                out=h_bf[:, f * TT:f * TT + (t1 - t0)],
                in_=x_bf[:, f * TT + t0:f * TT + t1], func=AF.Square), wait=w_sq)
            w1 = xld + (t_ones,) + ((ev1p,) if (f == 0 and ev1p) else ())
            tok1 = p.op("tensor", lambda t, f=f, t0=t0, t1=t1, cw_=cw_: t.matmul(
                ps_a[0:1, 0:cw_], ones_bf[:, :], x_bf[:, f * TT + t0:f * TT + t1],
                start=(f == 0), stop=(f == NF - 1)), wait=w1, inc=True)
            w2 = (sqt,) + ((ev2p,) if (f == 0 and ev2p) else ())
            tok2 = p.op("tensor", lambda t, f=f, cw_=cw_: t.matmul(
                ps_b[0:1, 0:cw_], ones_bf[:, :], h_bf[:, f * TT:f * TT + cw_],
                start=(f == 0), stop=(f == NF - 1)), wait=w2, inc=True)
            sq_prev[f] = tok2
        ev1p = p.op("vector", lambda v, t0=t0, t1=t1: v.tensor_copy(
            rows[0:1, 0 * TT + t0:0 * TT + t1], ps_a[0:1, 0:t1 - t0]), wait=(tok1,))
        ev2p = p.op("vector", lambda v, t0=t0, t1=t1: v.tensor_copy(
            rows[0:1, 1 * TT + t0:1 * TT + t1], ps_b[0:1, 0:t1 - t0]), wait=(tok2,))

    S_ = lambda i: rows[0:1, i * TT:(i + 1) * TT]
    rtok = (ev1p, ev2p, t_eps)
    r1 = p.op("vector", lambda v: v.tensor_single_scalar(S_(2), S_(0), 1.0 / D, ALU.mult), wait=rtok)
    r2 = p.op("vector", lambda v: v.tensor_single_scalar(S_(3), S_(1), 1.0 / D, ALU.mult), wait=rtok)
    r3 = p.op("vector", lambda v: v.tensor_mul(S_(0), S_(2), S_(2)), wait=(r1,))
    r4 = p.op("vector", lambda v: v.tensor_sub(S_(1), S_(3), S_(0)), wait=(r2, r3))
    r5 = p.op("scalar", lambda s: s.activation(out=S_(0), in_=S_(1), func=AF.Sqrt,
                                               bias=eps_sb[0:1, 0:1]), wait=(r4,))
    r6 = p.op("vector", lambda v: v.reciprocal(S_(3), S_(0)), wait=(r5,))
    ab_dma = []
    for b in range(B):
        ab_dma.append(p.dma("sync", ab_d[0, b, :],
                            rows[0:1, 3 * TT + b * TH + HALO:3 * TT + (b + 1) * TH], wait=(r6,)))
    r7 = p.op("vector", lambda v: v.tensor_mul(S_(0), S_(0), S_(0)), wait=(r6,) + tuple(ab_dma))
    r8 = p.op("vector", lambda v: v.reciprocal(S_(0), S_(0)), wait=(r7,))
    r9 = p.op("vector", lambda v: v.tensor_mul(S_(0), S_(1), S_(0)), wait=(r8,))
    r10 = p.op("scalar", lambda s: s.activation(out=S_(0), in_=S_(0), func=AF.Sqrt,
                                                bias=eps_sb[0:1, 0:1]), wait=(r9,))
    r11 = p.op("vector", lambda v: v.reciprocal(S_(0), S_(0)), wait=(r10,))
    r12 = p.op("vector", lambda v: v.tensor_mul(S_(1), S_(2), S_(3)), wait=(r11,))
    r13 = p.op("vector", lambda v: v.tensor_single_scalar(S_(1), S_(1), -1.0, ALU.mult), wait=(r12,))
    for b in range(B):
        ab_dma.append(p.dma("sync", ab_d[1, b, :],
                            rows[0:1, 1 * TT + b * TH + HALO:1 * TT + (b + 1) * TH], wait=(r13,)))
    r14 = p.op("vector", lambda v: v.tensor_mul(S_(2), S_(3), S_(0)), wait=(r13,))
    r15 = p.op("vector", lambda v: v.tensor_mul(S_(3), S_(1), S_(0)),
               wait=(r14,) + tuple(ab_dma))

    bc_toks = []
    ea = eb = None
    for (t0, t1) in chunks_of(TT):
        tba = p.op("tensor", lambda t, t0=t0, t1=t1: t.matmul(
            ps_a[:, 0:t1 - t0], ones_row[:, :], rows[0:1, 2 * TT + t0:2 * TT + t1],
            start=True, stop=True), wait=(r14, ev1p, t_ones2) + ((ea,) if ea else ()))
        tbb = p.op("tensor", lambda t, t0=t0, t1=t1: t.matmul(
            ps_b[:, 0:t1 - t0], ones_row[:, :], rows[0:1, 3 * TT + t0:3 * TT + t1],
            start=True, stop=True), wait=(r15, ev2p) + ((eb,) if eb else ()))
        ea = p.op("vector", lambda v, t0=t0, t1=t1: v.tensor_copy(
            ab_sb[:, t0:t1], ps_a[:, 0:t1 - t0]), wait=(tba,))
        eb = p.op("vector", lambda v, t0=t0, t1=t1: v.tensor_copy(
            ab_sb[:, TT + t0:TT + t1], ps_b[:, 0:t1 - t0]), wait=(tbb,))
        bc_toks += [ea, eb]
    bc_toks = tuple(bc_toks)

    h_toks = []
    prev = None
    for f in range(NF):
        w = bc_toks + xld + (cld[0], cld[1]) + ((prev,) if prev else ())
        t1_ = p.op("vector", lambda v, f=f: v.scalar_tensor_tensor(
            out=tmp_f[:, :], in0=x_bf[:, f * TT:(f + 1) * TT], scalar=1.0,
            in1=ab_sb[:, 0:TT], op0=ALU.bypass, op1=ALU.mult), wait=w)
        t2_ = p.op("vector", lambda v: v.tensor_add(tmp_f[:, :], tmp_f[:, :],
                                                    ab_sb[:, TT:2 * TT]), wait=(t1_,))
        t3_ = p.op("vector", lambda v, f=f: v.tensor_scalar(
            out=h_bf[:, f * TT:(f + 1) * TT], in0=tmp_f[:, :],
            scalar1=gam[:, f:f + 1], scalar2=gam[:, NF + f:NF + f + 1],
            op0=ALU.mult, op1=ALU.add), wait=(t2_,))
        prev = t3_
        h_toks.append(t3_)
    h_toks = tuple(h_toks)

    z_dma = []
    war_res = {}
    for m in range(16):
        evs = []
        for (t0, t1) in chunks_of(TT):
            bank, ps, war = pool.get()
            tok = None
            for k in range(NF):
                wk = h_toks + wld1 + war + tuple(war_res.get(0, ())) if k == 0 else ()

                def fn(t, m=m, k=k, t0=t0, t1=t1, ps=ps):
                    return t.matmul(ps[:, 0:t1 - t0],
                                    w_sb[:, k * 2048 + m * 128:k * 2048 + (m + 1) * 128],
                                    h_bf[:, k * TT + t0:k * TT + t1],
                                    start=(k == 0), stop=(k == NF - 1))
                tok = p.op("tensor", fn, wait=wk, inc=(k == NF - 1))
            ev = p.op("scalar", lambda s, t0=t0, t1=t1, ps=ps: s.activation(
                out=xpre_st[:, t0:t1], in_=ps[:, 0:t1 - t0], func=AF.Copy), wait=(tok,))
            evs.append(ev)
            pool.consumed(bank, ev)
        drs = []
        for b in range(B):
            dr = p.dma("gpsimd", z_d[b, m * 128:(m + 1) * 128, :],
                       xpre_st[:, b * TH + HALO:(b + 1) * TH], wait=tuple(evs))
            drs.append(dr)
            z_dma.append(dr)
        war_res[0] = tuple(drs) + tuple(evs)

    w2wait = tuple(t for t in z_dma) + h_toks
    wld2 = tuple(p.dma("gpsimd", w_sb[:, f * 2320:(f + 1) * 2320],
                       win_d[f * 128:(f + 1) * 128, 2048:4368], wait=w2wait) for f in range(NF))
    fin_dma = list(z_dma) + ab_dma
    war_xp = {}
    war_xc = {}
    for m in range(18):
        ping = m % 2
        evs = []
        for (t0, t1) in chunks_of(TT):
            bank, ps, war = pool.get()
            tok = None
            for k in range(NF):
                wk = wld2 + war + tuple(war_xp.get(ping, ())) if k == 0 else ()

                def fn(t, m=m, k=k, t0=t0, t1=t1, ps=ps):
                    return t.matmul(ps[:, 0:t1 - t0],
                                    w_sb[:, k * 2320 + m * 128:k * 2320 + (m + 1) * 128],
                                    h_bf[:, k * TT + t0:k * TT + t1],
                                    start=(k == 0), stop=(k == NF - 1))
                tok = p.op("tensor", fn, wait=wk, inc=(k == NF - 1))
            ev = p.op("scalar", lambda s, t0=t0, t1=t1, ps=ps, ping=ping: s.activation(
                out=xpre_st[:, ping * TT + t0:ping * TT + t1], in_=ps[:, 0:t1 - t0],
                func=AF.Copy), wait=(tok,))
            evs.append(ev)
            pool.consumed(bank, ev)
        conv_toks = []
        for b in range(B):
            base = ping * TT + b * TH
            cbank = b
            w0 = tuple(evs) + (cld[2], cld[3]) + tuple(war_xc.get(0, ()))
            c1 = p.op("vector", lambda v, m=m, base=base, cbank=cbank: v.tensor_scalar(
                out=conv_st[:, cbank * LLOC:(cbank + 1) * LLOC],
                in0=xpre_st[:, base + 0:base + 0 + LLOC],
                scalar1=cw_sb[:, m * 4 + 0:m * 4 + 1], scalar2=cb_sb[:, m:m + 1],
                op0=ALU.mult, op1=ALU.add), wait=w0)
            prev_c = c1
            for j in range(1, 4):
                def fnc(v, m=m, j=j, base=base, cbank=cbank):
                    dst_is_tmp = (j % 2 == 1)
                    dst = tmp_f[:, cbank * LLOC:(cbank + 1) * LLOC] if dst_is_tmp \
                        else conv_st[:, cbank * LLOC:(cbank + 1) * LLOC]
                    srcb = conv_st[:, cbank * LLOC:(cbank + 1) * LLOC] if dst_is_tmp \
                        else tmp_f[:, cbank * LLOC:(cbank + 1) * LLOC]
                    return v.scalar_tensor_tensor(
                        out=dst, in0=xpre_st[:, base + j:base + j + LLOC],
                        scalar=cw_sb[:, m * 4 + j:m * 4 + j + 1], in1=srcb,
                        op0=ALU.mult, op1=ALU.add)
                prev_c = p.op("vector", fnc, wait=(prev_c,))
            conv_toks.append(prev_c)
            sil = p.op("scalar", lambda s, b=b, cbank=cbank: s.activation(
                out=xc_st[:, b * LLOC:(b + 1) * LLOC],
                in_=tmp_f[:, cbank * LLOC:(cbank + 1) * LLOC], func=AF.Silu),
                wait=(prev_c,) + tuple(war_xc.get(0, ())))
            if m < 16:
                dst = xhc_d[b, m * 128:(m + 1) * 128, :]
            elif m == 16:
                dst = bm_d[b, :, :]
            else:
                dst = cm_d[b, :, :]
            dd = p.dma("sync", dst, xc_st[:, b * LLOC:(b + 1) * LLOC], wait=(sil,))
            fin_dma.append(dd)
            war_xc[0] = (dd, sil)
        war_xp[ping] = tuple(conv_toks)

    dt_evs = []
    for (t0, t1) in chunks_of(TT):
        bank, ps, war = pool.get()
        tok = None
        for k in range(NF):
            wk = wld2 + war if k == 0 else ()

            def fn(t, k=k, t0=t0, t1=t1, ps=ps):
                return t.matmul(ps[0:16, 0:t1 - t0],
                                w_sb[:, k * 2320 + 2304:k * 2320 + 2320],
                                h_bf[:, k * TT + t0:k * TT + t1],
                                start=(k == 0), stop=(k == NF - 1))
            tok = p.op("tensor", fn, wait=wk, inc=(k == NF - 1))
        ev = p.op("vector", lambda v, t0=t0, t1=t1, ps=ps: v.tensor_scalar_add(
            dt_st[0:16, t0:t1], ps[0:16, 0:t1 - t0], dtb_sb[:, 0:1]),
            wait=(tok, cld[4]))
        dt_evs.append(ev)
        pool.consumed(bank, ev)
    dt_evs = tuple(dt_evs)
    s0a = p.op("vector", lambda v: v.tensor_single_scalar(
        dt_st[0:16, TT:2 * TT], dt_st[0:16, 0:TT], -1.0, ALU.mult), wait=dt_evs)
    s1 = p.op("vector", lambda v: v.tensor_max(
        dt_st[0:16, TT:2 * TT], dt_st[0:16, TT:2 * TT], dt_st[0:16, 0:TT]), wait=(s0a,))
    s2 = p.op("scalar", lambda s: s.activation(
        out=dt_st[0:16, TT:2 * TT], in_=dt_st[0:16, TT:2 * TT], func=AF.Exp,
        scale=-1.0), wait=(s1,))
    s3 = p.op("scalar", lambda s: s.activation(
        out=dt_st[0:16, TT:2 * TT], in_=dt_st[0:16, TT:2 * TT], func=AF.Ln,
        bias=one_col[0:16, 0:1]), wait=(s2, t_one1))
    s4 = p.op("vector", lambda v: v.tensor_relu(dt_st[0:16, 0:TT],
                                                dt_st[0:16, 0:TT]), wait=(s3,) + dt_evs)
    s5 = p.op("vector", lambda v: v.tensor_add(dt_st[0:16, 0:TT],
                                               dt_st[0:16, 0:TT],
                                               dt_st[0:16, TT:2 * TT]), wait=(s4,))
    dtd = []
    for b in range(B):
        dtd.append(p.dma("sync", dt_d[b, :, :],
                         dt_st[0:16, b * TH + HALO:(b + 1) * TH], wait=(s5,)))
    s6 = p.op("vector", lambda v: v.scalar_tensor_tensor(
        out=dt_st[0:16, TT:2 * TT], in0=dt_st[0:16, 0:TT], scalar=acol[:, 0:1],
        in1=dt_st[0:16, 0:TT], op0=ALU.mult, op1=ALU.bypass), wait=(s5, ta2))
    fin_dma += dtd
    for b in range(B):
        fin_dma.append(p.dma("sync", dta_d[b, :, :],
                             dt_st[0:16, TT + b * TH + HALO:TT + (b + 1) * TH],
                             wait=(s6,)))
    p.op("sync", lambda s: s.nop(), wait=tuple(fin_dma), inc=False)
    return p


def emit_l1ssd(p, io):
    S = 8192
    NCH, SLAB = 64, 16
    x_d = io["X"]
    bm_d = io["BuB"]
    cm_d = io["CuB"]
    bsel_d = io["bsel"]
    dt_d = io["dt"]
    dta_d = io["dtA"]
    du_d = io["Du"]
    y_d = io["y"]
    scr = p.dram_tmp("scr", [2, 2048])

    X_sb = p.sbuf("X_sb", [128, S], BF16)
    B_sb = p.sbuf("B_sb", [128, S], BF16)
    C_sb = p.sbuf("C_sb", [128, S], BF16)
    rows = p.sbuf("rows", [1, 4 * 2048], F32)
    r16 = p.sbuf("r16", [1, 64], F32)
    zrow = p.sbuf("zrow", [1, 2048], F32)
    scol = p.sbuf("scol", [128, 16], F32)
    dtcol = p.sbuf("dtcol", [128, 16], F32)
    ecol = p.sbuf("ecol", [128, 16], F32)
    eqc = p.sbuf("eqc", [128, 16], F32)
    sqb = p.sbuf("sqb", [128, 16], F32)
    qcol = p.sbuf("qcol", [128, 16], F32)
    dcol = p.sbuf("dcol", [128, 4], F32)
    ident = p.sbuf("ident", [128, 128], BF16)
    mask01 = p.sbuf("mask01", [128, 128], F32)
    ones_row = p.sbuf("ones_row", [1, 128], F32)
    io_sc = p.sbuf("io_sc", [128, 128], F32)
    xt_sb = p.sbuf("xt_sb", [128, 128], BF16)
    xq_sb = p.sbuf("xq_sb", [128, 128], BF16)
    bt_sb = p.sbuf("bt_sb", [128, 128], BF16)
    dex_sb = p.sbuf("dex_sb", [128, 128], F32)
    dm_sb = p.sbuf("dm_sb", [128, 128], F32)
    m_sb = p.sbuf("m_sb", [128, 128], BF16)
    ysb = p.sbuf("ysb", [128, 128], BF16)
    y1s = p.sbuf("y1s", [128, 128], F32)
    H_sb = p.sbuf("H_sb", [128, 128], F32)
    H_bf = p.sbuf("H_bf", [128, 128], BF16)
    yst = p.sbuf("yst", [128, 2048], F32)
    ps_a1 = p.psum("ps_a1", [128, 512], F32)
    ps_gt = p.psum("ps_gt", [128, 512], F32)
    ps_y1 = p.psum("ps_y1", [128, 512], F32)
    ps_y2 = p.psum("ps_y2", [128, 512], F32)
    ps_cb = p.psum("ps_cb", [128, 512], F32)
    ps_t1 = p.psum("ps_t1", [128, 1024], BF16)
    ps_t2 = p.psum("ps_t2", [128, 1024], BF16)

    t_ones = p.op("vector", lambda v: v.memset(ones_row[:, :], 1.0))
    t_z = p.op("vector", lambda v: v.memset(zrow[:, :], 0.0))
    i1 = p.op("gpsimd", lambda g: g.iota(io_sc[:, :], pattern=[[1, 128]], base=0,
                                         channel_multiplier=0,
                                         allow_small_or_imprecise_dtypes=True))
    i2 = p.op("gpsimd", lambda g: g.iota(mask01[:, :], pattern=[[0, 128]], base=0,
                                         channel_multiplier=1,
                                         allow_small_or_imprecise_dtypes=True))
    i3 = p.op("vector", lambda v: v.tensor_tensor(ident[:, :], io_sc[:, :], mask01[:, :],
                                                  ALU.is_equal), wait=(i1, i2))
    i4 = p.op("gpsimd", lambda g: g.memset(mask01[:, :], 1.0), wait=(i3,))
    i5 = p.op("gpsimd", lambda g: g.affine_select(mask01[:, :], mask01[:, :],
                                                  pattern=[[1, 128]], compare_op=ALU.is_ge,
                                                  fill=0.0, base=0, channel_multiplier=-1),
              wait=(i4,))
    ldd = p.dma("sync", dcol[0:1, :], du_d)
    dbc = p.op("tensor", lambda t: t.matmul(ps_cb[0:128, 0:4], ones_row[:, :],
                                            dcol[0:1, :], start=True, stop=True),
               wait=(ldd, t_ones))
    dbe = p.op("vector", lambda v: v.tensor_copy(dcol[:, :], ps_cb[:, 0:4]), wait=(dbc,))

    btmp = p.sbuf("btmp", [128, S], BF16)
    bsel_sb = p.sbuf("bsel_sb", [128, 2], F32)
    bsl = p.dma("sync", bsel_sb[:, :], bsel_d)
    lb0 = p.dma("gpsimd", B_sb[:, :], bm_d[0])
    lb1 = p.dma("gpsimd", btmp[:, :], bm_d[1])
    sB1 = p.op("vector", lambda v: v.tensor_scalar(
        out=B_sb[:, :], in0=B_sb[:, :], scalar1=bsel_sb[:, 0:1], scalar2=None,
        op0=ALU.mult), wait=(lb0, bsl))
    sB2 = p.op("vector", lambda v: v.scalar_tensor_tensor(
        out=B_sb[:, :], in0=btmp[:, :], scalar=bsel_sb[:, 1:2], in1=B_sb[:, :],
        op0=ALU.mult, op1=ALU.add), wait=(lb1, sB1))
    lcb0 = p.dma("gpsimd", C_sb[:, :], cm_d[0])
    lcb1 = p.dma("gpsimd", btmp[:, :], cm_d[1], wait=(sB2,))
    sC1 = p.op("vector", lambda v: v.tensor_scalar(
        out=C_sb[:, :], in0=C_sb[:, :], scalar1=bsel_sb[:, 0:1], scalar2=None,
        op0=ALU.mult), wait=(lcb0, bsl))
    sC2 = p.op("vector", lambda v: v.scalar_tensor_tensor(
        out=C_sb[:, :], in0=btmp[:, :], scalar=bsel_sb[:, 1:2], in1=C_sb[:, :],
        op0=ALU.mult, op1=ALU.add), wait=(lcb1, sC1))

    prev_unit = ()
    for u in range(4):
        lX = p.dma("gpsimd", X_sb[:, :], x_d[u], wait=prev_unit)
        hz = p.op("vector", lambda v: v.memset(H_sb[:, :], 0.0), wait=prev_unit)
        hzb = p.op("vector", lambda v: v.memset(H_bf[:, :], 0.0), wait=prev_unit)
        unit_toks = [hz, hzb]
        ldw = (lX, sB2, sC2)
        prev_slab = (hz, hzb)
        h_toks = (hz, hzb)
        for sb_i in range(4):
            s0 = sb_i * 2048
            w0 = prev_slab
            lr1 = p.dma("sync", rows[0:1, 0:2048], dta_d[u, s0:s0 + 2048], wait=w0)
            lr2 = p.dma("sync", rows[0:1, 2 * 2048:3 * 2048], dt_d[u, s0:s0 + 2048], wait=w0)
            sg = p.op("vector", lambda v: v.tensor_tensor_scan(
                rows[0:1, 2048:2 * 2048], rows[0:1, 0:2048], zrow[0:1, :], 0.0,
                ALU.add, ALU.add), wait=(lr1, t_z) + w0)
            of1 = p.op("vector", lambda v: v.memset(r16[0:1, 0:1], 0.0), wait=w0)
            of2 = p.op("vector", lambda v: v.tensor_copy(
                r16[0:1, 1:16],
                rows[0:1, 2048:2 * 2048].rearrange("a (c t) -> a c t", t=128)[:, 0:15, 127]),
                wait=(sg, of1))
            sl = p.op("vector", lambda v: v.tensor_sub(
                rows[0:1, 2048:2 * 2048].rearrange("a (c t) -> a c t", t=128),
                rows[0:1, 2048:2 * 2048].rearrange("a (c t) -> a c t", t=128),
                r16[0:1, 0:16][:, :, None].broadcast_to([1, 16, 128])), wait=(of2,))
            sq_ = p.op("vector", lambda v: v.tensor_copy(
                r16[0:1, 16:32],
                rows[0:1, 2048:2 * 2048].rearrange("a (c t) -> a c t", t=128)[:, :, 127]),
                wait=(sl,))
            eq_ = p.op("scalar", lambda s: s.activation(out=r16[0:1, 32:48],
                                                        in_=r16[0:1, 16:32], func=AF.Exp),
                       wait=(sq_,))
            bo1 = p.dma("sync", scr[0, :], rows[0:1, 2048:2 * 2048], wait=(sl,))
            bo2 = p.dma("sync", scr[1, :], rows[0:1, 2 * 2048:3 * 2048], wait=(lr2,))
            lc1 = p.dma("sync", scol[:, :], scr[0].rearrange("(c t) -> t c", t=128),
                        wait=(bo1,), allow_slow_non_contiguous=True)
            lc2 = p.dma("sync", dtcol[:, :], scr[1].rearrange("(c t) -> t c", t=128),
                        wait=(bo2,), allow_slow_non_contiguous=True)
            ec = p.op("scalar", lambda s: s.activation(out=ecol[:, :], in_=scol[:, :],
                                                       func=AF.Exp), wait=(lc1,) + prev_slab)
            tb1 = p.op("tensor", lambda t: t.matmul(ps_cb[:, 0:16], ones_row[:, :],
                                                    r16[0:1, 32:48], start=True, stop=True),
                       wait=(eq_, dbe) + prev_slab)
            eb1 = p.op("vector", lambda v: v.tensor_copy(eqc[:, :], ps_cb[:, 0:16]),
                       wait=(tb1,))
            tb2 = p.op("tensor", lambda t: t.matmul(ps_cb[:, 16:32], ones_row[:, :],
                                                    r16[0:1, 16:32], start=True, stop=True),
                       wait=(sq_, eb1))
            eb2 = p.op("vector", lambda v: v.tensor_copy(sqb[:, :], ps_cb[:, 16:32]),
                       wait=(tb2,))
            q1 = p.op("vector", lambda v: v.tensor_sub(qcol[:, :], sqb[:, :], scol[:, :]),
                      wait=(eb2, lc1) + prev_slab)
            q2 = p.op("scalar", lambda s: s.activation(out=qcol[:, :], in_=qcol[:, :],
                                                       func=AF.Exp), wait=(q1,))
            q3 = p.op("vector", lambda v: v.tensor_mul(qcol[:, :], qcol[:, :], dtcol[:, :]),
                      wait=(q2, lc2))
            slab_toks = (ec, q3, eb1)
            yst_w = []
            for ci in range(SLAB):
                c = s0 + ci * 128
                csl = slice(c, c + 128)
                tx = p.op("tensor", lambda t, csl=csl: t.transpose(
                    ps_t1[:, 0:128], X_sb[:, csl], ident[:, :]),
                    wait=(ldw[0], i3) + h_toks + tuple(yst_w[-1:]))
                tb_ = p.op("tensor", lambda t, csl=csl: t.transpose(
                    ps_t2[:, 0:128], B_sb[:, csl], ident[:, :]), wait=(ldw[1], i3))
                ex1 = p.op("vector", lambda v, ci=ci: v.tensor_scalar(
                    out=xt_sb[:, :], in0=ps_t1[:, 0:128], scalar1=dtcol[:, ci:ci + 1],
                    scalar2=None, op0=ALU.mult), wait=(tx, lc2) + slab_toks)
                ex2 = p.op("vector", lambda v, ci=ci: v.tensor_scalar(
                    out=xq_sb[:, :], in0=ps_t1[:, 0:128], scalar1=qcol[:, ci:ci + 1],
                    scalar2=None, op0=ALU.mult), wait=(tx, q3))
                ex3 = p.op("vector", lambda v: v.tensor_copy(bt_sb[:, :], ps_t2[:, 0:128]),
                           wait=(tb_,))
                g1 = p.op("tensor", lambda t, csl=csl: t.matmul(
                    ps_gt[:, 0:128], B_sb[:, csl], C_sb[:, csl], start=True, stop=True),
                    wait=(ldw[1], ldw[2], ex3))
                a1 = p.op("tensor", lambda t, ci=ci: t.matmul(
                    ps_a1[:, 0:128], ones_row[:, :],
                    rows[0:1, 2048 + ci * 128:2048 + (ci + 1) * 128],
                    start=True, stop=True), wait=(sl, t_ones, ex1, ex2))
                d1 = p.op("vector", lambda v, ci=ci: v.tensor_scalar(
                    out=dex_sb[:, :], in0=ps_a1[:, 0:128], scalar1=scol[:, ci:ci + 1],
                    scalar2=0.0, op0=ALU.subtract, op1=ALU.min), wait=(a1, lc1))
                d2 = p.op("scalar", lambda s: s.activation(out=dex_sb[:, :], in_=dex_sb[:, :],
                                                           func=AF.Exp), wait=(d1,))
                d3 = p.op("vector", lambda v: v.tensor_mul(dm_sb[:, :], dex_sb[:, :],
                                                           mask01[:, :]), wait=(d2, i5))
                mm_ = p.op("vector", lambda v: v.tensor_mul(m_sb[:, :], ps_gt[:, 0:128],
                                                            dm_sb[:, :]), wait=(g1, d3))
                y1 = p.op("tensor", lambda t: t.matmul(ps_y1[:, 0:128], m_sb[:, :],
                                                       xt_sb[:, :], start=True, stop=True),
                          wait=(mm_, ex1))
                y2 = p.op("tensor", lambda t, csl=csl: t.matmul(
                    ps_y2[:, 0:128], C_sb[:, csl], H_bf[:, :], start=True, stop=True),
                    wait=(ldw[2],) + h_toks)
                cb_ = p.op("tensor", lambda t: t.matmul(ps_cb[:, 128:256], bt_sb[:, :],
                                                        xq_sb[:, :], start=True, stop=True),
                           wait=(ex3, ex2, eb2))
                hupd = p.op("vector", lambda v, ci=ci: v.scalar_tensor_tensor(
                    out=H_sb[:, :], in0=H_sb[:, :], scalar=eqc[:, ci:ci + 1],
                    in1=ps_cb[:, 128:256], op0=ALU.mult, op1=ALU.add),
                    wait=(cb_, y2, eb1))
                hcp = p.op("scalar", lambda s: s.activation(out=H_bf[:, :], in_=H_sb[:, :],
                                                            func=AF.Copy), wait=(hupd,))
                h_toks = (hupd, hcp)
                ye0 = p.op("vector", lambda v: v.tensor_copy(y1s[:, :], ps_y1[:, 0:128]),
                           wait=(y1,))
                ye = p.op("vector", lambda v, ci=ci: v.scalar_tensor_tensor(
                    out=ysb[:, :], in0=ps_y2[:, 0:128], scalar=ecol[:, ci:ci + 1],
                    in1=y1s[:, :], op0=ALU.mult, op1=ALU.add), wait=(ye0, y2, ec))
                ty = p.op("tensor", lambda t: t.transpose(ps_t1[:, 128:256], ysb[:, :],
                                                          ident[:, :]), wait=(ye,))
                last_ty = ty
                yo = p.op("vector", lambda v, ci=ci, csl=csl, u=u: v.scalar_tensor_tensor(
                    out=yst[:, ci * 128:(ci + 1) * 128], in0=X_sb[:, csl],
                    scalar=dcol[:, u:u + 1], in1=ps_t1[:, 128:256],
                    op0=ALU.mult, op1=ALU.add), wait=(ty, dbe) + tuple(yst_w[-1:]))
                yst_w.append(yo)
            yd = p.dma("gpsimd", y_d[u, :, s0:s0 + 2048], yst[:, :], wait=tuple(yst_w))
            prev_slab = (yd, last_ty, lc1, lc2) + tuple(yst_w[-1:]) + (sl, sq_, of2)
            unit_toks.append(yd)
        prev_unit = tuple(unit_toks) + h_toks + prev_slab
    p.op("sync", lambda s: s.nop(), wait=tuple(unit_toks) + prev_unit, inc=False)


def emit_l1post(p, io):
    S = LLOC
    T = B * S
    x_d = io["x"]
    ab_d = io["ab"]
    y_dd = io["y"]
    z_d = io["z"]
    rw_d = io["rms_w"]
    wout_d = io["W_out"]
    s1_d = io["s1"]

    wo_sb = p.sbuf("wo_sb", [128, 16 * 1024], BF16)
    rw_sb = p.sbuf("rw_sb", [128, 16], F32)
    ab_sb = p.sbuf("ab_sb", [128, 2 * T], F32)
    rows = p.sbuf("rows", [1, 2 * T], F32)
    ones_row = p.sbuf("ones_row", [1, 128], F32)
    ones_bf = p.sbuf("ones_bf", [128, 1], BF16)
    eps_sb = p.sbuf("eps_sb", [1, 1], F32)
    y_t = p.sbuf("y_t", [128, 2 * 512], BF16)
    z_t = p.sbuf("z_t", [128, 2 * 512], BF16)
    sz_f = p.sbuf("sz_f", [128, 512], F32)
    yg_sb = p.sbuf("yg_sb", [128, 16 * 512], BF16)
    sq_t = p.sbuf("sq_t", [128, 512], BF16)
    rb_sb = p.sbuf("rb_sb", [128, 512], F32)
    u_bf = p.sbuf("u_bf", [128, 2 * 512], BF16)
    x0_t = p.sbuf("x0_t", [128, 1024], F32)
    s0_t = p.sbuf("s0_t", [128, 512], F32)
    out_ps = [p.psum(f"ops{i}", [128, 512], F32) for i in range(8)]

    wld = [p.dma("gpsimd", wo_sb[:, c * 1024:(c + 1) * 1024],
                 wout_d[c * 128:(c + 1) * 128, :]) for c in range(16)]
    wld.append(load_col_vec(p, rw_sb, rw_d))
    rld = [p.dma("sync", rows[0:1, b * S:(b + 1) * S], ab_d[0, b, :]) for b in range(B)]
    rld += [p.dma("sync", rows[0:1, T + b * S:T + (b + 1) * S], ab_d[1, b, :]) for b in range(B)]
    t_ones = p.op("vector", lambda v: v.memset(ones_row[:, :], 1.0))
    t_ob = p.op("vector", lambda v: v.memset(ones_bf[:, :], 1.0))
    t_eps = p.op("vector", lambda v: v.memset(eps_sb[:, :], 1e-5))

    bc_toks = []
    ea = eb = None
    for (t0, t1) in chunks_of(T):
        ta = p.op("tensor", lambda t, t0=t0, t1=t1: t.matmul(
            out_ps[0][:, 0:t1 - t0], ones_row[:, :], rows[0:1, t0:t1],
            start=True, stop=True), wait=tuple(rld) + (t_ones,) + ((ea,) if ea else ()))
        tb = p.op("tensor", lambda t, t0=t0, t1=t1: t.matmul(
            out_ps[1][:, 0:t1 - t0], ones_row[:, :], rows[0:1, T + t0:T + t1],
            start=True, stop=True), wait=(t_ones,) + ((eb,) if eb else ()))
        ea = p.op("vector", lambda v, t0=t0, t1=t1: v.tensor_copy(
            ab_sb[:, t0:t1], out_ps[0][:, 0:t1 - t0]), wait=(ta,))
        eb = p.op("vector", lambda v, t0=t0, t1=t1: v.tensor_copy(
            ab_sb[:, T + t0:T + t1], out_ps[1][:, 0:t1 - t0]), wait=(tb,))
        bc_toks += [ea, eb]
    bc_toks = tuple(bc_toks)

    s1_dma = []
    war_out = {}
    prev_j = ()
    for j in range(4):
        b, sj = divmod(j * 512, S)
        row_ev = None
        sq_prev = ()
        ygt = {}
        for ct in range(16):
            ping = ct % 2
            l1 = p.dma("gpsimd", y_t[:, ping * 512:(ping + 1) * 512],
                       y_dd[b, ct * 128:(ct + 1) * 128, sj:sj + 512],
                       wait=tuple(ygt.get(ping, ())))
            l2 = p.dma("scalar", z_t[:, ping * 512:(ping + 1) * 512],
                       z_d[b, ct * 128:(ct + 1) * 128, sj:sj + 512],
                       wait=tuple(ygt.get(ping, ())))
            sz = p.op("scalar", lambda s, ping=ping: s.activation(
                out=sz_f[:, :], in_=z_t[:, ping * 512:(ping + 1) * 512], func=AF.Silu),
                wait=(l2,) + tuple(ygt.get(ping, ())) + prev_j)
            yg = p.op("vector", lambda v, ct=ct, ping=ping: v.tensor_mul(
                yg_sb[:, ct * 512:(ct + 1) * 512], y_t[:, ping * 512:(ping + 1) * 512],
                sz_f[:, :]), wait=(sz, l1) + prev_j)
            sq = p.op("scalar", lambda s, ct=ct: s.activation(
                out=sq_t[:, :], in_=yg_sb[:, ct * 512:(ct + 1) * 512], func=AF.Square),
                wait=(yg,) + sq_prev)
            mm = p.op("tensor", lambda t, ct=ct: t.matmul(
                out_ps[2][0:1, :], ones_bf[:, :], sq_t[:, :],
                start=(ct == 0), stop=(ct == 15)),
                wait=(sq, t_ob) + (bc_toks if j == 0 and ct == 0 else ()) +
                     (tuple(war_out.get(2, ())) if ct == 0 else ()), inc=True)
            sq_prev = (mm,)
            ygt[ping] = (yg, sz)
        row_ev = p.op("vector", lambda v: v.tensor_copy(rows[0:1, 0:512], out_ps[2][0:1, :]),
                      wait=(mm,))
        r1 = p.op("vector", lambda v: v.tensor_single_scalar(
            rows[0:1, 0:512], rows[0:1, 0:512], 1.0 / 2048, ALU.mult), wait=(row_ev,))
        r2 = p.op("scalar", lambda s: s.activation(out=rows[0:1, 0:512], in_=rows[0:1, 0:512],
                                                   func=AF.Sqrt, bias=eps_sb[0:1, 0:1]),
                  wait=(r1, t_eps))
        r3 = p.op("vector", lambda v: v.reciprocal(rows[0:1, 0:512], rows[0:1, 0:512]),
                  wait=(r2,))
        tb_ = p.op("tensor", lambda t: t.matmul(out_ps[2][:, :], ones_row[:, :],
                                                rows[0:1, 0:512], start=True, stop=True),
                   wait=(r3,))
        rbe = p.op("vector", lambda v: v.tensor_copy(rb_sb[:, :], out_ps[2][:, :]),
                   wait=(tb_,))
        mm_last = [None] * 8
        war_u = {}
        for ct in range(16):
            ping = ct % 2
            tu = p.op("vector", lambda v, ct=ct, ping=ping: v.scalar_tensor_tensor(
                out=u_bf[:, ping * 512:(ping + 1) * 512],
                in0=yg_sb[:, ct * 512:(ct + 1) * 512], scalar=rw_sb[:, ct:ct + 1],
                in1=rb_sb[:, :], op0=ALU.mult, op1=ALU.mult),
                wait=(rbe, wld[-1]) + tuple(war_u.get(ping, ())))
            mts = []
            for mt in range(8):
                def fm_(t, ct=ct, mt=mt, ping=ping):
                    return t.matmul(out_ps[mt][:, :],
                                    wo_sb[:, ct * 1024 + mt * 128:ct * 1024 + (mt + 1) * 128],
                                    u_bf[:, ping * 512:(ping + 1) * 512],
                                    start=(ct == 0), stop=(ct == 15), skip_group_check=True)
                w_mm = (tu,)
                if ct == 0:
                    w_mm = w_mm + tuple(wld) + tuple(war_out.get(mt, ()))
                mts.append(p.op("tensor", fm_, wait=w_mm, inc=(ct == 15 or mt == 7)))
            war_u[ping] = (mts[7],)
            if ct == 15:
                mm_last = mts
        prev_s0 = prev_j
        for f in range(8):
            lx = p.dma("sync", x0_t[:, 0:512], x_d[b, f * 128:(f + 1) * 128, sj:sj + 512],
                       wait=prev_s0)
            e1 = p.op("vector", lambda v, f=f, j=j: v.scalar_tensor_tensor(
                out=s0_t[:, :], in0=x0_t[:, 0:512], scalar=1.0,
                in1=ab_sb[:, j * 512:(j + 1) * 512], op0=ALU.bypass, op1=ALU.mult),
                wait=(lx, mm_last[f]) + prev_s0)
            e2 = p.op("vector", lambda v, j=j: v.tensor_add(
                s0_t[:, :], s0_t[:, :], ab_sb[:, T + j * 512:T + (j + 1) * 512]), wait=(e1,))
            e3 = p.op("vector", lambda v, f=f: v.tensor_add(
                s0_t[:, :], s0_t[:, :], out_ps[f][:, :]), wait=(e2,))
            sd = p.dma("gpsimd", s1_d[b, f * 128:(f + 1) * 128, sj:sj + 512], s0_t[:, :],
                       wait=(e3,))
            prev_s0 = (sd, e3)
            war_out[f] = (e3,)
            s1_dma.append(sd)
        war_out[2] = war_out.get(2, ()) + (rbe,)
        prev_j = prev_s0
    p.op("sync", lambda s: s.nop(), wait=tuple(s1_dma), inc=False)


def emit_l1ffn(p, io):
    S = LLOC
    T = B * S
    x_d = io["x"]
    ab_d = io["ab"]
    s1_d = io["s1"]
    rw_d = io["router_w"]
    rb_d = io["router_b"]
    ew_d = io["expert_w"]
    eb_d = io["expert_b"]
    x2_d = io["x2"]

    xn_bf = p.sbuf("xn_bf", [128, 8 * T], BF16)
    xe_bf = p.sbuf("xe_bf", [128, 8 * T], BF16)
    ew_sb = p.sbuf("ew_sb", [128, 8 * 1024], BF16)
    ebias = p.sbuf("ebias", [8, 1024], BF16)
    y_acc = p.sbuf("y_acc", [128, 8 * T], BF16)
    ab_sb = p.sbuf("ab_sb", [128, 2 * T], F32)
    rows = p.sbuf("rows", [1, 2 * T], F32)
    ones_row = p.sbuf("ones_row", [1, 128], F32)
    rw_sb = p.sbuf("rw_sb", [128, 8 * 8], BF16)
    rbias = p.sbuf("rbias", [1, 8], F32)
    lg_sb = p.sbuf("lg_sb", [8, T], F32)
    lt_sb = p.sbuf("lt_sb", [128, 8], F32)
    lcol = p.sbuf("lcol", [128, 4], F32)
    crow = p.sbuf("crow", [8, T], F32)
    crow_bf = p.sbuf("crow_bf", [8, T], BF16)
    sel_sb = p.sbuf("sel_sb", [8, 8 * 128], BF16)
    io_sc = p.sbuf("io_sc", [8, 8 * 128], F32)
    cb_ps = p.sbuf("cb_ps", [128, 512], BF16)
    identf = p.sbuf("identf", [128, 128], F32)
    mk_sb = p.sbuf("mk_sb", [128, 8], F32)
    id_sc = p.sbuf("id_sc", [128, 128], F32)
    x0_t = p.sbuf("x0_t", [128, 1024], F32)
    x1_t = p.sbuf("x1_t", [128, 512], F32)
    s0_t = p.sbuf("s0_t", [128, 512], BF16)
    g_t = p.sbuf("g_t", [128, 512], BF16)
    pool = PsumPool(p, "ps", n=4)
    ps_r = p.psum("ps_r", [128, 512], F32)
    ps_cbp = p.psum("ps_cbp", [128, 512], F32)
    ps_tf = p.psum("ps_tf", [128, 512], F32)

    t_ones = p.op("vector", lambda v: v.memset(ones_row[:, :], 1.0))
    j1 = p.op("gpsimd", lambda g: g.iota(id_sc[:, :], pattern=[[1, 128]], base=0,
                                         channel_multiplier=-1,
                                         allow_small_or_imprecise_dtypes=True))
    j3 = p.op("vector", lambda v: v.tensor_single_scalar(identf[:, :], id_sc[:, :], 0.0,
                                                         ALU.is_equal), wait=(j1,))

    k1 = p.op("gpsimd", lambda g: g.iota(io_sc[:, :].rearrange("p (e q) -> p e q", e=8),
                                         pattern=[[1, 8], [0, 128]], base=0,
                                         channel_multiplier=-1,
                                         allow_small_or_imprecise_dtypes=True))
    k2 = p.op("vector", lambda v: v.tensor_single_scalar(sel_sb[:, :], io_sc[:, :], 0.0,
                                                         ALU.is_equal), wait=(k1,))

    wldc = [p.dma("gpsimd", rw_sb[:, :].rearrange("f (k e) -> f k e", e=8)[:, :, e],
                  rw_d.rearrange("(k f) e -> f k e", f=128)[:, :, e],
                  allow_slow_non_contiguous=True) for e in range(8)]
    wldc.append(p.dma("sync", rbias[:, :], rb_d[None, :]))
    wldc.append(p.dma("gpsimd", ebias[:, :], eb_d))
    rlda = [p.dma("sync", rows[0:1, b * S:(b + 1) * S], ab_d[0, b, :]) for b in range(B)]
    rlda += [p.dma("sync", rows[0:1, T + b * S:T + (b + 1) * S], ab_d[1, b, :])
             for b in range(B)]

    bc_toks = []
    ea = eb = None
    for (t0, t1) in chunks_of(T):
        ta = p.op("tensor", lambda t, t0=t0, t1=t1: t.matmul(
            ps_r[:, 0:t1 - t0], ones_row[:, :], rows[0:1, t0:t1],
            start=True, stop=True), wait=tuple(rlda) + (t_ones,) + ((ea,) if ea else ()))
        ea = p.op("vector", lambda v, t0=t0, t1=t1: v.tensor_copy(
            ab_sb[:, t0:t1], ps_r[:, 0:t1 - t0]), wait=(ta,))
        bc_toks.append(ea)
    for (t0, t1) in chunks_of(T):
        tb = p.op("tensor", lambda t, t0=t0, t1=t1: t.matmul(
            ps_r[:, 0:t1 - t0], ones_row[:, :], rows[0:1, T + t0:T + t1],
            start=True, stop=True), wait=(bc_toks[-1],) + ((eb,) if eb else ()))
        eb = p.op("vector", lambda v, t0=t0, t1=t1: v.tensor_copy(
            ab_sb[:, T + t0:T + t1], ps_r[:, 0:t1 - t0]), wait=(tb,))
        bc_toks.append(eb)
    bc_toks = tuple(bc_toks)

    xn_toks = []
    xn_rd = {0: (), 1: ()}
    xi = 0
    for f in range(8):
        for b in range(B):
            for (c0, c1_) in chunks_of(S):
                pp = xi % 2
                xi += 1
                lx = p.dma("sync", x0_t[:, pp * 512:pp * 512 + (c1_ - c0)],
                           x_d[b, f * 128:(f + 1) * 128, c0:c1_], wait=xn_rd[pp])
                tok = p.op("vector", lambda v, f=f, b=b, c0=c0, c1_=c1_, pp=pp: v.scalar_tensor_tensor(
                    out=xn_bf[:, f * T + b * S + c0:f * T + b * S + c1_],
                    in0=x0_t[:, pp * 512:pp * 512 + (c1_ - c0)], scalar=1.0,
                    in1=ab_sb[:, b * S + c0:b * S + c1_], op0=ALU.bypass, op1=ALU.mult),
                    wait=(lx,) + bc_toks)
                xn_rd[pp] = (tok,)
                tok2 = p.op("vector", lambda v, f=f, b=b, c0=c0, c1_=c1_: v.tensor_add(
                    xn_bf[:, f * T + b * S + c0:f * T + b * S + c1_],
                    xn_bf[:, f * T + b * S + c0:f * T + b * S + c1_],
                    ab_sb[:, T + b * S + c0:T + b * S + c1_]), wait=(tok,))
                xn_toks.append(tok2)
    xn_toks = tuple(xn_toks)

    lg_toks = []
    le_prev = None
    for (t0, t1) in chunks_of(T):
        tok = None
        for k in range(8):
            def fg(t, k=k, t0=t0, t1=t1):
                return t.matmul(ps_r[0:8, 0:t1 - t0], rw_sb[:, k * 8:(k + 1) * 8],
                                xn_bf[:, k * T + t0:k * T + t1],
                                start=(k == 0), stop=(k == 7))
            tok = p.op("tensor", fg, wait=(xn_toks + tuple(wldc) +
                                           ((le_prev,) if le_prev else ())) if k == 0 else (),
                       inc=(k == 7))
        le_prev = p.op("vector", lambda v, t0=t0, t1=t1: v.tensor_copy(
            lg_sb[0:8, t0:t1], ps_r[0:8, 0:t1 - t0]), wait=(tok,))
        lg_toks.append(le_prev)
    lg_toks = tuple(lg_toks)

    tr_rb = p.op("tensor", lambda t: t.matmul(ps_r[:, 0:8], ones_row[:, :], rbias[0:1, 0:8],
                                              start=True, stop=True),
                 wait=(wldc[8], t_ones, bc_toks[-1]) + lg_toks)
    rb_t = p.sbuf("rb_t", [128, 8], F32)
    ev_rb = p.op("vector", lambda v: v.tensor_copy(rb_t[:, :], ps_r[:, 0:8]), wait=(tr_rb,))

    crow_toks = []
    prev_tt = ()
    for tt in range(16):
        tsl = slice(tt * 128, (tt + 1) * 128)
        tr1 = p.op("tensor", lambda t, tsl=tsl: t.transpose(
            ps_tf[:, 0:8], lg_sb[0:8, tsl], identf[0:8, 0:8]), wait=lg_toks + (j3,) + prev_tt)
        ev1 = p.op("vector", lambda v: v.tensor_copy(lt_sb[:, 0:8], ps_tf[:, 0:8]),
                   wait=(tr1,) + prev_tt)
        a1 = p.op("vector", lambda v: v.tensor_add(lt_sb[:, 0:8], lt_sb[:, 0:8], rb_t[:, :]),
                  wait=(ev1, ev_rb))
        ngm = p.op("vector", lambda v: v.reduce_max(lcol[:, 0:1], lt_sb[:, 0:8], axis=AX.X,
                                                    negate=True), wait=(a1,))
        mk0 = p.op("vector", lambda v: v.tensor_scalar(
            out=mk_sb[:, :], in0=lt_sb[:, 0:8], scalar1=lcol[:, 0:1], scalar2=0.0,
            op0=ALU.add, op1=ALU.is_ge), wait=(ngm,))
        ex = p.op("scalar", lambda s: s.activation(out=lt_sb[:, 0:8], in_=lt_sb[:, 0:8],
                                                   func=AF.Exp, bias=lcol[:, 0:1]),
                  wait=(mk0,))
        sm = p.op("vector", lambda v: v.reduce_sum(lcol[:, 1:2], lt_sb[:, 0:8], axis=AX.X),
                  wait=(ex,))
        gt = p.op("vector", lambda v: v.reciprocal(lcol[:, 2:3], lcol[:, 1:2]), wait=(sm,))
        mk = p.op("vector", lambda v: v.tensor_scalar(
            out=lt_sb[:, 0:8], in0=mk_sb[:, :], scalar1=lcol[:, 2:3], scalar2=None,
            op0=ALU.mult), wait=(gt,))
        trb = p.op("tensor", lambda t: t.transpose(ps_tf[0:8, 128:256], lt_sb[:, 0:8],
                                                   identf[:, :]), wait=(mk,))
        evc = p.op("vector", lambda v, tsl=tsl: v.tensor_copy(crow[0:8, tsl],
                                                              ps_tf[0:8, 128:256]), wait=(trb,))
        evcb = p.op("scalar", lambda s, tsl=tsl: s.activation(out=crow_bf[0:8, tsl],
                                                              in_=crow[0:8, tsl],
                                                              func=AF.Copy), wait=(evc,))
        crow_toks.append(evcb)
        prev_tt = (evc, mk, evcb)
    crow_toks = tuple(crow_toks)

    yacc_tok = {}
    wload_prev = ()
    for e in range(8):
        wl = tuple(p.dma("gpsimd", ew_sb[:, k * 1024:(k + 1) * 1024],
                         ew_d[e, k * 128:(k + 1) * 128, :], wait=wload_prev)
                   for k in range(8))
        cb_toks = []
        for (t0, t1) in chunks_of(T):
            tcb = p.op("tensor", lambda t, e=e, t0=t0, t1=t1: t.matmul(
                ps_cbp[:, 0:t1 - t0],
                sel_sb[0:8, e * 128:(e + 1) * 128], crow_bf[0:8, t0:t1],
                start=True, stop=True), wait=crow_toks + (k2,) +
                tuple(yacc_tok.get(("cb",), ())))
            ecb = p.op("vector", lambda v, t0=t0, t1=t1: v.tensor_copy(
                cb_ps[:, 0:t1 - t0], ps_cbp[:, 0:t1 - t0]), wait=(tcb,))
            mks = []
            for f in range(8):
                mks.append(p.op("vector", lambda v, f=f, t0=t0, t1=t1: v.tensor_mul(
                    xe_bf[:, f * T + t0:f * T + t1], xn_bf[:, f * T + t0:f * T + t1],
                    cb_ps[:, 0:t1 - t0]), wait=(ecb,) + xn_toks +
                    tuple(yacc_tok.get("xe_war", ()))))
            cb_toks.append((t0, t1, tuple(mks), ecb))
            yacc_tok[("cb",)] = (ecb,) + tuple(mks)
        for j in range(4):
            t0 = j * 512
            mk_toks = cb_toks[j][2]
            for mt in range(8):
                bank, ps, war = pool.get()
                tok = None
                for k in range(8):
                    def fe(t, mt=mt, k=k, t0=t0, ps=ps):
                        return t.matmul(ps[:, :],
                                        ew_sb[:, k * 1024 + mt * 128:k * 1024 + (mt + 1) * 128],
                                        xe_bf[:, k * T + t0:k * T + t0 + 512],
                                        start=(k == 0), stop=(k == 7))
                    tok = p.op("tensor", fe, wait=(wl + war + mk_toks) if k == 0 else (),
                               inc=(k == 7))
                yacc_tok["xe_war"] = (tok,)
                prev_y = yacc_tok.get((j, mt), ())
                if e == 0:
                    ev = p.op("vector", lambda v, mt=mt, t0=t0, ps=ps: v.tensor_copy(
                        y_acc[:, mt * T + t0:mt * T + t0 + 512], ps[:, :]),
                        wait=(tok,) + prev_y)
                else:
                    ev = p.op("vector", lambda v, mt=mt, t0=t0, ps=ps: v.tensor_add(
                        y_acc[:, mt * T + t0:mt * T + t0 + 512],
                        y_acc[:, mt * T + t0:mt * T + t0 + 512], ps[:, :]),
                        wait=(tok,) + prev_y)
                pool.consumed(bank, ev)
                yacc_tok[(j, mt)] = (ev,)
        wload_prev = tuple(yacc_tok[(j, mt)][0] for j in range(4) for mt in range(8))
    for j in range(4):
        t0 = j * 512
        for mt in range(8):
            bank, ps, war = pool.get()

            def fb(t, mt=mt, t0=t0, ps=ps):
                return t.matmul(ps[:, :], ebias[0:8, mt * 128:(mt + 1) * 128],
                                crow_bf[0:8, t0:t0 + 512], start=True, stop=True)
            tok = p.op("tensor", fb, wait=crow_toks + war + (wldc[9],), inc=True)
            ev = p.op("vector", lambda v, mt=mt, t0=t0, ps=ps: v.tensor_add(
                y_acc[:, mt * T + t0:mt * T + t0 + 512],
                y_acc[:, mt * T + t0:mt * T + t0 + 512], ps[:, :]),
                wait=(tok,) + yacc_tok[(j, mt)])
            pool.consumed(bank, ev)
            yacc_tok[(j, mt)] = (ev,)

    fin = []
    prev = (xn_toks[-1],)
    for j in range(4):
        b, sj = divmod(j * 512, S)
        for f in range(8):
            gt = p.op("scalar", lambda s, f=f, j=j: s.activation(
                out=g_t[:, :], in_=y_acc[:, f * T + j * 512:f * T + (j + 1) * 512],
                func=AF.Silu), wait=tuple(yacc_tok[(j, f)]) + prev)
            lx = p.dma("sync", x0_t[:, 0:512], x_d[b, f * 128:(f + 1) * 128, sj:sj + 512],
                       wait=prev)
            ls = p.dma("scalar", s0_t[:, :], s1_d[b, f * 128:(f + 1) * 128, sj:sj + 512],
                       wait=prev)
            t1_ = p.op("vector", lambda v: v.tensor_mul(
                x1_t[:, :], s0_t[:, :], g_t[:, :]), wait=(gt, ls) + prev)
            t2_ = p.op("vector", lambda v: v.tensor_add(x1_t[:, :], x1_t[:, :], x0_t[:, 0:512]),
                       wait=(t1_, lx))
            xd = p.dma("gpsimd", x2_d[b, f * 128:(f + 1) * 128, sj:sj + 512], x1_t[:, :],
                       wait=(t2_,))
            prev = (xd, t2_)
            fin.append(xd)
    p.op("sync", lambda s: s.nop(), wait=tuple(fin), inc=False)


def emit_l2pre(p, io):
    D, NF = 1024, 8
    T = B * LLOC
    x_d = io["x"]
    lnw_d = io["ln_w"]
    lnb_d = io["ln_b"]
    wq_d = io["Wq"]
    wk_d = io["Wk"]
    wv_d = io["Wv"]
    bq_d = io["bq"]
    bk_d = io["bk"]
    bv_d = io["bv"]
    q_o = io["q"]
    k_o = io["k"]
    v_o = io["v"]
    ab_d = io["ab"]

    x_bf = p.sbuf("x_bf", [128, NF * T], BF16)
    w_sb = p.sbuf("w_sb", [128, NF * D], BF16)
    bias_sb = p.sbuf("bias_sb", [128, 3 * NF], F32)
    gam = p.sbuf("gam", [128, 2 * NF], F32)
    ones_bf = p.sbuf("ones_bf", [128, 1], BF16)
    ones_row = p.sbuf("ones_row", [1, 128], F32)
    bd2 = p.sbuf("bd2", [2, 128], F32)
    bdr = p.sbuf("bdr", [128, 2], BF16)
    bdrf = p.sbuf("bdrf", [128, 2], F32)
    identf = p.sbuf("identf", [128, 128], F32)
    id_sc = p.sbuf("id_sc", [128, 128], F32)
    eps_sb = p.sbuf("eps_sb", [1, 1], F32)
    e12 = p.sbuf("e12", [1, 1], F32)
    rows = p.sbuf("rows", [1, 4 * T], F32)
    hr = p.sbuf("hr", [2, 2 * T], F32)
    ab_sb = p.sbuf("ab_sb", [128, 2 * T], F32)
    h_bf = p.sbuf("h_bf", [128, NF * T], BF16)
    tmp_f = p.sbuf("tmp_f", [128, T], F32)
    st_f = p.sbuf("st_f", [128, 512], F32)
    st_bf = p.sbuf("st_bf", [128, 512], BF16)
    nrm_sb = p.sbuf("nrm_sb", [128, 512], F32)
    pool = PsumPool(p, "ps", n=4)
    ps_a = p.psum("ps_a", [128, 512], F32)
    ps_b = p.psum("ps_b", [128, 512], F32)

    xld = []
    for f in range(NF):
        for b in range(B):
            xld.append(p.dma("gpsimd", x_bf[:, f * T + b * LLOC: f * T + (b + 1) * LLOC],
                             x_d[b, f * 128:(f + 1) * 128, :]))
    xld = tuple(xld)
    cld = [load_col_vec(p, gam[:, 0:NF], lnw_d),
           p.dma("sync", gam[:, NF:2 * NF], lnb_d.rearrange("(m f) -> f m", f=128),
                 allow_slow_non_contiguous=True),
           p.dma("sync", bias_sb[:, 0:NF], bq_d.rearrange("(m f) -> f m", f=128),
                 allow_slow_non_contiguous=True),
           p.dma("sync", bias_sb[:, NF:2 * NF], bk_d.rearrange("(m f) -> f m", f=128),
                 allow_slow_non_contiguous=True),
           p.dma("sync", bias_sb[:, 2 * NF:3 * NF], bv_d.rearrange("(m f) -> f m", f=128),
                 allow_slow_non_contiguous=True)]
    t_ones = p.op("vector", lambda v: v.memset(ones_bf[:, :], 1.0))
    t_ones2 = p.op("vector", lambda v: v.memset(ones_row[:, :], 1.0))
    t_eps = p.op("vector", lambda v: v.memset(eps_sb[:, :], 1e-5))
    t_e12 = p.op("vector", lambda v: v.memset(e12[:, :], 1e-12))
    b4 = p.op("vector", lambda v: v.memset(bdr[:, :], 0.0))
    b5 = p.op("vector", lambda v: v.memset(bdr[0:64, 0:1], 1.0), wait=(b4,))
    b6 = p.op("vector", lambda v: v.memset(bdr[64:128, 1:2], 1.0), wait=(b5,))
    b4f = p.op("vector", lambda v: v.memset(bdrf[:, :], 0.0))
    b5f = p.op("vector", lambda v: v.memset(bdrf[0:64, 0:1], 1.0), wait=(b4f,))
    b6f = p.op("vector", lambda v: v.memset(bdrf[64:128, 1:2], 1.0), wait=(b5f,))
    j1 = p.op("gpsimd", lambda g: g.iota(id_sc[:, :], pattern=[[1, 128]], base=0,
                                         channel_multiplier=-1,
                                         allow_small_or_imprecise_dtypes=True))
    j3 = p.op("vector", lambda v: v.tensor_single_scalar(identf[:, :], id_sc[:, :], 0.0,
                                                         ALU.is_equal), wait=(j1,))
    tb0 = p.op("tensor", lambda t: t.matmul(ps_a[0:2, 0:128], bdrf[:, :], identf[:, :],
                                            start=True, stop=True), wait=(b6f, j3))
    b3 = p.op("vector", lambda v: v.tensor_copy(bd2[:, :], ps_a[0:2, 0:128]), wait=(tb0,))
    p._wprev = {}

    sq_prev = {}
    ev1p = ev2p = None
    for (t0, t1) in chunks_of(T):
        cw_ = t1 - t0
        tok1 = tok2 = None
        for f in range(NF):
            w_sq = xld + ((sq_prev[f],) if f in sq_prev else ())
            sqt = p.op("scalar", lambda s, f=f, t0=t0, t1=t1: s.activation(
                out=h_bf[:, f * T:f * T + (t1 - t0)],
                in_=x_bf[:, f * T + t0:f * T + t1], func=AF.Square), wait=w_sq)
            w1 = xld + (t_ones, b3) + ((ev1p,) if (f == 0 and ev1p) else ())
            tok1 = p.op("tensor", lambda t, f=f, t0=t0, t1=t1, cw_=cw_: t.matmul(
                ps_a[0:1, 0:cw_], ones_bf[:, :], x_bf[:, f * T + t0:f * T + t1],
                start=(f == 0), stop=(f == NF - 1)), wait=w1, inc=True)
            w2 = (sqt,) + ((ev2p,) if (f == 0 and ev2p) else ())
            tok2 = p.op("tensor", lambda t, f=f, cw_=cw_: t.matmul(
                ps_b[0:1, 0:cw_], ones_bf[:, :], h_bf[:, f * T:f * T + cw_],
                start=(f == 0), stop=(f == NF - 1)), wait=w2, inc=True)
            sq_prev[f] = tok2
        ev1p = p.op("vector", lambda v, t0=t0, t1=t1: v.tensor_copy(
            rows[0:1, 0 * T + t0:0 * T + t1], ps_a[0:1, 0:t1 - t0]), wait=(tok1,))
        ev2p = p.op("vector", lambda v, t0=t0, t1=t1: v.tensor_copy(
            rows[0:1, 1 * T + t0:1 * T + t1], ps_b[0:1, 0:t1 - t0]), wait=(tok2,))
    S_ = lambda i: rows[0:1, i * T:(i + 1) * T]
    rtok = (ev1p, ev2p, t_eps)
    r1 = p.op("vector", lambda v: v.tensor_single_scalar(S_(2), S_(0), 1.0 / D, ALU.mult), wait=rtok)
    r2 = p.op("vector", lambda v: v.tensor_single_scalar(S_(3), S_(1), 1.0 / D, ALU.mult), wait=rtok)
    r3 = p.op("vector", lambda v: v.tensor_mul(S_(0), S_(2), S_(2)), wait=(r1,))
    r4 = p.op("vector", lambda v: v.tensor_sub(S_(1), S_(3), S_(0)), wait=(r2, r3))
    r5 = p.op("scalar", lambda s: s.activation(out=S_(0), in_=S_(1), func=AF.Sqrt,
                                               bias=eps_sb[0:1, 0:1]), wait=(r4,))
    r6 = p.op("vector", lambda v: v.reciprocal(S_(3), S_(0)), wait=(r5,))
    ab_dma = []
    for b in range(B):
        ab_dma.append(p.dma("sync", ab_d[0, b, :],
                            rows[0:1, 3 * T + b * LLOC:3 * T + (b + 1) * LLOC], wait=(r6,)))
    r7 = p.op("vector", lambda v: v.tensor_mul(S_(0), S_(0), S_(0)), wait=(r6,) + tuple(ab_dma))
    r8 = p.op("vector", lambda v: v.reciprocal(S_(0), S_(0)), wait=(r7,))
    r9 = p.op("vector", lambda v: v.tensor_mul(S_(0), S_(1), S_(0)), wait=(r8,))
    r10 = p.op("scalar", lambda s: s.activation(out=S_(0), in_=S_(0), func=AF.Sqrt,
                                                bias=eps_sb[0:1, 0:1]), wait=(r9,))
    r11 = p.op("vector", lambda v: v.reciprocal(S_(0), S_(0)), wait=(r10,))
    r12 = p.op("vector", lambda v: v.tensor_mul(S_(1), S_(2), S_(3)), wait=(r11,))
    r13 = p.op("vector", lambda v: v.tensor_single_scalar(S_(1), S_(1), -1.0, ALU.mult), wait=(r12,))
    for b in range(B):
        ab_dma.append(p.dma("sync", ab_d[1, b, :],
                            rows[0:1, 1 * T + b * LLOC:1 * T + (b + 1) * LLOC], wait=(r13,)))
    r14 = p.op("vector", lambda v: v.tensor_mul(S_(2), S_(3), S_(0)), wait=(r13,))
    r15 = p.op("vector", lambda v: v.tensor_mul(S_(3), S_(1), S_(0)),
               wait=(r14,) + tuple(ab_dma))
    bc_toks = []
    ea = eb = None
    for (t0, t1) in chunks_of(T):
        tba = p.op("tensor", lambda t, t0=t0, t1=t1: t.matmul(
            ps_a[:, 0:t1 - t0], ones_row[:, :], rows[0:1, 2 * T + t0:2 * T + t1],
            start=True, stop=True), wait=(r14, ev1p, t_ones2) + ((ea,) if ea else ()))
        tbb = p.op("tensor", lambda t, t0=t0, t1=t1: t.matmul(
            ps_b[:, 0:t1 - t0], ones_row[:, :], rows[0:1, 3 * T + t0:3 * T + t1],
            start=True, stop=True), wait=(r15, ev2p) + ((eb,) if eb else ()))
        ea = p.op("vector", lambda v, t0=t0, t1=t1: v.tensor_copy(
            ab_sb[:, t0:t1], ps_a[:, 0:t1 - t0]), wait=(tba,))
        eb = p.op("vector", lambda v, t0=t0, t1=t1: v.tensor_copy(
            ab_sb[:, T + t0:T + t1], ps_b[:, 0:t1 - t0]), wait=(tbb,))
        bc_toks += [ea, eb]
    bc_toks = tuple(bc_toks)

    h_toks = []
    prev = None
    for f in range(NF):
        w = bc_toks + xld + (cld[0], cld[1]) + ((prev,) if prev else ())
        t1_ = p.op("vector", lambda v, f=f: v.scalar_tensor_tensor(
            out=tmp_f[:, :], in0=x_bf[:, f * T:(f + 1) * T], scalar=1.0,
            in1=ab_sb[:, 0:T], op0=ALU.bypass, op1=ALU.mult), wait=w)
        t2_ = p.op("vector", lambda v: v.tensor_add(tmp_f[:, :], tmp_f[:, :],
                                                    ab_sb[:, T:2 * T]), wait=(t1_,))
        t3_ = p.op("vector", lambda v, f=f: v.tensor_scalar(
            out=h_bf[:, f * T:(f + 1) * T], in0=tmp_f[:, :],
            scalar1=gam[:, f:f + 1], scalar2=gam[:, NF + f:NF + f + 1],
            op0=ALU.mult, op1=ALU.add), wait=(t2_,))
        prev = t3_
        h_toks.append(t3_)
    h_toks = tuple(h_toks)

    for wi, (wd, out_d, bcol, do_norm) in enumerate([
            (wq_d, q_o, 0, True), (wk_d, k_o, 1, True), (wv_d, v_o, 2, False)]):
        wl = tuple(p.dma("gpsimd", w_sb[:, f * D:(f + 1) * D],
                         wd[f * 128:(f + 1) * 128, :],
                         wait=tuple(p._wprev.get(wi, ()))) for f in range(NF))
        st_prev = ()
        for m in range(NF):
            for j4, (t0, t1) in enumerate(chunks_of(T)):
                b, sj = divmod(t0, LLOC)
                bank, ps, war = pool.get()
                tok = None
                for k in range(NF):
                    wk_ = (wl + war + h_toks) if k == 0 else ()

                    def fn(t, m=m, k=k, t0=t0, t1=t1, ps=ps):
                        return t.matmul(ps[:, 0:t1 - t0],
                                        w_sb[:, k * D + m * 128:k * D + (m + 1) * 128],
                                        h_bf[:, k * T + t0:k * T + t1],
                                        start=(k == 0), stop=(k == NF - 1))
                    tok = p.op("tensor", fn, wait=wk_, inc=(k == NF - 1))
                e1 = p.op("vector", lambda v, ps=ps, m=m, bcol=bcol: v.tensor_scalar_add(
                    st_f[:, :], ps[:, :], bias_sb[:, bcol * NF + m:bcol * NF + m + 1]),
                    wait=(tok, cld[2 + bcol]) + st_prev)
                if not do_norm:
                    e2 = p.op("scalar", lambda s: s.activation(out=st_bf[:, :],
                                                               in_=st_f[:, :], func=AF.Copy),
                              wait=(e1,))
                    dd = p.dma("sync", out_d[b, m * 128:(m + 1) * 128, sj:sj + 512],
                               st_bf[:, :], wait=(e2,))
                    st_prev = (e2, dd)
                    pool.consumed(bank, e1)
                    continue
                sq2 = p.op("scalar", lambda s: s.activation(out=st_bf[:, :], in_=st_f[:, :],
                                                            func=AF.Square), wait=(e1,))
                mm2 = p.op("tensor", lambda t: t.matmul(ps_a[0:2, 0:512],
                                                        bdr[:, :], st_bf[:, :],
                                                        start=True, stop=True),
                           wait=(sq2, b6) + st_prev)
                h1 = p.op("vector", lambda v: v.tensor_copy(hr[0:2, 0:512], ps_a[0:2, 0:512]),
                          wait=(mm2,))
                h2 = p.op("scalar", lambda s: s.activation(out=hr[0:2, 512:1024],
                                                           in_=hr[0:2, 0:512], func=AF.Sqrt),
                          wait=(h1,))
                h3 = p.op("vector", lambda v: v.tensor_single_scalar(
                    hr[0:2, 512:1024], hr[0:2, 512:1024], 1e-12, ALU.max), wait=(h2,))
                h4 = p.op("vector", lambda v: v.reciprocal(hr[0:2, 512:1024],
                                                           hr[0:2, 512:1024]), wait=(h3,))
                mm3 = p.op("tensor", lambda t: t.matmul(ps_b[:, 0:512], bd2[:, :],
                                                        hr[0:2, 512:1024],
                                                        start=True, stop=True),
                           wait=(h4, b3))
                nrm = p.op("vector", lambda v: v.tensor_copy(nrm_sb[:, :], ps_b[:, 0:512]),
                           wait=(mm3,))
                e3 = p.op("vector", lambda v: v.tensor_mul(st_bf[:, :], st_f[:, :],
                                                           nrm_sb[:, :]), wait=(nrm, e1))
                dd = p.dma("sync", out_d[b, m * 128:(m + 1) * 128, sj:sj + 512],
                           st_bf[:, :], wait=(e3,))
                st_prev = (e3, dd, sq2, h1, nrm)
                pool.consumed(bank, e3)
        p._wprev = getattr(p, "_wprev", {})
        p._wprev[wi + 1] = st_prev
    p.op("sync", lambda s: s.nop(), wait=tuple(ab_dma) + st_prev, inc=False)


def emit_l2attn(p, io):
    S = 8192
    Q = LLOC
    q_d = io["q"]
    k_d = io["k"]
    v_d = io["v"]
    o_d = io["state"]

    ksb = [p.sbuf(f"ksb{i}", [64, S], BF16) for i in range(2)]
    vsb = [p.sbuf(f"vsb{i}", [64, S], BF16) for i in range(2)]
    vT = [p.sbuf(f"vT{i}", [128, 64 * 65], BF16) for i in range(2)]
    st1 = [p.sbuf(f"st1{i}", [64, Q], BF16) for i in range(2)]
    st2 = [p.sbuf(f"st2{i}", [64, Q], BF16) for i in range(2)]
    pr_sb = [p.sbuf(f"pr_sb{i}", [128, 2 * 512], BF16) for i in range(2)]
    rb_sb = [p.sbuf(f"rb_sb{i}", [1, 512], F32) for i in range(2)]
    rb64 = [p.sbuf(f"rb64{i}", [64, 512], F32) for i in range(2)]
    ones1 = p.sbuf("ones1", [1, 64], F32)
    identb = p.sbuf("identb", [128, 128], BF16)
    id_sc = p.sbuf("id_sc", [128, 128], F32)
    ps_sc = [[p.psum(f"ps_sc{i}_{j}", [128, 512], F32) for j in range(2)]
             for i in range(2)]
    ps_cx = [p.psum(f"ps_cx{i}", [128, 512], F32) for i in range(2)]
    ps_tr = p.psum("ps_tr", [128, 512], BF16)
    ps_rb = p.psum("ps_rb", [128, 512], F32)

    j1 = p.op("gpsimd", lambda g: g.iota(id_sc[:, :], pattern=[[1, 128]], base=0,
                                         channel_multiplier=-1,
                                         allow_small_or_imprecise_dtypes=True))
    j3 = p.op("vector", lambda v: v.tensor_single_scalar(identb[:, :], id_sc[:, :], 0.0,
                                                         ALU.is_equal), wait=(j1,))
    t_o1 = p.op("vector", lambda v: v.memset(ones1[:, :], 1.0))

    prev_pair = {0: (), 1: ()}
    last_tr_ev = None
    rb_war = {0: (), 1: ()}
    out_dma = []
    for b in range(B):
        for hp in range(8):
            heads = (2 * hp, 2 * hp + 1)
            lk = {}
            lq = {}
            ev_toks = {}
            for i, h in enumerate(heads):
                lk[i] = tuple(p.dma("gpsimd", ksb[i][:, r * LLOC:(r + 1) * LLOC],
                                    k_d[r, b, h * 64:(h + 1) * 64, :],
                                    wait=prev_pair[i]) for r in range(8))
                lv = tuple(p.dma("gpsimd", vsb[i][:, r * LLOC:(r + 1) * LLOC],
                                 v_d[r, b, h * 64:(h + 1) * 64, :],
                                 wait=prev_pair[i]) for r in range(8))
                lq[i] = p.dma("sync", st1[i][:, :], q_d[b, h * 64:(h + 1) * 64, :],
                              wait=prev_pair[i])
                vm = p.op("vector", lambda v, i=i: v.memset(vT[i][:, :], 1.0),
                          wait=prev_pair[i])
                evs = []
                prev_tr = None
                for kt in range(64):
                    w0 = ()
                    if kt == 0:
                        w0 = lv + (j3,)
                        if last_tr_ev is not None:
                            w0 = w0 + (last_tr_ev,)
                    tr = p.op("tensor", lambda t, i=i, kt=kt: t.transpose(
                        ps_tr[:, 0:64], vsb[i][:, kt * 128:(kt + 1) * 128],
                        identb[0:64, 0:64]),
                        wait=w0 + ((prev_tr,) if prev_tr else ()))
                    ev = p.op("vector", lambda v, i=i, kt=kt: v.tensor_copy(
                        vT[i][:, kt * 65:kt * 65 + 64], ps_tr[:, 0:64]), wait=(tr, vm))
                    prev_tr = ev
                    evs.append(ev)
                ev_toks[i] = tuple(evs)
                last_tr_ev = evs[-1]
            st_cur, st_nxt = st1, st2
            st_tok = {0: (lq[0],), 1: (lq[1],)}
            prev_sn = {0: (), 1: ()}
            for step in range(2):
                new_tok = {0: [], 1: []}
                for qc in range(2):
                    qsl = slice(qc * 512, (qc + 1) * 512)
                    exv_t = {}
                    ctx_t = {}
                    for kk in range(65):
                        if kk < 64:
                            kt, pb = kk, kk % 2
                            for i in range(2):
                                wsc = lk[i] + st_tok[i] + (
                                    (exv_t[(i, kt - 2)],) if kt >= 2 else ())

                                def fsc(t, i=i, kt=kt, pb=pb, qsl=qsl, sc_=st_cur):
                                    return t.matmul(ps_sc[i][pb][:, :],
                                                    ksb[i][:, kt * 128:(kt + 1) * 128],
                                                    sc_[i][:, qsl],
                                                    start=True, stop=True)
                                msc = p.op("tensor", fsc, wait=wsc, inc=True)
                                exv = p.op("scalar", lambda s, i=i, pb=pb: s.activation(
                                    out=pr_sb[i][:, pb * 512:(pb + 1) * 512],
                                    in_=ps_sc[i][pb][:, :], func=AF.Exp),
                                    wait=(msc,) + ((ctx_t[(i, kt - 2)],)
                                                   if kt >= 2 else ()))
                                exv_t[(i, kt)] = exv
                        if kk >= 1:
                            kt, pb = kk - 1, (kk - 1) % 2
                            for i in range(2):
                                def fcx(t, i=i, kt=kt, pb=pb):
                                    return t.matmul(ps_cx[i][0:65, :],
                                                    vT[i][:, kt * 65:(kt + 1) * 65],
                                                    pr_sb[i][:, pb * 512:(pb + 1) * 512],
                                                    start=(kt == 0), stop=(kt == 63))
                                ctx = p.op("tensor", fcx,
                                           wait=(exv_t[(i, kt)],) +
                                           ((ev_toks[i] + prev_sn[i])
                                            if kt == 0 else ()),
                                           inc=(kt == 63))
                                ctx_t[(i, kt)] = ctx
                    for i in range(2):
                        rc = p.op("vector", lambda v, i=i: v.reciprocal(
                            rb_sb[i][0:1, :], ps_cx[i][64:65, :]),
                            wait=(ctx_t[(i, 63)],) + rb_war[i])
                        tb = p.op("tensor", lambda t, i=i: t.matmul(
                            ps_rb[i * 64:(i + 1) * 64, :], ones1[:, :], rb_sb[i][0:1, :],
                            start=True, stop=True), wait=(rc, t_o1))
                        eb = p.op("vector", lambda v, i=i: v.tensor_copy(
                            rb64[i][:, :], ps_rb[i * 64:(i + 1) * 64, :]), wait=(tb,))
                        sn = p.op("vector", lambda v, i=i, qsl=qsl, sn_=st_nxt: v.tensor_mul(
                            sn_[i][:, qsl], ps_cx[i][0:64, :], rb64[i][:, :]), wait=(eb,))
                        rb_war[i] = (sn,)
                        prev_sn[i] = (sn,)
                        new_tok[i].append(sn)
                st_tok = {i: tuple(new_tok[i]) for i in range(2)}
                st_cur, st_nxt = st_nxt, st_cur
            for i, h in enumerate(heads):
                od = p.dma("sync", o_d[b, h * 64:(h + 1) * 64, :], st_cur[i][:, :],
                           wait=st_tok[i])
                out_dma.append(od)
                prev_pair[i] = st_tok[i] + (od,)
    p.op("sync", lambda s: s.nop(), wait=tuple(out_dma), inc=False)



def emit_l2post(p, io):
    SQ = LLOC
    T = B * SQ
    x_d = io["x"]
    ab_d = io["ab"]
    st_d = io["state"]
    wo_d = io["Wo"]
    bo_d = io["bo"]
    wfT_d = io["wfT"]
    wiT_d = io["wiT"]
    wcT_d = io["wcT"]
    ufT_d = io["ufT"]
    uiT_d = io["uiT"]
    ucT_d = io["ucT"]
    woT_d = io["woT"]
    uoT_d = io["uoT"]
    bf_d = io["bf_"]
    bi_d = io["bi"]
    bc_d = io["bc"]
    bo2_d = io["bo2"]
    inc_d = io["in_c"]
    x3_d = io["x3"]

    wbuf = p.sbuf("wbuf", [128, 6 * 8 * 1024], BF16)
    cols = p.sbuf("cols", [128, 10 * 8], F32)
    inc_bf = p.sbuf("inc_bf", [128, 8], BF16)
    ab_sb = p.sbuf("ab_sb", [128, 2 * T], F32)
    rows = p.sbuf("rows", [1, T], F32)
    ones_row = p.sbuf("ones_row", [1, 128], F32)
    xn_bf = p.sbuf("xn_bf", [128, 8 * T], BF16)
    gf = p.sbuf("gf", [128, 8 * 512], BF16)
    gi = p.sbuf("gi", [128, 8 * 512], BF16)
    gc = p.sbuf("gc", [128, 8 * 512], BF16)
    c_all = p.sbuf("c_all", [128, 8 * SQ], BF16)
    tmp_t = p.sbuf("tmp_t", [128, 512], F32)
    tmp2_t = p.sbuf("tmp2_t", [128, 512], F32)
    x0_t = p.sbuf("x0_t", [128, 1024], F32)
    x1_t = p.sbuf("x1_t", [128, 512], F32)
    s2_t = p.sbuf("s2_t", [128, 512], F32)
    h_t = p.sbuf("h_t", [128, 512], BF16)
    pool = PsumPool(p, "ps", n=6)
    ps_r = p.psum("ps_r", [128, 512], F32)

    t_ones = p.op("vector", lambda v: v.memset(ones_row[:, :], 1.0))
    CB = lambda i: cols[:, i * 8:(i + 1) * 8]
    cld = [load_col_vec(p, CB(0), bo_d), load_col_vec(p, CB(1), bf_d),
           load_col_vec(p, CB(2), bi_d), load_col_vec(p, CB(3), bc_d),
           load_col_vec(p, CB(4), bo2_d), load_col_vec(p, CB(5), inc_d)]
    cv1 = p.op("scalar", lambda s: s.activation(out=inc_bf[:, :], in_=CB(5), func=AF.Copy),
               wait=(cld[5],))
    WSLOT = lambda g, k: wbuf[:, (g * 8 + k) * 1024:(g * 8 + k) * 1024 + 1024]
    wl1 = tuple(p.dma("gpsimd", WSLOT(g, f), wd[f * 128:(f + 1) * 128, :])
                for g, wd in enumerate([wfT_d, wiT_d, wcT_d, ufT_d, uiT_d, ucT_d])
                for f in range(8))

    bc_toks = []
    ea = None
    rlda = [p.dma("sync", rows[0:1, b * SQ:(b + 1) * SQ], ab_d[0, b, :]) for b in range(B)]
    for (t0, t1) in chunks_of(T):
        ta = p.op("tensor", lambda t, t0=t0, t1=t1: t.matmul(
            ps_r[:, 0:t1 - t0], ones_row[:, :], rows[0:1, t0:t1],
            start=True, stop=True), wait=tuple(rlda) + (t_ones,) + ((ea,) if ea else ()))
        ea = p.op("vector", lambda v, t0=t0, t1=t1: v.tensor_copy(
            ab_sb[:, t0:t1], ps_r[:, 0:t1 - t0]), wait=(ta,))
        bc_toks.append(ea)
    rldb = [p.dma("sync", rows[0:1, b * SQ:(b + 1) * SQ], ab_d[1, b, :],
                  wait=tuple(bc_toks)) for b in range(B)]
    eb = None
    for (t0, t1) in chunks_of(T):
        tb = p.op("tensor", lambda t, t0=t0, t1=t1: t.matmul(
            ps_r[:, 0:t1 - t0], ones_row[:, :], rows[0:1, t0:t1],
            start=True, stop=True), wait=tuple(rldb) + ((eb,) if eb else (bc_toks[-1],)))
        eb = p.op("vector", lambda v, t0=t0, t1=t1: v.tensor_copy(
            ab_sb[:, T + t0:T + t1], ps_r[:, 0:t1 - t0]), wait=(tb,))
        bc_toks.append(eb)
    bc_toks = tuple(bc_toks)

    xn_toks = []
    xn_rd = {0: (), 1: ()}
    xi = 0
    for f in range(8):
        for b in range(B):
            for (c0, c1_) in chunks_of(SQ):
                pp = xi % 2
                xi += 1
                lx = p.dma("sync", x0_t[:, pp * 512:pp * 512 + (c1_ - c0)],
                           x_d[b, f * 128:(f + 1) * 128, c0:c1_], wait=xn_rd[pp])
                tok = p.op("vector", lambda v, f=f, b=b, c0=c0, c1_=c1_, pp=pp: v.scalar_tensor_tensor(
                    out=xn_bf[:, f * T + b * SQ + c0:f * T + b * SQ + c1_],
                    in0=x0_t[:, pp * 512:pp * 512 + (c1_ - c0)], scalar=1.0,
                    in1=ab_sb[:, b * SQ + c0:b * SQ + c1_], op0=ALU.bypass, op1=ALU.mult),
                    wait=(lx,) + bc_toks)
                xn_rd[pp] = (tok,)
                tok2 = p.op("vector", lambda v, f=f, b=b, c0=c0, c1_=c1_: v.tensor_add(
                    xn_bf[:, f * T + b * SQ + c0:f * T + b * SQ + c1_],
                    xn_bf[:, f * T + b * SQ + c0:f * T + b * SQ + c1_],
                    ab_sb[:, T + b * SQ + c0:T + b * SQ + c1_]), wait=(tok,))
                xn_toks.append(tok2)
    xn_toks = tuple(xn_toks)

    iu_toks = []
    for g in range(3):
        for m in range(8):
            bank, ps, war = pool.get()
            tok = None
            for k in range(8):
                def fiu(t, g=g, m=m, k=k, ps=ps):
                    return t.matmul(ps[0:128, 0:1],
                                    wbuf[:, ((3 + g) * 8 + k) * 1024 + m * 128:
                                         ((3 + g) * 8 + k) * 1024 + (m + 1) * 128],
                                    inc_bf[:, k:k + 1], start=(k == 0), stop=(k == 7))
                tok = p.op("tensor", fiu, wait=(wl1 + war + (cv1,) + bc_toks)
                           if k == 0 else (), inc=(k == 7))
            ev = p.op("vector", lambda v, g=g, m=m, ps=ps: v.tensor_copy(
                cols[:, (6 + g) * 8 + m:(6 + g) * 8 + m + 1], ps[0:128, 0:1]), wait=(tok,))
            pool.consumed(bank, ev)
            iu_toks.append(ev)
    iu_toks = tuple(iu_toks)
    b0c = []
    for g in range(3):
        b0c.append(p.op("vector", lambda v, g=g: v.tensor_add(
            CB(6 + g), CB(6 + g), CB(1 + g)), wait=iu_toks + (cld[1 + g],)))
    b0c = tuple(b0c)

    gate_fn = {0: AF.Sigmoid, 1: AF.Sigmoid, 2: AF.Tanh}
    gbuf = {0: gf, 1: gi, 2: gc}
    ph1_toks = []
    c_toks = {}
    gate_war = {}
    for j in range(2):
        g_toks = {}
        for g in range(3):
            for m in range(8):
                bank, ps, war = pool.get()
                tok = None
                for k in range(8):
                    def f0(t, g=g, m=m, k=k, j=j, ps=ps):
                        return t.matmul(ps[:, :], WSLOT(g, k)[:, m * 128:(m + 1) * 128],
                                        xn_bf[:, k * T + 0 * SQ + j * 512:
                                              k * T + 0 * SQ + (j + 1) * 512],
                                        start=(k == 0), stop=(k == 7))
                    tok = p.op("tensor", f0,
                               wait=(wl1 + war + xn_toks + b0c +
                                     tuple(gate_war.get((g, m), ()))) if k == 0 else (),
                               inc=(k == 7))
                ev = p.op("scalar", lambda s, g=g, m=m, ps=ps: s.activation(
                    out=gbuf[g][:, m * 512:(m + 1) * 512], in_=ps[:, :], func=gate_fn[g],
                    bias=cols[:, (6 + g) * 8 + m:(6 + g) * 8 + m + 1]),
                    wait=(tok,) + tuple(gate_war.get((g, m), ())))
                pool.consumed(bank, ev)
                g_toks[(g, m)] = ev
        c1_toks = []
        prev_c1 = ()
        for m in range(8):
            t1_ = p.op("vector", lambda v, m=m: v.tensor_mul(
                tmp_t[:, :], gi[:, m * 512:(m + 1) * 512], gc[:, m * 512:(m + 1) * 512]),
                wait=(g_toks[(1, m)], g_toks[(2, m)]) + tuple(c_toks.get((j, m), ())) +
                prev_c1)
            t2_ = p.op("vector", lambda v, m=m, j=j: v.scalar_tensor_tensor(
                out=c_all[:, m * SQ + j * 512:m * SQ + (j + 1) * 512],
                in0=gf[:, m * 512:(m + 1) * 512], scalar=cols[:, 5 * 8 + m:5 * 8 + m + 1],
                in1=tmp_t[:, :], op0=ALU.mult, op1=ALU.add),
                wait=(t1_, g_toks[(0, m)], cld[5]))
            prev_c1 = (t2_,)
            c1_toks.append(t2_)
            c_toks[(j, m)] = (t2_,)
        c1_toks = tuple(c1_toks)
        for g in range(3):
            for m in range(8):
                bank, ps, war = pool.get()
                tok = None
                for k in range(8):
                    def f1a(t, g=g, m=m, k=k, j=j, ps=ps):
                        return t.matmul(ps[:, :], WSLOT(g, k)[:, m * 128:(m + 1) * 128],
                                        xn_bf[:, k * T + 1 * SQ + j * 512:
                                              k * T + 1 * SQ + (j + 1) * 512],
                                        start=(k == 0), stop=False)
                    tok = p.op("tensor", f1a,
                               wait=(war + c1_toks +
                                     tuple(gate_war.get((g, m), ()))) if k == 0 else (),
                               inc=False)
                for k in range(8):
                    def f1b(t, g=g, m=m, k=k, j=j, ps=ps):
                        return t.matmul(ps[:, :], WSLOT(3 + g, k)[:, m * 128:(m + 1) * 128],
                                        c_all[:, k * SQ + j * 512:k * SQ + (j + 1) * 512],
                                        start=False, stop=(k == 7))
                    tok = p.op("tensor", f1b, wait=(), inc=(k == 7))
                ev = p.op("scalar", lambda s, g=g, m=m, ps=ps: s.activation(
                    out=gbuf[g][:, m * 512:(m + 1) * 512], in_=ps[:, :], func=gate_fn[g],
                    bias=cols[:, (1 + g) * 8 + m:(1 + g) * 8 + m + 1]),
                    wait=(tok,) + tuple(gate_war.get((g, m), ())) + c1_toks)
                pool.consumed(bank, ev)
                g_toks[(g, m)] = ev
                gate_war[(g, m)] = (ev,)
        prev_c2 = ()
        all_evacs = tuple(g_toks.values())
        for m in range(8):
            t1_ = p.op("vector", lambda v, m=m: v.tensor_mul(
                tmp_t[:, :], gi[:, m * 512:(m + 1) * 512], gc[:, m * 512:(m + 1) * 512]),
                wait=all_evacs + prev_c2)
            t2_ = p.op("vector", lambda v, m=m, j=j: v.tensor_mul(
                tmp2_t[:, :], gf[:, m * 512:(m + 1) * 512],
                c_all[:, m * SQ + j * 512:m * SQ + (j + 1) * 512]),
                wait=(t1_, g_toks[(0, m)]))
            t3_ = p.op("vector", lambda v, m=m, j=j: v.tensor_add(
                c_all[:, m * SQ + j * 512:m * SQ + (j + 1) * 512], tmp2_t[:, :], tmp_t[:, :]),
                wait=(t2_,))
            prev_c2 = (t3_,)
            c_toks[(j, m)] = (t3_,)
            ph1_toks.append(t3_)
        for key in list(gate_war):
            gate_war[key] = gate_war[key] + prev_c2
    ph1_toks = tuple(ph1_toks) + tuple(gate_war[k][0] for k in gate_war)

    wl2 = tuple(p.dma("gpsimd", WSLOT(g, f), wd[f * 128:(f + 1) * 128, :], wait=ph1_toks)
                for g, wd in enumerate([woT_d, uoT_d, wo_d]) for f in range(8))
    fin = []
    prev = (xn_toks[-1],)
    st_ld_prev = ph1_toks
    for t in range(2):
        for j in range(2):
            stl = tuple(p.dma("scalar", gf[:, k * 512:(k + 1) * 512],
                              st_d[t, k * 128:(k + 1) * 128, j * 512:(j + 1) * 512],
                              wait=st_ld_prev) for k in range(8))
            new_prev = []
            for m in range(8):
                bank, ps, war = pool.get()
                tok = None
                for k in range(8):
                    def fh(t_, m=m, k=k, t=t, j=j, ps=ps):
                        return t_.matmul(ps[:, :], WSLOT(0, k)[:, m * 128:(m + 1) * 128],
                                         xn_bf[:, k * T + t * SQ + j * 512:
                                               k * T + t * SQ + (j + 1) * 512],
                                         start=(k == 0), stop=False)
                    tok = p.op("tensor", fh, wait=(wl2 + war + ph1_toks) if k == 0 else (),
                               inc=False)
                for k in range(8):
                    def fh2(t_, m=m, k=k, j=j, ps=ps):
                        return t_.matmul(ps[:, :], WSLOT(1, k)[:, m * 128:(m + 1) * 128],
                                        c_all[:, k * SQ + j * 512:k * SQ + (j + 1) * 512],
                                        start=False, stop=(k == 7))
                    tok = p.op("tensor", fh2, wait=(), inc=(k == 7))
                evh = p.op("scalar", lambda s, m=m, ps=ps: s.activation(
                    out=h_t[:, :], in_=ps[:, :], func=AF.Tanh,
                    bias=cols[:, 4 * 8 + m:4 * 8 + m + 1]), wait=(tok,) + prev)
                pool.consumed(bank, evh)
                bank2, ps2, war2 = pool.get()
                tok2 = None
                for k in range(8):
                    def fo(t_, m=m, k=k, t=t, j=j, ps2=ps2):
                        return t_.matmul(ps2[:, :], WSLOT(2, k)[:, m * 128:(m + 1) * 128],
                                         gf[:, k * 512:(k + 1) * 512],
                                         start=(k == 0), stop=(k == 7))
                    tok2 = p.op("tensor", fo, wait=(stl + war2) if k == 0 else (),
                                inc=(k == 7))
                e1 = p.op("vector", lambda v, m=m, ps2=ps2: v.tensor_scalar_add(
                    s2_t[:, :], ps2[:, :], cols[:, 0 * 8 + m:0 * 8 + m + 1]),
                    wait=(tok2, cld[0]) + prev)
                e2 = p.op("vector", lambda v, m=m, t=t, j=j: v.tensor_add(
                    s2_t[:, :], s2_t[:, :],
                    xn_bf[:, m * T + t * SQ + j * 512:m * T + t * SQ + (j + 1) * 512]),
                    wait=(e1,))
                pool.consumed(bank2, e2)
                gs = p.op("scalar", lambda s: s.activation(out=x1_t[:, :], in_=h_t[:, :],
                                                           func=AF.Silu), wait=(evh,) + prev)
                lx = p.dma("sync", x0_t[:, 0:512], x_d[t, m * 128:(m + 1) * 128,
                                                   j * 512:(j + 1) * 512], wait=prev)
                t1_ = p.op("vector", lambda v: v.tensor_mul(x1_t[:, :], x1_t[:, :],
                                                            s2_t[:, :]), wait=(gs, e2))
                t2_ = p.op("vector", lambda v: v.tensor_add(x1_t[:, :], x1_t[:, :],
                                                            x0_t[:, 0:512]), wait=(t1_, lx))
                xd = p.dma("gpsimd", x3_d[t, m * 128:(m + 1) * 128, j * 512:(j + 1) * 512],
                           x1_t[:, :], wait=(t2_,))
                prev = (xd, t2_, gs, e2)
                new_prev.append(t2_)
                fin.append(xd)
            st_ld_prev = tuple(new_prev) + (prev[0],)
    p.op("sync", lambda s: s.nop(), wait=tuple(fin), inc=False)


def emit_l3pre(p, io):
    D, NF = 1024, 8
    T = B * LLOC
    x_d = io["x"]
    wq_d = io["Wq"]
    wk_d = io["Wk"]
    wv_d = io["Wv"]
    q_o = io["q"]
    k_o = io["k"]
    v_o = io["v"]
    ab_d = io["ab"]

    x_bf = p.sbuf("x_bf", [128, NF * T], BF16)
    w_sb = p.sbuf("w_sb", [128, NF * D], BF16)
    ones_bf = p.sbuf("ones_bf", [128, 1], BF16)
    ones_row = p.sbuf("ones_row", [1, 128], F32)
    eps_sb = p.sbuf("eps_sb", [1, 1], F32)
    rows = p.sbuf("rows", [1, 4 * T], F32)
    ab_sb = p.sbuf("ab_sb", [128, 2 * T], F32)
    h_bf = p.sbuf("h_bf", [128, NF * T], BF16)
    st_bf = p.sbuf("st_bf", [128, 512], BF16)
    pool = PsumPool(p, "ps", n=4)
    ps_a = p.psum("ps_a", [128, 512], F32)
    ps_b = p.psum("ps_b", [128, 512], F32)

    xld = []
    for f in range(NF):
        for b in range(B):
            xld.append(p.dma("gpsimd", x_bf[:, f * T + b * LLOC: f * T + (b + 1) * LLOC],
                             x_d[b, f * 128:(f + 1) * 128, :]))
    xld = tuple(xld)
    t_ones = p.op("vector", lambda v: v.memset(ones_bf[:, :], 1.0))
    t_ones2 = p.op("vector", lambda v: v.memset(ones_row[:, :], 1.0))
    t_eps = p.op("vector", lambda v: v.memset(eps_sb[:, :], 1e-5))

    sq_prev = {}
    ev1p = ev2p = None
    for (t0, t1) in chunks_of(T):
        cw_ = t1 - t0
        tok1 = tok2 = None
        for f in range(NF):
            w_sq = xld + ((sq_prev[f],) if f in sq_prev else ())
            sqt = p.op("scalar", lambda s, f=f, t0=t0, t1=t1: s.activation(
                out=h_bf[:, f * T:f * T + (t1 - t0)],
                in_=x_bf[:, f * T + t0:f * T + t1], func=AF.Square), wait=w_sq)
            w1 = xld + (t_ones,) + ((ev1p,) if (f == 0 and ev1p) else ())
            tok1 = p.op("tensor", lambda t, f=f, t0=t0, t1=t1, cw_=cw_: t.matmul(
                ps_a[0:1, 0:cw_], ones_bf[:, :], x_bf[:, f * T + t0:f * T + t1],
                start=(f == 0), stop=(f == NF - 1)), wait=w1, inc=True)
            w2 = (sqt,) + ((ev2p,) if (f == 0 and ev2p) else ())
            tok2 = p.op("tensor", lambda t, f=f, cw_=cw_: t.matmul(
                ps_b[0:1, 0:cw_], ones_bf[:, :], h_bf[:, f * T:f * T + cw_],
                start=(f == 0), stop=(f == NF - 1)), wait=w2, inc=True)
            sq_prev[f] = tok2
        ev1p = p.op("vector", lambda v, t0=t0, t1=t1: v.tensor_copy(
            rows[0:1, 0 * T + t0:0 * T + t1], ps_a[0:1, 0:t1 - t0]), wait=(tok1,))
        ev2p = p.op("vector", lambda v, t0=t0, t1=t1: v.tensor_copy(
            rows[0:1, 1 * T + t0:1 * T + t1], ps_b[0:1, 0:t1 - t0]), wait=(tok2,))
    S_ = lambda i: rows[0:1, i * T:(i + 1) * T]
    rtok = (ev1p, ev2p, t_eps)
    r1 = p.op("vector", lambda v: v.tensor_single_scalar(S_(2), S_(0), 1.0 / D, ALU.mult), wait=rtok)
    r2 = p.op("vector", lambda v: v.tensor_single_scalar(S_(3), S_(1), 1.0 / D, ALU.mult), wait=rtok)
    r3 = p.op("vector", lambda v: v.tensor_mul(S_(0), S_(2), S_(2)), wait=(r1,))
    r4 = p.op("vector", lambda v: v.tensor_sub(S_(1), S_(3), S_(0)), wait=(r2, r3))
    r5 = p.op("scalar", lambda s: s.activation(out=S_(0), in_=S_(1), func=AF.Sqrt,
                                               bias=eps_sb[0:1, 0:1]), wait=(r4,))
    r6 = p.op("vector", lambda v: v.reciprocal(S_(3), S_(0)), wait=(r5,))
    ab_dma = []
    for b in range(B):
        ab_dma.append(p.dma("sync", ab_d[0, b, :],
                            rows[0:1, 3 * T + b * LLOC:3 * T + (b + 1) * LLOC], wait=(r6,)))
    r12 = p.op("vector", lambda v: v.tensor_mul(S_(1), S_(2), S_(3)), wait=(r6,))
    r13 = p.op("vector", lambda v: v.tensor_single_scalar(S_(1), S_(1), -1.0, ALU.mult), wait=(r12,))
    for b in range(B):
        ab_dma.append(p.dma("sync", ab_d[1, b, :],
                            rows[0:1, 1 * T + b * LLOC:1 * T + (b + 1) * LLOC], wait=(r13,)))
    bc_toks = []
    ea = eb = None
    for (t0, t1) in chunks_of(T):
        tba = p.op("tensor", lambda t, t0=t0, t1=t1: t.matmul(
            ps_a[:, 0:t1 - t0], ones_row[:, :], rows[0:1, 3 * T + t0:3 * T + t1],
            start=True, stop=True), wait=(r6, ev1p, t_ones2) + ((ea,) if ea else ()))
        tbb = p.op("tensor", lambda t, t0=t0, t1=t1: t.matmul(
            ps_b[:, 0:t1 - t0], ones_row[:, :], rows[0:1, 1 * T + t0:1 * T + t1],
            start=True, stop=True), wait=(r13, ev2p) + ((eb,) if eb else ()))
        ea = p.op("vector", lambda v, t0=t0, t1=t1: v.tensor_copy(
            ab_sb[:, t0:t1], ps_a[:, 0:t1 - t0]), wait=(tba,))
        eb = p.op("vector", lambda v, t0=t0, t1=t1: v.tensor_copy(
            ab_sb[:, T + t0:T + t1], ps_b[:, 0:t1 - t0]), wait=(tbb,))
        bc_toks += [ea, eb]
    bc_toks = tuple(bc_toks)
    h_toks = []
    prev = None
    for f in range(NF):
        w = bc_toks + xld + ((prev,) if prev else ())
        t1_ = p.op("vector", lambda v, f=f: v.scalar_tensor_tensor(
            out=h_bf[:, f * T:(f + 1) * T], in0=x_bf[:, f * T:(f + 1) * T], scalar=1.0,
            in1=ab_sb[:, 0:T], op0=ALU.bypass, op1=ALU.mult), wait=w)
        t2_ = p.op("vector", lambda v, f=f: v.tensor_add(
            h_bf[:, f * T:(f + 1) * T], h_bf[:, f * T:(f + 1) * T],
            ab_sb[:, T:2 * T]), wait=(t1_,))
        prev = t2_
        h_toks.append(t2_)
    h_toks = tuple(h_toks)

    st_prev = ()
    for wi, (wd, out_d) in enumerate([(wq_d, q_o), (wk_d, k_o), (wv_d, v_o)]):
        wl = tuple(p.dma("gpsimd", w_sb[:, f * D:(f + 1) * D],
                         wd[f * 128:(f + 1) * 128, :], wait=st_prev) for f in range(NF))
        for m in range(NF):
            for (t0, t1) in chunks_of(T):
                b, sj = divmod(t0, LLOC)
                bank, ps, war = pool.get()
                tok = None
                for k in range(NF):
                    wk_ = (wl + war + h_toks + st_prev) if k == 0 else ()

                    def fn(t, m=m, k=k, t0=t0, t1=t1, ps=ps):
                        return t.matmul(ps[:, 0:t1 - t0],
                                        w_sb[:, k * D + m * 128:k * D + (m + 1) * 128],
                                        h_bf[:, k * T + t0:k * T + t1],
                                        start=(k == 0), stop=(k == NF - 1))
                    tok = p.op("tensor", fn, wait=wk_, inc=(k == NF - 1))
                e1 = p.op("scalar", lambda s, ps=ps: s.activation(
                    out=st_bf[:, :], in_=ps[:, :], func=AF.Copy), wait=(tok,) + st_prev)
                dd = p.dma("sync", out_d[b, m * 128:(m + 1) * 128, sj:sj + 512],
                           st_bf[:, :], wait=(e1,))
                st_prev = (e1, dd)
                pool.consumed(bank, e1)
    p.op("sync", lambda s: s.nop(), wait=tuple(ab_dma) + st_prev, inc=False)


def emit_l3attn(p, io):
    S = 8192
    q_d = io["q"]
    k_d = io["k"]
    v_d = io["v"]
    bm_d = io["bandmul"]
    ba_d = io["bandadd"]
    r16_d = io["rel16"]
    ctx_d = io["ctx"]

    qsb = [p.sbuf(f"qsb{i}", [64, S], BF16) for i in range(2)]
    ksb = [p.sbuf(f"ksb{i}", [64, S], BF16) for i in range(2)]
    vsb = [p.sbuf(f"vsb{i}", [64, S], BF16) for i in range(2)]
    vT = [p.sbuf(f"vT{i}", [128, 64 * 65], BF16) for i in range(2)]
    cx_sb = [p.sbuf(f"cx_sb{i}", [64, S], BF16) for i in range(2)]
    tmpf = [p.sbuf(f"tmpf{i}", [128, 2 * 512], F32) for i in range(2)]
    pr_sb = [p.sbuf(f"pr_sb{i}", [128, 2 * 512], BF16) for i in range(2)]
    bmul = p.sbuf("bmul", [128, 5 * 512], F32)
    badd = p.sbuf("badd", [128, 5 * 512], F32)
    r16c = p.sbuf("r16c", [128, 1], F32)
    m30 = p.sbuf("m30", [128, 1], F32)
    rb_sb = [p.sbuf(f"rb_sb{i}", [1, 512], F32) for i in range(2)]
    rb64 = [p.sbuf(f"rb64{i}", [64, 512], F32) for i in range(2)]
    ones1 = p.sbuf("ones1", [1, 64], F32)
    identb = p.sbuf("identb", [128, 128], BF16)
    id_sc = p.sbuf("id_sc", [128, 128], F32)
    ps_sc = [[p.psum(f"ps_sc{i}_{j}", [128, 512], F32) for j in range(2)]
             for i in range(2)]
    ps_cx = [p.psum(f"ps_cx{i}", [128, 512], F32) for i in range(2)]
    ps_tr = p.psum("ps_tr", [128, 512], BF16)
    ps_rb = p.psum("ps_rb", [128, 512], F32)

    j1 = p.op("gpsimd", lambda g: g.iota(id_sc[:, :], pattern=[[1, 128]], base=0,
                                         channel_multiplier=-1,
                                         allow_small_or_imprecise_dtypes=True))
    j3 = p.op("vector", lambda v: v.tensor_single_scalar(identb[:, :], id_sc[:, :], 0.0,
                                                         ALU.is_equal), wait=(j1,))
    t_o1 = p.op("vector", lambda v: v.memset(ones1[:, :], 1.0))
    t_m3 = p.op("vector", lambda v: v.memset(m30[:, :], -30.0))
    bl = [p.dma("sync", bmul[:, :].rearrange("p (d q) -> p d q", d=5), bm_d.transpose([1, 0, 2])),
          p.dma("sync", badd[:, :].rearrange("p (d q) -> p d q", d=5), ba_d.transpose([1, 0, 2])),
          p.dma("sync", r16c[:, :], r16_d)]
    s16 = p.op("vector", lambda v: v.tensor_single_scalar(r16c[:, :], r16c[:, :], 0.125,
                                                          ALU.mult), wait=(bl[2],))

    out_dma = []
    prev_pair = {0: (), 1: ()}
    last_tr_ev = None
    rb_war = {0: (), 1: ()}
    tmpf_war = {0: (), 1: ()}
    for up in range(2):
        units = (2 * up, 2 * up + 1)
        lq = {}
        lk = {}
        ev_toks = {}
        for i, u in enumerate(units):
            lq[i] = p.dma("gpsimd", qsb[i][:, :], q_d[u], wait=prev_pair[i])
            lk[i] = p.dma("gpsimd", ksb[i][:, :], k_d[u], wait=prev_pair[i])
            lv = p.dma("gpsimd", vsb[i][:, :], v_d[u], wait=prev_pair[i])
            vm = p.op("vector", lambda v, i=i: v.memset(vT[i][:, :], 1.0),
                      wait=prev_pair[i])
            evs = []
            prev_tr = None
            for kt in range(64):
                w0 = ()
                if kt == 0:
                    w0 = (lv, j3)
                    if last_tr_ev is not None:
                        w0 = w0 + (last_tr_ev,)
                tr = p.op("tensor", lambda t, i=i, kt=kt: t.transpose(
                    ps_tr[:, 0:64], vsb[i][:, kt * 128:(kt + 1) * 128],
                    identb[0:64, 0:64]),
                    wait=w0 + ((prev_tr,) if prev_tr else ()))
                ev = p.op("vector", lambda v, i=i, kt=kt: v.tensor_copy(
                    vT[i][:, kt * 65:kt * 65 + 64], ps_tr[:, 0:64]), wait=(tr, vm))
                prev_tr = ev
                evs.append(ev)
            ev_toks[i] = tuple(evs)
            last_tr_ev = evs[-1]
        prev_sn = {0: (), 1: ()}
        unit_toks = {0: [], 1: []}
        for qch in range(16):
            qsl = slice(qch * 512, (qch + 1) * 512)
            n_kt = 4 * qch + 4
            exv_t = {}
            ctx_t = {}
            ps_rd = {}
            for kk in range(n_kt + 1):
                if kk < n_kt:
                    kt, pb = kk, kk % 2
                    d = kt - 4 * qch
                    for i in range(2):
                        wsc = (lq[i], lk[i]) + ((ps_rd[(i, kt - 2)],) if kt >= 2 else ())

                        def fsc(t, i=i, kt=kt, pb=pb, qsl=qsl):
                            return t.matmul(ps_sc[i][pb][:, :],
                                            ksb[i][:, kt * 128:(kt + 1) * 128],
                                            qsb[i][:, qsl], start=True, stop=True)
                        msc = p.op("tensor", fsc, wait=wsc, inc=True)
                        if d >= -1:
                            di = d + 1
                            b1 = p.op("vector", lambda v, i=i, pb=pb, di=di: v.tensor_mul(
                                tmpf[i][:, pb * 512:(pb + 1) * 512], ps_sc[i][pb][:, :],
                                bmul[:, di * 512:(di + 1) * 512]),
                                wait=(msc, bl[0]) +
                                (((exv_t[(i, kt - 2)],) if kt >= 2 else tmpf_war[i])))
                            b2 = p.op("vector", lambda v, i=i, pb=pb, di=di: v.tensor_add(
                                tmpf[i][:, pb * 512:(pb + 1) * 512],
                                tmpf[i][:, pb * 512:(pb + 1) * 512],
                                badd[:, di * 512:(di + 1) * 512]),
                                wait=(b1, bl[1]))
                            exv = p.op("scalar", lambda s, i=i, pb=pb: s.activation(
                                out=pr_sb[i][:, pb * 512:(pb + 1) * 512],
                                in_=tmpf[i][:, pb * 512:(pb + 1) * 512],
                                func=AF.Exp, scale=0.125, bias=m30[:, 0:1]),
                                wait=(b2, t_m3) + ((ctx_t[(i, kt - 2)],)
                                                   if kt >= 2 else ()))
                            ps_rd[(i, kt)] = b1
                        else:
                            exv = p.op("scalar", lambda s, i=i, pb=pb: s.activation(
                                out=pr_sb[i][:, pb * 512:(pb + 1) * 512],
                                in_=ps_sc[i][pb][:, :],
                                func=AF.Exp, scale=r16c[:, 0:1], bias=m30[:, 0:1]),
                                wait=(msc, s16, t_m3) + ((ctx_t[(i, kt - 2)],)
                                                         if kt >= 2 else ()))
                            ps_rd[(i, kt)] = exv
                        exv_t[(i, kt)] = exv
                if kk >= 1:
                    kt, pb = kk - 1, (kk - 1) % 2
                    for i in range(2):
                        def fcx(t, i=i, kt=kt, pb=pb, n_kt=n_kt):
                            return t.matmul(ps_cx[i][0:65, :],
                                            vT[i][:, kt * 65:(kt + 1) * 65],
                                            pr_sb[i][:, pb * 512:(pb + 1) * 512],
                                            start=(kt == 0), stop=(kt == n_kt - 1))
                        ctx = p.op("tensor", fcx,
                                   wait=(exv_t[(i, kt)],) +
                                   ((ev_toks[i] + prev_sn[i]) if kt == 0 else ()),
                                   inc=(kt == n_kt - 1))
                        ctx_t[(i, kt)] = ctx
            for i in range(2):
                tmpf_war[i] = tuple(exv_t[k] for k in
                                    [(i, n_kt - 2), (i, n_kt - 1)] if k in exv_t)
                rc = p.op("vector", lambda v, i=i: v.reciprocal(
                    rb_sb[i][0:1, :], ps_cx[i][64:65, :]),
                    wait=(ctx_t[(i, n_kt - 1)],) + rb_war[i])
                tb = p.op("tensor", lambda t, i=i: t.matmul(
                    ps_rb[i * 64:(i + 1) * 64, :], ones1[:, :],
                    rb_sb[i][0:1, :], start=True, stop=True), wait=(rc, t_o1))
                eb = p.op("vector", lambda v, i=i: v.tensor_copy(
                    rb64[i][:, :], ps_rb[i * 64:(i + 1) * 64, :]), wait=(tb,))
                sn = p.op("vector", lambda v, i=i, qsl=qsl: v.tensor_mul(
                    cx_sb[i][:, qsl], ps_cx[i][0:64, :], rb64[i][:, :]), wait=(eb,))
                rb_war[i] = (sn,)
                prev_sn[i] = (sn,)
                unit_toks[i].append(sn)
        for i, u in enumerate(units):
            od = p.dma("sync", ctx_d[u], cx_sb[i][:, :], wait=tuple(unit_toks[i]))
            out_dma.append(od)
            prev_pair[i] = (od,) + tuple(unit_toks[i][-1:]) + tmpf_war[i]
    p.op("sync", lambda s: s.nop(), wait=tuple(out_dma), inc=False)



def emit_l3post(p, io):
    S = LLOC
    T = B * S
    x_d = io["x"]
    ab_d = io["ab"]
    ctx_d = io["ctx"]
    wo_d = io["Wo"]
    bo_d = io["bo"]
    gw_d = io["gate_w"]
    w1_d = io["w1"]
    b1_d = io["b1"]
    w2_d = io["w2"]
    b2_d = io["b2"]
    out_d = io["out"]

    xn_bf = p.sbuf("xn_bf", [128, 8 * T], BF16)
    xe_bf = p.sbuf("xe_bf", [128, 8 * 512], BF16)
    wo_sb = p.sbuf("wo_sb", [128, 8 * 1024], BF16)
    w1_sb = p.sbuf("w1_sb", [128, 8 * 512], BF16)
    w2_sb = p.sbuf("w2_sb", [128, 4 * 1024], BF16)
    h_sb = p.sbuf("h_sb", [128, 4 * 512], BF16)
    eb1 = p.sbuf("eb1", [128, 8 * 4], F32)
    eb2 = p.sbuf("eb2", [8, 1024], BF16)
    s3_bf = p.sbuf("s3_bf", [128, 8 * T], BF16)
    y_acc = p.sbuf("y_acc", [128, 8 * T], BF16)
    ab_sb = p.sbuf("ab_sb", [128, 2 * T], F32)
    rows = p.sbuf("rows", [1, T], F32)
    ones_row = p.sbuf("ones_row", [1, 128], F32)
    gw_sb = p.sbuf("gw_sb", [128, 8 * 8], BF16)
    lg_sb = p.sbuf("lg_sb", [8, T], F32)
    lt_sb = p.sbuf("lt_sb", [128, 8], F32)
    et_sb = p.sbuf("et_sb", [128, 8], F32)
    mk_sb = p.sbuf("mk_sb", [128, 8], F32)
    lcol = p.sbuf("lcol", [128, 6], F32)
    crow = p.sbuf("crow", [8, T], F32)
    crow_bf = p.sbuf("crow_bf", [8, T], BF16)
    sel_sb = p.sbuf("sel_sb", [8, 8 * 128], BF16)
    io_sc = p.sbuf("io_sc", [8, 8 * 128], F32)
    cb_ps = p.sbuf("cb_ps", [128, 512], BF16)
    identf = p.sbuf("identf", [128, 128], F32)
    id_sc = p.sbuf("id_sc", [128, 128], F32)
    bocol = p.sbuf("bocol", [128, 8], F32)
    x0_t = p.sbuf("x0_t", [128, 1024], F32)
    x1_t = p.sbuf("x1_t", [128, 512], F32)
    g_t = p.sbuf("g_t", [128, 512], BF16)
    ct_t = p.sbuf("ct_t", [128, 512], BF16)
    pool = PsumPool(p, "ps", n=4)
    ps_r = p.psum("ps_r", [128, 512], F32)
    ps_cbp = p.psum("ps_cbp", [128, 512], F32)
    ps_tf = p.psum("ps_tf", [128, 512], F32)

    t_ones = p.op("vector", lambda v: v.memset(ones_row[:, :], 1.0))
    j1 = p.op("gpsimd", lambda g: g.iota(id_sc[:, :], pattern=[[1, 128]], base=0,
                                         channel_multiplier=-1,
                                         allow_small_or_imprecise_dtypes=True))
    j3 = p.op("vector", lambda v: v.tensor_single_scalar(identf[:, :], id_sc[:, :], 0.0,
                                                         ALU.is_equal), wait=(j1,))
    k1 = p.op("gpsimd", lambda g: g.iota(io_sc[:, :].rearrange("p (e q) -> p e q", e=8),
                                         pattern=[[1, 8], [0, 128]], base=0,
                                         channel_multiplier=-1,
                                         allow_small_or_imprecise_dtypes=True))
    k2 = p.op("vector", lambda v: v.tensor_single_scalar(sel_sb[:, :], io_sc[:, :], 0.0,
                                                         ALU.is_equal), wait=(k1,))
    wldc = [p.dma("gpsimd", gw_sb[:, :].rearrange("f (k e) -> f k e", e=8)[:, :, e],
                  gw_d.rearrange("(k f) e -> f k e", f=128)[:, :, e],
                  allow_slow_non_contiguous=True) for e in range(8)]
    wldc.append(p.dma("gpsimd", eb2[:, :], b2_d))
    wldc.append(p.dma("sync", eb1[:, :].rearrange("f (e m) -> f e m", e=8),
                      b1_d.rearrange("e (m f) -> f e m", f=128),
                      allow_slow_non_contiguous=True))
    wldc.append(load_col_vec(p, bocol, bo_d))
    wlo = tuple(p.dma("gpsimd", wo_sb[:, f * 1024:(f + 1) * 1024],
                      wo_d[f * 128:(f + 1) * 128, :]) for f in range(8))
    rlda = [p.dma("sync", rows[0:1, b * S:(b + 1) * S], ab_d[0, b, :]) for b in range(B)]

    bc_toks = []
    ea = None
    for (t0, t1) in chunks_of(T):
        ta = p.op("tensor", lambda t, t0=t0, t1=t1: t.matmul(
            ps_r[:, 0:t1 - t0], ones_row[:, :], rows[0:1, t0:t1],
            start=True, stop=True), wait=tuple(rlda) + (t_ones,) + ((ea,) if ea else ()))
        ea = p.op("vector", lambda v, t0=t0, t1=t1: v.tensor_copy(
            ab_sb[:, t0:t1], ps_r[:, 0:t1 - t0]), wait=(ta,))
        bc_toks.append(ea)
    rldb = [p.dma("sync", rows[0:1, b * S:(b + 1) * S], ab_d[1, b, :],
                  wait=tuple(bc_toks)) for b in range(B)]
    eb_ = None
    for (t0, t1) in chunks_of(T):
        tb = p.op("tensor", lambda t, t0=t0, t1=t1: t.matmul(
            ps_r[:, 0:t1 - t0], ones_row[:, :], rows[0:1, t0:t1],
            start=True, stop=True), wait=tuple(rldb) + ((eb_,) if eb_ else (bc_toks[-1],)))
        eb_ = p.op("vector", lambda v, t0=t0, t1=t1: v.tensor_copy(
            ab_sb[:, T + t0:T + t1], ps_r[:, 0:t1 - t0]), wait=(tb,))
        bc_toks.append(eb_)
    bc_toks = tuple(bc_toks)

    xn_toks = []
    xn_rd = {0: (), 1: ()}
    xi = 0
    for f in range(8):
        for b in range(B):
            for (c0, c1_) in chunks_of(S):
                pp = xi % 2
                xi += 1
                lx = p.dma("sync", x0_t[:, pp * 512:pp * 512 + (c1_ - c0)],
                           x_d[b, f * 128:(f + 1) * 128, c0:c1_], wait=xn_rd[pp])
                tok = p.op("vector", lambda v, f=f, b=b, c0=c0, c1_=c1_, pp=pp: v.scalar_tensor_tensor(
                    out=xn_bf[:, f * T + b * S + c0:f * T + b * S + c1_],
                    in0=x0_t[:, pp * 512:pp * 512 + (c1_ - c0)], scalar=1.0,
                    in1=ab_sb[:, b * S + c0:b * S + c1_], op0=ALU.bypass, op1=ALU.mult),
                    wait=(lx,) + bc_toks)
                xn_rd[pp] = (tok,)
                tok2 = p.op("vector", lambda v, f=f, b=b, c0=c0, c1_=c1_: v.tensor_add(
                    xn_bf[:, f * T + b * S + c0:f * T + b * S + c1_],
                    xn_bf[:, f * T + b * S + c0:f * T + b * S + c1_],
                    ab_sb[:, T + b * S + c0:T + b * S + c1_]), wait=(tok,))
                xn_toks.append(tok2)
    xn_toks = tuple(xn_toks)

    s3_toks = {}
    prev_s3 = ()
    for j in range(4):
        b, sj = divmod(j * 512, S)
        ctl = tuple(p.dma("scalar", xe_bf[:, k * 512:(k + 1) * 512],
                          ctx_d[b, k * 128:(k + 1) * 128, sj:sj + 512], wait=prev_s3)
                    for k in range(8))
        newp = []
        for m in range(8):
            bank, ps, war = pool.get()
            tok = None
            for k in range(8):
                def fo(t, m=m, k=k, ps=ps):
                    return t.matmul(ps[:, :], wo_sb[:, k * 1024 + m * 128:
                                                    k * 1024 + (m + 1) * 128],
                                    xe_bf[:, k * 512:(k + 1) * 512],
                                    start=(k == 0), stop=(k == 7))
                tok = p.op("tensor", fo, wait=(ctl + war + wlo) if k == 0 else (),
                           inc=(k == 7))
            ev = p.op("vector", lambda v, m=m, j=j, ps=ps: v.tensor_scalar_add(
                s3_bf[:, m * T + j * 512:m * T + (j + 1) * 512], ps[:, :],
                bocol[:, m:m + 1]), wait=(tok, wldc[-1]))
            pool.consumed(bank, ev)
            newp.append(ev)
        prev_s3 = tuple(newp) + (tok,)
        s3_toks[j] = tuple(newp)
    s3_all = tuple(t for j in s3_toks for t in s3_toks[j])

    lg_toks = []
    le_prev = None
    for (t0, t1) in chunks_of(T):
        tok = None
        for k in range(8):
            def fg(t, k=k, t0=t0, t1=t1):
                return t.matmul(ps_r[0:8, 0:t1 - t0], gw_sb[:, k * 8:(k + 1) * 8],
                                xn_bf[:, k * T + t0:k * T + t1],
                                start=(k == 0), stop=(k == 7))
            tok = p.op("tensor", fg, wait=(xn_toks + tuple(wldc) +
                                           ((le_prev,) if le_prev else ())) if k == 0 else (),
                       inc=(k == 7))
        le_prev = p.op("vector", lambda v, t0=t0, t1=t1: v.tensor_copy(
            lg_sb[0:8, t0:t1], ps_r[0:8, 0:t1 - t0]), wait=(tok,))
        lg_toks.append(le_prev)
    lg_toks = tuple(lg_toks)

    crow_toks = []
    prev_tt = ()
    for tt in range(16):
        tsl = slice(tt * 128, (tt + 1) * 128)
        tr1 = p.op("tensor", lambda t, tsl=tsl: t.transpose(
            ps_tf[:, 0:8], lg_sb[0:8, tsl], identf[0:8, 0:8]), wait=lg_toks + (j3,) + prev_tt)
        ev1 = p.op("vector", lambda v: v.tensor_copy(lt_sb[:, 0:8], ps_tf[:, 0:8]),
                   wait=(tr1,) + prev_tt)
        ngm = p.op("vector", lambda v: v.reduce_max(lcol[:, 0:1], lt_sb[:, 0:8], axis=AX.X,
                                                    negate=True), wait=(ev1,))
        ex = p.op("scalar", lambda s: s.activation(out=et_sb[:, :], in_=lt_sb[:, 0:8],
                                                   func=AF.Exp, bias=lcol[:, 0:1]),
                  wait=(ngm,) + prev_tt)
        sm = p.op("vector", lambda v: v.reduce_sum(lcol[:, 1:2], et_sb[:, :], axis=AX.X),
                  wait=(ex,))
        rden = p.op("vector", lambda v: v.reciprocal(lcol[:, 2:3], lcol[:, 1:2]), wait=(sm,))
        pr = p.op("vector", lambda v: v.tensor_scalar(
            out=et_sb[:, :], in0=et_sb[:, :], scalar1=lcol[:, 2:3], scalar2=None,
            op0=ALU.mult), wait=(rden,))
        p1 = p.op("vector", lambda v: v.reduce_max(lcol[:, 3:4], et_sb[:, :], axis=AX.X),
                  wait=(pr,))
        m1 = p.op("vector", lambda v: v.tensor_scalar(
            out=mk_sb[:, :], in0=et_sb[:, :], scalar1=lcol[:, 3:4], scalar2=None,
            op0=ALU.is_lt), wait=(p1,))
        m2 = p.op("vector", lambda v: v.tensor_mul(mk_sb[:, :], mk_sb[:, :], et_sb[:, :]),
                  wait=(m1,))
        p2 = p.op("vector", lambda v: v.reduce_max(lcol[:, 4:5], mk_sb[:, :], axis=AX.X),
                  wait=(m2,))
        tkw = p.op("vector", lambda v: v.tensor_add(lcol[:, 5:6], lcol[:, 3:4],
                                                    lcol[:, 4:5]), wait=(p2,))
        s1_ = p.op("vector", lambda v: v.tensor_scalar(
            out=mk_sb[:, :], in0=et_sb[:, :], scalar1=lcol[:, 4:5], scalar2=None,
            op0=ALU.is_ge), wait=(p2, tkw))
        c1 = p.op("vector", lambda v: v.tensor_scalar(
            out=lt_sb[:, 0:8], in0=mk_sb[:, :], scalar1=lcol[:, 5:6], scalar2=None,
            op0=ALU.mult), wait=(s1_,))
        trb = p.op("tensor", lambda t: t.transpose(ps_tf[0:8, 128:256], lt_sb[:, 0:8],
                                                   identf[:, :]), wait=(c1,))
        evc = p.op("vector", lambda v, tsl=tsl: v.tensor_copy(crow[0:8, tsl],
                                                              ps_tf[0:8, 128:256]),
                   wait=(trb,))
        evcb = p.op("scalar", lambda s, tsl=tsl: s.activation(out=crow_bf[0:8, tsl],
                                                              in_=crow[0:8, tsl],
                                                              func=AF.Copy), wait=(evc,))
        crow_toks.append(evcb)
        prev_tt = (evc, evcb, c1, ex)
    crow_toks = tuple(crow_toks)

    yacc_tok = {}
    wload_prev = ()
    for e in range(8):
        wl = tuple(p.dma("gpsimd", w1_sb[:, k * 512:(k + 1) * 512],
                         w1_d[e, k * 128:(k + 1) * 128, :], wait=wload_prev)
                   for k in range(8))
        wl += tuple(p.dma("gpsimd", w2_sb[:, k * 1024:(k + 1) * 1024],
                          w2_d[e, k * 128:(k + 1) * 128, :], wait=wload_prev)
                    for k in range(4))
        for j in range(4):
            t0 = j * 512
            tcb = p.op("tensor", lambda t, e=e, t0=t0: t.matmul(
                ps_cbp[:, :], sel_sb[0:8, e * 128:(e + 1) * 128],
                crow_bf[0:8, t0:t0 + 512], start=True, stop=True),
                wait=crow_toks + (k2,) + tuple(yacc_tok.get("cb", ())))
            ecb = p.op("vector", lambda v: v.tensor_copy(cb_ps[:, :], ps_cbp[:, :]),
                       wait=(tcb,))
            yacc_tok["cb"] = (ecb,)
            htoks = []
            for mt in range(4):
                bank, ps, war = pool.get()
                tok = None
                for k in range(8):
                    def f1(t, mt=mt, k=k, t0=t0, ps=ps):
                        return t.matmul(ps[:, :],
                                        w1_sb[:, k * 512 + mt * 128:k * 512 + (mt + 1) * 128],
                                        xn_bf[:, k * T + t0:k * T + t0 + 512],
                                        start=(k == 0), stop=(k == 7))
                    tok = p.op("tensor", f1, wait=(wl + war + xn_toks) if k == 0 else (),
                               inc=(k == 7))
                ev = p.op("scalar", lambda s, mt=mt, e=e, ps=ps: s.activation(
                    out=h_sb[:, mt * 512:(mt + 1) * 512], in_=ps[:, :], func=AF.Gelu,
                    bias=eb1[:, e * 4 + mt:e * 4 + mt + 1]),
                    wait=(tok, wldc[-2]) + tuple(yacc_tok.get(("h", mt), ())))
                ev2 = p.op("vector", lambda v, mt=mt: v.tensor_mul(
                    h_sb[:, mt * 512:(mt + 1) * 512], h_sb[:, mt * 512:(mt + 1) * 512],
                    cb_ps[:, :]), wait=(ev, ecb))
                pool.consumed(bank, ev2)
                htoks.append(ev2)
            htoks = tuple(htoks)
            for mt in range(8):
                bank, ps, war = pool.get()
                tok = None
                for k in range(4):
                    def f2(t, mt=mt, k=k, ps=ps):
                        return t.matmul(ps[:, :],
                                        w2_sb[:, k * 1024 + mt * 128:k * 1024 + (mt + 1) * 128],
                                        h_sb[:, k * 512:(k + 1) * 512],
                                        start=(k == 0), stop=(k == 3))
                    tok = p.op("tensor", f2, wait=(htoks + war) if k == 0 else (),
                               inc=(k == 3))
                for hmt in range(4):
                    yacc_tok[("h", hmt)] = (tok,)
                prev_y = yacc_tok.get((j, mt), ())
                if e == 0:
                    ev = p.op("vector", lambda v, mt=mt, t0=t0, ps=ps: v.tensor_copy(
                        y_acc[:, mt * T + t0:mt * T + t0 + 512], ps[:, :]),
                        wait=(tok,) + prev_y)
                else:
                    ev = p.op("vector", lambda v, mt=mt, t0=t0, ps=ps: v.tensor_add(
                        y_acc[:, mt * T + t0:mt * T + t0 + 512],
                        y_acc[:, mt * T + t0:mt * T + t0 + 512], ps[:, :]),
                        wait=(tok,) + prev_y)
                pool.consumed(bank, ev)
                yacc_tok[(j, mt)] = (ev,)
        wload_prev = tuple(yacc_tok[(j, mt)][0] for j in range(4) for mt in range(8))
    for j in range(4):
        t0 = j * 512
        for mt in range(8):
            bank, ps, war = pool.get()

            def fb(t, mt=mt, t0=t0, ps=ps):
                return t.matmul(ps[:, :], eb2[0:8, mt * 128:(mt + 1) * 128],
                                crow_bf[0:8, t0:t0 + 512], start=True, stop=True)
            tok = p.op("tensor", fb, wait=crow_toks + war + (wldc[8],), inc=True)
            ev = p.op("vector", lambda v, mt=mt, t0=t0, ps=ps: v.tensor_add(
                y_acc[:, mt * T + t0:mt * T + t0 + 512],
                y_acc[:, mt * T + t0:mt * T + t0 + 512], ps[:, :]),
                wait=(tok,) + yacc_tok[(j, mt)])
            pool.consumed(bank, ev)
            yacc_tok[(j, mt)] = (ev,)

    fin = []
    prev = (xn_toks[-1],) + s3_all
    for j in range(4):
        b, sj = divmod(j * 512, S)
        for f in range(8):
            gt = p.op("scalar", lambda s, f=f, j=j: s.activation(
                out=g_t[:, :], in_=y_acc[:, f * T + j * 512:f * T + (j + 1) * 512],
                func=AF.Silu), wait=tuple(yacc_tok[(j, f)]) + prev)
            lx = p.dma("sync", x0_t[:, 0:512], x_d[b, f * 128:(f + 1) * 128, sj:sj + 512],
                       wait=prev)
            t1_ = p.op("vector", lambda v, f=f, j=j: v.tensor_mul(
                x1_t[:, :], s3_bf[:, f * T + j * 512:f * T + (j + 1) * 512], g_t[:, :]),
                wait=(gt,) + prev)
            t2_ = p.op("vector", lambda v: v.tensor_add(x1_t[:, :], x1_t[:, :], x0_t[:, 0:512]),
                       wait=(t1_, lx))
            xd = p.dma("gpsimd", out_d[b, f * 128:(f + 1) * 128, sj:sj + 512], x1_t[:, :],
                       wait=(t2_,))
            prev = (xd, t2_, gt)
            fin.append(xd)
    p.op("sync", lambda s: s.nop(), wait=tuple(fin), inc=False)



def build_fused():
    p = Prog("mega")
    ein = p.dram_in
    tmp = p.dram_tmp

    xh = ein("xh", [B, 1024, TH])
    w_a0 = dict(W_in=ein("a0_W_in", [1024, 4096]), norm_w=ein("a0_norm_w", [1024]),
                conv_w=ein("a0_conv_w", [2048, 4]), conv_b=ein("a0_conv_b", [2048]),
                W_x=ein("a0_W_x", [2048, 96]))
    w_s0 = dict(W_dt=ein("s0_W_dt", [64, 256]), b_dt=ein("s0_b_dt", [256]),
                A=ein("s0_A", [256, 16]))
    w_p0 = dict(D=ein("p0_D", [2048]), W_out=ein("p0_W_out", [2048, 1024]))
    w_f0 = dict(gate_w=ein("f0_gate_w", [2, 1024]), w1=ein("f0_w1", [2, 1024, 4096]),
                b1=ein("f0_b1", [2, 4096]), w2=ein("f0_w2", [2, 4096, 1024]),
                b2=ein("f0_b2", [2, 1024]), sw1=ein("f0_sw1", [1024, 4096]),
                sb1=ein("f0_sb1", [4096]), sw2=ein("f0_sw2", [4096, 1024]),
                sb2=ein("f0_sb2", [1024]))
    w_a1 = dict(W_in=ein("a1_W_in", [1024, 4368]), ln_w=ein("a1_ln_w", [1024]),
                ln_b=ein("a1_ln_b", [1024]), conv_w=ein("a1_conv_w", [2304, 4]),
                conv_b=ein("a1_conv_b", [2304]), dt_bias=ein("a1_dt_bias", [16]),
                A_log=ein("a1_A_log", [16]), hmask=ein("a1_hmask", [128, 8]))
    w_u1 = dict(bsel=ein("u1_bsel", [128, 2]), Du=ein("u1_Du", [1, 4]))
    w_p1 = dict(rms_w=ein("p1_rms_w", [2048]), W_out=ein("p1_W_out", [2048, 1024]))
    w_f1 = dict(router_w=ein("f1_router_w", [1024, 8]), router_b=ein("f1_router_b", [8]),
                expert_w=ein("f1_expert_w", [8, 1024, 1024]),
                expert_b=ein("f1_expert_b", [8, 1024]))
    w_a2 = dict(ln_w=ein("a2_ln_w", [1024]), ln_b=ein("a2_ln_b", [1024]),
                Wq=ein("a2_Wq", [1024, 1024]), Wk=ein("a2_Wk", [1024, 1024]),
                Wv=ein("a2_Wv", [1024, 1024]), bq=ein("a2_bq", [1024]),
                bk=ein("a2_bk", [1024]), bv=ein("a2_bv", [1024]))
    w_p2 = dict(Wo=ein("p2_Wo", [1024, 1024]), bo=ein("p2_bo", [1024]),
                wfT=ein("p2_wfT", [1024, 1024]), wiT=ein("p2_wiT", [1024, 1024]),
                wcT=ein("p2_wcT", [1024, 1024]), ufT=ein("p2_ufT", [1024, 1024]),
                uiT=ein("p2_uiT", [1024, 1024]), ucT=ein("p2_ucT", [1024, 1024]),
                woT=ein("p2_woT", [1024, 1024]), uoT=ein("p2_uoT", [1024, 1024]),
                bf_=ein("p2_bf", [1024]), bi=ein("p2_bi", [1024]),
                bc=ein("p2_bc", [1024]), bo2=ein("p2_bo2", [1024]),
                in_c=ein("p2_in_c", [1024]))
    w_a3 = dict(Wq=ein("a3_Wq", [1024, 1024]), Wk=ein("a3_Wk", [1024, 1024]),
                Wv=ein("a3_Wv", [1024, 1024]))
    w_t3 = dict(bandmul=ein("t3_bandmul", [5, 128, 512]),
                bandadd=ein("t3_bandadd", [5, 128, 512]),
                rel16=ein("t3_rel16", [128, 1]))
    w_f3 = dict(Wo=ein("f3_Wo", [1024, 1024]), bo=ein("f3_bo", [1024]),
                gate_w=ein("f3_gate_w", [1024, 8]), w1=ein("f3_w1", [8, 1024, 512]),
                b1=ein("f3_b1", [8, 512]), w2=ein("f3_w2", [8, 512, 1024]),
                b2=ein("f3_b2", [8, 1024]))
    out_d = p.dram_out("out", [B, 1024, LLOC], F32)

    res_loc = tmp("t_res", [B, 2048, LLOC], BF16)
    ab0 = tmp("t_ab0", [2, B, LLOC])
    xc_prod = tmp("t_xc_p", [B, 2048, LLOC], BF16)
    xc_ai = tmp("t_xc_ai", [8, B, 256, LLOC], BF16)
    xc_ao = tmp("t_xc_ao", [8, B, 256, LLOC], BF16)
    xc_cons = tmp("t_xc_c", [B, 256, NCORES * LLOC], BF16)
    dlt_loc = tmp("t_dlt", [B, 64, LLOC], BF16)
    dlt_ag = tmp("t_dlt_ag", [8, B, 64, LLOC], BF16)
    dlt_cons = tmp("t_dlt_c", [B, 64, NCORES * LLOC], BF16)
    bm0_loc = tmp("t_bm0", [B, 16, LLOC], BF16)
    bm0_ag = tmp("t_bm0_ag", [8, B, 16, LLOC], BF16)
    bm0_cons = tmp("t_bm0_c", [B, 16, NCORES * LLOC], BF16)
    cm0_loc = tmp("t_cm0", [B, 16, LLOC], BF16)
    cm0_ag = tmp("t_cm0_ag", [8, B, 16, LLOC], BF16)
    cm0_cons = tmp("t_cm0_c", [B, 16, NCORES * LLOC], BF16)
    y0_prod = tmp("t_y0_p", [B, 256, NCORES * LLOC], BF16)
    y0_ai = tmp("t_y0_ai", [8, B, 256, LLOC], BF16)
    y0_ao = tmp("t_y0_ao", [8, B, 256, LLOC], BF16)
    y0_cons = tmp("t_y0_c", [B, 2048, LLOC], BF16)
    s0_loc = tmp("t_s0", [B, 1024, LLOC], BF16)
    x1_loc = tmp("t_x1", [B, 1024, LLOC])
    halo_ai = tmp("t_halo_ai", [48, 128])
    halo_ag = tmp("t_halo_ag", [8 * 48, 128])
    ab1 = tmp("t_ab1", [2, B, LLOC])
    z_loc = tmp("t_z", [B, 2048, LLOC], BF16)
    X_ai = tmp("t_X_ai", [8, 4, 128, LLOC], BF16)
    X_ao = tmp("t_X_ao", [8, 4, 128, LLOC], BF16)
    X_cons = tmp("t_X_c", [4, 128, NCORES * LLOC], BF16)
    bm1_loc = tmp("t_bm1", [B, 128, LLOC], BF16)
    bm1_ag = tmp("t_bm1_ag", [8, B, 128, LLOC], BF16)
    bm1_cons = tmp("t_bm1_c", [B, 128, NCORES * LLOC], BF16)
    cm1_loc = tmp("t_cm1", [B, 128, LLOC], BF16)
    cm1_ag = tmp("t_cm1_ag", [8, B, 128, LLOC], BF16)
    cm1_cons = tmp("t_cm1_c", [B, 128, NCORES * LLOC], BF16)
    dt_ai = tmp("t_dt_ai", [8, 4, LLOC])
    dt_ao = tmp("t_dt_ao", [8, 4, LLOC])
    dt_cons = tmp("t_dt_c", [4, NCORES * LLOC])
    dta_ai = tmp("t_dta_ai", [8, 4, LLOC])
    dta_ao = tmp("t_dta_ao", [8, 4, LLOC])
    dta_cons = tmp("t_dta_c", [4, NCORES * LLOC])
    y1_prod = tmp("t_y1_p", [4, 128, NCORES * LLOC], BF16)
    y1_ai = tmp("t_y1_ai", [8, 4, 128, LLOC], BF16)
    y1_ao = tmp("t_y1_ao", [8, 4, 128, LLOC], BF16)
    s1_loc = tmp("t_s1", [B, 1024, LLOC], BF16)
    x2_loc = tmp("t_x2", [B, 1024, LLOC])
    ab2 = tmp("t_ab2", [2, B, LLOC])
    q2_loc = tmp("t_q2", [B, 1024, LLOC], BF16)
    k2_loc = tmp("t_k2", [B, 1024, LLOC], BF16)
    v2_loc = tmp("t_v2", [B, 1024, LLOC], BF16)
    k2_ag = tmp("t_k2_ag", [8, B, 1024, LLOC], BF16)
    v2_ag = tmp("t_v2_ag", [8, B, 1024, LLOC], BF16)
    st_loc = tmp("t_st", [B, 1024, LLOC], BF16)
    x3_loc = tmp("t_x3", [B, 1024, LLOC])
    ab3 = tmp("t_ab3", [2, B, LLOC])
    q3_ai = tmp("t_q3_ai", [8, 4, 64, LLOC], BF16)
    q3_ao = tmp("t_q3_ao", [8, 4, 64, LLOC], BF16)
    q3_cons = tmp("t_q3_c", [4, 64, NCORES * LLOC], BF16)
    k3_ai = tmp("t_k3_ai", [8, 4, 64, LLOC], BF16)
    k3_ao = tmp("t_k3_ao", [8, 4, 64, LLOC], BF16)
    k3_cons = tmp("t_k3_c", [4, 64, NCORES * LLOC], BF16)
    v3_ai = tmp("t_v3_ai", [8, 4, 64, LLOC], BF16)
    v3_ao = tmp("t_v3_ao", [8, 4, 64, LLOC], BF16)
    v3_cons = tmp("t_v3_c", [4, 64, NCORES * LLOC], BF16)
    ctx_prod = tmp("t_ctx_p", [4, 64, NCORES * LLOC], BF16)
    ctx_ai = tmp("t_ctx_ai", [8, 4, 64, LLOC], BF16)
    ctx_ao = tmp("t_ctx_ao", [8, 4, 64, LLOC], BF16)

    x0_plain = xh[:, :, HALO:]
    R2 = lambda a: a.rearrange("r b c t -> (r b c) t")
    R2u = lambda a: a.rearrange("r i c t -> (r i c) t")
    R2d = lambda a: a.rearrange("r i t -> (r i) t")
    R2h = lambda a: a.rearrange("r i d t -> (r i d) t")
    QS = ("sync", "scalar", "gpsimd")

    def boundary(prod_pairs, colls, cons_pairs):
        p.join()
        pt = tuple(p.dma(QS[i % 3], dst, src)
                   for i, (dst, src) in enumerate(prod_pairs))
        for kind, i_, o_ in colls:
            p.collective(kind, [i_], [o_], wait=pt)
        p.join()
        for i, (dst, src) in enumerate(cons_pairs):
            p.dma(QS[i % 3], dst, src)
        p.join()

    def sc_pairs(ai, prod, rows):
        return [(ai[r, b], prod[b, r * rows:(r + 1) * rows, :])
                for r in range(8) for b in range(B)]

    def tslice_pairs(ai, prod, n0):
        return [(ai[r, u], prod[u, :, r * LLOC:(r + 1) * LLOC])
                for r in range(8) for u in range(n0)]

    def cons_seq_pairs(cons, ao, n0):
        return [(cons[u, :, s * LLOC:(s + 1) * LLOC], ao[s, u])
                for s in range(8) for u in range(n0)]

    def cons_row_pairs(cons, ao, rows):
        return [(cons[b, s * rows:(s + 1) * rows, :], ao[s, b])
                for s in range(8) for b in range(B)]

    p.begin_stage("a0")
    emit_l0pre(p, dict(xh=xh, **w_a0, res=res_loc, xc=xc_prod,
                       dlt=dlt_loc, Bm=bm0_loc, Cm=cm0_loc, ab=ab0))
    p.end_stage()
    boundary(
        sc_pairs(xc_ai, xc_prod, 256),
        [("AllToAll", R2(xc_ai), R2(xc_ao)),
         ("AllGather", dlt_loc.rearrange("b c t -> (b c) t"), R2(dlt_ag)),
         ("AllGather", bm0_loc.rearrange("b c t -> (b c) t"), R2(bm0_ag)),
         ("AllGather", cm0_loc.rearrange("b c t -> (b c) t"), R2(cm0_ag))],
        cons_seq_pairs(xc_cons, xc_ao, B)
        + cons_seq_pairs(dlt_cons, dlt_ag, B)
        + cons_seq_pairs(bm0_cons, bm0_ag, B)
        + cons_seq_pairs(cm0_cons, cm0_ag, B))

    p.begin_stage("s0")
    emit_l0scan(p, dict(xc=xc_cons, dlt=dlt_cons, **w_s0,
                        Bm=bm0_cons, Cm=cm0_cons, y=y0_prod))
    p.end_stage()
    boundary(
        tslice_pairs(y0_ai, y0_prod, B),
        [("AllToAll", R2(y0_ai), R2(y0_ao))],
        cons_row_pairs(y0_cons, y0_ao, 256))

    p.begin_stage("p0")
    emit_l0post(p, dict(x=x0_plain, ab=ab0, y=y0_cons, xc=xc_prod,
                        res=res_loc, **w_p0, s0=s0_loc))
    p.end_stage()
    p.join()

    p.begin_stage("f0")
    emit_l0ffn(p, dict(x=x0_plain, ab=ab0, s0=s0_loc, **w_f0, x1=x1_loc,
                       halo_in=halo_ai))
    p.end_stage()
    boundary([], [("AllGather", halo_ai[:, :], halo_ag[:, :])], [])

    p.begin_stage("a1")
    emit_l1pre(p, dict(x=x1_loc, halo_ag=halo_ag, **w_a1, z=z_loc,
                       xh_c=X_ai.rearrange("(b r) i c t -> b (r i c) t", b=2),
                       Bm=bm1_loc, Cm=cm1_loc,
                       dt1=dt_ai.rearrange("(b r) i t -> b (r i) t", b=2),
                       dtA=dta_ai.rearrange("(b r) i t -> b (r i) t", b=2),
                       ab=ab1))
    p.end_stage()
    boundary(
        [],
        [("AllToAll", R2u(X_ai), R2u(X_ao)),
         ("AllGather", bm1_loc.rearrange("b c t -> (b c) t"), R2(bm1_ag)),
         ("AllGather", cm1_loc.rearrange("b c t -> (b c) t"), R2(cm1_ag)),
         ("AllToAll", R2d(dt_ai), R2d(dt_ao)),
         ("AllToAll", R2d(dta_ai), R2d(dta_ao))],
        cons_seq_pairs(X_cons, X_ao, 4)
        + cons_seq_pairs(bm1_cons, bm1_ag, B)
        + cons_seq_pairs(cm1_cons, cm1_ag, B)
        + [(dt_cons[:, s * LLOC:(s + 1) * LLOC], dt_ao[s]) for s in range(8)]
        + [(dta_cons[:, s * LLOC:(s + 1) * LLOC], dta_ao[s]) for s in range(8)])

    p.begin_stage("u1")
    emit_l1ssd(p, dict(X=X_cons, BuB=bm1_cons, CuB=cm1_cons, **w_u1,
                       dt=dt_cons, dtA=dta_cons, y=y1_prod))
    p.end_stage()
    boundary(
        tslice_pairs(y1_ai, y1_prod, 4),
        [("AllToAll", R2u(y1_ai), R2u(y1_ao))],
        [])

    p.begin_stage("p1")
    emit_l1post(p, dict(x=x1_loc, ab=ab1,
                        y=y1_ao.rearrange("(b s) i c t -> b (s i c) t", b=2),
                        z=z_loc, **w_p1, s1=s1_loc))
    p.end_stage()
    p.join()

    p.begin_stage("f1")
    emit_l1ffn(p, dict(x=x1_loc, ab=ab1, s1=s1_loc, **w_f1, x2=x2_loc))
    p.end_stage()
    p.join()

    p.begin_stage("a2")
    emit_l2pre(p, dict(x=x2_loc, **w_a2, q=q2_loc, k=k2_loc, v=v2_loc, ab=ab2))
    p.end_stage()
    boundary(
        [],
        [("AllGather", k2_loc.rearrange("b f t -> (b f) t"),
          k2_ag.rearrange("r b f t -> (r b f) t")),
         ("AllGather", v2_loc.rearrange("b f t -> (b f) t"),
          v2_ag.rearrange("r b f t -> (r b f) t"))],
        [])

    p.begin_stage("t2")
    emit_l2attn(p, dict(q=q2_loc, k=k2_ag, v=v2_ag, state=st_loc))
    p.end_stage()
    p.join()

    p.begin_stage("p2")
    emit_l2post(p, dict(x=x2_loc, ab=ab2, state=st_loc, **w_p2, x3=x3_loc))
    p.end_stage()
    p.join()

    p.begin_stage("a3")
    emit_l3pre(p, dict(x=x3_loc, **w_a3,
                       q=q3_ai.rearrange("(b r) i d t -> b (r i d) t", b=2),
                       k=k3_ai.rearrange("(b r) i d t -> b (r i d) t", b=2),
                       v=v3_ai.rearrange("(b r) i d t -> b (r i d) t", b=2),
                       ab=ab3))
    p.end_stage()
    boundary(
        [],
        [("AllToAll", R2h(q3_ai), R2h(q3_ao)),
         ("AllToAll", R2h(k3_ai), R2h(k3_ao)),
         ("AllToAll", R2h(v3_ai), R2h(v3_ao))],
        cons_seq_pairs(q3_cons, q3_ao, 4)
        + cons_seq_pairs(k3_cons, k3_ao, 4)
        + cons_seq_pairs(v3_cons, v3_ao, 4))

    p.begin_stage("t3")
    emit_l3attn(p, dict(q=q3_cons, k=k3_cons, v=v3_cons, **w_t3, ctx=ctx_prod))
    p.end_stage()
    boundary(
        tslice_pairs(ctx_ai, ctx_prod, 4),
        [("AllToAll", R2h(ctx_ai), R2h(ctx_ao))],
        [])

    p.begin_stage("f3")
    emit_l3post(p, dict(x=x3_loc, ab=ab3,
                        ctx=ctx_ao.rearrange("(b s) i d t -> b (s i d) t", b=2),
                        **w_f3, out=out_d))
    p.end_stage()
    p.join()
    return p


def kernel(**inp):
    import ml_dtypes
    A = lambda a: np.ascontiguousarray(np.asarray(a, np.float32))
    tr = lambda a: np.ascontiguousarray(np.asarray(a, np.float32).T)
    x0 = np.ascontiguousarray(np.asarray(inp["L0_attn_hidden_states"]).transpose(0, 2, 1))
    xpad = np.concatenate([np.zeros((B, 1024, HALO), np.float32), x0], axis=2)

    rel = np.asarray(inp["L3_attn_rel_emb"])
    bmb = np.zeros((5, 128, 512), np.float32)
    bab = np.zeros((5, 128, 512), np.float32)
    for di, d in enumerate([-1, 0, 1, 2, 3]):
        kk = np.arange(128)[:, None]
        qq = np.arange(512)[None, :]
        diff = qq - kk - 128 * d
        dist = np.clip(np.abs(diff), 0, 16)
        bmb[di] = rel[dist]
        bab[di] = np.where(diff >= 0, 0.0, -1e30)
    r16 = np.ascontiguousarray(np.repeat(rel[16:17], 128)[:, None].astype(np.float32))

    shared = {
        "a0_W_in": A(inp["L0_attn_W_in"]), "a0_norm_w": A(inp["L0_attn_norm_w"]),
        "a0_conv_w": A(inp["L0_attn_conv_w"]), "a0_conv_b": A(inp["L0_attn_conv_b"]),
        "a0_W_x": A(inp["L0_attn_W_x"]),
        "p0_D": A(inp["L0_attn_D"]), "p0_W_out": A(inp["L0_attn_W_out"]),
        "f0_gate_w": A(inp["L0_ffn_gate_w"]), "f0_w1": A(inp["L0_ffn_w1"]),
        "f0_b1": A(inp["L0_ffn_b1"]), "f0_w2": A(inp["L0_ffn_w2"]),
        "f0_b2": A(inp["L0_ffn_b2"]), "f0_sw1": A(inp["L0_ffn_sw1"]),
        "f0_sb1": A(inp["L0_ffn_sb1"]), "f0_sw2": A(inp["L0_ffn_sw2"]),
        "f0_sb2": A(inp["L0_ffn_sb2"]),
        "a1_W_in": A(inp["L1_attn_W_in"]), "a1_ln_w": A(inp["L1_attn_ln_w"]),
        "a1_ln_b": A(inp["L1_attn_ln_b"]), "a1_conv_w": A(inp["L1_attn_conv_w"]),
        "a1_conv_b": A(inp["L1_attn_conv_b"]), "a1_dt_bias": A(inp["L1_attn_dt_bias"]),
        "a1_A_log": A(inp["L1_attn_A_log"]),
        "p1_rms_w": A(inp["L1_attn_rms_w"]), "p1_W_out": A(inp["L1_attn_W_out"]),
        "f1_router_w": A(inp["L1_ffn_router_w"]), "f1_router_b": A(inp["L1_ffn_router_b"]),
        "f1_expert_w": A(inp["L1_ffn_expert_w"]), "f1_expert_b": A(inp["L1_ffn_expert_b"]),
        "a2_ln_w": A(inp["L2_attn_ln_w"]), "a2_ln_b": A(inp["L2_attn_ln_b"]),
        "a2_Wq": A(inp["L2_attn_Wq"]), "a2_Wk": A(inp["L2_attn_Wk"]),
        "a2_Wv": A(inp["L2_attn_Wv"]), "a2_bq": A(inp["L2_attn_bq"]),
        "a2_bk": A(inp["L2_attn_bk"]), "a2_bv": A(inp["L2_attn_bv"]),
        "p2_Wo": A(inp["L2_attn_Wo"]), "p2_bo": A(inp["L2_attn_bo"]),
        "p2_wfT": tr(inp["L2_ffn_wf"]), "p2_wiT": tr(inp["L2_ffn_wi"]),
        "p2_wcT": tr(inp["L2_ffn_wc"]), "p2_ufT": tr(inp["L2_ffn_uf"]),
        "p2_uiT": tr(inp["L2_ffn_ui"]), "p2_ucT": tr(inp["L2_ffn_uc"]),
        "p2_woT": tr(inp["L2_ffn_wo"]), "p2_uoT": tr(inp["L2_ffn_uo"]),
        "p2_bf": A(inp["L2_ffn_bf"]), "p2_bi": A(inp["L2_ffn_bi"]),
        "p2_bc": A(inp["L2_ffn_bc"]), "p2_bo2": A(inp["L2_ffn_bo"]),
        "p2_in_c": A(inp["L2_ffn_in_c"]),
        "a3_Wq": A(inp["L3_attn_Wq"]), "a3_Wk": A(inp["L3_attn_Wk"]),
        "a3_Wv": A(inp["L3_attn_Wv"]),
        "t3_bandmul": bmb, "t3_bandadd": bab, "t3_rel16": r16,
        "f3_Wo": A(inp["L3_attn_Wo"]), "f3_bo": A(inp["L3_attn_bo"]),
        "f3_gate_w": A(inp["L3_ffn_gate_w"]), "f3_w1": A(inp["L3_ffn_w1"]),
        "f3_b1": A(inp["L3_ffn_b1"]), "f3_w2": A(inp["L3_ffn_w2"]),
        "f3_b2": A(inp["L3_ffn_b2"]),
    }
    wdt = np.asarray(inp["L0_attn_W_dt"])
    bdt = np.asarray(inp["L0_attn_b_dt"])
    a0m = np.asarray(inp["L0_attn_A"])
    d1 = np.asarray(inp["L1_attn_D"])
    maps = []
    for c in range(NCORES):
        cs = slice(c * 256, (c + 1) * 256)
        units = [c * 4 + i for i in range(4)]
        hmask = np.zeros((128, 8), np.float32)
        if c > 0:
            hmask[:, c - 1] = 1.0
        bsel = np.zeros((128, 2), np.float32)
        bsel[:, c // 4] = 1.0
        m = dict(shared)
        m["xh"] = np.ascontiguousarray(xpad[:, :, c * LLOC:c * LLOC + TH])
        m["s0_W_dt"] = np.ascontiguousarray(wdt[:, cs].astype(np.float32))
        m["s0_b_dt"] = np.ascontiguousarray(bdt[cs].astype(np.float32))
        m["s0_A"] = np.ascontiguousarray(a0m[cs, :].astype(np.float32))
        m["a1_hmask"] = hmask
        m["u1_bsel"] = bsel
        m["u1_Du"] = np.asarray([d1[u % 16] for u in units],
                                np.float32)[None, :]
        maps.append(m)
    pgm = build_fused()
    res = pgm.run(maps)
    out = np.concatenate([np.asarray(r["out"]) for r in res.results], axis=-1)
    return np.ascontiguousarray(out.transpose(0, 2, 1))
